# version 06
# baseline (speedup 1.0000x reference)
import numpy as np
import ml_dtypes
import concourse.bass as bass
import concourse.mybir as mybir
from concourse.ap import AP
from concourse.bass_utils import run_bass_kernel_spmd
from contextlib import ExitStack

F32 = mybir.dt.float32
BF16 = mybir.dt.bfloat16
I32 = mybir.dt.int32
AF = mybir.ActivationFunctionType
ALU = mybir.AluOpType
AX = mybir.AxisListType.X

NCORES = 8
T = 4096
D = 1024
NST = 32
TWO_PI = 6.283185307179586
INV_2PI = 0.15915494309189535
MAGIC = 12582912.0
HALF_PI = 1.5707963267948966


class Ctr:
    def __init__(self, h):
        self.h = h
        self.v = 0

    def inc(self, n=1):
        self.v += n
        return self.h


def build_nc():
    nc = bass.Bass("TRN2", target_bir_lowering=False)

    x_in = nc.dram_tensor("x_in", [T, D], F32, kind="ExternalInput")
    in_W = nc.dram_tensor("in_W", [D, D], F32, kind="ExternalInput")
    in_b = nc.dram_tensor("in_b", [D], F32, kind="ExternalInput")
    log_dt = nc.dram_tensor("log_dt", [128], F32, kind="ExternalInput")
    C_p = nc.dram_tensor("C_p", [128, NST, 2], F32, kind="ExternalInput")
    lar_t = nc.dram_tensor("lar_t", [128, NST], F32, kind="ExternalInput")
    aim_t = nc.dram_tensor("aim_t", [128, NST], F32, kind="ExternalInput")
    dsk_t = nc.dram_tensor("dsk_t", [128], F32, kind="ExternalInput")
    glu_W = nc.dram_tensor("glu_W", [2 * D, D], F32, kind="ExternalInput")
    glu_b = nc.dram_tensor("glu_b", [2 * D], F32, kind="ExternalInput")
    out_W = nc.dram_tensor("out_W", [D, D], F32, kind="ExternalInput")
    out_b = nc.dram_tensor("out_b", [D], F32, kind="ExternalInput")
    esel = nc.dram_tensor("esel", [16, D], F32, kind="ExternalInput")
    keys_t = nc.dram_tensor("keys_t", [16, D, 256], F32, kind="ExternalInput")
    vals_t = nc.dram_tensor("vals_t", [16, 256, D], F32, kind="ExternalInput")
    rscale = nc.dram_tensor("rscale", [1], F32, kind="ExternalInput")
    ident = nc.dram_tensor("ident", [128, 128], BF16, kind="ExternalInput")
    trQ_t = nc.dram_tensor("trQ_t", [128, 128], F32, kind="ExternalInput")
    trK_t = nc.dram_tensor("trK_t", [128, 128], F32, kind="ExternalInput")
    trP_t = nc.dram_tensor("trP_t", [128, 1], F32, kind="ExternalInput")
    trP2_t = nc.dram_tensor("trP2_t", [128, 2], F32, kind="ExternalInput")
    uoff_t = nc.dram_tensor("uoff_t", [1], I32, kind="ExternalInput")
    yoff_t = nc.dram_tensor("yoff_t", [1], I32, kind="ExternalInput")
    out_t = nc.dram_tensor("out", [T, D], F32, kind="ExternalOutput")

    xnT_d = nc.dram_tensor("xnT_d", [D, T], BF16)
    w1bf = nc.dram_tensor("w1bf", [D, D], BF16)
    glubf = nc.dram_tensor("glubf", [2 * D, D], BF16)
    outbf = nc.dram_tensor("outbf", [D, D], BF16)
    eselbf = nc.dram_tensor("eselbf", [16, D], BF16)
    keysbf = nc.dram_tensor("keysbf", [16, D, 256], BF16)
    valsbf = nc.dram_tensor("valsbf", [16, 256, D], BF16)
    prow = nc.dram_tensor("prow", [4, 4096], F32)
    k2r = nc.dram_tensor("k2r", [256, 128], BF16)
    u_loc = nc.dram_tensor("u_loc", [T, D], BF16)
    u_gth = nc.dram_tensor("u_gth", [NCORES * T, D], BF16, addr_space="Shared")
    y_loc = nc.dram_tensor("y_loc", [128, 4 * 64 * 128], BF16)
    y_gth = nc.dram_tensor("y_gth", [NCORES * 128, 4 * 64 * 128], BF16, addr_space="Shared")
    gT_d = nc.dram_tensor("gT_d", [16, T], BF16)
    d2_d = nc.dram_tensor("d2_d", [64, 4 * 4096], BF16)
    pp_d = nc.dram_tensor("pp_d", [128, 4 * 4096], BF16)
    ffnP = nc.dram_tensor("ffnP", [T, D], F32)
    ffnQ = nc.dram_tensor("ffnQ", [T, D], F32)

    es = ExitStack()

    def sbuf(name, shape, dt, st=None):
        return (st or es).enter_context(nc.sbuf_tensor(name, shape, dt))

    def sem(name):
        return Ctr(es.enter_context(nc.semaphore(name)))

    s_id = sbuf("s_id", [128, 128], BF16)
    s_trP = sbuf("s_trP", [128, 1], F32)
    s_trP2 = sbuf("s_trP2", [128, 2], F32)
    s_hpi = sbuf("s_hpi", [128, 1], F32)
    s_mu = sbuf("s_mu", [128, 32], F32)
    s_rs = sbuf("s_rs", [128, 32], F32)
    s_inb = sbuf("s_inb", [128, D], F32)
    s_outb = sbuf("s_outb", [128, D], F32)
    s_glb = sbuf("s_glb", [128, 16], F32)
    s_g = sbuf("s_g", [128, 32 * 16], F32)
    s_rsc = sbuf("s_rsc", [128, 1], F32)
    s_uoff = sbuf("s_uoff", [1, 1], I32)
    s_yoff = sbuf("s_yoff", [1, 1], I32)
    s_q2dr = sbuf("s_q2dr", [64, 128], F32)
    s_q2di = sbuf("s_q2di", [64, 128], F32)
    s_q2er = sbuf("s_q2er", [64, 128], F32)
    s_q2ei = sbuf("s_q2ei", [64, 128], F32)
    s_W1 = sbuf("s_W1", [64, 128], F32)
    s_W2 = sbuf("s_W2", [64, 128], F32)
    s_W3 = sbuf("s_W3", [64, 128], F32)

    p_a = es.enter_context(nc.psum_tensor("p_a", [128, 512], F32))
    p_b = es.enter_context(nc.psum_tensor("p_b", [128, 512], F32))
    p_c = es.enter_context(nc.psum_tensor("p_c", [128, 512], F32))
    p_d = es.enter_context(nc.psum_tensor("p_d", [128, 512], F32))
    p_sel = es.enter_context(nc.psum_tensor("p_sel", [128, 16], F32))
    p_tr = es.enter_context(nc.psum_tensor("p_tr", [128, 128], BF16))
    p_tr2 = es.enter_context(nc.psum_tensor("p_tr2", [128, 128], BF16))
    p_tr3 = es.enter_context(nc.psum_tensor("p_tr3", [128, 128], BF16))

    smc = sem("smc"); smprm = sem("smprm")
    smcastA = sem("smcastA"); smcastB = sem("smcastB"); smcastC = sem("smcastC")
    sm0 = [sem(f"sm0_{i}") for i in range(20)]
    smx = [sem("smx0"), sem("smx1")]
    smln = sem("smln"); smrs = sem("smrs"); smlv = sem("smlv")
    smtp = sem("smtp"); smcpy = sem("smcpy"); smxst = sem("smxst")
    smyt1 = sem("smyt1"); smytr = sem("smytr")
    smw2 = sem("smw2")
    smu1 = sem("smu1"); smu2 = sem("smu2")
    smu3 = [sem("smu3A"), sem("smu3B")]
    smlg = sem("smlg"); smsg = sem("smsg"); smtk = sem("smtk")
    smgt = sem("smgt"); smge = sem("smge"); smgo = sem("smgo")
    smcc = sem("smcc"); smucs = sem("smucs")
    sme1 = sem("sme1"); sme2 = sem("sme2")
    smr1 = sem("smr1")
    smkv = [sem("smkv0"), sem("smkv1")]
    smxw2 = [sem("smxw2_0"), sem("smxw2_1")]
    smgx = sem("smgx")
    smk1 = sem("smk1"); smk2 = sem("smk2"); smk3 = sem("smk3")
    smv1 = sem("smv1"); smv2 = sem("smv2")
    smv3 = [sem("smv3A"), sem("smv3B")]
    smyf = sem("smyf"); smwt3 = sem("smwt3")
    smg1 = sem("smg1"); smg2 = sem("smg2"); smg3 = sem("smg3")
    smo1 = sem("smo1"); smo2 = sem("smo2")
    smf1 = [sem("smf1A"), sem("smf1B")]
    smf2 = [sem("smf2A"), sem("smf2B")]
    smf3 = [sem("smf3A"), sem("smf3B")]
    smout = [sem("smoutA"), sem("smoutB")]

    def wrap_sin_cos(vector, src_ang, dst_sin_arg, dst_cos_arg):
        vector.tensor_scalar(dst_sin_arg, src_ang, INV_2PI, MAGIC, ALU.mult, ALU.add)
        vector.drain()
        vector.tensor_scalar(dst_sin_arg, dst_sin_arg, MAGIC, None, ALU.subtract)
        vector.drain()
        vector.scalar_tensor_tensor(dst_sin_arg, dst_sin_arg, -TWO_PI, src_ang, ALU.mult, ALU.add)
        vector.drain()
        vector.tensor_scalar_min(dst_sin_arg, dst_sin_arg, 3.141592653589793)
        vector.drain()
        vector.tensor_scalar_max(dst_sin_arg, dst_sin_arg, -3.141592653589793)
        vector.drain()

    def act_sin_cos(scalar, wrapped, cos_out, pcount):
        scalar.activation(cos_out, wrapped, AF.Abs)
        scalar.drain()
        scalar.activation(cos_out, cos_out, AF.Sin, bias=s_hpi[0:pcount, 0:1], scale=-1.0)
        return scalar.activation(wrapped, wrapped, AF.Sin)

    with nc.Block() as blk:
        @blk.sync
        def _(sync):
            sync.dma_start(out=s_id[:], in_=ident.ap()).then_inc(smc.inc(16), 16)
            sync.dma_start(out=s_trP[:], in_=trP_t.ap()).then_inc(smc.inc(16), 16)
            sync.dma_start(out=s_trP2[:], in_=trP2_t.ap()).then_inc(smc.inc(16), 16)
            sync.dma_start(out=s_inb[:], in_=AP(in_b, 0, [[0, 128], [1, D]])).then_inc(smc.inc(16), 16)
            sync.dma_start(out=s_outb[:], in_=AP(out_b, 0, [[0, 128], [1, D]])).then_inc(smc.inc(16), 16)
            sync.dma_start(out=s_glb[:], in_=AP(glu_b, 0, [[1, 128], [128, 16]]),
                           allow_slow_non_contiguous=True).then_inc(smc.inc(16), 16)
            sync.dma_start(out=s_rsc[:], in_=AP(rscale, 0, [[0, 128], [1, 1]])).then_inc(smc.inc(16), 16)
            sync.dma_start(out=s_uoff[:], in_=uoff_t.ap().unsqueeze(0)).then_inc(smc.inc(16), 16)
            sync.dma_start(out=s_yoff[:], in_=yoff_t.ap().unsqueeze(0)).then_inc(smc.inc(16), 16)

        @blk.gpsimd
        def _(gp):
            gp.dma_start(out=w1bf.ap(), in_=in_W.ap()).then_inc(smcastA.inc(16), 16)
            gp.dma_start(out=eselbf.ap(), in_=esel.ap()).then_inc(smcastA.inc(16), 16)
            gp.wait_ge(smx[1].h, 16)
            gp.dma_start(out=glubf.ap(), in_=glu_W.ap()).then_inc(smcastB.inc(16), 16)
            gp.dma_start(out=outbf.ap(), in_=out_W.ap()).then_inc(smcastB.inc(16), 16)
            gp.dma_start(out=keysbf.ap(), in_=keys_t.ap()).then_inc(smcastC.inc(16), 16)
            gp.dma_start(out=valsbf.ap(), in_=vals_t.ap()).then_inc(smcastC.inc(16), 16)

        g0 = ExitStack()
        s_dtc = sbuf("s_dtc", [128, 1], F32, g0)
        s_lar = sbuf("s_lar", [128, NST], F32, g0)
        s_aim = sbuf("s_aim", [128, NST], F32, g0)
        s_cr = sbuf("s_cr", [128, NST], F32, g0)
        s_ci = sbuf("s_ci", [128, NST], F32, g0)
        s_dskc = sbuf("s_dskc", [128, 1], F32, g0)
        s_ema = sbuf("s_ema", [128, NST], F32, g0)
        s_dtar = sbuf("s_dtar", [128, NST], F32, g0)
        s_dtai = sbuf("s_dtai", [128, NST], F32, g0)
        s_absw = sbuf("s_absw", [128, NST], F32, g0)
        s_ang = sbuf("s_ang", [128, NST], F32, g0)
        s_ang2 = sbuf("s_ang2", [128, NST], F32, g0)
        s_sinw = sbuf("s_sinw", [128, NST], F32, g0)
        s_cosw = sbuf("s_cosw", [128, NST], F32, g0)
        s_t1 = sbuf("s_t1", [128, NST], F32, g0)
        s_t2 = sbuf("s_t2", [128, NST], F32, g0)
        s_t3 = sbuf("s_t3", [128, NST], F32, g0)
        s_t4 = sbuf("s_t4", [128, NST], F32, g0)
        s_den = sbuf("s_den", [128, NST], F32, g0)
        s_trr = sbuf("s_trr", [128, NST], F32, g0)
        s_tii = sbuf("s_tii", [128, NST], F32, g0)
        s_er = sbuf("s_er", [128, NST], F32, g0)
        s_ei = sbuf("s_ei", [128, NST], F32, g0)
        s_wa = sbuf("s_wa", [64, 128], F32, g0)
        s_wb = sbuf("s_wb", [64, 128], F32, g0)

        @blk.sync
        def _(sync):
            sync.dma_start(out=s_dtc[:], in_=log_dt.ap().unsqueeze(1)).then_inc(smprm.inc(16), 16)
            sync.dma_start(out=s_lar[:], in_=lar_t.ap()).then_inc(smprm.inc(16), 16)
            sync.dma_start(out=s_aim[:], in_=aim_t.ap()).then_inc(smprm.inc(16), 16)
            sync.dma_start(out=s_cr[:], in_=AP(C_p, 0, [[NST * 2, 128], [2, NST]]),
                           allow_slow_non_contiguous=True).then_inc(smprm.inc(16), 16)
            sync.dma_start(out=s_ci[:], in_=AP(C_p, 1, [[NST * 2, 128], [2, NST]]),
                           allow_slow_non_contiguous=True).then_inc(smprm.inc(16), 16)
            sync.dma_start(out=s_dskc[:], in_=dsk_t.ap().unsqueeze(1)).then_inc(smprm.inc(16), 16)

        @blk.scalar
        def _(scalar):
            scalar.wait_ge(smprm.h, smprm.v)
            scalar.activation(s_dtc[:], s_dtc[:], AF.Exp)
            scalar.drain()
            scalar.activation(s_ema[:], s_lar[:], AF.Exp).then_inc(sm0[0].inc(), 1)

        @blk.vector
        def _(vector):
            vector.wait_ge(sm0[0].h, 1)
            vector.memset(s_hpi[:], HALF_PI)
            vector.drain()
            vector.tensor_scalar(s_dtar[:], s_ema[:], s_dtc[:, 0:1], -1.0, ALU.mult, ALU.mult)
            vector.tensor_scalar(s_dtai[:], s_aim[:], s_dtc[:, 0:1], None, ALU.mult)
            vector.drain()
            wrap_sin_cos(vector, s_dtai[:], s_ang[:], s_ang2[:])
            vector.tensor_copy(s_t1[:], s_dtar[:])
            vector.drain().then_inc(sm0[1].inc(), 1)

        @blk.scalar
        def _(scalar):
            scalar.wait_ge(sm0[1].h, 1)
            scalar.activation(s_absw[:], s_dtar[:], AF.Exp)
            act_sin_cos(scalar, s_ang[:], s_cosw[:], 128)
            scalar.drain()
            scalar.activation(s_sinw[:], s_ang[:], AF.Identity).then_inc(sm0[2].inc(), 1)

        @blk.vector
        def _(vector):
            vector.wait_ge(sm0[2].h, 1)
            vector.tensor_mul(s_t1[:], s_absw[:], s_cosw[:])
            vector.tensor_mul(s_t2[:], s_absw[:], s_sinw[:])
            vector.tensor_mul(s_den[:], s_ema[:], s_ema[:])
            vector.tensor_mul(s_t3[:], s_aim[:], s_aim[:])
            vector.drain()
            vector.tensor_add(s_den[:], s_den[:], s_t3[:])
            vector.drain()
            vector.reciprocal(s_den[:], s_den[:])
            vector.tensor_scalar(s_t1[:], s_t1[:], -1.0, None, ALU.add)
            vector.drain()
            vector.tensor_mul(s_t3[:], s_t1[:], s_ema[:])
            vector.tensor_mul(s_t4[:], s_t2[:], s_aim[:])
            vector.drain()
            vector.tensor_sub(s_trr[:], s_t4[:], s_t3[:])
            vector.drain()
            vector.tensor_mul(s_t3[:], s_t2[:], s_ema[:])
            vector.tensor_mul(s_t4[:], s_t1[:], s_aim[:])
            vector.drain()
            vector.tensor_add(s_tii[:], s_t3[:], s_t4[:])
            vector.drain()
            vector.tensor_scalar(s_tii[:], s_tii[:], -1.0, None, ALU.mult)
            vector.drain()
            vector.tensor_mul(s_t3[:], s_cr[:], s_trr[:])
            vector.tensor_mul(s_t4[:], s_ci[:], s_tii[:])
            vector.drain()
            vector.tensor_sub(s_er[:], s_t3[:], s_t4[:])
            vector.drain()
            vector.tensor_mul(s_t3[:], s_cr[:], s_tii[:])
            vector.tensor_mul(s_t4[:], s_ci[:], s_trr[:])
            vector.drain()
            vector.tensor_add(s_ei[:], s_t3[:], s_t4[:])
            vector.drain()
            vector.tensor_mul(s_er[:], s_er[:], s_den[:])
            vector.tensor_mul(s_ei[:], s_ei[:], s_den[:])
            vector.drain().then_inc(sm0[3].inc(), 1)

        @blk.sync
        def _(sync):
            sync.wait_ge(sm0[3].h, 1)
            for row, src in [(0, s_dtar), (1, s_dtai), (2, s_er), (3, s_ei)]:
                sync.dma_start(out=AP(prow, row * 4096, [[NST, 128], [1, NST]]),
                               in_=src[:]).then_inc(sm0[4].inc(16), 16)
            sync.wait_ge(sm0[4].h, sm0[4].v)
            for dst, row in [(s_q2dr, 0), (s_q2di, 1), (s_q2er, 2), (s_q2ei, 3)]:
                sync.dma_start(out=dst[0:32, :], in_=AP(prow, row * 4096, [[1, NST], [NST, 128]]),
                               allow_slow_non_contiguous=True).then_inc(sm0[5].inc(16), 16)
                sync.dma_start(out=dst[32:64, :], in_=AP(prow, row * 4096, [[1, NST], [NST, 128]]),
                               allow_slow_non_contiguous=True).then_inc(sm0[5].inc(16), 16)

        @blk.vector
        def _(vector):
            vector.wait_ge(sm0[5].h, sm0[5].v)
            vector.tensor_scalar(s_W1[:], s_q2dr[:], 128.0, None, ALU.mult)
            vector.tensor_scalar(s_W2[:], s_q2di[:], 128.0, None, ALU.mult)
            vector.drain()
            wrap_sin_cos(vector, s_W2[:], s_wa[:], s_wb[:])
            vector.drain().then_inc(sm0[6].inc(), 1)

        @blk.scalar
        def _(scalar):
            scalar.wait_ge(sm0[6].h, 1)
            scalar.activation(s_W1[:], s_W1[:], AF.Exp)
            act_sin_cos(scalar, s_wa[:], s_wb[:], 64).then_inc(sm0[7].inc(), 1)

        @blk.vector
        def _(vector):
            vector.wait_ge(sm0[7].h, 1)
            vector.tensor_mul(s_wa[:], s_W1[:], s_wa[:])
            vector.drain()
            vector.tensor_mul(s_W1[:], s_W1[:], s_wb[:])
            vector.drain()
            vector.tensor_scalar(s_W2[0:32, :], s_wa[0:32, :], -1.0, None, ALU.mult)
            vector.tensor_copy(s_W2[32:64, :], s_wa[32:64, :])
            vector.tensor_copy(s_W3[0:32, :], s_wa[0:32, :])
            vector.tensor_scalar(s_W3[32:64, :], s_wa[32:64, :], -1.0, None, ALU.mult)
            vector.drain().then_inc(sm0[8].inc(), 1)

        g2 = ExitStack()
        s_x = [sbuf("s_x0", [128, D], F32, g2), sbuf("s_x1", [128, D], F32, g2)]
        s_xsq = sbuf("s_xsq", [128, D], F32, g2)
        s_xn = [sbuf("s_xn0", [128, D], BF16, g2), sbuf("s_xn1", [128, D], BF16, g2)]
        s_st = sbuf("s_st", [128, 8], F32, g2)
        w1t = [sbuf(f"w1t{k}", [128, D], BF16, g2) for k in range(8)]
        selT = sbuf("selT", [128, 8 * 16], BF16, g2)
        s_xg = [[sbuf(f"s_xg{g}_{k}", [128, 1024], BF16, g2) for k in range(8)] for g in range(2)]
        s_ust = [sbuf(f"s_ust{pp}", [128, D], BF16, g2) for pp in range(2)]
        s_lgo = sbuf("s_lgo", [128, 16], F32, g2)
        s_lpr = sbuf("s_lpr", [128, 16], F32, g2)
        s_lgw = sbuf("s_lgw", [128, 16], F32, g2)
        s_lmx = sbuf("s_lmx", [128, 1], F32, g2)
        s_leq = sbuf("s_leq", [128, 16], F32, g2)
        s_lmk = sbuf("s_lmk", [128, 16], F32, g2)
        s_lsum = sbuf("s_lsum", [128, 1], F32, g2)
        s_gbf = sbuf("s_gbf", [128, 32 * 16], BF16, g2)
        s_gT = sbuf("s_gT", [16, T], BF16, g2)

        @blk.sync
        def _(sync):
            sync.dma_start(out=s_x[0][:], in_=AP(x_in, 0, [[D, 128], [1, D]])).then_inc(smx[0].inc(16), 16)
            sync.dma_start(out=s_x[1][:], in_=AP(x_in, 128 * D, [[D, 128], [1, D]])).then_inc(smx[1].inc(16), 16)
            sync.wait_ge(smcastA.h, 32)
            for k in range(8):
                sync.dma_start_transpose(w1t[k][:],
                                         AP(w1bf, 128 * k, [[D, D], [1, 128]])).then_inc(smw2.inc(16), 16)
                sync.dma_start_transpose(selT[:, 16 * k:16 * k + 16],
                                         AP(eselbf, 128 * k, [[D, 16], [1, 128]])).then_inc(smw2.inc(16), 16)
            for i in range(32):
                if i + 2 < 32:
                    sync.wait_ge(smlv.h, i + 1)
                    sync.dma_start(out=s_x[i % 2][:],
                                   in_=AP(x_in, (i + 2) * 128 * D, [[D, 128], [1, D]])).then_inc(smx[i % 2].inc(16), 16)
                sync.wait_ge(smu2.h, i + 1)
                sync.dma_start(out=AP(u_loc, i * 128 * D, [[D, 128], [1, D]]),
                               in_=s_ust[i % 2][:]).then_inc(smu3[i % 2].inc(16), 16)
                if i % 8 == 7:
                    g = i // 8
                    sync.wait_ge(smcpy.h, 8 * (i + 1))
                    for k in range(8):
                        sync.dma_start(out=AP(xnT_d, (128 * k) * T + 1024 * g, [[T, 128], [1, 1024]]),
                                       in_=s_xg[g % 2][k][:]).then_inc(smxst.inc(16), 16)
            sync.wait_ge(smge.h, 32)
            sync.dma_start(out=gT_d.ap(), in_=s_gT[:]).then_inc(smgo.inc(16), 16)

        @blk.tensor
        def _(tensor):
            tensor.wait_ge(smc.h, smc.v)
            for i in range(32):
                tensor.wait_ge(smlv.h, i + 1)
                for k in range(8):
                    t_idx = 8 * i + k
                    if t_idx >= 2:
                        tensor.wait_ge(smcpy.h, t_idx - 1)
                    tensor.transpose([p_tr, p_tr2][k % 2][:], s_xn[i % 2][:, 128 * k:128 * k + 128],
                                     s_id[:]).then_inc(smtp.inc(), 1)
                if i == 0:
                    tensor.wait_ge(smw2.h, 256)
                tensor.wait_ge(smcpy.h, 8 * (i + 1))
                if i >= 1:
                    tensor.wait_ge(smu2.h, i)
                pu = [(p_a, p_b), (p_c, p_d)][i % 2]
                col = 128 * (i % 8)
                gb = (i // 8) % 2
                for k in range(8):
                    lhs = s_xg[gb][k][:, col:col + 128]
                    tensor.matmul(pu[0][:], lhs, w1t[k][:, 0:512], start=(k == 0), stop=(k == 7))
                    tensor.matmul(pu[1][:], lhs, w1t[k][:, 512:1024], start=(k == 0), stop=(k == 7))
                    mm = tensor.matmul(p_sel[:], lhs, selT[:, 16 * k:16 * k + 16],
                                       start=(k == 0), stop=(k == 7))
                mm.then_inc(smu1.inc(), 1)
            for ch in range(32):
                tensor.wait_ge(smtk.h, ch + 1)
                if ch >= 1:
                    tensor.wait_ge(smge.h, ch)
                tensor.transpose(p_tr3[0:16, :], s_gbf[:, 16 * ch:16 * ch + 16], s_id[:]).then_inc(smgt.inc(), 1)

        @blk.scalar
        def _(scalar):
            for i in range(32):
                scalar.wait_ge(smln.h, i + 1)
                scalar.activation(s_st[:, 5:6], s_st[:, 4:5], AF.Ln)
                scalar.drain()
                scalar.activation(s_rs[:, i:i + 1], s_st[:, 5:6], AF.Exp, scale=-0.5).then_inc(smrs.inc(), 1)
                g = i // 8
                if i % 8 == 0 and g >= 2:
                    scalar.wait_ge(smxst.h, 128 * (g - 1))
                for k in range(8):
                    scalar.wait_ge(smtp.h, 8 * i + k + 1)
                    scalar.activation(s_xg[g % 2][k][:, 128 * (i % 8):128 * (i % 8) + 128],
                                      [p_tr, p_tr2][k % 2][:], AF.Identity).then_inc(smcpy.inc(), 1)
                if i >= 1:
                    scalar.wait_ge(smlg.h, i)
                    scalar.activation(s_lpr[:], s_lgo[:], AF.Sigmoid).then_inc(smsg.inc(), 1)
            scalar.wait_ge(smlg.h, 32)
            scalar.activation(s_lpr[:], s_lgo[:], AF.Sigmoid).then_inc(smsg.inc(), 1)

        @blk.vector
        def _(vector):
            vector.wait_ge(smc.h, smc.v)

            def evac_topk(j):
                vector.wait_ge(smu1.h, j + 1)
                if j >= 2:
                    vector.wait_ge(smu3[j % 2].h, 16 * (j // 2))
                pu = [(p_a, p_b), (p_c, p_d)][j % 2]
                vector.tensor_add(s_ust[j % 2][:, 0:512], pu[0][:], s_inb[:, 0:512])
                vector.tensor_add(s_ust[j % 2][:, 512:1024], pu[1][:], s_inb[:, 512:1024])
                vector.tensor_copy(s_lgo[:], p_sel[:])
                vector.drain().then_inc(smu2.inc(), 1)
                vector.tensor_copy(s_lgw[:], s_lgo[:])
                vector.memset(s_lmk[:], 0.0)
                vector.drain().then_inc(smlg.inc(), 1)
                for r in range(4):
                    vector.reduce_max(s_lmx[:], s_lgw[:], axis=AX)
                    vector.drain()
                    vector.tensor_scalar(s_leq[:], s_lgw[:], s_lmx[:, 0:1], None, ALU.is_ge)
                    vector.drain()
                    vector.tensor_add(s_lmk[:], s_lmk[:], s_leq[:])
                    vector.scalar_tensor_tensor(s_lgw[:], s_leq[:], -1e30, s_lgw[:], ALU.mult, ALU.add)
                    vector.drain()
                vector.wait_ge(smsg.h, j + 1)
                vector.tensor_mul(s_lpr[:], s_lpr[:], s_lmk[:])
                vector.drain()
                vector.reduce_sum(s_lsum[:], s_lpr[:], axis=AX)
                vector.drain()
                vector.tensor_scalar(s_lsum[:], s_lsum[:], 1e-9, None, ALU.max)
                vector.drain()
                vector.reciprocal(s_lsum[:], s_lsum[:])
                vector.drain()
                vector.tensor_mul(s_lsum[:], s_lsum[:], s_rsc[:])
                vector.drain()
                vector.tensor_scalar(s_g[:, 16 * j:16 * j + 16], s_lpr[:], s_lsum[:, 0:1], None, ALU.mult)
                vector.drain()
                vector.tensor_copy(s_gbf[:, 16 * j:16 * j + 16], s_g[:, 16 * j:16 * j + 16])
                vector.drain().then_inc(smtk.inc(), 1)

            for i in range(33):
                if i < 32:
                    vector.wait_ge(smx[i % 2].h, 16 * (i // 2 + 1))
                    if i >= 2:
                        vector.wait_ge(smtp.h, 8 * i - 8)
                    xb = s_x[i % 2]
                    vector.tensor_mul(s_xsq[:], xb[:], xb[:])
                    vector.reduce_sum(s_st[:, 0:1], xb[:], axis=AX)
                    vector.drain()
                    vector.reduce_sum(s_st[:, 1:2], s_xsq[:], axis=AX)
                    vector.drain()
                    vector.tensor_scalar(s_mu[:, i:i + 1], s_st[:, 0:1], 1.0 / D, None, ALU.mult)
                    vector.tensor_scalar(s_st[:, 2:3], s_st[:, 1:2], 1.0 / D, None, ALU.mult)
                    vector.drain()
                    vector.tensor_mul(s_st[:, 3:4], s_mu[:, i:i + 1], s_mu[:, i:i + 1])
                    vector.drain()
                    vector.tensor_sub(s_st[:, 4:5], s_st[:, 2:3], s_st[:, 3:4])
                    vector.drain()
                    vector.tensor_scalar(s_st[:, 4:5], s_st[:, 4:5], 1e-5, None, ALU.add)
                    vector.drain().then_inc(smln.inc(), 1)
                    vector.wait_ge(smrs.h, i + 1)
                    vector.tensor_scalar(s_xn[i % 2][:], xb[:], s_mu[:, i:i + 1], s_rs[:, i:i + 1],
                                         ALU.subtract, ALU.mult)
                    vector.drain().then_inc(smlv.inc(), 1)
                if i >= 1:
                    evac_topk(i - 1)
            for ch in range(32):
                vector.wait_ge(smgt.h, ch + 1)
                vector.tensor_copy(s_gT[:, 128 * ch:128 * ch + 128], p_tr3[0:16, :])
                vector.drain().then_inc(smge.inc(), 1)


        gB = ExitStack()
        s_ba = sbuf("s_ba", [128, 4096], F32, gB)
        s_bb = sbuf("s_bb", [128, 4096], F32, gB)
        s_bc = sbuf("s_bc", [128, 4096], F32, gB)
        s_bd = sbuf("s_bd", [128, 4096], F32, gB)
        s_trK = sbuf("s_trK", [128, 128], F32, gB)
        s_ksb = sbuf("s_ksb", [128, 128], F32, gB)
        s_kbf = sbuf("s_kbf", [128, 128], BF16, gB)
        s_ktr = sbuf("s_ktr", [128, 128], BF16, gB)
        s_zro = sbuf("s_zro", [128, 128], BF16, gB)
        s_trQ0 = sbuf("s_trQ0", [128, 128], F32, gB)
        s_d2s = sbuf("s_d2s", [64, 4096], BF16, gB)
        s_ppt = sbuf("s_ppt", [128, 4096], BF16, gB)

        @blk.sync
        def _(sync):
            sync.wait_ge(sm0[3].h, 1)
            sync.wait_ge(sm0[4].h, sm0[4].v)
            sync.dma_start(out=s_trK[:], in_=trK_t.ap()).then_inc(sm0[9].inc(16), 16)
            sync.dma_start(out=s_ba[:], in_=AP(prow, 0, [[0, 128], [1, 4096]])).then_inc(sm0[9].inc(16), 16)
            sync.dma_start(out=s_bb[:], in_=AP(prow, 4096, [[0, 128], [1, 4096]])).then_inc(sm0[9].inc(16), 16)
            sync.wait_ge(sm0[12].h, 1)
            sync.dma_start(out=AP(pp_d, 0, [[16384, 128], [1, 4096]]),
                           in_=s_ppt[:]).then_inc(sm0[19].inc(16), 16)
            sync.wait_ge(sm0[12].h, 2)
            sync.dma_start(out=AP(pp_d, 4096, [[16384, 128], [1, 4096]]),
                           in_=s_ppt[:]).then_inc(sm0[19].inc(16), 16)
            sync.dma_start(out=s_ba[:], in_=AP(prow, 0, [[0, 128], [1, 4096]])).then_inc(sm0[9].inc(16), 16)
            sync.dma_start(out=s_bb[:], in_=AP(prow, 4096, [[0, 128], [1, 4096]])).then_inc(sm0[9].inc(16), 16)
            sync.wait_ge(sm0[12].h, 3)
            sync.dma_start(out=AP(pp_d, 8192, [[16384, 128], [1, 4096]]),
                           in_=s_ppt[:]).then_inc(sm0[19].inc(16), 16)
            sync.wait_ge(sm0[12].h, 4)
            sync.dma_start(out=AP(pp_d, 12288, [[16384, 128], [1, 4096]]),
                           in_=s_ppt[:]).then_inc(sm0[19].inc(16), 16)

        @blk.vector
        def _(vector):
            vector.wait_ge(smc.h, smc.v)
            for pas in range(2):
                vector.wait_ge(sm0[9].h, 16 + 32 * (pas + 1))
                if pas == 1:
                    vector.wait_ge(sm0[19].h, 32)
                ramp_r = s_trP[:, 0:1] if pas == 0 else s_trP2[:, 0:1]
                ramp_i = s_trP[:, 0:1] if pas == 0 else s_trP2[:, 1:2]
                vector.tensor_scalar(s_ba[:], s_ba[:], ramp_r, None, ALU.mult)
                vector.tensor_scalar(s_bb[:], s_bb[:], ramp_i, None, ALU.mult)
                vector.drain()
                wrap_sin_cos(vector, s_bb[:], s_bc[:], s_bd[:])
                vector.tensor_copy(s_t2[:, 0:1], s_t1[:, 0:1])
                vector.drain().then_inc(sm0[10].inc(), 1)
                vector.wait_ge(sm0[11].h, pas + 1)
                vector.tensor_mul(s_ppt[:], s_ba[:], s_bd[:])
                vector.drain()
                if pas == 1:
                    vector.tensor_scalar(s_ppt[:], s_ppt[:], -1.0, None, ALU.mult)
                    vector.drain()
                vector.tensor_copy(s_t2[:, 1:2], s_t1[:, 0:1])
                vector.drain().then_inc(sm0[12].inc(), 1)
                vector.wait_ge(sm0[19].h, 16 + 32 * pas)
                vector.tensor_mul(s_ppt[:], s_ba[:], s_bc[:])
                vector.drain()
                if pas == 1:
                    vector.tensor_scalar(s_ppt[:], s_ppt[:], -1.0, None, ALU.mult)
                    vector.drain()
                vector.tensor_copy(s_t2[:, 2:3], s_t1[:, 0:1])
                vector.drain().then_inc(sm0[12].inc(), 1)

        @blk.scalar
        def _(scalar):
            for pas in range(2):
                scalar.wait_ge(sm0[10].h, pas + 1)
                scalar.activation(s_ba[:], s_ba[:], AF.Exp)
                act_sin_cos(scalar, s_bc[:], s_bd[:], 128).then_inc(sm0[11].inc(), 1)


        @blk.vector
        def _(vector):
            vector.wait_ge(sm0[12].h, 4)
            a_dr = s_dtar[:].unsqueeze(2).broadcast_to([128, 32, 128])
            a_di = s_dtai[:].unsqueeze(2).broadcast_to([128, 32, 128])
            a_tk = s_trK[:].unsqueeze(1).broadcast_to([128, 32, 128])
            vector.tensor_tensor(s_ba[:].rearrange("p (n m) -> p n m", m=128), a_dr, a_tk, ALU.mult)
            vector.tensor_tensor(s_bb[:].rearrange("p (n m) -> p n m", m=128), a_di, a_tk, ALU.mult)
            vector.drain()
            wrap_sin_cos(vector, s_bb[:], s_bc[:], s_bd[:])
            vector.tensor_copy(s_t2[:], s_t1[:])
            vector.drain().then_inc(sm0[13].inc(), 1)

        @blk.scalar
        def _(scalar):
            scalar.wait_ge(sm0[13].h, 1)
            scalar.activation(s_ba[:], s_ba[:], AF.Exp)
            act_sin_cos(scalar, s_bc[:], s_bd[:], 128).then_inc(sm0[14].inc(), 1)

        @blk.vector
        def _(vector):
            vector.wait_ge(sm0[14].h, 1)
            vector.tensor_mul(s_bd[:], s_ba[:], s_bd[:])
            vector.tensor_mul(s_bc[:], s_ba[:], s_bc[:])
            vector.drain()
            e_r = s_er[:].unsqueeze(2).broadcast_to([128, 32, 128])
            e_i = s_ei[:].unsqueeze(2).broadcast_to([128, 32, 128])
            v_ba = s_ba[:].rearrange("p (n m) -> p n m", m=128)
            v_bb = s_bb[:].rearrange("p (n m) -> p n m", m=128)
            v_bc = s_bc[:].rearrange("p (n m) -> p n m", m=128)
            v_bd = s_bd[:].rearrange("p (n m) -> p n m", m=128)
            vector.tensor_tensor(v_ba, e_r, v_bd, ALU.mult)
            vector.tensor_tensor(v_bb, e_i, v_bc, ALU.mult)
            vector.drain()
            vector.tensor_sub(s_ba[:], s_ba[:], s_bb[:])
            vector.drain()
            vector.reduce_sum(s_ksb[:].unsqueeze(2),
                              AP(s_ba[:].tensor, 0, [[4096, 128], [1, 128], [128, 32]]),
                              axis=AX)
            vector.drain()
            vector.tensor_scalar(s_ksb[:], s_ksb[:], 2.0, None, ALU.mult)
            vector.drain()
            vector.tensor_add(s_ksb[:, 127:128], s_ksb[:, 127:128], s_dskc[:])
            vector.drain()
            vector.tensor_copy(s_kbf[:], s_ksb[:])
            vector.drain()
            vector.memset(s_zro[:], 0.0)
            vector.drain().then_inc(sm0[15].inc(), 1)

        @blk.tensor
        def _(tensor):
            tensor.wait_ge(sm0[15].h, 1)
            tensor.wait_ge(smc.h, smc.v)
            tensor.transpose(p_tr[:], s_kbf[:], s_id[:]).then_inc(sm0[16].inc(), 1)

        @blk.vector
        def _(vector):
            vector.wait_ge(sm0[16].h, 1)
            vector.tensor_copy(s_ktr[:], p_tr[:])
            vector.drain().then_inc(sm0[17].inc(), 1)

        @blk.sync
        def _(sync):
            sync.wait_ge(sm0[17].h, 1)
            sync.dma_start(out=AP(k2r, 129 * 128, [[128, 127], [1, 128]]),
                           in_=s_zro[0:127, :]).then_inc(sm0[18].inc(16), 16)
            sync.dma_start(out=AP(k2r, 0, [[128, 1], [1, 128]]),
                           in_=s_zro[0:1, :]).then_inc(sm0[18].inc(16), 16)
            sync.dma_start(out=AP(k2r, 128, [[128, 128], [1, 128]]),
                           in_=s_ktr[:]).then_inc(sm0[18].inc(16), 16)
            sync.dma_start(out=s_trQ0[:], in_=trQ_t.ap()).then_inc(sm0[18].inc(16), 16)

        smdb = sem("smdb"); smdbs = sem("smdbs"); smdbv = sem("smdbv"); smdbo = sem("smdbo")
        s_P2 = sbuf("s_P2", [64, 128], F32, gB)
        s_Q2 = sbuf("s_Q2", [64, 128], F32, gB)

        @blk.vector
        def _(vector):
            vector.wait_ge(sm0[15].h, 1)
            vector.wait_ge(sm0[18].h, sm0[18].v)
            vector.tensor_copy(s_P2[0:32, :], s_q2er[0:32, :])
            vector.tensor_scalar(s_P2[32:64, :], s_q2ei[32:64, :], -1.0, None, ALU.mult)
            vector.tensor_scalar(s_Q2[0:32, :], s_q2ei[0:32, :], -1.0, None, ALU.mult)
            vector.tensor_scalar(s_Q2[32:64, :], s_q2er[32:64, :], -1.0, None, ALU.mult)
            vector.drain()
            for slab in range(4):
                if slab >= 1:
                    vector.wait_ge(smdbo.h, 16 * slab)
                dr_b = s_q2dr[:, 32 * slab:32 * slab + 32].unsqueeze(2).broadcast_to([64, 32, 128])
                di_b = s_q2di[:, 32 * slab:32 * slab + 32].unsqueeze(2).broadcast_to([64, 32, 128])
                tq_b = s_trQ0[0:64, :].unsqueeze(1).broadcast_to([64, 32, 128])
                vector.tensor_tensor(s_ba[0:64, :].rearrange("p (c t) -> p c t", t=128), dr_b, tq_b, ALU.mult)
                vector.tensor_tensor(s_bb[0:64, :].rearrange("p (c t) -> p c t", t=128), di_b, tq_b, ALU.mult)
                vector.drain()
                wrap_sin_cos(vector, s_bb[0:64, :], s_bc[0:64, :], s_bd[0:64, :])
                vector.tensor_copy(s_t2[:, 3:4], s_t1[:, 0:1])
                vector.drain().then_inc(smdb.inc(), 1)
                vector.wait_ge(smdbs.h, slab + 1)
                vector.tensor_mul(s_bd[0:64, :], s_ba[0:64, :], s_bd[0:64, :])
                vector.tensor_mul(s_bc[0:64, :], s_ba[0:64, :], s_bc[0:64, :])
                vector.drain()
                p_b_ = s_P2[:, 32 * slab:32 * slab + 32].unsqueeze(2).broadcast_to([64, 32, 128])
                q_b_ = s_Q2[:, 32 * slab:32 * slab + 32].unsqueeze(2).broadcast_to([64, 32, 128])
                v_ba = s_ba[0:64, :].rearrange("p (c t) -> p c t", t=128)
                v_bb = s_bb[0:64, :].rearrange("p (c t) -> p c t", t=128)
                v_bc = s_bc[0:64, :].rearrange("p (c t) -> p c t", t=128)
                v_bd = s_bd[0:64, :].rearrange("p (c t) -> p c t", t=128)
                vector.tensor_tensor(v_ba, p_b_, v_bd, ALU.mult)
                vector.tensor_tensor(v_bb, q_b_, v_bc, ALU.mult)
                vector.drain()
                vector.tensor_add(s_ba[0:64, :], s_ba[0:64, :], s_bb[0:64, :])
                vector.drain()
                vector.tensor_scalar(s_d2s[:], s_ba[0:64, :], 2.0, None, ALU.mult)
                vector.drain().then_inc(smdbv.inc(), 1)

        @blk.scalar
        def _(scalar):
            for slab in range(4):
                scalar.wait_ge(smdb.h, slab + 1)
                scalar.activation(s_ba[0:64, :], s_ba[0:64, :], AF.Exp)
                act_sin_cos(scalar, s_bc[0:64, :], s_bd[0:64, :], 64).then_inc(smdbs.inc(), 1)

        @blk.sync
        def _(sync):
            for slab in range(4):
                sync.wait_ge(smdbv.h, slab + 1)
                sync.dma_start(out=AP(d2_d, slab * 4096, [[4 * 4096, 64], [1, 4096]]),
                               in_=s_d2s[:]).then_inc(smdbo.inc(16), 16)

        gB.close()
        g2.close()
        g0.close()
        g3 = ExitStack()
        s_ucs = sbuf("s_ucs", [128, 4 * 64 * 128], BF16, g3)
        s_a2 = sbuf("s_a2", [64, 128 * 4 * 64], BF16, g3)
        g3mA = ExitStack()
        s_pp2 = sbuf("s_pp2", [128, 8192], BF16, g3mA)
        g3m = ExitStack()
        s_aswp = sbuf("s_aswp", [64, 16384], BF16, g3m)
        s_sN = sbuf("s_sN", [64, 512], F32, g3m)
        s_sS = sbuf("s_sS", [64, 512], F32, g3m)
        s_rx = sbuf("s_rx", [64, 256], F32, g3m)
        s_ry = sbuf("s_ry", [64, 256], F32, g3m)
        s_ry2 = sbuf("s_ry2", [64, 256], F32, g3m)

        @blk.gpsimd
        def _(gp):
            gp.wait_ge(smu3[0].h, smu3[0].v)
            gp.wait_ge(smu3[1].h, smu3[1].v)
            gp.collective_compute(
                "AllGather", ALU.bypass,
                replica_groups=[list(range(NCORES))],
                ins=[u_loc.ap().opt()],
                outs=[u_gth.ap().opt()],
            ).then_inc(smcc.inc(), 1)

        @blk.sync
        def _(sync):
            sync.wait_ge(smcc.h, 1)
            sync.wait_ge(smc.h, smc.v)
            sync.wait_ge(smgo.h, 16)
            sync.wait_ge(smdbo.h, 64)
            sync.wait_ge(sm0[18].h, sm0[18].v)
            sync.wait_ge(sm0[19].h, 32)
            uoff = sync.value_load(s_uoff[0:1, 0:1])
            sync.wait_ge(sm0[19].h, 64)
            sync.dma_start(out=s_pp2[:], in_=AP(pp_d, 0, [[16384, 128], [1, 8192]])).then_inc(smucs.inc(16), 16)
            for r in range(NCORES):
                b, h = r // 2, r % 2
                sync.dma_start(
                    out=AP(s_ucs[:].tensor, b * 8192 + h * 4096,
                           [[32768, 128], [128, 32], [1, 128]]),
                    in_=AP(u_gth, uoff + r * T * D, [[D, 128], [128 * D, 32], [1, 128]]),
                ).then_inc(smucs.inc(16), 16)

        @blk.tensor
        def _(tensor):
            tensor.wait_ge(smucs.h, smucs.v)
            for pair in range(64):
                pe_bank = [p_a, p_b][pair % 2]
                if pair >= 2:
                    tensor.wait_ge(sme2.h, pair - 1)
                for two in range(2):
                    ch = 2 * pair + two
                    for ri in range(2):
                        mm = tensor.matmul(
                            pe_bank[32 * ri:32 * ri + 32, 256 * two:256 * two + 256],
                            s_pp2[:, ri * 4096 + ch * 32: ri * 4096 + ch * 32 + 32],
                            AP(s_ucs[:].tensor, ch, [[32768, 128], [128, 64], [8192, 4]]),
                            start=True, stop=True, tile_position=(0, 32 * ri))
                mm.then_inc(sme1.inc(), 1)

        @blk.vector
        def _(vector):
            for pair in range(64):
                pe_bank = [p_a, p_b][pair % 2]
                vector.wait_ge(sme1.h, pair + 1)
                vector.tensor_copy(
                    AP(s_a2[:].tensor, 2 * pair * 256, [[32768, 64], [256, 2], [1, 256]]),
                    AP(pe_bank[:].tensor, 0, [[512, 64], [256, 2], [1, 256]]))
                vector.drain().then_inc(sme2.inc(), 1)

        smsw = sem("smsw")

        @blk.sync
        def _(sync):
            for hp in range(2):
                if hp == 0:
                    sync.wait_ge(sme2.h, sme2.v)
                else:
                    sync.wait_ge(smr1.h, 1)
                sync.dma_start(out=s_aswp[0:32, :],
                               in_=s_a2[32:64, 16384 * hp:16384 * hp + 16384]).then_inc(smsw.inc(16), 16)
                sync.dma_start(out=s_aswp[32:64, :],
                               in_=s_a2[0:32, 16384 * hp:16384 * hp + 16384]).then_inc(smsw.inc(16), 16)

        @blk.vector
        def _(vector):
            vector.wait_ge(sm0[8].h, 1)
            for hp in range(2):
                vector.wait_ge(smsw.h, 32 * (hp + 1))
                vector.memset(s_sN[:], 0.0)
                vector.memset(s_sS[:], 0.0)
                vector.drain()
                w1b = s_W1[:, 64 * hp:64 * hp + 64].unsqueeze(2).broadcast_to([64, 64, 4])
                w2b = s_W2[:, 64 * hp:64 * hp + 64].unsqueeze(2).broadcast_to([64, 64, 4])
                w3b = s_W3[:, 64 * hp:64 * hp + 64].unsqueeze(2).broadcast_to([64, 64, 4])
                for J in range(64):
                    pv, cu = (J % 2), ((J + 1) % 2)
                    nP = s_sN[:, 256 * pv:256 * pv + 256].rearrange("p (c b) -> p c b", b=4)
                    nC = s_sN[:, 256 * cu:256 * cu + 256].rearrange("p (c b) -> p c b", b=4)
                    sP = s_sS[:, 256 * pv:256 * pv + 256].rearrange("p (c b) -> p c b", b=4)
                    sC = s_sS[:, 256 * cu:256 * cu + 256].rearrange("p (c b) -> p c b", b=4)
                    a_j = AP(s_a2[:].tensor, 16384 * hp + 4 * J, [[32768, 64], [256, 64], [1, 4]])
                    w_j = AP(s_aswp[:].tensor, 4 * J, [[16384, 64], [256, 64], [1, 4]])
                    vector.tensor_tensor(s_rx[:].rearrange("p (c b) -> p c b", b=4), nP, w1b, ALU.mult)
                    vector.tensor_tensor(s_ry[:].rearrange("p (c b) -> p c b", b=4), sP, w2b, ALU.mult)
                    vector.drain()
                    vector.tensor_add(s_rx[:], s_rx[:], s_ry[:])
                    vector.tensor_tensor(s_ry2[:].rearrange("p (c b) -> p c b", b=4), nP, w3b, ALU.mult)
                    vector.drain()
                    vector.tensor_tensor(nC, s_rx[:].rearrange("p (c b) -> p c b", b=4), a_j, ALU.add)
                    vector.drain()
                    vector.tensor_tensor(s_rx[:].rearrange("p (c b) -> p c b", b=4), sP, w1b, ALU.mult)
                    vector.drain()
                    vector.tensor_add(s_rx[:], s_rx[:], s_ry2[:])
                    vector.drain()
                    vector.tensor_tensor(sC, s_rx[:].rearrange("p (c b) -> p c b", b=4), w_j, ALU.add)
                    vector.drain()
                    vector.tensor_copy(a_j, nC)
                    vector.drain()
                vector.tensor_copy(s_rx[:, 0:1], s_sN[:, 0:1])
                vector.drain().then_inc(smr1.inc(), 1)

        g3m.close()
        g3mA.close()

        g3b = ExitStack()
        s_tp = [sbuf("s_tp0", [128, 32 * 128], BF16, g3b)]
        s_d2w = [sbuf("s_d2w0", [64, 4096], BF16, g3b)]
        s_yst = sbuf("s_yst", [128, 4 * 32 * 128], BF16, g3b)

        smtl = sem("smtl")
        smd1 = sem("smd1")
        smd2 = sem("smd2")
        smyo2 = sem("smyo2")

        @blk.sync
        def _(sync):
            sync.wait_ge(sm0[18].h, sm0[18].v)
            sync.wait_ge(smdbo.h, smdbo.v)
            sync.wait_ge(smr1.h, 2)
            for half in range(2):
                for slab in range(4):
                    idx = half * 4 + slab
                    pp = 0
                    if idx >= 1:
                        sync.wait_ge(smd1.h, 8 * idx)
                    sync.dma_start(out=s_tp[pp][:].rearrange("p (t c) -> p t c", c=32),
                                   in_=AP(k2r, 128 * 128 + 32 * slab, [[128, 128], [-128, 128], [1, 32]]),
                                   allow_slow_non_contiguous=True).then_inc(smtl.inc(16), 16)
                    sync.dma_start(out=s_d2w[pp][:],
                                   in_=AP(d2_d, slab * 4096, [[4 * 4096, 64], [1, 4096]])).then_inc(smtl.inc(16), 16)
                for b4 in range(4):
                    sync.wait_ge(smytr.h, half * 64 + 16 * (b4 + 1))
                    sync.dma_start(
                        out=AP(y_loc, b4 * 8192 + half * 4096, [[4 * 64 * 128, 128], [1, 4096]]),
                        in_=s_yst[:, b4 * 4096:b4 * 4096 + 4096],
                    ).then_inc(smyo2.inc(16), 16)

        @blk.tensor
        def _(tensor):
            tensor.wait_ge(smr1.h, 2)
            for half in range(2):
                for slab in range(4):
                    idx = half * 4 + slab
                    pp = 0
                    tensor.wait_ge(smtl.h, 32 * (idx + 1))
                    nJ = 31 if half == 0 else 32
                    oJ = 1 if half == 0 else 0
                    sJ0 = 0 if half == 0 else 31
                    for grp in range(8):
                        pe_bank = [p_c, p_d][grp % 2]
                        gi = idx * 8 + grp
                        if gi >= 2:
                            tensor.wait_ge(smd2.h, gi - 1)
                        for cp in range(4):
                            cl = grp * 4 + cp
                            ch = 32 * slab + cl
                            tensor.matmul(
                                pe_bank[:, 128 * cp:128 * cp + 128].rearrange("p (j b) -> p j b", b=4),
                                AP(s_tp[pp][:].tensor, cl, [[4096, 128], [32, 128]]),
                                AP(s_ucs[:].tensor, ch + half * 32 * 128,
                                   [[32768, 128], [128, 32], [8192, 4]]),
                                start=True, stop=False)
                            mm = tensor.matmul(
                                AP(pe_bank[:].tensor, 128 * cp + 4 * oJ, [[512, 128], [1, 4 * nJ]]),
                                s_d2w[pp][:, 128 * cl:128 * cl + 128],
                                AP(s_a2[:].tensor, ch * 256 + 4 * sJ0, [[32768, 64], [1, 4 * nJ]]),
                                start=False, stop=True)
                        mm.then_inc(smd1.inc(), 1)
                tensor.wait_ge(smd2.h, 32 * (half + 1))
                for tt in range(128):
                    gi2 = half * 128 + tt
                    b4, Jl = tt // 32, tt % 32
                    if gi2 >= 2:
                        tensor.wait_ge(smytr.h, gi2 // 2)
                    tensor.transpose([p_tr, p_tr2][tt % 2][:],
                                     s_yst[:, b4 * 4096 + Jl * 128:b4 * 4096 + Jl * 128 + 128],
                                     s_id[:]).then_inc(smyt1.inc(), 1)

        @blk.scalar
        def _(scalar):
            for idx in range(8):
                half, slab = idx // 4, idx % 4
                for grp in range(8):
                    pe_bank = [p_c, p_d][grp % 2]
                    gi = idx * 8 + grp
                    if gi == 32:
                        scalar.wait_ge(smyo2.h, 64)
                    scalar.wait_ge(smd1.h, gi + 1)
                    scalar.activation(
                        AP(s_yst[:].tensor, 32 * slab + 4 * grp,
                           [[16384, 128], [1, 4], [128, 32], [4096, 4]]),
                        AP(pe_bank[:].tensor, 0, [[512, 128], [128, 4], [4, 32], [1, 4]]),
                        AF.Gelu).then_inc(smd2.inc(), 1)

        @blk.vector
        def _(vector):
            for half in range(2):
                for tt in range(0, 128, 2):
                    gi2 = half * 128 + tt
                    vector.wait_ge(smyt1.h, gi2 + 2)
                    for u2 in range(2):
                        b4, Jl = (tt + u2) // 32, (tt + u2) % 32
                        vector.tensor_copy(
                            s_yst[:, b4 * 4096 + Jl * 128:b4 * 4096 + Jl * 128 + 128],
                            [p_tr, p_tr2][(tt + u2) % 2][:])
                    vector.drain().then_inc(smytr.inc(), 1)

        @blk.gpsimd
        def _(gp):
            gp.wait_ge(smyo2.h, smyo2.v)
            gp.collective_compute(
                "AllGather", ALU.bypass,
                replica_groups=[list(range(NCORES))],
                ins=[y_loc.ap().opt()],
                outs=[y_gth.ap().opt()],
            ).then_inc(smcc.inc(), 1)

        g3b.close()
        g3.close()

        g4 = ExitStack()
        s_key = [sbuf(f"s_key{e}", [128, 8 * 256], BF16, g4) for e in range(8)]
        s_val = [sbuf(f"s_val{e}", [128, 2 * D], BF16, g4) for e in range(8)]
        s_sal = sbuf("s_sal", [128, 16 * 512], BF16, g4)
        s_grep = sbuf("s_grep", [128, 8 * 512], BF16, g4)
        s_xw2 = [[sbuf(f"s_xw2{pp}_{k}", [128, 512], BF16, g4) for k in range(8)] for pp in range(2)]
        s_fst = [sbuf(f"s_fst{pp}", [128, D], F32, g4) for pp in range(2)]

        @blk.sync
        def _(sync):
            sync.wait_ge(smcastC.h, 32)
            sync.wait_ge(smgo.h, 16)
            sync.wait_ge(smxst.h, 512)
            sync.wait_ge(smyo2.h, 128)
            def store_slab(sync, slab):
                for cc in range(4):
                    chunk = slab * 4 + cc
                    sync.wait_ge(smv2.h, chunk + 1)
                    dst = [ffnP, ffnQ][(chunk // 32)]
                    sync.dma_start(out=AP(dst, (chunk % 32) * 128 * D, [[D, 128], [1, D]]),
                                   in_=s_fst[chunk % 2][:]).then_inc(smv3[chunk % 2].inc(16), 16)

            for half in range(2):
                if half == 1:
                    store_slab(sync, 7)
                    sync.wait_ge(smv1.h, 32)
                for e in range(8):
                    eg = 8 * half + e
                    sync.dma_start(out=s_key[e][:].rearrange("p (k h2) -> p k h2", h2=256),
                                   in_=AP(keysbf, eg * D * 256, [[256, 128], [128 * 256, 8], [1, 256]])).then_inc(smkv[half].inc(16), 16)
                    sync.dma_start(out=s_val[e][:].rearrange("p (k d) -> p k d", d=D),
                                   in_=AP(valsbf, eg * 256 * D, [[D, 128], [128 * D, 2], [1, D]])).then_inc(smkv[half].inc(16), 16)
                for s in range(8):
                    slab = half * 8 + s
                    pp = s % 2
                    if s >= 1:
                        store_slab(sync, slab - 1)
                    for k in range(8):
                        sync.dma_start(
                            out=s_xw2[pp][k][:],
                            in_=AP(xnT_d, (128 * k) * T + 512 * s, [[T, 128], [1, 512]])).then_inc(smxw2[pp].inc(16), 16)
                    for e in range(8):
                        eg = 8 * half + e
                        sync.dma_start(out=s_grep[:, 512 * e:512 * e + 512],
                                       in_=AP(gT_d, eg * T + 512 * s, [[0, 128], [1, 512]])).then_inc(smgx.inc(16), 16)
            store_slab(sync, 15)

        @blk.tensor
        def _(tensor):
            for half in range(2):
                tensor.wait_ge(smkv[half].h, smkv[half].v)
                for s in range(8):
                    slab = half * 8 + s
                    pp = s % 2
                    tensor.wait_ge(smxw2[pp].h, 16 * 8 * (slab // 2 + 1))
                    for e in range(8):
                        if slab * 8 + e >= 1:
                            tensor.wait_ge(smk2.h, 2 * (slab * 8 + e))
                        for hh in range(2):
                            ps = [p_a, p_b][hh]
                            for k in range(8):
                                mm = tensor.matmul(ps[:], s_key[e][:, 256 * k + 128 * hh:256 * k + 128 * hh + 128],
                                                   s_xw2[pp][k][:], start=(k == 0), stop=(k == 7))
                            mm.then_inc(smk1.inc(), 1)
                    for cc in range(4):
                        chunk = slab * 4 + cc
                        if chunk >= 1:
                            tensor.wait_ge(smv2.h, chunk)
                        for dd in range(2):
                            po = [p_c, p_d][dd]
                            first = True
                            for e in range(8):
                                if dd == 0 and cc == 0:
                                    tensor.wait_ge(smk3.h, slab * 8 + e + 1)
                                for hh in range(2):
                                    mm = tensor.matmul(
                                        po[:],
                                        s_sal[:, (e * 2 + hh) * 512 + 128 * cc:(e * 2 + hh) * 512 + 128 * cc + 128],
                                        s_val[e][:, hh * D + 512 * dd:hh * D + 512 * dd + 512],
                                        start=first, stop=(e == 7 and hh == 1))
                                    first = False
                        mm.then_inc(smv1.inc(), 1)

        @blk.scalar
        def _(scalar):
            for half in range(2):
                for s in range(8):
                    slab = half * 8 + s
                    for e in range(8):
                        if slab >= 1:
                            scalar.wait_ge(smv1.h, 4 * slab)
                        for hh in range(2):
                            idx = (slab * 8 + e) * 2 + hh
                            scalar.wait_ge(smk1.h, idx + 1)
                            ps = [p_a, p_b][hh]
                            scalar.activation(s_sal[:, (e * 2 + hh) * 512:(e * 2 + hh) * 512 + 512],
                                              ps[:], AF.Relu).then_inc(smk2.inc(), 1)

        @blk.vector
        def _(vector):
            for half in range(2):
                for s in range(8):
                    slab = half * 8 + s
                    vector.wait_ge(smgx.h, 16 * 8 * (slab + 1))
                    for e in range(8):
                        vector.wait_ge(smk2.h, 2 * (slab * 8 + e + 1))
                        ge = s_grep[:, 512 * e:512 * e + 512]
                        vector.tensor_tensor(s_sal[:, (e * 2) * 512:(e * 2) * 512 + 512],
                                             s_sal[:, (e * 2) * 512:(e * 2) * 512 + 512], ge, ALU.mult)
                        vector.drain()
                        vector.tensor_tensor(s_sal[:, (e * 2 + 1) * 512:(e * 2 + 1) * 512 + 512],
                                             s_sal[:, (e * 2 + 1) * 512:(e * 2 + 1) * 512 + 512], ge, ALU.mult)
                        vector.drain().then_inc(smk3.inc(), 1)
                    for cc in range(4):
                        chunk = slab * 4 + cc
                        pp3 = chunk % 2
                        vector.wait_ge(smv1.h, chunk + 1)
                        if chunk >= 2:
                            vector.wait_ge(smv3[chunk % 2].h, 16 * (chunk // 2))
                        if half == 0:
                            vector.tensor_copy(s_fst[pp3][:, 0:512], p_c[:])
                            vector.tensor_copy(s_fst[pp3][:, 512:1024], p_d[:])
                        else:
                            vector.tensor_add(s_fst[pp3][:, 0:512], p_c[:], s_outb[:, 0:512])
                            vector.tensor_add(s_fst[pp3][:, 512:1024], p_d[:], s_outb[:, 512:1024])
                        vector.drain().then_inc(smv2.inc(), 1)

        g4.close()

        g5 = ExitStack()
        y_fm = [sbuf(f"y_fm{r}", [128, T], BF16, g5) for r in range(8)]
        gluT = [sbuf(f"gluT{k}", [128, 2 * D], BF16, g5) for k in range(8)]
        outT = [sbuf(f"outT{k}", [128, D], BF16, g5) for k in range(8)]
        s_ag = [sbuf(f"s_ag{o}", [128, 512], BF16, g5) for o in range(16)]
        s_y2 = [sbuf(f"s_y2{k}", [128, 512], BF16, g5) for k in range(8)]
        s_xr = [sbuf(f"s_xr{pp}", [128, D], F32, g5) for pp in range(2)]
        s_fp = [sbuf(f"s_fp{pp}", [128, D], F32, g5) for pp in range(2)]
        s_fq = [sbuf(f"s_fq{pp}", [128, D], F32, g5) for pp in range(2)]
        s_fin = [sbuf(f"s_fin{pp}", [128, D], F32, g5) for pp in range(2)]
        s_aff = sbuf("s_aff", [128, 4], F32, g5)

        @blk.sync
        def _(sync):
            sync.wait_ge(smcc.h, 2)
            sync.wait_ge(smcastB.h, 32)
            sync.wait_ge(smv3[0].h, smv3[0].v)
            sync.wait_ge(smv3[1].h, smv3[1].v)
            yoff = sync.value_load(s_yoff[0:1, 0:1])
            for r in range(8):
                sync.dma_start(
                    out=y_fm[r][:],
                    in_=AP(y_gth, yoff + r * 128 * 32768, [[32768, 128], [1, 4096]])).then_inc(smyf.inc(16), 16)
            for k in range(8):
                sync.dma_start_transpose(gluT[k][:],
                                         AP(glubf, 128 * k, [[D, 2 * D], [1, 128]])).then_inc(smwt3.inc(16), 16)
                sync.dma_start_transpose(outT[k][:],
                                         AP(outbf, 128 * k, [[D, D], [1, 128]])).then_inc(smwt3.inc(16), 16)
            for chunk in range(2):
                p9 = chunk % 2
                sync.dma_start(out=s_xr[p9][:],
                               in_=AP(x_in, chunk * 128 * D, [[D, 128], [1, D]])).then_inc(smf1[p9].inc(16), 16)
                sync.dma_start(out=s_fp[p9][:],
                               in_=AP(ffnP, chunk * 128 * D, [[D, 128], [1, D]])).then_inc(smf2[p9].inc(16), 16)
                sync.dma_start(out=s_fq[p9][:],
                               in_=AP(ffnQ, chunk * 128 * D, [[D, 128], [1, D]])).then_inc(smf3[p9].inc(16), 16)
            for chunk in range(32):
                p9 = chunk % 2
                sync.wait_ge(smo2.h, chunk + 1)
                sync.dma_start(out=AP(out_t, chunk * 128 * D, [[D, 128], [1, D]]),
                               in_=s_fin[p9][:]).then_inc(smout[p9].inc(16), 16)
                if chunk + 2 < 32:
                    sync.dma_start(out=s_xr[p9][:],
                                   in_=AP(x_in, (chunk + 2) * 128 * D, [[D, 128], [1, D]])).then_inc(smf1[p9].inc(16), 16)
                    sync.dma_start(out=s_fp[p9][:],
                                   in_=AP(ffnP, (chunk + 2) * 128 * D, [[D, 128], [1, D]])).then_inc(smf2[p9].inc(16), 16)
                    sync.dma_start(out=s_fq[p9][:],
                                   in_=AP(ffnQ, (chunk + 2) * 128 * D, [[D, 128], [1, D]])).then_inc(smf3[p9].inc(16), 16)
            sync.wait_ge(smout[0].h, smout[0].v)
            sync.wait_ge(smout[1].h, smout[1].v)

        @blk.tensor
        def _(tensor):
            tensor.wait_ge(smyf.h, smyf.v)
            tensor.wait_ge(smwt3.h, smwt3.v)
            for s in range(8):
                for o in range(16):
                    pg = [p_a, p_b][o % 2]
                    if s * 16 + o >= 2:
                        tensor.wait_ge(smg2.h, s * 16 + o - 1)
                    for r in range(8):
                        mm = tensor.matmul(pg[:], gluT[r][:, 128 * o:128 * o + 128],
                                           y_fm[r][:, 512 * s:512 * s + 512],
                                           start=(r == 0), stop=(r == 7))
                    mm.then_inc(smg1.inc(), 1)
                for cc in range(4):
                    chunk = s * 4 + cc
                    if chunk >= 1:
                        tensor.wait_ge(smo2.h, chunk)
                    for k in range(8):
                        tensor.wait_ge(smg3.h, s * 8 + k + 1)
                        lhs = s_y2[k][:, 128 * cc:128 * cc + 128]
                        tensor.matmul(p_c[:], lhs, outT[k][:, 0:512], start=(k == 0), stop=(k == 7))
                        mm = tensor.matmul(p_d[:], lhs, outT[k][:, 512:1024], start=(k == 0), stop=(k == 7))
                    mm.then_inc(smo1.inc(), 1)

        @blk.scalar
        def _(scalar):
            for s in range(8):
                for o in range(16):
                    pg = [p_a, p_b][o % 2]
                    scalar.wait_ge(smg1.h, s * 16 + o + 1)
                    if s >= 1:
                        scalar.wait_ge(smg3.h, 8 * s)
                    if o < 8:
                        scalar.activation(s_ag[o][:], pg[:], AF.Identity,
                                          bias=s_glb[:, o:o + 1]).then_inc(smg2.inc(), 1)
                    else:
                        scalar.activation(s_ag[o][:], pg[:], AF.Sigmoid,
                                          bias=s_glb[:, o:o + 1]).then_inc(smg2.inc(), 1)

        @blk.vector
        def _(vector):
            for s in range(8):
                for k in range(8):
                    vector.wait_ge(smg2.h, s * 16 + 16)
                    if s >= 1:
                        vector.wait_ge(smo1.h, 4 * s)
                    vector.tensor_mul(s_y2[k][:], s_ag[k][:], s_ag[k + 8][:])
                    vector.drain().then_inc(smg3.inc(), 1)
                for cc in range(4):
                    chunk = s * 4 + cc
                    pp4 = chunk % 2
                    vector.wait_ge(smo1.h, chunk + 1)
                    vector.wait_ge(smf1[pp4].h, 16 * (chunk // 2 + 1))
                    vector.wait_ge(smf2[pp4].h, 16 * (chunk // 2 + 1))
                    vector.wait_ge(smf3[pp4].h, 16 * (chunk // 2 + 1))
                    if chunk >= 2:
                        vector.wait_ge(smout[pp4].h, 16 * (chunk // 2))
                    vector.tensor_scalar(s_aff[:, 0:1], s_rs[:, chunk:chunk + 1], 1.0, None, ALU.add)
                    vector.tensor_mul(s_aff[:, 1:2], s_mu[:, chunk:chunk + 1], s_rs[:, chunk:chunk + 1])
                    vector.drain()
                    vector.tensor_scalar(s_aff[:, 1:2], s_aff[:, 1:2], -1.0, None, ALU.mult)
                    vector.drain()
                    vector.tensor_scalar(s_fin[pp4][:], s_xr[pp4][:], s_aff[:, 0:1], s_aff[:, 1:2],
                                         ALU.mult, ALU.add)
                    vector.drain()
                    vector.tensor_add(s_fin[pp4][:, 0:512], s_fin[pp4][:, 0:512], p_c[:])
                    vector.tensor_add(s_fin[pp4][:, 512:1024], s_fin[pp4][:, 512:1024], p_d[:])
                    vector.drain()
                    vector.tensor_add(s_fin[pp4][:], s_fin[pp4][:], s_fp[pp4][:])
                    vector.drain()
                    vector.tensor_add(s_fin[pp4][:], s_fin[pp4][:], s_fq[pp4][:])
                    vector.drain().then_inc(smo2.inc(), 1)

        g5.close()

    es.close()
    return nc


_NC_CACHE = None


def kernel(L0_attn_x, L0_attn_in_W, L0_attn_in_b, L0_attn_log_dt, L0_attn_C,
           L0_attn_log_A_real, L0_attn_A_imag, L0_attn_D_skip, L0_attn_glu_W,
           L0_attn_glu_b, L0_attn_out_W, L0_attn_out_b, L0_ffn_expert_sel,
           L0_ffn_keys, L0_ffn_values, L0_ffn_route_scale):
    global _NC_CACHE
    if _NC_CACHE is None:
        _NC_CACHE = build_nc()
    nc = _NC_CACHE

    x = np.ascontiguousarray(np.asarray(L0_attn_x, dtype=np.float32))
    ident = np.eye(128, dtype=ml_dtypes.bfloat16)
    t_idx = np.arange(128, dtype=np.float32)
    trQ = np.tile((t_idx + 1.0)[None, :], (128, 1))
    trK = np.tile((127.0 - t_idx)[None, :], (128, 1))
    trP = (127.0 - t_idx)[:, None].copy()
    trP2 = np.stack([128.0 - t_idx, 126.0 - t_idx], axis=1).astype(np.float32)

    in_maps = []
    for c in range(NCORES):
        b, h = c // 2, c % 2
        cs = slice(128 * c, 128 * (c + 1))
        in_maps.append(dict(
            x_in=np.ascontiguousarray(x[b, h * 4096:(h + 1) * 4096, :]),
            in_W=np.asarray(L0_attn_in_W, np.float32),
            in_b=np.asarray(L0_attn_in_b, np.float32),
            log_dt=np.ascontiguousarray(np.asarray(L0_attn_log_dt, np.float32)[cs]),
            C_p=np.ascontiguousarray(np.asarray(L0_attn_C, np.float32)[cs]),
            lar_t=np.ascontiguousarray(np.asarray(L0_attn_log_A_real, np.float32)[cs]),
            aim_t=np.ascontiguousarray(np.asarray(L0_attn_A_imag, np.float32)[cs]),
            dsk_t=np.ascontiguousarray(np.asarray(L0_attn_D_skip, np.float32)[cs]),
            glu_W=np.asarray(L0_attn_glu_W, np.float32),
            glu_b=np.asarray(L0_attn_glu_b, np.float32),
            out_W=np.asarray(L0_attn_out_W, np.float32),
            out_b=np.asarray(L0_attn_out_b, np.float32),
            esel=np.asarray(L0_ffn_expert_sel, np.float32),
            keys_t=np.asarray(L0_ffn_keys, np.float32),
            vals_t=np.asarray(L0_ffn_values, np.float32),
            rscale=np.asarray(L0_ffn_route_scale, np.float32),
            ident=ident, trQ_t=trQ, trK_t=trK, trP_t=trP, trP2_t=trP2,
            uoff_t=np.array([128 * c], np.int32),
            yoff_t=np.array([b * 8192 + h * 4096], np.int32),
        ))

    res = run_bass_kernel_spmd(nc, in_maps, core_ids=list(range(NCORES)))
    out = np.empty((4, 8192, 1024), np.float32)
    for c in range(NCORES):
        b, h = c // 2, c % 2
        out[b, h * 4096:(h + 1) * 4096, :] = np.asarray(res.results[c]["out"])
    return out


# version 08
# speedup vs baseline: 1.0551x; 1.0551x over previous
import numpy as np
import ml_dtypes
import concourse.bass as bass
import concourse.mybir as mybir
from concourse.ap import AP
from concourse.bass_utils import run_bass_kernel_spmd
from contextlib import ExitStack

F32 = mybir.dt.float32
BF16 = mybir.dt.bfloat16
I32 = mybir.dt.int32
AF = mybir.ActivationFunctionType
ALU = mybir.AluOpType
AX = mybir.AxisListType.X

NCORES = 8
T = 4096
D = 1024
NST = 32
TWO_PI = 6.283185307179586
INV_2PI = 0.15915494309189535
MAGIC = 12582912.0
HALF_PI = 1.5707963267948966


class Ctr:
    def __init__(self, h):
        self.h = h
        self.v = 0

    def inc(self, n=1):
        self.v += n
        return self.h


def build_nc():
    nc = bass.Bass("TRN2", target_bir_lowering=False)

    x_in = nc.dram_tensor("x_in", [T, D], F32, kind="ExternalInput")
    in_W = nc.dram_tensor("in_W", [D, D], F32, kind="ExternalInput")
    in_b = nc.dram_tensor("in_b", [D], F32, kind="ExternalInput")
    log_dt = nc.dram_tensor("log_dt", [128], F32, kind="ExternalInput")
    C_p = nc.dram_tensor("C_p", [128, NST, 2], F32, kind="ExternalInput")
    lar_t = nc.dram_tensor("lar_t", [128, NST], F32, kind="ExternalInput")
    aim_t = nc.dram_tensor("aim_t", [128, NST], F32, kind="ExternalInput")
    dsk_t = nc.dram_tensor("dsk_t", [128], F32, kind="ExternalInput")
    glu_W = nc.dram_tensor("glu_W", [2 * D, D], F32, kind="ExternalInput")
    glu_b = nc.dram_tensor("glu_b", [2 * D], F32, kind="ExternalInput")
    out_W = nc.dram_tensor("out_W", [D, D], F32, kind="ExternalInput")
    out_b = nc.dram_tensor("out_b", [D], F32, kind="ExternalInput")
    esel = nc.dram_tensor("esel", [16, D], F32, kind="ExternalInput")
    keys_t = nc.dram_tensor("keys_t", [16, D, 256], F32, kind="ExternalInput")
    vals_t = nc.dram_tensor("vals_t", [16, 256, D], F32, kind="ExternalInput")
    rscale = nc.dram_tensor("rscale", [1], F32, kind="ExternalInput")
    ident = nc.dram_tensor("ident", [128, 128], BF16, kind="ExternalInput")
    trQ_t = nc.dram_tensor("trQ_t", [128, 128], F32, kind="ExternalInput")
    trK_t = nc.dram_tensor("trK_t", [128, 128], F32, kind="ExternalInput")
    trP_t = nc.dram_tensor("trP_t", [128, 1], F32, kind="ExternalInput")
    trP2_t = nc.dram_tensor("trP2_t", [128, 2], F32, kind="ExternalInput")
    uoff_t = nc.dram_tensor("uoff_t", [1], I32, kind="ExternalInput")
    yoff_t = nc.dram_tensor("yoff_t", [1], I32, kind="ExternalInput")
    out_t = nc.dram_tensor("out", [T, D], F32, kind="ExternalOutput")

    xnT_d = nc.dram_tensor("xnT_d", [D, T], BF16)
    w1bf = nc.dram_tensor("w1bf", [D, D], BF16)
    glubf = nc.dram_tensor("glubf", [2 * D, D], BF16)
    outbf = nc.dram_tensor("outbf", [D, D], BF16)
    eselbf = nc.dram_tensor("eselbf", [16, D], BF16)
    keysbf = nc.dram_tensor("keysbf", [16, D, 256], BF16)
    valsbf = nc.dram_tensor("valsbf", [16, 256, D], BF16)
    prow = nc.dram_tensor("prow", [4, 4096], F32)
    k2r = nc.dram_tensor("k2r", [256, 128], BF16)
    u_loc = nc.dram_tensor("u_loc", [T, D], BF16)
    u_gth = nc.dram_tensor("u_gth", [NCORES * T, D], BF16, addr_space="Shared")
    y_loc = nc.dram_tensor("y_loc", [128, 4 * 64 * 128], BF16)
    y_gth = nc.dram_tensor("y_gth", [NCORES * 128, 4 * 64 * 128], BF16, addr_space="Shared")
    gT_d = nc.dram_tensor("gT_d", [16, T], BF16)
    d2_d = nc.dram_tensor("d2_d", [64, 4 * 4096], BF16)
    pp_d = nc.dram_tensor("pp_d", [128, 4 * 4096], BF16)
    ffnP = nc.dram_tensor("ffnP", [T, D], F32)
    ffnQ = nc.dram_tensor("ffnQ", [T, D], F32)

    es = ExitStack()

    def sbuf(name, shape, dt, st=None):
        return (st or es).enter_context(nc.sbuf_tensor(name, shape, dt))

    def sem(name):
        return Ctr(es.enter_context(nc.semaphore(name)))

    s_id = sbuf("s_id", [128, 128], BF16)
    s_trP = sbuf("s_trP", [128, 1], F32)
    s_trP2 = sbuf("s_trP2", [128, 2], F32)
    s_hpi = sbuf("s_hpi", [128, 1], F32)
    s_mu = sbuf("s_mu", [128, 32], F32)
    s_rs = sbuf("s_rs", [128, 32], F32)
    s_inb = sbuf("s_inb", [128, D], F32)
    s_outb = sbuf("s_outb", [128, D], F32)
    s_glb = sbuf("s_glb", [128, 16], F32)
    s_g = sbuf("s_g", [128, 32 * 16], F32)
    s_rsc = sbuf("s_rsc", [128, 1], F32)
    s_uoff = sbuf("s_uoff", [1, 1], I32)
    s_yoff = sbuf("s_yoff", [1, 1], I32)
    s_q2dr = sbuf("s_q2dr", [64, 128], F32)
    s_q2di = sbuf("s_q2di", [64, 128], F32)
    s_q2er = sbuf("s_q2er", [64, 128], F32)
    s_q2ei = sbuf("s_q2ei", [64, 128], F32)
    s_W1 = sbuf("s_W1", [64, 128], F32)
    s_W2 = sbuf("s_W2", [64, 128], F32)
    s_W3 = sbuf("s_W3", [64, 128], F32)

    p_a = es.enter_context(nc.psum_tensor("p_a", [128, 512], F32))
    p_b = es.enter_context(nc.psum_tensor("p_b", [128, 512], F32))
    p_c = es.enter_context(nc.psum_tensor("p_c", [128, 512], F32))
    p_d = es.enter_context(nc.psum_tensor("p_d", [128, 512], F32))
    p_sel = es.enter_context(nc.psum_tensor("p_sel", [128, 16], F32))
    p_tr = es.enter_context(nc.psum_tensor("p_tr", [128, 128], BF16))
    p_tr2 = es.enter_context(nc.psum_tensor("p_tr2", [128, 128], BF16))
    p_tr3 = es.enter_context(nc.psum_tensor("p_tr3", [128, 128], BF16))

    smc = sem("smc"); smprm = sem("smprm")
    smcastA = sem("smcastA"); smcastB = sem("smcastB"); smcastC = sem("smcastC")
    sm0 = [sem(f"sm0_{i}") for i in range(20)]
    smx = [sem("smx0"), sem("smx1")]
    smln = sem("smln"); smrs = sem("smrs"); smlv = sem("smlv")
    smtp = sem("smtp"); smcpy = sem("smcpy"); smxst = sem("smxst")
    smyt1 = sem("smyt1"); smytr = sem("smytr")
    smw2 = sem("smw2")
    smu1 = sem("smu1"); smu2 = sem("smu2")
    smu3 = [sem("smu3A"), sem("smu3B")]
    smlg = sem("smlg"); smsg = sem("smsg"); smtk = sem("smtk")
    smgt = sem("smgt"); smge = sem("smge"); smgo = sem("smgo")
    smcc = sem("smcc"); smucs = sem("smucs")
    sme1 = sem("sme1"); sme2 = sem("sme2")
    smr1 = sem("smr1"); smra = sem("smra")
    smkv = [sem("smkv0"), sem("smkv1")]
    smxw2 = [sem("smxw2_0"), sem("smxw2_1")]
    smgx = sem("smgx")
    smk1 = sem("smk1"); smk2 = sem("smk2"); smk3 = sem("smk3")
    smv1 = sem("smv1"); smv2 = sem("smv2")
    smv3 = [sem("smv3A"), sem("smv3B")]
    smyf = sem("smyf"); smwt3 = sem("smwt3")
    smg1 = sem("smg1"); smg2 = sem("smg2"); smg3 = sem("smg3")
    smo1 = sem("smo1"); smo2 = sem("smo2")
    smf1 = [sem("smf1A"), sem("smf1B")]
    smf2 = [sem("smf2A"), sem("smf2B")]
    smf3 = [sem("smf3A"), sem("smf3B")]
    smout = [sem("smoutA"), sem("smoutB")]

    def wrap_sin_cos(vector, src_ang, dst_sin_arg, dst_cos_arg):
        vector.tensor_scalar(dst_sin_arg, src_ang, INV_2PI, MAGIC, ALU.mult, ALU.add)
        vector.drain()
        vector.tensor_scalar(dst_sin_arg, dst_sin_arg, MAGIC, None, ALU.subtract)
        vector.drain()
        vector.scalar_tensor_tensor(dst_sin_arg, dst_sin_arg, -TWO_PI, src_ang, ALU.mult, ALU.add)
        vector.drain()
        vector.tensor_scalar_min(dst_sin_arg, dst_sin_arg, 3.141592653589793)
        vector.drain()
        vector.tensor_scalar_max(dst_sin_arg, dst_sin_arg, -3.141592653589793)
        vector.drain()

    def act_sin_cos(scalar, wrapped, cos_out, pcount):
        scalar.activation(cos_out, wrapped, AF.Abs)
        scalar.drain()
        scalar.activation(cos_out, cos_out, AF.Sin, bias=s_hpi[0:pcount, 0:1], scale=-1.0)
        return scalar.activation(wrapped, wrapped, AF.Sin)

    with nc.Block() as blk:
        @blk.sync
        def _(sync):
            sync.dma_start(out=s_id[:], in_=ident.ap()).then_inc(smc.inc(16), 16)
            sync.dma_start(out=s_trP[:], in_=trP_t.ap()).then_inc(smc.inc(16), 16)
            sync.dma_start(out=s_trP2[:], in_=trP2_t.ap()).then_inc(smc.inc(16), 16)
            sync.dma_start(out=s_inb[:], in_=AP(in_b, 0, [[0, 128], [1, D]])).then_inc(smc.inc(16), 16)
            sync.dma_start(out=s_outb[:], in_=AP(out_b, 0, [[0, 128], [1, D]])).then_inc(smc.inc(16), 16)
            sync.dma_start(out=s_glb[:], in_=AP(glu_b, 0, [[1, 128], [128, 16]]),
                           allow_slow_non_contiguous=True).then_inc(smc.inc(16), 16)
            sync.dma_start(out=s_rsc[:], in_=AP(rscale, 0, [[0, 128], [1, 1]])).then_inc(smc.inc(16), 16)
            sync.dma_start(out=s_uoff[:], in_=uoff_t.ap().unsqueeze(0)).then_inc(smc.inc(16), 16)
            sync.dma_start(out=s_yoff[:], in_=yoff_t.ap().unsqueeze(0)).then_inc(smc.inc(16), 16)

        @blk.gpsimd
        def _(gp):
            gp.dma_start(out=w1bf.ap(), in_=in_W.ap()).then_inc(smcastA.inc(16), 16)
            gp.dma_start(out=eselbf.ap(), in_=esel.ap()).then_inc(smcastA.inc(16), 16)
            gp.wait_ge(smx[1].h, 16)
            gp.dma_start(out=glubf.ap(), in_=glu_W.ap()).then_inc(smcastB.inc(16), 16)
            gp.dma_start(out=outbf.ap(), in_=out_W.ap()).then_inc(smcastB.inc(16), 16)
            gp.dma_start(out=keysbf.ap(), in_=keys_t.ap()).then_inc(smcastC.inc(16), 16)
            gp.dma_start(out=valsbf.ap(), in_=vals_t.ap()).then_inc(smcastC.inc(16), 16)

        g0 = ExitStack()
        s_dtc = sbuf("s_dtc", [128, 1], F32, g0)
        s_lar = sbuf("s_lar", [128, NST], F32, g0)
        s_aim = sbuf("s_aim", [128, NST], F32, g0)
        s_cr = sbuf("s_cr", [128, NST], F32, g0)
        s_ci = sbuf("s_ci", [128, NST], F32, g0)
        s_dskc = sbuf("s_dskc", [128, 1], F32, g0)
        s_ema = sbuf("s_ema", [128, NST], F32, g0)
        s_dtar = sbuf("s_dtar", [128, NST], F32, g0)
        s_dtai = sbuf("s_dtai", [128, NST], F32, g0)
        s_absw = sbuf("s_absw", [128, NST], F32, g0)
        s_ang = sbuf("s_ang", [128, NST], F32, g0)
        s_ang2 = sbuf("s_ang2", [128, NST], F32, g0)
        s_sinw = sbuf("s_sinw", [128, NST], F32, g0)
        s_cosw = sbuf("s_cosw", [128, NST], F32, g0)
        s_t1 = sbuf("s_t1", [128, NST], F32, g0)
        s_t2 = sbuf("s_t2", [128, NST], F32, g0)
        s_t3 = sbuf("s_t3", [128, NST], F32, g0)
        s_t4 = sbuf("s_t4", [128, NST], F32, g0)
        s_den = sbuf("s_den", [128, NST], F32, g0)
        s_trr = sbuf("s_trr", [128, NST], F32, g0)
        s_tii = sbuf("s_tii", [128, NST], F32, g0)
        s_er = sbuf("s_er", [128, NST], F32, g0)
        s_ei = sbuf("s_ei", [128, NST], F32, g0)
        s_wa = sbuf("s_wa", [64, 128], F32, g0)
        s_wb = sbuf("s_wb", [64, 128], F32, g0)

        @blk.sync
        def _(sync):
            sync.dma_start(out=s_dtc[:], in_=log_dt.ap().unsqueeze(1)).then_inc(smprm.inc(16), 16)
            sync.dma_start(out=s_lar[:], in_=lar_t.ap()).then_inc(smprm.inc(16), 16)
            sync.dma_start(out=s_aim[:], in_=aim_t.ap()).then_inc(smprm.inc(16), 16)
            sync.dma_start(out=s_cr[:], in_=AP(C_p, 0, [[NST * 2, 128], [2, NST]]),
                           allow_slow_non_contiguous=True).then_inc(smprm.inc(16), 16)
            sync.dma_start(out=s_ci[:], in_=AP(C_p, 1, [[NST * 2, 128], [2, NST]]),
                           allow_slow_non_contiguous=True).then_inc(smprm.inc(16), 16)
            sync.dma_start(out=s_dskc[:], in_=dsk_t.ap().unsqueeze(1)).then_inc(smprm.inc(16), 16)

        @blk.scalar
        def _(scalar):
            scalar.wait_ge(smprm.h, smprm.v)
            scalar.activation(s_dtc[:], s_dtc[:], AF.Exp)
            scalar.drain()
            scalar.activation(s_ema[:], s_lar[:], AF.Exp).then_inc(sm0[0].inc(), 1)

        @blk.vector
        def _(vector):
            vector.wait_ge(sm0[0].h, 1)
            vector.memset(s_hpi[:], HALF_PI)
            vector.drain()
            vector.tensor_scalar(s_dtar[:], s_ema[:], s_dtc[:, 0:1], -1.0, ALU.mult, ALU.mult)
            vector.tensor_scalar(s_dtai[:], s_aim[:], s_dtc[:, 0:1], None, ALU.mult)
            vector.drain()
            wrap_sin_cos(vector, s_dtai[:], s_ang[:], s_ang2[:])
            vector.tensor_copy(s_t1[:], s_dtar[:])
            vector.drain().then_inc(sm0[1].inc(), 1)

        @blk.scalar
        def _(scalar):
            scalar.wait_ge(sm0[1].h, 1)
            scalar.activation(s_absw[:], s_dtar[:], AF.Exp)
            act_sin_cos(scalar, s_ang[:], s_cosw[:], 128)
            scalar.drain()
            scalar.activation(s_sinw[:], s_ang[:], AF.Identity).then_inc(sm0[2].inc(), 1)

        @blk.vector
        def _(vector):
            vector.wait_ge(sm0[2].h, 1)
            vector.tensor_mul(s_t1[:], s_absw[:], s_cosw[:])
            vector.tensor_mul(s_t2[:], s_absw[:], s_sinw[:])
            vector.tensor_mul(s_den[:], s_ema[:], s_ema[:])
            vector.tensor_mul(s_t3[:], s_aim[:], s_aim[:])
            vector.drain()
            vector.tensor_add(s_den[:], s_den[:], s_t3[:])
            vector.drain()
            vector.reciprocal(s_den[:], s_den[:])
            vector.tensor_scalar(s_t1[:], s_t1[:], -1.0, None, ALU.add)
            vector.drain()
            vector.tensor_mul(s_t3[:], s_t1[:], s_ema[:])
            vector.tensor_mul(s_t4[:], s_t2[:], s_aim[:])
            vector.drain()
            vector.tensor_sub(s_trr[:], s_t4[:], s_t3[:])
            vector.drain()
            vector.tensor_mul(s_t3[:], s_t2[:], s_ema[:])
            vector.tensor_mul(s_t4[:], s_t1[:], s_aim[:])
            vector.drain()
            vector.tensor_add(s_tii[:], s_t3[:], s_t4[:])
            vector.drain()
            vector.tensor_scalar(s_tii[:], s_tii[:], -1.0, None, ALU.mult)
            vector.drain()
            vector.tensor_mul(s_t3[:], s_cr[:], s_trr[:])
            vector.tensor_mul(s_t4[:], s_ci[:], s_tii[:])
            vector.drain()
            vector.tensor_sub(s_er[:], s_t3[:], s_t4[:])
            vector.drain()
            vector.tensor_mul(s_t3[:], s_cr[:], s_tii[:])
            vector.tensor_mul(s_t4[:], s_ci[:], s_trr[:])
            vector.drain()
            vector.tensor_add(s_ei[:], s_t3[:], s_t4[:])
            vector.drain()
            vector.tensor_mul(s_er[:], s_er[:], s_den[:])
            vector.tensor_mul(s_ei[:], s_ei[:], s_den[:])
            vector.drain().then_inc(sm0[3].inc(), 1)

        @blk.sync
        def _(sync):
            sync.wait_ge(sm0[3].h, 1)
            for row, src in [(0, s_dtar), (1, s_dtai), (2, s_er), (3, s_ei)]:
                sync.dma_start(out=AP(prow, row * 4096, [[NST, 128], [1, NST]]),
                               in_=src[:]).then_inc(sm0[4].inc(16), 16)
            sync.wait_ge(sm0[4].h, sm0[4].v)
            for dst, row in [(s_q2dr, 0), (s_q2di, 1), (s_q2er, 2), (s_q2ei, 3)]:
                sync.dma_start(out=dst[0:32, :], in_=AP(prow, row * 4096, [[1, NST], [NST, 128]]),
                               allow_slow_non_contiguous=True).then_inc(sm0[5].inc(16), 16)
                sync.dma_start(out=dst[32:64, :], in_=AP(prow, row * 4096, [[1, NST], [NST, 128]]),
                               allow_slow_non_contiguous=True).then_inc(sm0[5].inc(16), 16)

        @blk.vector
        def _(vector):
            vector.wait_ge(sm0[5].h, sm0[5].v)
            vector.tensor_scalar(s_W1[:], s_q2dr[:], 128.0, None, ALU.mult)
            vector.tensor_scalar(s_W2[:], s_q2di[:], 128.0, None, ALU.mult)
            vector.drain()
            wrap_sin_cos(vector, s_W2[:], s_wa[:], s_wb[:])
            vector.drain().then_inc(sm0[6].inc(), 1)

        @blk.scalar
        def _(scalar):
            scalar.wait_ge(sm0[6].h, 1)
            scalar.activation(s_W1[:], s_W1[:], AF.Exp)
            act_sin_cos(scalar, s_wa[:], s_wb[:], 64).then_inc(sm0[7].inc(), 1)

        @blk.vector
        def _(vector):
            vector.wait_ge(sm0[7].h, 1)
            vector.tensor_mul(s_wa[:], s_W1[:], s_wa[:])
            vector.drain()
            vector.tensor_mul(s_W1[:], s_W1[:], s_wb[:])
            vector.tensor_scalar(s_W2[0:32, :], s_wa[0:32, :], -1.0, None, ALU.mult)
            vector.tensor_copy(s_W2[32:64, :], s_wa[32:64, :])
            vector.tensor_copy(s_W3[0:32, :], s_wa[0:32, :])
            vector.tensor_scalar(s_W3[32:64, :], s_wa[32:64, :], -1.0, None, ALU.mult)
            vector.drain().then_inc(sm0[8].inc(), 1)

        g2 = ExitStack()
        s_x = [sbuf("s_x0", [128, D], F32, g2), sbuf("s_x1", [128, D], F32, g2)]
        s_xsq = sbuf("s_xsq", [128, D], F32, g2)
        s_xn = [sbuf("s_xn0", [128, D], BF16, g2), sbuf("s_xn1", [128, D], BF16, g2)]
        s_st = sbuf("s_st", [128, 8], F32, g2)
        w1t = [sbuf(f"w1t{k}", [128, D], BF16, g2) for k in range(8)]
        selT = sbuf("selT", [128, 8 * 16], BF16, g2)
        s_xg = [[sbuf(f"s_xg{g}_{k}", [128, 1024], BF16, g2) for k in range(8)] for g in range(2)]
        s_ust = [sbuf(f"s_ust{pp}", [128, D], BF16, g2) for pp in range(2)]
        s_lgo = sbuf("s_lgo", [128, 16], F32, g2)
        s_lpr = sbuf("s_lpr", [128, 16], F32, g2)
        s_lgw = sbuf("s_lgw", [128, 16], F32, g2)
        s_lmx = sbuf("s_lmx", [128, 1], F32, g2)
        s_leq = sbuf("s_leq", [128, 16], F32, g2)
        s_lmk = sbuf("s_lmk", [128, 16], F32, g2)
        s_lsum = sbuf("s_lsum", [128, 1], F32, g2)
        s_gbf = sbuf("s_gbf", [128, 32 * 16], BF16, g2)
        s_gT = sbuf("s_gT", [16, T], BF16, g2)

        @blk.sync
        def _(sync):
            sync.dma_start(out=s_x[0][:], in_=AP(x_in, 0, [[D, 128], [1, D]])).then_inc(smx[0].inc(16), 16)
            sync.dma_start(out=s_x[1][:], in_=AP(x_in, 128 * D, [[D, 128], [1, D]])).then_inc(smx[1].inc(16), 16)
            sync.wait_ge(smcastA.h, 32)
            for k in range(8):
                sync.dma_start_transpose(w1t[k][:],
                                         AP(w1bf, 128 * k, [[D, D], [1, 128]])).then_inc(smw2.inc(16), 16)
                sync.dma_start_transpose(selT[:, 16 * k:16 * k + 16],
                                         AP(eselbf, 128 * k, [[D, 16], [1, 128]])).then_inc(smw2.inc(16), 16)
            for i in range(32):
                if i + 2 < 32:
                    sync.wait_ge(smlv.h, i + 1)
                    sync.dma_start(out=s_x[i % 2][:],
                                   in_=AP(x_in, (i + 2) * 128 * D, [[D, 128], [1, D]])).then_inc(smx[i % 2].inc(16), 16)
                sync.wait_ge(smu2.h, i + 1)
                sync.dma_start(out=AP(u_loc, i * 128 * D, [[D, 128], [1, D]]),
                               in_=s_ust[i % 2][:]).then_inc(smu3[i % 2].inc(16), 16)
                if i % 8 == 7:
                    g = i // 8
                    sync.wait_ge(smcpy.h, 8 * (i + 1))
                    for k in range(8):
                        sync.dma_start(out=AP(xnT_d, (128 * k) * T + 1024 * g, [[T, 128], [1, 1024]]),
                                       in_=s_xg[g % 2][k][:]).then_inc(smxst.inc(16), 16)
            sync.wait_ge(smge.h, 32)
            sync.dma_start(out=gT_d.ap(), in_=s_gT[:]).then_inc(smgo.inc(16), 16)

        @blk.tensor
        def _(tensor):
            tensor.wait_ge(smc.h, smc.v)
            for i in range(32):
                tensor.wait_ge(smlv.h, i + 1)
                for k in range(8):
                    t_idx = 8 * i + k
                    if t_idx >= 2:
                        tensor.wait_ge(smcpy.h, t_idx - 1)
                    tensor.transpose([p_tr, p_tr2][k % 2][:], s_xn[i % 2][:, 128 * k:128 * k + 128],
                                     s_id[:]).then_inc(smtp.inc(), 1)
                if i == 0:
                    tensor.wait_ge(smw2.h, 256)
                tensor.wait_ge(smcpy.h, 8 * (i + 1))
                if i >= 1:
                    tensor.wait_ge(smu2.h, i)
                pu = [(p_a, p_b), (p_c, p_d)][i % 2]
                col = 128 * (i % 8)
                gb = (i // 8) % 2
                for k in range(8):
                    lhs = s_xg[gb][k][:, col:col + 128]
                    tensor.matmul(pu[0][:], lhs, w1t[k][:, 0:512], start=(k == 0), stop=(k == 7))
                    tensor.matmul(pu[1][:], lhs, w1t[k][:, 512:1024], start=(k == 0), stop=(k == 7))
                    mm = tensor.matmul(p_sel[:], lhs, selT[:, 16 * k:16 * k + 16],
                                       start=(k == 0), stop=(k == 7))
                mm.then_inc(smu1.inc(), 1)
            for ch in range(32):
                tensor.wait_ge(smtk.h, ch + 1)
                if ch >= 1:
                    tensor.wait_ge(smge.h, ch)
                tensor.transpose(p_tr3[0:16, :], s_gbf[:, 16 * ch:16 * ch + 16], s_id[:]).then_inc(smgt.inc(), 1)

        @blk.scalar
        def _(scalar):
            for i in range(32):
                scalar.wait_ge(smln.h, i + 1)
                scalar.activation(s_st[:, 5:6], s_st[:, 4:5], AF.Ln)
                scalar.drain()
                scalar.activation(s_rs[:, i:i + 1], s_st[:, 5:6], AF.Exp, scale=-0.5).then_inc(smrs.inc(), 1)
                g = i // 8
                if i % 8 == 0 and g >= 2:
                    scalar.wait_ge(smxst.h, 128 * (g - 1))
                for k in range(8):
                    scalar.wait_ge(smtp.h, 8 * i + k + 1)
                    scalar.activation(s_xg[g % 2][k][:, 128 * (i % 8):128 * (i % 8) + 128],
                                      [p_tr, p_tr2][k % 2][:], AF.Identity).then_inc(smcpy.inc(), 1)
                if i >= 1:
                    scalar.wait_ge(smlg.h, i)
                    scalar.activation(s_lpr[:], s_lgo[:], AF.Sigmoid).then_inc(smsg.inc(), 1)
            scalar.wait_ge(smlg.h, 32)
            scalar.activation(s_lpr[:], s_lgo[:], AF.Sigmoid).then_inc(smsg.inc(), 1)

        @blk.vector
        def _(vector):
            vector.wait_ge(smc.h, smc.v)

            def evac_topk(j):
                vector.wait_ge(smu1.h, j + 1)
                if j >= 2:
                    vector.wait_ge(smu3[j % 2].h, 16 * (j // 2))
                pu = [(p_a, p_b), (p_c, p_d)][j % 2]
                vector.tensor_add(s_ust[j % 2][:, 0:512], pu[0][:], s_inb[:, 0:512])
                vector.tensor_add(s_ust[j % 2][:, 512:1024], pu[1][:], s_inb[:, 512:1024])
                vector.tensor_copy(s_lgo[:], p_sel[:])
                vector.drain().then_inc(smu2.inc(), 1)
                vector.tensor_copy(s_lgw[:], s_lgo[:])
                vector.memset(s_lmk[:], 0.0)
                vector.drain().then_inc(smlg.inc(), 1)
                for r in range(4):
                    vector.reduce_max(s_lmx[:], s_lgw[:], axis=AX)
                    vector.drain()
                    vector.tensor_scalar(s_leq[:], s_lgw[:], s_lmx[:, 0:1], None, ALU.is_ge)
                    vector.drain()
                    vector.tensor_add(s_lmk[:], s_lmk[:], s_leq[:])
                    vector.scalar_tensor_tensor(s_lgw[:], s_leq[:], -1e30, s_lgw[:], ALU.mult, ALU.add)
                    vector.drain()
                vector.wait_ge(smsg.h, j + 1)
                vector.tensor_mul(s_lpr[:], s_lpr[:], s_lmk[:])
                vector.drain()
                vector.reduce_sum(s_lsum[:], s_lpr[:], axis=AX)
                vector.drain()
                vector.tensor_scalar(s_lsum[:], s_lsum[:], 1e-9, None, ALU.max)
                vector.drain()
                vector.reciprocal(s_lsum[:], s_lsum[:])
                vector.drain()
                vector.tensor_mul(s_lsum[:], s_lsum[:], s_rsc[:])
                vector.drain()
                vector.tensor_scalar(s_g[:, 16 * j:16 * j + 16], s_lpr[:], s_lsum[:, 0:1], None, ALU.mult)
                vector.drain()
                vector.tensor_copy(s_gbf[:, 16 * j:16 * j + 16], s_g[:, 16 * j:16 * j + 16])
                vector.drain().then_inc(smtk.inc(), 1)

            for i in range(33):
                if i < 32:
                    vector.wait_ge(smx[i % 2].h, 16 * (i // 2 + 1))
                    if i >= 2:
                        vector.wait_ge(smtp.h, 8 * i - 8)
                    xb = s_x[i % 2]
                    vector.tensor_mul(s_xsq[:], xb[:], xb[:])
                    vector.reduce_sum(s_st[:, 0:1], xb[:], axis=AX)
                    vector.drain()
                    vector.reduce_sum(s_st[:, 1:2], s_xsq[:], axis=AX)
                    vector.drain()
                    vector.tensor_scalar(s_mu[:, i:i + 1], s_st[:, 0:1], 1.0 / D, None, ALU.mult)
                    vector.tensor_scalar(s_st[:, 2:3], s_st[:, 1:2], 1.0 / D, None, ALU.mult)
                    vector.drain()
                    vector.tensor_mul(s_st[:, 3:4], s_mu[:, i:i + 1], s_mu[:, i:i + 1])
                    vector.drain()
                    vector.tensor_sub(s_st[:, 4:5], s_st[:, 2:3], s_st[:, 3:4])
                    vector.drain()
                    vector.tensor_scalar(s_st[:, 4:5], s_st[:, 4:5], 1e-5, None, ALU.add)
                    vector.drain().then_inc(smln.inc(), 1)
                    vector.wait_ge(smrs.h, i + 1)
                    vector.tensor_scalar(s_xn[i % 2][:], xb[:], s_mu[:, i:i + 1], s_rs[:, i:i + 1],
                                         ALU.subtract, ALU.mult)
                    vector.drain().then_inc(smlv.inc(), 1)
                if i >= 1:
                    evac_topk(i - 1)
            for ch in range(32):
                vector.wait_ge(smgt.h, ch + 1)
                vector.tensor_copy(s_gT[:, 128 * ch:128 * ch + 128], p_tr3[0:16, :])
                vector.drain().then_inc(smge.inc(), 1)


        gB = ExitStack()
        s_ba = sbuf("s_ba", [128, 4096], F32, gB)
        s_bb = sbuf("s_bb", [128, 4096], F32, gB)
        s_bc = sbuf("s_bc", [128, 4096], F32, gB)
        s_bd = sbuf("s_bd", [128, 4096], F32, gB)
        s_trK = sbuf("s_trK", [128, 128], F32, gB)
        s_ksb = sbuf("s_ksb", [128, 128], F32, gB)
        s_kbf = sbuf("s_kbf", [128, 128], BF16, gB)
        s_ktr = sbuf("s_ktr", [128, 128], BF16, gB)
        s_zro = sbuf("s_zro", [128, 128], BF16, gB)
        s_trQ0 = sbuf("s_trQ0", [128, 128], F32, gB)
        s_d2s = sbuf("s_d2s", [64, 4096], BF16, gB)
        s_ppt = sbuf("s_ppt", [128, 4096], BF16, gB)

        @blk.sync
        def _(sync):
            sync.wait_ge(sm0[3].h, 1)
            sync.wait_ge(sm0[4].h, sm0[4].v)
            sync.dma_start(out=s_trK[:], in_=trK_t.ap()).then_inc(sm0[9].inc(16), 16)
            sync.dma_start(out=s_ba[:], in_=AP(prow, 0, [[0, 128], [1, 4096]])).then_inc(sm0[9].inc(16), 16)
            sync.dma_start(out=s_bb[:], in_=AP(prow, 4096, [[0, 128], [1, 4096]])).then_inc(sm0[9].inc(16), 16)
            sync.wait_ge(sm0[12].h, 1)
            sync.dma_start(out=AP(pp_d, 0, [[16384, 128], [1, 4096]]),
                           in_=s_ppt[:]).then_inc(sm0[19].inc(16), 16)
            sync.wait_ge(sm0[12].h, 2)
            sync.dma_start(out=AP(pp_d, 4096, [[16384, 128], [1, 4096]]),
                           in_=s_ppt[:]).then_inc(sm0[19].inc(16), 16)
            sync.dma_start(out=s_ba[:], in_=AP(prow, 0, [[0, 128], [1, 4096]])).then_inc(sm0[9].inc(16), 16)
            sync.dma_start(out=s_bb[:], in_=AP(prow, 4096, [[0, 128], [1, 4096]])).then_inc(sm0[9].inc(16), 16)
            sync.wait_ge(sm0[12].h, 3)
            sync.dma_start(out=AP(pp_d, 8192, [[16384, 128], [1, 4096]]),
                           in_=s_ppt[:]).then_inc(sm0[19].inc(16), 16)
            sync.wait_ge(sm0[12].h, 4)
            sync.dma_start(out=AP(pp_d, 12288, [[16384, 128], [1, 4096]]),
                           in_=s_ppt[:]).then_inc(sm0[19].inc(16), 16)

        @blk.vector
        def _(vector):
            vector.wait_ge(smc.h, smc.v)
            for pas in range(2):
                vector.wait_ge(sm0[9].h, 16 + 32 * (pas + 1))
                if pas == 1:
                    vector.wait_ge(sm0[19].h, 32)
                ramp_r = s_trP[:, 0:1] if pas == 0 else s_trP2[:, 0:1]
                ramp_i = s_trP[:, 0:1] if pas == 0 else s_trP2[:, 1:2]
                vector.tensor_scalar(s_ba[:], s_ba[:], ramp_r, None, ALU.mult)
                vector.tensor_scalar(s_bb[:], s_bb[:], ramp_i, None, ALU.mult)
                vector.drain()
                wrap_sin_cos(vector, s_bb[:], s_bc[:], s_bd[:])
                vector.tensor_copy(s_t2[:, 0:1], s_t1[:, 0:1])
                vector.drain().then_inc(sm0[10].inc(), 1)
                vector.wait_ge(sm0[11].h, pas + 1)
                vector.tensor_mul(s_ppt[:], s_ba[:], s_bd[:])
                vector.drain()
                if pas == 1:
                    vector.tensor_scalar(s_ppt[:], s_ppt[:], -1.0, None, ALU.mult)
                    vector.drain()
                vector.tensor_copy(s_t2[:, 1:2], s_t1[:, 0:1])
                vector.drain().then_inc(sm0[12].inc(), 1)
                vector.wait_ge(sm0[19].h, 16 + 32 * pas)
                vector.tensor_mul(s_ppt[:], s_ba[:], s_bc[:])
                vector.drain()
                if pas == 1:
                    vector.tensor_scalar(s_ppt[:], s_ppt[:], -1.0, None, ALU.mult)
                    vector.drain()
                vector.tensor_copy(s_t2[:, 2:3], s_t1[:, 0:1])
                vector.drain().then_inc(sm0[12].inc(), 1)

        @blk.scalar
        def _(scalar):
            for pas in range(2):
                scalar.wait_ge(sm0[10].h, pas + 1)
                scalar.activation(s_ba[:], s_ba[:], AF.Exp)
                act_sin_cos(scalar, s_bc[:], s_bd[:], 128).then_inc(sm0[11].inc(), 1)


        @blk.vector
        def _(vector):
            vector.wait_ge(sm0[12].h, 4)
            a_dr = s_dtar[:].unsqueeze(2).broadcast_to([128, 32, 128])
            a_di = s_dtai[:].unsqueeze(2).broadcast_to([128, 32, 128])
            a_tk = s_trK[:].unsqueeze(1).broadcast_to([128, 32, 128])
            vector.tensor_tensor(s_ba[:].rearrange("p (n m) -> p n m", m=128), a_dr, a_tk, ALU.mult)
            vector.tensor_tensor(s_bb[:].rearrange("p (n m) -> p n m", m=128), a_di, a_tk, ALU.mult)
            vector.drain()
            wrap_sin_cos(vector, s_bb[:], s_bc[:], s_bd[:])
            vector.tensor_copy(s_t2[:], s_t1[:])
            vector.drain().then_inc(sm0[13].inc(), 1)

        @blk.scalar
        def _(scalar):
            scalar.wait_ge(sm0[13].h, 1)
            scalar.activation(s_ba[:], s_ba[:], AF.Exp)
            act_sin_cos(scalar, s_bc[:], s_bd[:], 128).then_inc(sm0[14].inc(), 1)

        @blk.vector
        def _(vector):
            vector.wait_ge(sm0[14].h, 1)
            vector.tensor_mul(s_bd[:], s_ba[:], s_bd[:])
            vector.tensor_mul(s_bc[:], s_ba[:], s_bc[:])
            vector.drain()
            e_r = s_er[:].unsqueeze(2).broadcast_to([128, 32, 128])
            e_i = s_ei[:].unsqueeze(2).broadcast_to([128, 32, 128])
            v_ba = s_ba[:].rearrange("p (n m) -> p n m", m=128)
            v_bb = s_bb[:].rearrange("p (n m) -> p n m", m=128)
            v_bc = s_bc[:].rearrange("p (n m) -> p n m", m=128)
            v_bd = s_bd[:].rearrange("p (n m) -> p n m", m=128)
            vector.tensor_tensor(v_ba, e_r, v_bd, ALU.mult)
            vector.tensor_tensor(v_bb, e_i, v_bc, ALU.mult)
            vector.drain()
            vector.tensor_sub(s_ba[:], s_ba[:], s_bb[:])
            vector.drain()
            vector.reduce_sum(s_ksb[:].unsqueeze(2),
                              AP(s_ba[:].tensor, 0, [[4096, 128], [1, 128], [128, 32]]),
                              axis=AX)
            vector.drain()
            vector.tensor_scalar(s_ksb[:], s_ksb[:], 2.0, None, ALU.mult)
            vector.drain()
            vector.tensor_add(s_ksb[:, 127:128], s_ksb[:, 127:128], s_dskc[:])
            vector.drain()
            vector.tensor_copy(s_kbf[:], s_ksb[:])
            vector.drain()
            vector.memset(s_zro[:], 0.0)
            vector.drain().then_inc(sm0[15].inc(), 1)

        @blk.tensor
        def _(tensor):
            tensor.wait_ge(sm0[15].h, 1)
            tensor.wait_ge(smc.h, smc.v)
            tensor.transpose(p_tr[:], s_kbf[:], s_id[:]).then_inc(sm0[16].inc(), 1)

        @blk.vector
        def _(vector):
            vector.wait_ge(sm0[16].h, 1)
            vector.tensor_copy(s_ktr[:], p_tr[:])
            vector.drain().then_inc(sm0[17].inc(), 1)

        @blk.sync
        def _(sync):
            sync.wait_ge(sm0[17].h, 1)
            sync.dma_start(out=AP(k2r, 129 * 128, [[128, 127], [1, 128]]),
                           in_=s_zro[0:127, :]).then_inc(sm0[18].inc(16), 16)
            sync.dma_start(out=AP(k2r, 0, [[128, 1], [1, 128]]),
                           in_=s_zro[0:1, :]).then_inc(sm0[18].inc(16), 16)
            sync.dma_start(out=AP(k2r, 128, [[128, 128], [1, 128]]),
                           in_=s_ktr[:]).then_inc(sm0[18].inc(16), 16)
            sync.dma_start(out=s_trQ0[:], in_=trQ_t.ap()).then_inc(sm0[18].inc(16), 16)

        smdb = sem("smdb"); smdbs = sem("smdbs"); smdbv = sem("smdbv"); smdbo = sem("smdbo")
        s_P2 = sbuf("s_P2", [64, 128], F32, gB)
        s_Q2 = sbuf("s_Q2", [64, 128], F32, gB)

        @blk.vector
        def _(vector):
            vector.wait_ge(sm0[15].h, 1)
            vector.wait_ge(sm0[18].h, sm0[18].v)
            vector.tensor_copy(s_P2[0:32, :], s_q2er[0:32, :])
            vector.tensor_scalar(s_P2[32:64, :], s_q2ei[32:64, :], -1.0, None, ALU.mult)
            vector.tensor_scalar(s_Q2[0:32, :], s_q2ei[0:32, :], -1.0, None, ALU.mult)
            vector.tensor_scalar(s_Q2[32:64, :], s_q2er[32:64, :], -1.0, None, ALU.mult)
            vector.drain()
            for slab in range(4):
                if slab >= 1:
                    vector.wait_ge(smdbo.h, 16 * slab)
                dr_b = s_q2dr[:, 32 * slab:32 * slab + 32].unsqueeze(2).broadcast_to([64, 32, 128])
                di_b = s_q2di[:, 32 * slab:32 * slab + 32].unsqueeze(2).broadcast_to([64, 32, 128])
                tq_b = s_trQ0[0:64, :].unsqueeze(1).broadcast_to([64, 32, 128])
                vector.tensor_tensor(s_ba[0:64, :].rearrange("p (c t) -> p c t", t=128), dr_b, tq_b, ALU.mult)
                vector.tensor_tensor(s_bb[0:64, :].rearrange("p (c t) -> p c t", t=128), di_b, tq_b, ALU.mult)
                vector.drain()
                wrap_sin_cos(vector, s_bb[0:64, :], s_bc[0:64, :], s_bd[0:64, :])
                vector.tensor_copy(s_t2[:, 3:4], s_t1[:, 0:1])
                vector.drain().then_inc(smdb.inc(), 1)
                vector.wait_ge(smdbs.h, slab + 1)
                vector.tensor_mul(s_bd[0:64, :], s_ba[0:64, :], s_bd[0:64, :])
                vector.tensor_mul(s_bc[0:64, :], s_ba[0:64, :], s_bc[0:64, :])
                vector.drain()
                p_b_ = s_P2[:, 32 * slab:32 * slab + 32].unsqueeze(2).broadcast_to([64, 32, 128])
                q_b_ = s_Q2[:, 32 * slab:32 * slab + 32].unsqueeze(2).broadcast_to([64, 32, 128])
                v_ba = s_ba[0:64, :].rearrange("p (c t) -> p c t", t=128)
                v_bb = s_bb[0:64, :].rearrange("p (c t) -> p c t", t=128)
                v_bc = s_bc[0:64, :].rearrange("p (c t) -> p c t", t=128)
                v_bd = s_bd[0:64, :].rearrange("p (c t) -> p c t", t=128)
                vector.tensor_tensor(v_ba, p_b_, v_bd, ALU.mult)
                vector.tensor_tensor(v_bb, q_b_, v_bc, ALU.mult)
                vector.drain()
                vector.tensor_add(s_ba[0:64, :], s_ba[0:64, :], s_bb[0:64, :])
                vector.drain()
                vector.tensor_scalar(s_d2s[:], s_ba[0:64, :], 2.0, None, ALU.mult)
                vector.drain().then_inc(smdbv.inc(), 1)

        @blk.scalar
        def _(scalar):
            for slab in range(4):
                scalar.wait_ge(smdb.h, slab + 1)
                scalar.activation(s_ba[0:64, :], s_ba[0:64, :], AF.Exp)
                act_sin_cos(scalar, s_bc[0:64, :], s_bd[0:64, :], 64).then_inc(smdbs.inc(), 1)

        @blk.sync
        def _(sync):
            for slab in range(4):
                sync.wait_ge(smdbv.h, slab + 1)
                sync.dma_start(out=AP(d2_d, slab * 4096, [[4 * 4096, 64], [1, 4096]]),
                               in_=s_d2s[:]).then_inc(smdbo.inc(16), 16)

        gB.close()
        g2.close()
        g0.close()
        g3 = ExitStack()
        s_ucs = sbuf("s_ucs", [128, 4 * 64 * 128], BF16, g3)
        s_a2 = sbuf("s_a2", [64, 128 * 4 * 64], BF16, g3)
        g3mA = ExitStack()
        s_pp2 = sbuf("s_pp2", [128, 8192], BF16, g3mA)
        g3m = ExitStack()
        s_aswp = sbuf("s_aswp", [64, 16384], BF16, g3m)
        s_sN = sbuf("s_sN", [64, 512], F32, g3m)
        s_sS = sbuf("s_sS", [64, 512], F32, g3m)
        s_rx = sbuf("s_rx", [64, 1024], F32, g3m)

        @blk.gpsimd
        def _(gp):
            gp.wait_ge(smu3[0].h, smu3[0].v)
            gp.wait_ge(smu3[1].h, smu3[1].v)
            gp.collective_compute(
                "AllGather", ALU.bypass,
                replica_groups=[list(range(NCORES))],
                ins=[u_loc.ap().opt()],
                outs=[u_gth.ap().opt()],
            ).then_inc(smcc.inc(), 1)

        @blk.sync
        def _(sync):
            sync.wait_ge(smcc.h, 1)
            sync.wait_ge(smc.h, smc.v)
            sync.wait_ge(smgo.h, 16)
            sync.wait_ge(smdbo.h, 64)
            sync.wait_ge(sm0[18].h, sm0[18].v)
            sync.wait_ge(sm0[19].h, 32)
            uoff = sync.value_load(s_uoff[0:1, 0:1])
            sync.wait_ge(sm0[19].h, 64)
            sync.dma_start(out=s_pp2[:], in_=AP(pp_d, 0, [[16384, 128], [1, 8192]])).then_inc(smucs.inc(16), 16)
            for r in range(NCORES):
                b, h = r // 2, r % 2
                sync.dma_start(
                    out=AP(s_ucs[:].tensor, b * 8192 + h * 4096,
                           [[32768, 128], [128, 32], [1, 128]]),
                    in_=AP(u_gth, uoff + r * T * D, [[D, 128], [128 * D, 32], [1, 128]]),
                ).then_inc(smucs.inc(16), 16)

        @blk.tensor
        def _(tensor):
            tensor.wait_ge(smucs.h, smucs.v)
            for pair in range(64):
                pe_bank = [p_a, p_b][pair % 2]
                if pair >= 2:
                    tensor.wait_ge(sme2.h, pair - 1)
                for two in range(2):
                    ch = 2 * pair + two
                    for ri in range(2):
                        mm = tensor.matmul(
                            pe_bank[32 * ri:32 * ri + 32, 256 * two:256 * two + 256],
                            s_pp2[:, ri * 4096 + ch * 32: ri * 4096 + ch * 32 + 32],
                            AP(s_ucs[:].tensor, ch, [[32768, 128], [128, 64], [8192, 4]]),
                            start=True, stop=True, tile_position=(0, 32 * ri))
                mm.then_inc(sme1.inc(), 1)

        @blk.scalar
        def _(scalar):
            for pair in range(64):
                pe_bank = [p_a, p_b][pair % 2]
                scalar.wait_ge(sme1.h, pair + 1)
                for two in range(2):
                    ch = 2 * pair + two
                    hp2, c2 = ch // 64, ch % 64
                    scalar.activation(
                        AP(s_a2[:].tensor, 16384 * hp2 + 4 * c2, [[32768, 64], [256, 64], [1, 4]]),
                        AP(pe_bank[:].tensor, 256 * two, [[512, 64], [1, 256]]),
                        AF.Identity)
                scalar.drain().then_inc(sme2.inc(), 1)

        smsw = sem("smsw")

        @blk.sync
        def _(sync):
            for hp in range(2):
                if hp == 0:
                    sync.wait_ge(sme2.h, sme2.v)
                else:
                    sync.wait_ge(smra.h, 1)
                sync.dma_start(out=s_aswp[0:32, :],
                               in_=s_a2[32:64, 16384 * hp:16384 * hp + 16384]).then_inc(smsw.inc(16), 16)
                sync.dma_start(out=s_aswp[32:64, :],
                               in_=s_a2[0:32, 16384 * hp:16384 * hp + 16384]).then_inc(smsw.inc(16), 16)

        @blk.vector
        def _(vector):
            vector.wait_ge(sm0[8].h, 1)
            for hp in range(2):
                vector.wait_ge(smsw.h, 32 * (hp + 1))
                vector.memset(s_sN[:], 0.0)
                vector.memset(s_sS[:], 0.0)
                vector.drain()
                w1b = s_W1[:, 64 * hp:64 * hp + 64].unsqueeze(2).broadcast_to([64, 64, 4])
                w2b = s_W2[:, 64 * hp:64 * hp + 64].unsqueeze(2).broadcast_to([64, 64, 4])
                w3b = s_W3[:, 64 * hp:64 * hp + 64].unsqueeze(2).broadcast_to([64, 64, 4])
                t1 = s_rx[:, 0:256].rearrange("p (c b) -> p c b", b=4)
                t2 = s_rx[:, 256:512].rearrange("p (c b) -> p c b", b=4)
                t3 = s_rx[:, 512:768].rearrange("p (c b) -> p c b", b=4)
                t4 = s_rx[:, 768:1024].rearrange("p (c b) -> p c b", b=4)
                for J in range(64):
                    pv, cu = (J % 2), ((J + 1) % 2)
                    nP = s_sN[:, 256 * pv:256 * pv + 256].rearrange("p (c b) -> p c b", b=4)
                    nC = s_sN[:, 256 * cu:256 * cu + 256]
                    sP = s_sS[:, 256 * pv:256 * pv + 256].rearrange("p (c b) -> p c b", b=4)
                    sC = s_sS[:, 256 * cu:256 * cu + 256]
                    aj0 = 16384 * hp + 256 * J
                    vector.tensor_tensor(t1, nP, w1b, ALU.mult)
                    vector.tensor_tensor(t2, sP, w2b, ALU.mult)
                    vector.tensor_tensor(t3, nP, w3b, ALU.mult)
                    vector.tensor_tensor(t4, sP, w1b, ALU.mult)
                    if J >= 1:
                        vector.tensor_copy(s_a2[:, aj0 - 256:aj0],
                                           s_sN[:, 256 * pv:256 * pv + 256])
                    dr1 = vector.drain()
                    if hp == 1 and J == 32:
                        dr1.then_inc(smr1.inc(), 1)
                    vector.tensor_add(s_rx[:, 0:256], s_rx[:, 0:256], s_rx[:, 256:512])
                    vector.tensor_add(s_rx[:, 768:1024], s_rx[:, 768:1024], s_rx[:, 512:768])
                    vector.drain()
                    vector.tensor_tensor(nC, s_rx[:, 0:256], s_a2[:, aj0:aj0 + 256], ALU.add)
                    vector.tensor_tensor(sC, s_rx[:, 768:1024], s_aswp[:, 256 * J:256 * J + 256], ALU.add)
                    dr3 = vector.drain()
                    if J == 63:
                        dr3.then_inc([smra, smr1][hp].inc(), 1)

        g3m.close()
        g3mA.close()

        g3b = ExitStack()
        s_tp = [sbuf("s_tp0", [128, 32 * 128], BF16, g3b)]
        s_d2w = [sbuf("s_d2w0", [64, 4096], BF16, g3b)]
        s_yst = sbuf("s_yst", [128, 4 * 32 * 128], BF16, g3b)

        smtl = sem("smtl")
        smd1 = sem("smd1")
        smd2 = sem("smd2")
        smyo2 = sem("smyo2")

        @blk.sync
        def _(sync):
            sync.wait_ge(sm0[18].h, sm0[18].v)
            sync.wait_ge(smdbo.h, smdbo.v)
            for half in range(2):
                for slab in range(4):
                    idx = half * 4 + slab
                    pp = 0
                    if idx >= 1:
                        sync.wait_ge(smd1.h, 8 * idx)
                    sync.dma_start(out=s_tp[pp][:].rearrange("p (t c) -> p t c", c=32),
                                   in_=AP(k2r, 128 * 128 + 32 * slab, [[128, 128], [-128, 128], [1, 32]]),
                                   allow_slow_non_contiguous=True).then_inc(smtl.inc(16), 16)
                    sync.dma_start(out=s_d2w[pp][:],
                                   in_=AP(d2_d, slab * 4096, [[4 * 4096, 64], [1, 4096]])).then_inc(smtl.inc(16), 16)
                for b4 in range(4):
                    sync.wait_ge(smytr.h, half * 64 + 16 * (b4 + 1))
                    sync.dma_start(
                        out=AP(y_loc, b4 * 8192 + half * 4096, [[4 * 64 * 128, 128], [1, 4096]]),
                        in_=s_yst[:, b4 * 4096:b4 * 4096 + 4096],
                    ).then_inc(smyo2.inc(16), 16)

        @blk.tensor
        def _(tensor):
            for half in range(2):
                tensor.wait_ge(smr1.h, half + 1)
                for slab in range(4):
                    idx = half * 4 + slab
                    pp = 0
                    tensor.wait_ge(smtl.h, 32 * (idx + 1))
                    nJ = 31 if half == 0 else 32
                    oJ = 1 if half == 0 else 0
                    sJ0 = 0 if half == 0 else 31
                    for grp in range(8):
                        pe_bank = [p_c, p_d][grp % 2]
                        gi = idx * 8 + grp
                        if gi >= 2:
                            tensor.wait_ge(smd2.h, gi - 1)
                        for cp in range(4):
                            cl = grp * 4 + cp
                            ch = 32 * slab + cl
                            tensor.matmul(
                                pe_bank[:, 128 * cp:128 * cp + 128].rearrange("p (j b) -> p j b", b=4),
                                AP(s_tp[pp][:].tensor, cl, [[4096, 128], [32, 128]]),
                                AP(s_ucs[:].tensor, ch + half * 32 * 128,
                                   [[32768, 128], [128, 32], [8192, 4]]),
                                start=True, stop=False)
                            hp2, c2 = ch // 64, ch % 64
                            mm = tensor.matmul(
                                AP(pe_bank[:].tensor, 128 * cp + 4 * oJ, [[512, 128], [1, 4 * nJ]]),
                                s_d2w[pp][:, 128 * cl:128 * cl + 128],
                                AP(s_a2[:].tensor, 16384 * hp2 + 4 * c2 + 256 * sJ0,
                                   [[32768, 64], [256, nJ], [1, 4]]),
                                start=False, stop=True)
                        mm.then_inc(smd1.inc(), 1)
                tensor.wait_ge(smd2.h, 32 * (half + 1))
                for tt in range(128):
                    gi2 = half * 128 + tt
                    b4, Jl = tt // 32, tt % 32
                    if gi2 >= 2:
                        tensor.wait_ge(smytr.h, gi2 // 2)
                    tensor.transpose([p_tr, p_tr2][tt % 2][:],
                                     s_yst[:, b4 * 4096 + Jl * 128:b4 * 4096 + Jl * 128 + 128],
                                     s_id[:]).then_inc(smyt1.inc(), 1)

        @blk.scalar
        def _(scalar):
            for idx in range(8):
                half, slab = idx // 4, idx % 4
                for grp in range(8):
                    pe_bank = [p_c, p_d][grp % 2]
                    gi = idx * 8 + grp
                    if gi == 32:
                        scalar.wait_ge(smyo2.h, 64)
                    scalar.wait_ge(smd1.h, gi + 1)
                    scalar.activation(
                        AP(s_yst[:].tensor, 32 * slab + 4 * grp,
                           [[16384, 128], [1, 4], [128, 32], [4096, 4]]),
                        AP(pe_bank[:].tensor, 0, [[512, 128], [128, 4], [4, 32], [1, 4]]),
                        AF.Gelu).then_inc(smd2.inc(), 1)

        @blk.vector
        def _(vector):
            for half in range(2):
                for tt in range(0, 128, 2):
                    gi2 = half * 128 + tt
                    vector.wait_ge(smyt1.h, gi2 + 2)
                    for u2 in range(2):
                        b4, Jl = (tt + u2) // 32, (tt + u2) % 32
                        vector.tensor_copy(
                            s_yst[:, b4 * 4096 + Jl * 128:b4 * 4096 + Jl * 128 + 128],
                            [p_tr, p_tr2][(tt + u2) % 2][:])
                    vector.drain().then_inc(smytr.inc(), 1)

        @blk.gpsimd
        def _(gp):
            gp.wait_ge(smyo2.h, smyo2.v)
            gp.collective_compute(
                "AllGather", ALU.bypass,
                replica_groups=[list(range(NCORES))],
                ins=[y_loc.ap().opt()],
                outs=[y_gth.ap().opt()],
            ).then_inc(smcc.inc(), 1)

        g3b.close()
        g3.close()

        g4 = ExitStack()
        s_key = [sbuf(f"s_key{e}", [128, 8 * 256], BF16, g4) for e in range(8)]
        s_val = [sbuf(f"s_val{e}", [128, 2 * D], BF16, g4) for e in range(8)]
        s_sal = sbuf("s_sal", [128, 16 * 512], BF16, g4)
        s_grep = sbuf("s_grep", [128, 8 * 512], BF16, g4)
        s_xw2 = [[sbuf(f"s_xw2{pp}_{k}", [128, 512], BF16, g4) for k in range(8)] for pp in range(2)]
        s_fst = [sbuf(f"s_fst{pp}", [128, D], F32, g4) for pp in range(2)]

        @blk.sync
        def _(sync):
            sync.wait_ge(smcastC.h, 32)
            sync.wait_ge(smgo.h, 16)
            sync.wait_ge(smxst.h, 512)
            sync.wait_ge(smyo2.h, 128)
            def store_slab(sync, slab):
                for cc in range(4):
                    chunk = slab * 4 + cc
                    sync.wait_ge(smv2.h, chunk + 1)
                    dst = [ffnP, ffnQ][(chunk // 32)]
                    sync.dma_start(out=AP(dst, (chunk % 32) * 128 * D, [[D, 128], [1, D]]),
                                   in_=s_fst[chunk % 2][:]).then_inc(smv3[chunk % 2].inc(16), 16)

            for half in range(2):
                if half == 1:
                    store_slab(sync, 7)
                    sync.wait_ge(smv1.h, 32)
                for e in range(8):
                    eg = 8 * half + e
                    sync.dma_start(out=s_key[e][:].rearrange("p (k h2) -> p k h2", h2=256),
                                   in_=AP(keysbf, eg * D * 256, [[256, 128], [128 * 256, 8], [1, 256]])).then_inc(smkv[half].inc(16), 16)
                    sync.dma_start(out=s_val[e][:].rearrange("p (k d) -> p k d", d=D),
                                   in_=AP(valsbf, eg * 256 * D, [[D, 128], [128 * D, 2], [1, D]])).then_inc(smkv[half].inc(16), 16)
                for s in range(8):
                    slab = half * 8 + s
                    pp = s % 2
                    if s >= 1:
                        store_slab(sync, slab - 1)
                    for k in range(8):
                        sync.dma_start(
                            out=s_xw2[pp][k][:],
                            in_=AP(xnT_d, (128 * k) * T + 512 * s, [[T, 128], [1, 512]])).then_inc(smxw2[pp].inc(16), 16)
                    for e in range(8):
                        eg = 8 * half + e
                        sync.dma_start(out=s_grep[:, 512 * e:512 * e + 512],
                                       in_=AP(gT_d, eg * T + 512 * s, [[0, 128], [1, 512]])).then_inc(smgx.inc(16), 16)
            store_slab(sync, 15)

        @blk.tensor
        def _(tensor):
            for half in range(2):
                tensor.wait_ge(smkv[half].h, smkv[half].v)
                for s in range(8):
                    slab = half * 8 + s
                    pp = s % 2
                    tensor.wait_ge(smxw2[pp].h, 16 * 8 * (slab // 2 + 1))
                    for e in range(8):
                        if slab * 8 + e >= 1:
                            tensor.wait_ge(smk2.h, 2 * (slab * 8 + e))
                        for hh in range(2):
                            ps = [p_a, p_b][hh]
                            for k in range(8):
                                mm = tensor.matmul(ps[:], s_key[e][:, 256 * k + 128 * hh:256 * k + 128 * hh + 128],
                                                   s_xw2[pp][k][:], start=(k == 0), stop=(k == 7))
                            mm.then_inc(smk1.inc(), 1)
                    for cc in range(4):
                        chunk = slab * 4 + cc
                        if chunk >= 1:
                            tensor.wait_ge(smv2.h, chunk)
                        for dd in range(2):
                            po = [p_c, p_d][dd]
                            first = True
                            for e in range(8):
                                if dd == 0 and cc == 0:
                                    tensor.wait_ge(smk3.h, slab * 8 + e + 1)
                                for hh in range(2):
                                    mm = tensor.matmul(
                                        po[:],
                                        s_sal[:, (e * 2 + hh) * 512 + 128 * cc:(e * 2 + hh) * 512 + 128 * cc + 128],
                                        s_val[e][:, hh * D + 512 * dd:hh * D + 512 * dd + 512],
                                        start=first, stop=(e == 7 and hh == 1))
                                    first = False
                        mm.then_inc(smv1.inc(), 1)

        @blk.scalar
        def _(scalar):
            for half in range(2):
                for s in range(8):
                    slab = half * 8 + s
                    for e in range(8):
                        if slab >= 1:
                            scalar.wait_ge(smv1.h, 4 * slab)
                        for hh in range(2):
                            idx = (slab * 8 + e) * 2 + hh
                            scalar.wait_ge(smk1.h, idx + 1)
                            ps = [p_a, p_b][hh]
                            scalar.activation(s_sal[:, (e * 2 + hh) * 512:(e * 2 + hh) * 512 + 512],
                                              ps[:], AF.Relu).then_inc(smk2.inc(), 1)

        @blk.vector
        def _(vector):
            for half in range(2):
                for s in range(8):
                    slab = half * 8 + s
                    vector.wait_ge(smgx.h, 16 * 8 * (slab + 1))
                    for e in range(8):
                        vector.wait_ge(smk2.h, 2 * (slab * 8 + e + 1))
                        ge = s_grep[:, 512 * e:512 * e + 512]
                        vector.tensor_tensor(s_sal[:, (e * 2) * 512:(e * 2) * 512 + 512],
                                             s_sal[:, (e * 2) * 512:(e * 2) * 512 + 512], ge, ALU.mult)
                        vector.drain()
                        vector.tensor_tensor(s_sal[:, (e * 2 + 1) * 512:(e * 2 + 1) * 512 + 512],
                                             s_sal[:, (e * 2 + 1) * 512:(e * 2 + 1) * 512 + 512], ge, ALU.mult)
                        vector.drain().then_inc(smk3.inc(), 1)
                    for cc in range(4):
                        chunk = slab * 4 + cc
                        pp3 = chunk % 2
                        vector.wait_ge(smv1.h, chunk + 1)
                        if chunk >= 2:
                            vector.wait_ge(smv3[chunk % 2].h, 16 * (chunk // 2))
                        if half == 0:
                            vector.tensor_copy(s_fst[pp3][:, 0:512], p_c[:])
                            vector.tensor_copy(s_fst[pp3][:, 512:1024], p_d[:])
                        else:
                            vector.tensor_add(s_fst[pp3][:, 0:512], p_c[:], s_outb[:, 0:512])
                            vector.tensor_add(s_fst[pp3][:, 512:1024], p_d[:], s_outb[:, 512:1024])
                        vector.drain().then_inc(smv2.inc(), 1)

        g4.close()

        g5 = ExitStack()
        y_fm = [sbuf(f"y_fm{r}", [128, T], BF16, g5) for r in range(8)]
        gluT = [sbuf(f"gluT{k}", [128, 2 * D], BF16, g5) for k in range(8)]
        outT = [sbuf(f"outT{k}", [128, D], BF16, g5) for k in range(8)]
        s_ag = [sbuf(f"s_ag{o}", [128, 512], BF16, g5) for o in range(16)]
        s_y2 = [sbuf(f"s_y2{k}", [128, 512], BF16, g5) for k in range(8)]
        s_xr = [sbuf(f"s_xr{pp}", [128, D], F32, g5) for pp in range(2)]
        s_fp = [sbuf(f"s_fp{pp}", [128, D], F32, g5) for pp in range(2)]
        s_fq = [sbuf(f"s_fq{pp}", [128, D], F32, g5) for pp in range(2)]
        s_fin = [sbuf(f"s_fin{pp}", [128, D], F32, g5) for pp in range(2)]
        s_aff = sbuf("s_aff", [128, 4], F32, g5)

        @blk.sync
        def _(sync):
            sync.wait_ge(smcc.h, 2)
            sync.wait_ge(smcastB.h, 32)
            sync.wait_ge(smv3[0].h, smv3[0].v)
            sync.wait_ge(smv3[1].h, smv3[1].v)
            yoff = sync.value_load(s_yoff[0:1, 0:1])
            for r in range(8):
                sync.dma_start(
                    out=y_fm[r][:],
                    in_=AP(y_gth, yoff + r * 128 * 32768, [[32768, 128], [1, 4096]])).then_inc(smyf.inc(16), 16)
            for k in range(8):
                sync.dma_start_transpose(gluT[k][:],
                                         AP(glubf, 128 * k, [[D, 2 * D], [1, 128]])).then_inc(smwt3.inc(16), 16)
                sync.dma_start_transpose(outT[k][:],
                                         AP(outbf, 128 * k, [[D, D], [1, 128]])).then_inc(smwt3.inc(16), 16)
            for chunk in range(2):
                p9 = chunk % 2
                sync.dma_start(out=s_xr[p9][:],
                               in_=AP(x_in, chunk * 128 * D, [[D, 128], [1, D]])).then_inc(smf1[p9].inc(16), 16)
                sync.dma_start(out=s_fp[p9][:],
                               in_=AP(ffnP, chunk * 128 * D, [[D, 128], [1, D]])).then_inc(smf2[p9].inc(16), 16)
                sync.dma_start(out=s_fq[p9][:],
                               in_=AP(ffnQ, chunk * 128 * D, [[D, 128], [1, D]])).then_inc(smf3[p9].inc(16), 16)
            for chunk in range(32):
                p9 = chunk % 2
                sync.wait_ge(smo2.h, chunk + 1)
                sync.dma_start(out=AP(out_t, chunk * 128 * D, [[D, 128], [1, D]]),
                               in_=s_fin[p9][:]).then_inc(smout[p9].inc(16), 16)
                if chunk + 2 < 32:
                    sync.dma_start(out=s_xr[p9][:],
                                   in_=AP(x_in, (chunk + 2) * 128 * D, [[D, 128], [1, D]])).then_inc(smf1[p9].inc(16), 16)
                    sync.dma_start(out=s_fp[p9][:],
                                   in_=AP(ffnP, (chunk + 2) * 128 * D, [[D, 128], [1, D]])).then_inc(smf2[p9].inc(16), 16)
                    sync.dma_start(out=s_fq[p9][:],
                                   in_=AP(ffnQ, (chunk + 2) * 128 * D, [[D, 128], [1, D]])).then_inc(smf3[p9].inc(16), 16)
            sync.wait_ge(smout[0].h, smout[0].v)
            sync.wait_ge(smout[1].h, smout[1].v)

        @blk.tensor
        def _(tensor):
            tensor.wait_ge(smyf.h, smyf.v)
            tensor.wait_ge(smwt3.h, smwt3.v)
            for s in range(8):
                for o in range(16):
                    pg = [p_a, p_b][o % 2]
                    if s * 16 + o >= 2:
                        tensor.wait_ge(smg2.h, s * 16 + o - 1)
                    for r in range(8):
                        mm = tensor.matmul(pg[:], gluT[r][:, 128 * o:128 * o + 128],
                                           y_fm[r][:, 512 * s:512 * s + 512],
                                           start=(r == 0), stop=(r == 7))
                    mm.then_inc(smg1.inc(), 1)
                for cc in range(4):
                    chunk = s * 4 + cc
                    if chunk >= 1:
                        tensor.wait_ge(smo2.h, chunk)
                    for k in range(8):
                        tensor.wait_ge(smg3.h, s * 8 + k + 1)
                        lhs = s_y2[k][:, 128 * cc:128 * cc + 128]
                        tensor.matmul(p_c[:], lhs, outT[k][:, 0:512], start=(k == 0), stop=(k == 7))
                        mm = tensor.matmul(p_d[:], lhs, outT[k][:, 512:1024], start=(k == 0), stop=(k == 7))
                    mm.then_inc(smo1.inc(), 1)

        @blk.scalar
        def _(scalar):
            for s in range(8):
                for o in range(16):
                    pg = [p_a, p_b][o % 2]
                    scalar.wait_ge(smg1.h, s * 16 + o + 1)
                    if s >= 1:
                        scalar.wait_ge(smg3.h, 8 * s)
                    if o < 8:
                        scalar.activation(s_ag[o][:], pg[:], AF.Identity,
                                          bias=s_glb[:, o:o + 1]).then_inc(smg2.inc(), 1)
                    else:
                        scalar.activation(s_ag[o][:], pg[:], AF.Sigmoid,
                                          bias=s_glb[:, o:o + 1]).then_inc(smg2.inc(), 1)

        @blk.vector
        def _(vector):
            for s in range(8):
                for k in range(8):
                    vector.wait_ge(smg2.h, s * 16 + 16)
                    if s >= 1:
                        vector.wait_ge(smo1.h, 4 * s)
                    vector.tensor_mul(s_y2[k][:], s_ag[k][:], s_ag[k + 8][:])
                    vector.drain().then_inc(smg3.inc(), 1)
                for cc in range(4):
                    chunk = s * 4 + cc
                    pp4 = chunk % 2
                    vector.wait_ge(smo1.h, chunk + 1)
                    vector.wait_ge(smf1[pp4].h, 16 * (chunk // 2 + 1))
                    vector.wait_ge(smf2[pp4].h, 16 * (chunk // 2 + 1))
                    vector.wait_ge(smf3[pp4].h, 16 * (chunk // 2 + 1))
                    if chunk >= 2:
                        vector.wait_ge(smout[pp4].h, 16 * (chunk // 2))
                    vector.tensor_scalar(s_aff[:, 0:1], s_rs[:, chunk:chunk + 1], 1.0, None, ALU.add)
                    vector.tensor_mul(s_aff[:, 1:2], s_mu[:, chunk:chunk + 1], s_rs[:, chunk:chunk + 1])
                    vector.drain()
                    vector.tensor_scalar(s_aff[:, 1:2], s_aff[:, 1:2], -1.0, None, ALU.mult)
                    vector.drain()
                    vector.tensor_scalar(s_fin[pp4][:], s_xr[pp4][:], s_aff[:, 0:1], s_aff[:, 1:2],
                                         ALU.mult, ALU.add)
                    vector.drain()
                    vector.tensor_add(s_fin[pp4][:, 0:512], s_fin[pp4][:, 0:512], p_c[:])
                    vector.tensor_add(s_fin[pp4][:, 512:1024], s_fin[pp4][:, 512:1024], p_d[:])
                    vector.drain()
                    vector.tensor_add(s_fin[pp4][:], s_fin[pp4][:], s_fp[pp4][:])
                    vector.drain()
                    vector.tensor_add(s_fin[pp4][:], s_fin[pp4][:], s_fq[pp4][:])
                    vector.drain().then_inc(smo2.inc(), 1)

        g5.close()

    es.close()
    return nc


_NC_CACHE = None


def kernel(L0_attn_x, L0_attn_in_W, L0_attn_in_b, L0_attn_log_dt, L0_attn_C,
           L0_attn_log_A_real, L0_attn_A_imag, L0_attn_D_skip, L0_attn_glu_W,
           L0_attn_glu_b, L0_attn_out_W, L0_attn_out_b, L0_ffn_expert_sel,
           L0_ffn_keys, L0_ffn_values, L0_ffn_route_scale):
    global _NC_CACHE
    if _NC_CACHE is None:
        _NC_CACHE = build_nc()
    nc = _NC_CACHE

    x = np.ascontiguousarray(np.asarray(L0_attn_x, dtype=np.float32))
    ident = np.eye(128, dtype=ml_dtypes.bfloat16)
    t_idx = np.arange(128, dtype=np.float32)
    trQ = np.tile((t_idx + 1.0)[None, :], (128, 1))
    trK = np.tile((127.0 - t_idx)[None, :], (128, 1))
    trP = (127.0 - t_idx)[:, None].copy()
    trP2 = np.stack([128.0 - t_idx, 126.0 - t_idx], axis=1).astype(np.float32)

    in_maps = []
    for c in range(NCORES):
        b, h = c // 2, c % 2
        cs = slice(128 * c, 128 * (c + 1))
        in_maps.append(dict(
            x_in=np.ascontiguousarray(x[b, h * 4096:(h + 1) * 4096, :]),
            in_W=np.asarray(L0_attn_in_W, np.float32),
            in_b=np.asarray(L0_attn_in_b, np.float32),
            log_dt=np.ascontiguousarray(np.asarray(L0_attn_log_dt, np.float32)[cs]),
            C_p=np.ascontiguousarray(np.asarray(L0_attn_C, np.float32)[cs]),
            lar_t=np.ascontiguousarray(np.asarray(L0_attn_log_A_real, np.float32)[cs]),
            aim_t=np.ascontiguousarray(np.asarray(L0_attn_A_imag, np.float32)[cs]),
            dsk_t=np.ascontiguousarray(np.asarray(L0_attn_D_skip, np.float32)[cs]),
            glu_W=np.asarray(L0_attn_glu_W, np.float32),
            glu_b=np.asarray(L0_attn_glu_b, np.float32),
            out_W=np.asarray(L0_attn_out_W, np.float32),
            out_b=np.asarray(L0_attn_out_b, np.float32),
            esel=np.asarray(L0_ffn_expert_sel, np.float32),
            keys_t=np.asarray(L0_ffn_keys, np.float32),
            vals_t=np.asarray(L0_ffn_values, np.float32),
            rscale=np.asarray(L0_ffn_route_scale, np.float32),
            ident=ident, trQ_t=trQ, trK_t=trK, trP_t=trP, trP2_t=trP2,
            uoff_t=np.array([128 * c], np.int32),
            yoff_t=np.array([b * 8192 + h * 4096], np.int32),
        ))

    res = run_bass_kernel_spmd(nc, in_maps, core_ids=list(range(NCORES)))
    out = np.empty((4, 8192, 1024), np.float32)
    for c in range(NCORES):
        b, h = c // 2, c % 2
        out[b, h * 4096:(h + 1) * 4096, :] = np.asarray(res.results[c]["out"])
    return out


# version 09
# speedup vs baseline: 1.0943x; 1.0372x over previous
import numpy as np
import ml_dtypes
import concourse.bass as bass
import concourse.mybir as mybir
from concourse.ap import AP
from concourse.bass_utils import run_bass_kernel_spmd
from contextlib import ExitStack

F32 = mybir.dt.float32
BF16 = mybir.dt.bfloat16
I32 = mybir.dt.int32
AF = mybir.ActivationFunctionType
ALU = mybir.AluOpType
AX = mybir.AxisListType.X

NCORES = 8
T = 4096
D = 1024
NST = 32
TWO_PI = 6.283185307179586
INV_2PI = 0.15915494309189535
MAGIC = 12582912.0
HALF_PI = 1.5707963267948966


class Ctr:
    def __init__(self, h):
        self.h = h
        self.v = 0

    def inc(self, n=1):
        self.v += n
        return self.h


def build_nc():
    nc = bass.Bass("TRN2", target_bir_lowering=False)

    x_in = nc.dram_tensor("x_in", [T, D], F32, kind="ExternalInput")
    in_W = nc.dram_tensor("in_W", [D, D], F32, kind="ExternalInput")
    in_b = nc.dram_tensor("in_b", [D], F32, kind="ExternalInput")
    log_dt = nc.dram_tensor("log_dt", [128], F32, kind="ExternalInput")
    C_p = nc.dram_tensor("C_p", [128, NST, 2], F32, kind="ExternalInput")
    lar_t = nc.dram_tensor("lar_t", [128, NST], F32, kind="ExternalInput")
    aim_t = nc.dram_tensor("aim_t", [128, NST], F32, kind="ExternalInput")
    dsk_t = nc.dram_tensor("dsk_t", [128], F32, kind="ExternalInput")
    glu_W = nc.dram_tensor("glu_W", [D, 2 * D], F32, kind="ExternalInput")
    glu_b = nc.dram_tensor("glu_b", [2 * D], F32, kind="ExternalInput")
    out_W = nc.dram_tensor("out_W", [D, D], F32, kind="ExternalInput")
    out_b = nc.dram_tensor("out_b", [D], F32, kind="ExternalInput")
    esel = nc.dram_tensor("esel", [D, 16], F32, kind="ExternalInput")
    keys_t = nc.dram_tensor("keys_t", [16, D, 256], F32, kind="ExternalInput")
    vals_t = nc.dram_tensor("vals_t", [16, 256, D], F32, kind="ExternalInput")
    rscale = nc.dram_tensor("rscale", [1], F32, kind="ExternalInput")
    ident = nc.dram_tensor("ident", [128, 128], BF16, kind="ExternalInput")
    trQ_t = nc.dram_tensor("trQ_t", [128, 128], F32, kind="ExternalInput")
    trK_t = nc.dram_tensor("trK_t", [128, 128], F32, kind="ExternalInput")
    trP_t = nc.dram_tensor("trP_t", [128, 1], F32, kind="ExternalInput")
    trP2_t = nc.dram_tensor("trP2_t", [128, 2], F32, kind="ExternalInput")
    uoff_t = nc.dram_tensor("uoff_t", [1], I32, kind="ExternalInput")
    yoff_t = nc.dram_tensor("yoff_t", [1], I32, kind="ExternalInput")
    out_t = nc.dram_tensor("out", [T, D], F32, kind="ExternalOutput")

    xnT_d = nc.dram_tensor("xnT_d", [D, T], BF16)
    w1bf = nc.dram_tensor("w1bf", [D, D], BF16)
    glubf = nc.dram_tensor("glubf", [D, 2 * D], BF16)
    outbf = nc.dram_tensor("outbf", [D, D], BF16)
    eselbf = nc.dram_tensor("eselbf", [D, 16], BF16)
    keysbf = nc.dram_tensor("keysbf", [16, D, 256], BF16)
    valsbf = nc.dram_tensor("valsbf", [16, 256, D], BF16)
    prow = nc.dram_tensor("prow", [4, 4096], F32)
    k2r = nc.dram_tensor("k2r", [256, 128], BF16)
    u_loc = nc.dram_tensor("u_loc", [T, D], BF16)
    u_gth = nc.dram_tensor("u_gth", [NCORES * T, D], BF16, addr_space="Shared")
    y_loc = nc.dram_tensor("y_loc", [128, 4 * 64 * 128], BF16)
    y_gth = nc.dram_tensor("y_gth", [NCORES * 128, 4 * 64 * 128], BF16, addr_space="Shared")
    gT_d = nc.dram_tensor("gT_d", [16, T], BF16)
    d2_d = nc.dram_tensor("d2_d", [64, 4 * 4096], BF16)
    pp_d = nc.dram_tensor("pp_d", [128, 4 * 4096], BF16)
    ffnP = nc.dram_tensor("ffnP", [T, D], F32)
    ffnQ = nc.dram_tensor("ffnQ", [T, D], F32)

    es = ExitStack()

    def sbuf(name, shape, dt, st=None):
        return (st or es).enter_context(nc.sbuf_tensor(name, shape, dt))

    def sem(name):
        return Ctr(es.enter_context(nc.semaphore(name)))

    s_id = sbuf("s_id", [128, 128], BF16)
    s_trP = sbuf("s_trP", [128, 1], F32)
    s_trP2 = sbuf("s_trP2", [128, 2], F32)
    s_hpi = sbuf("s_hpi", [128, 1], F32)
    s_mu = sbuf("s_mu", [128, 32], F32)
    s_rs = sbuf("s_rs", [128, 32], F32)
    s_inb = sbuf("s_inb", [128, D], F32)
    s_outb = sbuf("s_outb", [128, D], F32)
    s_glb = sbuf("s_glb", [128, 16], F32)
    s_g = sbuf("s_g", [128, 32 * 16], F32)
    s_rsc = sbuf("s_rsc", [128, 1], F32)
    s_uoff = sbuf("s_uoff", [1, 1], I32)
    s_yoff = sbuf("s_yoff", [1, 1], I32)
    s_q2dr = sbuf("s_q2dr", [64, 128], F32)
    s_q2di = sbuf("s_q2di", [64, 128], F32)
    s_q2er = sbuf("s_q2er", [64, 128], F32)
    s_q2ei = sbuf("s_q2ei", [64, 128], F32)
    s_W1 = sbuf("s_W1", [64, 128], F32)
    s_W2 = sbuf("s_W2", [64, 128], F32)
    s_W3 = sbuf("s_W3", [64, 128], F32)

    p_a = es.enter_context(nc.psum_tensor("p_a", [128, 512], F32))
    p_b = es.enter_context(nc.psum_tensor("p_b", [128, 512], F32))
    p_c = es.enter_context(nc.psum_tensor("p_c", [128, 512], F32))
    p_d = es.enter_context(nc.psum_tensor("p_d", [128, 512], F32))
    p_sel = es.enter_context(nc.psum_tensor("p_sel", [128, 16], F32))
    p_tr = es.enter_context(nc.psum_tensor("p_tr", [128, 128], BF16))
    p_tr2 = es.enter_context(nc.psum_tensor("p_tr2", [128, 128], BF16))
    p_tr3 = es.enter_context(nc.psum_tensor("p_tr3", [128, 128], BF16))

    smc = sem("smc"); smprm = sem("smprm")
    smcastA = sem("smcastA"); smcastB = sem("smcastB"); smcastC = sem("smcastC")
    sm0 = [sem(f"sm0_{i}") for i in range(20)]
    smx = [sem("smx0"), sem("smx1")]
    smln = sem("smln"); smrs = sem("smrs"); smlv = sem("smlv")
    smtp = sem("smtp"); smcpy = sem("smcpy"); smxst = sem("smxst")
    smyt1 = sem("smyt1"); smytr = sem("smytr")
    smw2 = sem("smw2")
    smu1 = sem("smu1"); smu2 = sem("smu2")
    smu3 = [sem("smu3A"), sem("smu3B")]
    smlg = sem("smlg"); smsg = sem("smsg"); smtk = sem("smtk")
    smgt = sem("smgt"); smge = sem("smge"); smgo = sem("smgo")
    smcc = sem("smcc"); smucs = sem("smucs")
    sme1 = sem("sme1"); sme2 = sem("sme2")
    smr1 = sem("smr1"); smra = sem("smra")
    smkv = [sem("smkv0"), sem("smkv1")]
    smxw2 = [sem("smxw2_0"), sem("smxw2_1")]
    smgx = sem("smgx")
    smk1 = sem("smk1"); smk2 = sem("smk2"); smk3 = sem("smk3")
    smv1 = sem("smv1"); smv2 = sem("smv2")
    smv3 = [sem("smv3A"), sem("smv3B")]
    smyf = sem("smyf"); smwt3 = sem("smwt3")
    smg1 = sem("smg1"); smg2 = sem("smg2"); smg3 = sem("smg3")
    smo1 = sem("smo1"); smo2 = sem("smo2")
    smf1 = [sem("smf1A"), sem("smf1B")]
    smf2 = [sem("smf2A"), sem("smf2B")]
    smf3 = [sem("smf3A"), sem("smf3B")]
    smout = [sem("smoutA"), sem("smoutB")]

    def wrap_sin_cos(vector, src_ang, dst_sin_arg, dst_cos_arg):
        vector.tensor_scalar(dst_sin_arg, src_ang, INV_2PI, MAGIC, ALU.mult, ALU.add)
        vector.drain()
        vector.tensor_scalar(dst_sin_arg, dst_sin_arg, MAGIC, None, ALU.subtract)
        vector.drain()
        vector.scalar_tensor_tensor(dst_sin_arg, dst_sin_arg, -TWO_PI, src_ang, ALU.mult, ALU.add)
        vector.drain()
        vector.tensor_scalar_min(dst_sin_arg, dst_sin_arg, 3.141592653589793)
        vector.drain()
        vector.tensor_scalar_max(dst_sin_arg, dst_sin_arg, -3.141592653589793)
        vector.drain()

    def act_sin_cos(scalar, wrapped, cos_out, pcount):
        scalar.activation(cos_out, wrapped, AF.Abs)
        scalar.drain()
        scalar.activation(cos_out, cos_out, AF.Sin, bias=s_hpi[0:pcount, 0:1], scale=-1.0)
        return scalar.activation(wrapped, wrapped, AF.Sin)

    with nc.Block() as blk:
        @blk.sync
        def _(sync):
            sync.dma_start(out=s_id[:], in_=ident.ap()).then_inc(smc.inc(16), 16)
            sync.dma_start(out=s_trP[:], in_=trP_t.ap()).then_inc(smc.inc(16), 16)
            sync.dma_start(out=s_trP2[:], in_=trP2_t.ap()).then_inc(smc.inc(16), 16)
            sync.dma_start(out=s_inb[:], in_=AP(in_b, 0, [[0, 128], [1, D]])).then_inc(smc.inc(16), 16)
            sync.dma_start(out=s_outb[:], in_=AP(out_b, 0, [[0, 128], [1, D]])).then_inc(smc.inc(16), 16)
            sync.dma_start(out=s_glb[:], in_=AP(glu_b, 0, [[1, 128], [128, 16]]),
                           allow_slow_non_contiguous=True).then_inc(smc.inc(16), 16)
            sync.dma_start(out=s_rsc[:], in_=AP(rscale, 0, [[0, 128], [1, 1]])).then_inc(smc.inc(16), 16)
            sync.dma_start(out=s_uoff[:], in_=uoff_t.ap().unsqueeze(0)).then_inc(smc.inc(16), 16)
            sync.dma_start(out=s_yoff[:], in_=yoff_t.ap().unsqueeze(0)).then_inc(smc.inc(16), 16)

        @blk.gpsimd
        def _(gp):
            gp.dma_start(out=w1bf.ap(), in_=in_W.ap()).then_inc(smcastA.inc(16), 16)
            gp.dma_start(out=eselbf.ap(), in_=esel.ap()).then_inc(smcastA.inc(16), 16)
            gp.wait_ge(smx[1].h, 16)
            gp.dma_start(out=glubf.ap(), in_=glu_W.ap()).then_inc(smcastB.inc(16), 16)
            gp.dma_start(out=outbf.ap(), in_=out_W.ap()).then_inc(smcastB.inc(16), 16)
            gp.dma_start(out=keysbf.ap(), in_=keys_t.ap()).then_inc(smcastC.inc(16), 16)
            gp.dma_start(out=valsbf.ap(), in_=vals_t.ap()).then_inc(smcastC.inc(16), 16)

        g0 = ExitStack()
        s_dtc = sbuf("s_dtc", [128, 1], F32, g0)
        s_lar = sbuf("s_lar", [128, NST], F32, g0)
        s_aim = sbuf("s_aim", [128, NST], F32, g0)
        s_cr = sbuf("s_cr", [128, NST], F32, g0)
        s_ci = sbuf("s_ci", [128, NST], F32, g0)
        s_dskc = sbuf("s_dskc", [128, 1], F32, g0)
        s_ema = sbuf("s_ema", [128, NST], F32, g0)
        s_dtar = sbuf("s_dtar", [128, NST], F32, g0)
        s_dtai = sbuf("s_dtai", [128, NST], F32, g0)
        s_absw = sbuf("s_absw", [128, NST], F32, g0)
        s_ang = sbuf("s_ang", [128, NST], F32, g0)
        s_ang2 = sbuf("s_ang2", [128, NST], F32, g0)
        s_sinw = sbuf("s_sinw", [128, NST], F32, g0)
        s_cosw = sbuf("s_cosw", [128, NST], F32, g0)
        s_t1 = sbuf("s_t1", [128, NST], F32, g0)
        s_t2 = sbuf("s_t2", [128, NST], F32, g0)
        s_t3 = sbuf("s_t3", [128, NST], F32, g0)
        s_t4 = sbuf("s_t4", [128, NST], F32, g0)
        s_den = sbuf("s_den", [128, NST], F32, g0)
        s_trr = sbuf("s_trr", [128, NST], F32, g0)
        s_tii = sbuf("s_tii", [128, NST], F32, g0)
        s_er = sbuf("s_er", [128, NST], F32, g0)
        s_ei = sbuf("s_ei", [128, NST], F32, g0)
        s_wa = sbuf("s_wa", [64, 128], F32, g0)
        s_wb = sbuf("s_wb", [64, 128], F32, g0)

        @blk.sync
        def _(sync):
            sync.dma_start(out=s_dtc[:], in_=log_dt.ap().unsqueeze(1)).then_inc(smprm.inc(16), 16)
            sync.dma_start(out=s_lar[:], in_=lar_t.ap()).then_inc(smprm.inc(16), 16)
            sync.dma_start(out=s_aim[:], in_=aim_t.ap()).then_inc(smprm.inc(16), 16)
            sync.dma_start(out=s_cr[:], in_=AP(C_p, 0, [[NST * 2, 128], [2, NST]]),
                           allow_slow_non_contiguous=True).then_inc(smprm.inc(16), 16)
            sync.dma_start(out=s_ci[:], in_=AP(C_p, 1, [[NST * 2, 128], [2, NST]]),
                           allow_slow_non_contiguous=True).then_inc(smprm.inc(16), 16)
            sync.dma_start(out=s_dskc[:], in_=dsk_t.ap().unsqueeze(1)).then_inc(smprm.inc(16), 16)

        @blk.scalar
        def _(scalar):
            scalar.wait_ge(smprm.h, smprm.v)
            scalar.activation(s_dtc[:], s_dtc[:], AF.Exp)
            scalar.drain()
            scalar.activation(s_ema[:], s_lar[:], AF.Exp).then_inc(sm0[0].inc(), 1)

        @blk.vector
        def _(vector):
            vector.wait_ge(sm0[0].h, 1)
            vector.memset(s_hpi[:], HALF_PI)
            vector.drain()
            vector.tensor_scalar(s_dtar[:], s_ema[:], s_dtc[:, 0:1], -1.0, ALU.mult, ALU.mult)
            vector.tensor_scalar(s_dtai[:], s_aim[:], s_dtc[:, 0:1], None, ALU.mult)
            vector.drain()
            wrap_sin_cos(vector, s_dtai[:], s_ang[:], s_ang2[:])
            vector.tensor_copy(s_t1[:], s_dtar[:])
            vector.drain().then_inc(sm0[1].inc(), 1)

        @blk.scalar
        def _(scalar):
            scalar.wait_ge(sm0[1].h, 1)
            scalar.activation(s_absw[:], s_dtar[:], AF.Exp)
            act_sin_cos(scalar, s_ang[:], s_cosw[:], 128)
            scalar.drain()
            scalar.activation(s_sinw[:], s_ang[:], AF.Identity).then_inc(sm0[2].inc(), 1)

        @blk.vector
        def _(vector):
            vector.wait_ge(sm0[2].h, 1)
            vector.tensor_mul(s_t1[:], s_absw[:], s_cosw[:])
            vector.tensor_mul(s_t2[:], s_absw[:], s_sinw[:])
            vector.tensor_mul(s_den[:], s_ema[:], s_ema[:])
            vector.tensor_mul(s_t3[:], s_aim[:], s_aim[:])
            vector.drain()
            vector.tensor_add(s_den[:], s_den[:], s_t3[:])
            vector.drain()
            vector.reciprocal(s_den[:], s_den[:])
            vector.tensor_scalar(s_t1[:], s_t1[:], -1.0, None, ALU.add)
            vector.drain()
            vector.tensor_mul(s_t3[:], s_t1[:], s_ema[:])
            vector.tensor_mul(s_t4[:], s_t2[:], s_aim[:])
            vector.drain()
            vector.tensor_sub(s_trr[:], s_t4[:], s_t3[:])
            vector.drain()
            vector.tensor_mul(s_t3[:], s_t2[:], s_ema[:])
            vector.tensor_mul(s_t4[:], s_t1[:], s_aim[:])
            vector.drain()
            vector.tensor_add(s_tii[:], s_t3[:], s_t4[:])
            vector.drain()
            vector.tensor_scalar(s_tii[:], s_tii[:], -1.0, None, ALU.mult)
            vector.drain()
            vector.tensor_mul(s_t3[:], s_cr[:], s_trr[:])
            vector.tensor_mul(s_t4[:], s_ci[:], s_tii[:])
            vector.drain()
            vector.tensor_sub(s_er[:], s_t3[:], s_t4[:])
            vector.drain()
            vector.tensor_mul(s_t3[:], s_cr[:], s_tii[:])
            vector.tensor_mul(s_t4[:], s_ci[:], s_trr[:])
            vector.drain()
            vector.tensor_add(s_ei[:], s_t3[:], s_t4[:])
            vector.drain()
            vector.tensor_mul(s_er[:], s_er[:], s_den[:])
            vector.tensor_mul(s_ei[:], s_ei[:], s_den[:])
            vector.drain().then_inc(sm0[3].inc(), 1)

        @blk.sync
        def _(sync):
            sync.wait_ge(sm0[3].h, 1)
            for row, src in [(0, s_dtar), (1, s_dtai), (2, s_er), (3, s_ei)]:
                sync.dma_start(out=AP(prow, row * 4096, [[NST, 128], [1, NST]]),
                               in_=src[:]).then_inc(sm0[4].inc(16), 16)
            sync.wait_ge(sm0[4].h, sm0[4].v)
            for dst, row in [(s_q2dr, 0), (s_q2di, 1), (s_q2er, 2), (s_q2ei, 3)]:
                sync.dma_start(out=dst[0:32, :], in_=AP(prow, row * 4096, [[1, NST], [NST, 128]]),
                               allow_slow_non_contiguous=True).then_inc(sm0[5].inc(16), 16)
                sync.dma_start(out=dst[32:64, :], in_=AP(prow, row * 4096, [[1, NST], [NST, 128]]),
                               allow_slow_non_contiguous=True).then_inc(sm0[5].inc(16), 16)

        @blk.vector
        def _(vector):
            vector.wait_ge(sm0[5].h, sm0[5].v)
            vector.tensor_scalar(s_W1[:], s_q2dr[:], 128.0, None, ALU.mult)
            vector.tensor_scalar(s_W2[:], s_q2di[:], 128.0, None, ALU.mult)
            vector.drain()
            wrap_sin_cos(vector, s_W2[:], s_wa[:], s_wb[:])
            vector.drain().then_inc(sm0[6].inc(), 1)

        @blk.scalar
        def _(scalar):
            scalar.wait_ge(sm0[6].h, 1)
            scalar.activation(s_W1[:], s_W1[:], AF.Exp)
            act_sin_cos(scalar, s_wa[:], s_wb[:], 64).then_inc(sm0[7].inc(), 1)

        @blk.vector
        def _(vector):
            vector.wait_ge(sm0[7].h, 1)
            vector.tensor_mul(s_wa[:], s_W1[:], s_wa[:])
            vector.drain()
            vector.tensor_mul(s_W1[:], s_W1[:], s_wb[:])
            vector.tensor_scalar(s_W2[0:32, :], s_wa[0:32, :], -1.0, None, ALU.mult)
            vector.tensor_copy(s_W2[32:64, :], s_wa[32:64, :])
            vector.tensor_copy(s_W3[0:32, :], s_wa[0:32, :])
            vector.tensor_scalar(s_W3[32:64, :], s_wa[32:64, :], -1.0, None, ALU.mult)
            vector.drain().then_inc(sm0[8].inc(), 1)

        g2 = ExitStack()
        s_x = [sbuf("s_x0", [128, D], F32, g2), sbuf("s_x1", [128, D], F32, g2)]
        s_xsq = sbuf("s_xsq", [128, D], F32, g2)
        s_xn = [sbuf("s_xn0", [128, D], BF16, g2), sbuf("s_xn1", [128, D], BF16, g2)]
        s_st = sbuf("s_st", [128, 8], F32, g2)
        w1t = [sbuf(f"w1t{k}", [128, D], BF16, g2) for k in range(8)]
        selT = sbuf("selT", [128, 8 * 16], BF16, g2)
        s_xg = [[sbuf(f"s_xg{g}_{k}", [128, 1024], BF16, g2) for k in range(8)] for g in range(2)]
        s_ust = [sbuf(f"s_ust{pp}", [128, D], BF16, g2) for pp in range(2)]
        s_lgo = sbuf("s_lgo", [128, 16], F32, g2)
        s_lpr = sbuf("s_lpr", [128, 16], F32, g2)
        s_lgw = sbuf("s_lgw", [128, 16], F32, g2)
        s_lmx = sbuf("s_lmx", [128, 1], F32, g2)
        s_leq = sbuf("s_leq", [128, 16], F32, g2)
        s_lmk = sbuf("s_lmk", [128, 16], F32, g2)
        s_lsum = sbuf("s_lsum", [128, 1], F32, g2)
        s_gbf = sbuf("s_gbf", [128, 32 * 16], BF16, g2)
        s_gT = sbuf("s_gT", [16, T], BF16, g2)

        @blk.sync
        def _(sync):
            sync.dma_start(out=s_x[0][:], in_=AP(x_in, 0, [[D, 128], [1, D]])).then_inc(smx[0].inc(16), 16)
            sync.dma_start(out=s_x[1][:], in_=AP(x_in, 128 * D, [[D, 128], [1, D]])).then_inc(smx[1].inc(16), 16)
            sync.wait_ge(smcastA.h, 32)
            for k in range(8):
                sync.dma_start(out=w1t[k][:],
                               in_=AP(w1bf, 128 * k * D, [[D, 128], [1, D]])).then_inc(smw2.inc(16), 16)
                sync.dma_start(out=selT[:, 16 * k:16 * k + 16],
                               in_=AP(eselbf, 128 * k * 16, [[16, 128], [1, 16]])).then_inc(smw2.inc(16), 16)
            for i in range(32):
                if i + 2 < 32:
                    sync.wait_ge(smlv.h, i + 1)
                    sync.dma_start(out=s_x[i % 2][:],
                                   in_=AP(x_in, (i + 2) * 128 * D, [[D, 128], [1, D]])).then_inc(smx[i % 2].inc(16), 16)
                sync.wait_ge(smu2.h, i + 1)
                sync.dma_start(out=AP(u_loc, i * 128 * D, [[D, 128], [1, D]]),
                               in_=s_ust[i % 2][:]).then_inc(smu3[i % 2].inc(16), 16)
                if i % 8 == 7:
                    g = i // 8
                    sync.wait_ge(smcpy.h, 8 * (i + 1))
                    for k in range(8):
                        sync.dma_start(out=AP(xnT_d, (128 * k) * T + 1024 * g, [[T, 128], [1, 1024]]),
                                       in_=s_xg[g % 2][k][:]).then_inc(smxst.inc(16), 16)
            sync.wait_ge(smge.h, 32)
            sync.dma_start(out=gT_d.ap(), in_=s_gT[:]).then_inc(smgo.inc(16), 16)

        @blk.tensor
        def _(tensor):
            tensor.wait_ge(smc.h, smc.v)
            for i in range(32):
                tensor.wait_ge(smlv.h, i + 1)
                for k in range(8):
                    t_idx = 8 * i + k
                    if t_idx >= 2:
                        tensor.wait_ge(smcpy.h, t_idx - 1)
                    tensor.transpose([p_tr, p_tr2][k % 2][:], s_xn[i % 2][:, 128 * k:128 * k + 128],
                                     s_id[:]).then_inc(smtp.inc(), 1)
                if i == 0:
                    tensor.wait_ge(smw2.h, 256)
                tensor.wait_ge(smcpy.h, 8 * (i + 1))
                if i >= 1:
                    tensor.wait_ge(smu2.h, i)
                pu = [(p_a, p_b), (p_c, p_d)][i % 2]
                col = 128 * (i % 8)
                gb = (i // 8) % 2
                for k in range(8):
                    lhs = s_xg[gb][k][:, col:col + 128]
                    tensor.matmul(pu[0][:], lhs, w1t[k][:, 0:512], start=(k == 0), stop=(k == 7))
                    tensor.matmul(pu[1][:], lhs, w1t[k][:, 512:1024], start=(k == 0), stop=(k == 7))
                    mm = tensor.matmul(p_sel[:], lhs, selT[:, 16 * k:16 * k + 16],
                                       start=(k == 0), stop=(k == 7))
                mm.then_inc(smu1.inc(), 1)
            for ch in range(32):
                tensor.wait_ge(smtk.h, ch + 1)
                if ch >= 1:
                    tensor.wait_ge(smge.h, ch)
                tensor.transpose(p_tr3[0:16, :], s_gbf[:, 16 * ch:16 * ch + 16], s_id[:]).then_inc(smgt.inc(), 1)

        @blk.scalar
        def _(scalar):
            for i in range(32):
                scalar.wait_ge(smln.h, i + 1)
                scalar.activation(s_st[:, 5:6], s_st[:, 4:5], AF.Ln)
                scalar.drain()
                scalar.activation(s_rs[:, i:i + 1], s_st[:, 5:6], AF.Exp, scale=-0.5).then_inc(smrs.inc(), 1)
                g = i // 8
                if i % 8 == 0 and g >= 2:
                    scalar.wait_ge(smxst.h, 128 * (g - 1))
                for k in range(8):
                    scalar.wait_ge(smtp.h, 8 * i + k + 1)
                    scalar.activation(s_xg[g % 2][k][:, 128 * (i % 8):128 * (i % 8) + 128],
                                      [p_tr, p_tr2][k % 2][:], AF.Identity).then_inc(smcpy.inc(), 1)
                if i >= 1:
                    scalar.wait_ge(smlg.h, i)
                    scalar.activation(s_lpr[:], s_lgo[:], AF.Sigmoid).then_inc(smsg.inc(), 1)
            scalar.wait_ge(smlg.h, 32)
            scalar.activation(s_lpr[:], s_lgo[:], AF.Sigmoid).then_inc(smsg.inc(), 1)

        @blk.vector
        def _(vector):
            vector.wait_ge(smc.h, smc.v)

            def evac_topk(j):
                vector.wait_ge(smu1.h, j + 1)
                if j >= 2:
                    vector.wait_ge(smu3[j % 2].h, 16 * (j // 2))
                pu = [(p_a, p_b), (p_c, p_d)][j % 2]
                vector.tensor_add(s_ust[j % 2][:, 0:512], pu[0][:], s_inb[:, 0:512])
                vector.tensor_add(s_ust[j % 2][:, 512:1024], pu[1][:], s_inb[:, 512:1024])
                vector.tensor_copy(s_lgo[:], p_sel[:])
                vector.drain().then_inc(smu2.inc(), 1)
                vector.tensor_copy(s_lgw[:], s_lgo[:])
                vector.memset(s_lmk[:], 0.0)
                vector.drain().then_inc(smlg.inc(), 1)
                for r in range(4):
                    vector.reduce_max(s_lmx[:], s_lgw[:], axis=AX)
                    vector.drain()
                    vector.tensor_scalar(s_leq[:], s_lgw[:], s_lmx[:, 0:1], None, ALU.is_ge)
                    vector.drain()
                    vector.tensor_add(s_lmk[:], s_lmk[:], s_leq[:])
                    vector.scalar_tensor_tensor(s_lgw[:], s_leq[:], -1e30, s_lgw[:], ALU.mult, ALU.add)
                    vector.drain()
                vector.wait_ge(smsg.h, j + 1)
                vector.tensor_mul(s_lpr[:], s_lpr[:], s_lmk[:])
                vector.drain()
                vector.reduce_sum(s_lsum[:], s_lpr[:], axis=AX)
                vector.drain()
                vector.tensor_scalar(s_lsum[:], s_lsum[:], 1e-9, None, ALU.max)
                vector.drain()
                vector.reciprocal(s_lsum[:], s_lsum[:])
                vector.drain()
                vector.tensor_mul(s_lsum[:], s_lsum[:], s_rsc[:])
                vector.drain()
                vector.tensor_scalar(s_g[:, 16 * j:16 * j + 16], s_lpr[:], s_lsum[:, 0:1], None, ALU.mult)
                vector.drain()
                vector.tensor_copy(s_gbf[:, 16 * j:16 * j + 16], s_g[:, 16 * j:16 * j + 16])
                vector.drain().then_inc(smtk.inc(), 1)

            for i in range(33):
                if i < 32:
                    vector.wait_ge(smx[i % 2].h, 16 * (i // 2 + 1))
                    if i >= 2:
                        vector.wait_ge(smtp.h, 8 * i - 8)
                    xb = s_x[i % 2]
                    vector.tensor_mul(s_xsq[:], xb[:], xb[:])
                    vector.reduce_sum(s_st[:, 0:1], xb[:], axis=AX)
                    vector.drain()
                    vector.reduce_sum(s_st[:, 1:2], s_xsq[:], axis=AX)
                    vector.drain()
                    vector.tensor_scalar(s_mu[:, i:i + 1], s_st[:, 0:1], 1.0 / D, None, ALU.mult)
                    vector.tensor_scalar(s_st[:, 2:3], s_st[:, 1:2], 1.0 / D, None, ALU.mult)
                    vector.drain()
                    vector.tensor_mul(s_st[:, 3:4], s_mu[:, i:i + 1], s_mu[:, i:i + 1])
                    vector.drain()
                    vector.tensor_sub(s_st[:, 4:5], s_st[:, 2:3], s_st[:, 3:4])
                    vector.drain()
                    vector.tensor_scalar(s_st[:, 4:5], s_st[:, 4:5], 1e-5, None, ALU.add)
                    vector.drain().then_inc(smln.inc(), 1)
                    vector.wait_ge(smrs.h, i + 1)
                    vector.tensor_scalar(s_xn[i % 2][:], xb[:], s_mu[:, i:i + 1], s_rs[:, i:i + 1],
                                         ALU.subtract, ALU.mult)
                    vector.drain().then_inc(smlv.inc(), 1)
                if i >= 1:
                    evac_topk(i - 1)
            for ch in range(32):
                vector.wait_ge(smgt.h, ch + 1)
                vector.tensor_copy(s_gT[:, 128 * ch:128 * ch + 128], p_tr3[0:16, :])
                vector.drain().then_inc(smge.inc(), 1)


        gB = ExitStack()
        s_ba = sbuf("s_ba", [128, 4096], F32, gB)
        s_bb = sbuf("s_bb", [128, 4096], F32, gB)
        s_bc = sbuf("s_bc", [128, 4096], F32, gB)
        s_bd = sbuf("s_bd", [128, 4096], F32, gB)
        s_trK = sbuf("s_trK", [128, 128], F32, gB)
        s_ksb = sbuf("s_ksb", [128, 128], F32, gB)
        s_kbf = sbuf("s_kbf", [128, 128], BF16, gB)
        s_ktr = sbuf("s_ktr", [128, 128], BF16, gB)
        s_zro = sbuf("s_zro", [128, 128], BF16, gB)
        s_trQ0 = sbuf("s_trQ0", [128, 128], F32, gB)
        s_d2s = sbuf("s_d2s", [64, 4096], BF16, gB)
        s_ppt = sbuf("s_ppt", [128, 4096], BF16, gB)

        @blk.sync
        def _(sync):
            sync.wait_ge(sm0[3].h, 1)
            sync.wait_ge(sm0[4].h, sm0[4].v)
            sync.dma_start(out=s_trK[:], in_=trK_t.ap()).then_inc(sm0[9].inc(16), 16)
            sync.dma_start(out=s_ba[:], in_=AP(prow, 0, [[0, 128], [1, 4096]])).then_inc(sm0[9].inc(16), 16)
            sync.dma_start(out=s_bb[:], in_=AP(prow, 4096, [[0, 128], [1, 4096]])).then_inc(sm0[9].inc(16), 16)
            sync.wait_ge(sm0[12].h, 1)
            sync.dma_start(out=AP(pp_d, 0, [[16384, 128], [1, 4096]]),
                           in_=s_ppt[:]).then_inc(sm0[19].inc(16), 16)
            sync.wait_ge(sm0[12].h, 2)
            sync.dma_start(out=AP(pp_d, 4096, [[16384, 128], [1, 4096]]),
                           in_=s_ppt[:]).then_inc(sm0[19].inc(16), 16)

        @blk.vector
        def _(vector):
            vector.wait_ge(smc.h, smc.v)
            vector.wait_ge(sm0[9].h, 48)
            vector.tensor_scalar(s_ba[:], s_ba[:], s_trP[:, 0:1], None, ALU.mult)
            vector.tensor_scalar(s_bb[:], s_bb[:], s_trP[:, 0:1], None, ALU.mult)
            vector.drain()
            wrap_sin_cos(vector, s_bb[:], s_bc[:], s_bd[:])
            vector.tensor_copy(s_t2[:, 0:1], s_t1[:, 0:1])
            vector.drain().then_inc(sm0[10].inc(), 1)
            vector.wait_ge(sm0[11].h, 1)
            vector.tensor_mul(s_ppt[:], s_ba[:], s_bd[:])
            vector.drain()
            vector.tensor_copy(s_t2[:, 1:2], s_t1[:, 0:1])
            vector.drain().then_inc(sm0[12].inc(), 1)
            vector.wait_ge(sm0[19].h, 16)
            vector.tensor_mul(s_ppt[:], s_ba[:], s_bc[:])
            vector.drain()
            vector.tensor_copy(s_t2[:, 2:3], s_t1[:, 0:1])
            vector.drain().then_inc(sm0[12].inc(), 1)

        @blk.scalar
        def _(scalar):
            scalar.wait_ge(sm0[10].h, 1)
            scalar.activation(s_ba[:], s_ba[:], AF.Exp)
            act_sin_cos(scalar, s_bc[:], s_bd[:], 128).then_inc(sm0[11].inc(), 1)


        @blk.vector
        def _(vector):
            vector.wait_ge(sm0[12].h, 2)
            a_dr = s_dtar[:].unsqueeze(2).broadcast_to([128, 32, 128])
            a_di = s_dtai[:].unsqueeze(2).broadcast_to([128, 32, 128])
            a_tk = s_trK[:].unsqueeze(1).broadcast_to([128, 32, 128])
            vector.tensor_tensor(s_ba[:].rearrange("p (n m) -> p n m", m=128), a_dr, a_tk, ALU.mult)
            vector.tensor_tensor(s_bb[:].rearrange("p (n m) -> p n m", m=128), a_di, a_tk, ALU.mult)
            vector.drain()
            wrap_sin_cos(vector, s_bb[:], s_bc[:], s_bd[:])
            vector.tensor_copy(s_t2[:], s_t1[:])
            vector.drain().then_inc(sm0[13].inc(), 1)

        @blk.scalar
        def _(scalar):
            scalar.wait_ge(sm0[13].h, 1)
            scalar.activation(s_ba[:], s_ba[:], AF.Exp)
            act_sin_cos(scalar, s_bc[:], s_bd[:], 128).then_inc(sm0[14].inc(), 1)

        @blk.vector
        def _(vector):
            vector.wait_ge(sm0[14].h, 1)
            vector.tensor_mul(s_bd[:], s_ba[:], s_bd[:])
            vector.tensor_mul(s_bc[:], s_ba[:], s_bc[:])
            vector.drain()
            e_r = s_er[:].unsqueeze(2).broadcast_to([128, 32, 128])
            e_i = s_ei[:].unsqueeze(2).broadcast_to([128, 32, 128])
            v_ba = s_ba[:].rearrange("p (n m) -> p n m", m=128)
            v_bb = s_bb[:].rearrange("p (n m) -> p n m", m=128)
            v_bc = s_bc[:].rearrange("p (n m) -> p n m", m=128)
            v_bd = s_bd[:].rearrange("p (n m) -> p n m", m=128)
            vector.tensor_tensor(v_ba, e_r, v_bd, ALU.mult)
            vector.tensor_tensor(v_bb, e_i, v_bc, ALU.mult)
            vector.drain()
            vector.tensor_sub(s_ba[:], s_ba[:], s_bb[:])
            vector.drain()
            vector.reduce_sum(s_ksb[:].unsqueeze(2),
                              AP(s_ba[:].tensor, 0, [[4096, 128], [1, 128], [128, 32]]),
                              axis=AX)
            vector.drain()
            vector.tensor_scalar(s_ksb[:], s_ksb[:], 2.0, None, ALU.mult)
            vector.drain()
            vector.tensor_add(s_ksb[:, 127:128], s_ksb[:, 127:128], s_dskc[:])
            vector.drain()
            vector.tensor_copy(s_kbf[:], s_ksb[:])
            vector.drain()
            vector.memset(s_zro[:], 0.0)
            vector.drain().then_inc(sm0[15].inc(), 1)

        @blk.tensor
        def _(tensor):
            tensor.wait_ge(sm0[15].h, 1)
            tensor.wait_ge(smc.h, smc.v)
            tensor.transpose(p_tr[:], s_kbf[:], s_id[:]).then_inc(sm0[16].inc(), 1)

        @blk.vector
        def _(vector):
            vector.wait_ge(sm0[16].h, 1)
            vector.tensor_copy(s_ktr[:], p_tr[:])
            vector.drain().then_inc(sm0[17].inc(), 1)

        @blk.sync
        def _(sync):
            sync.wait_ge(sm0[17].h, 1)
            sync.dma_start(out=AP(k2r, 129 * 128, [[128, 127], [1, 128]]),
                           in_=s_zro[0:127, :]).then_inc(sm0[18].inc(16), 16)
            sync.dma_start(out=AP(k2r, 0, [[128, 1], [1, 128]]),
                           in_=s_zro[0:1, :]).then_inc(sm0[18].inc(16), 16)
            sync.dma_start(out=AP(k2r, 128, [[128, 128], [1, 128]]),
                           in_=s_ktr[:]).then_inc(sm0[18].inc(16), 16)
            sync.dma_start(out=s_trQ0[:], in_=trQ_t.ap()).then_inc(sm0[18].inc(16), 16)

        smdb = sem("smdb"); smdbs = sem("smdbs"); smdbv = sem("smdbv"); smdbo = sem("smdbo")
        s_P2 = sbuf("s_P2", [64, 128], F32, gB)
        s_Q2 = sbuf("s_Q2", [64, 128], F32, gB)

        @blk.vector
        def _(vector):
            vector.wait_ge(sm0[15].h, 1)
            vector.wait_ge(sm0[18].h, sm0[18].v)
            vector.tensor_copy(s_P2[0:32, :], s_q2er[0:32, :])
            vector.tensor_scalar(s_P2[32:64, :], s_q2ei[32:64, :], -1.0, None, ALU.mult)
            vector.tensor_scalar(s_Q2[0:32, :], s_q2ei[0:32, :], -1.0, None, ALU.mult)
            vector.tensor_scalar(s_Q2[32:64, :], s_q2er[32:64, :], -1.0, None, ALU.mult)
            vector.drain()
            for slab in range(4):
                if slab >= 1:
                    vector.wait_ge(smdbo.h, 16 * slab)
                dr_b = s_q2dr[:, 32 * slab:32 * slab + 32].unsqueeze(2).broadcast_to([64, 32, 128])
                di_b = s_q2di[:, 32 * slab:32 * slab + 32].unsqueeze(2).broadcast_to([64, 32, 128])
                tq_b = s_trQ0[0:64, :].unsqueeze(1).broadcast_to([64, 32, 128])
                vector.tensor_tensor(s_ba[0:64, :].rearrange("p (c t) -> p c t", t=128), dr_b, tq_b, ALU.mult)
                vector.tensor_tensor(s_bb[0:64, :].rearrange("p (c t) -> p c t", t=128), di_b, tq_b, ALU.mult)
                vector.drain()
                wrap_sin_cos(vector, s_bb[0:64, :], s_bc[0:64, :], s_bd[0:64, :])
                vector.tensor_copy(s_t2[:, 3:4], s_t1[:, 0:1])
                vector.drain().then_inc(smdb.inc(), 1)
                vector.wait_ge(smdbs.h, slab + 1)
                vector.tensor_mul(s_bd[0:64, :], s_ba[0:64, :], s_bd[0:64, :])
                vector.tensor_mul(s_bc[0:64, :], s_ba[0:64, :], s_bc[0:64, :])
                vector.drain()
                p_b_ = s_P2[:, 32 * slab:32 * slab + 32].unsqueeze(2).broadcast_to([64, 32, 128])
                q_b_ = s_Q2[:, 32 * slab:32 * slab + 32].unsqueeze(2).broadcast_to([64, 32, 128])
                v_ba = s_ba[0:64, :].rearrange("p (c t) -> p c t", t=128)
                v_bb = s_bb[0:64, :].rearrange("p (c t) -> p c t", t=128)
                v_bc = s_bc[0:64, :].rearrange("p (c t) -> p c t", t=128)
                v_bd = s_bd[0:64, :].rearrange("p (c t) -> p c t", t=128)
                vector.tensor_tensor(v_ba, p_b_, v_bd, ALU.mult)
                vector.tensor_tensor(v_bb, q_b_, v_bc, ALU.mult)
                vector.drain()
                vector.tensor_add(s_ba[0:64, :], s_ba[0:64, :], s_bb[0:64, :])
                vector.drain()
                vector.tensor_scalar(s_d2s[:], s_ba[0:64, :], 2.0, None, ALU.mult)
                vector.drain().then_inc(smdbv.inc(), 1)

        @blk.scalar
        def _(scalar):
            for slab in range(4):
                scalar.wait_ge(smdb.h, slab + 1)
                scalar.activation(s_ba[0:64, :], s_ba[0:64, :], AF.Exp)
                act_sin_cos(scalar, s_bc[0:64, :], s_bd[0:64, :], 64).then_inc(smdbs.inc(), 1)

        @blk.sync
        def _(sync):
            for slab in range(4):
                sync.wait_ge(smdbv.h, slab + 1)
                sync.dma_start(out=AP(d2_d, slab * 4096, [[4 * 4096, 64], [1, 4096]]),
                               in_=s_d2s[:]).then_inc(smdbo.inc(16), 16)

        gB.close()
        g2.close()
        g0.close()
        g3 = ExitStack()
        s_ucs = sbuf("s_ucs", [128, 4 * 64 * 128], BF16, g3)
        s_a2 = sbuf("s_a2", [64, 128 * 4 * 64], BF16, g3)
        g3mA = ExitStack()
        s_pp2 = sbuf("s_pp2", [128, 8192], BF16, g3mA)
        g3m = ExitStack()
        s_aswp = sbuf("s_aswp", [64, 16384], BF16, g3m)
        s_sN = sbuf("s_sN", [64, 512], F32, g3m)
        s_sS = sbuf("s_sS", [64, 512], F32, g3m)
        s_rx = sbuf("s_rx", [64, 1024], F32, g3m)

        @blk.gpsimd
        def _(gp):
            gp.wait_ge(smu3[0].h, smu3[0].v)
            gp.wait_ge(smu3[1].h, smu3[1].v)
            gp.collective_compute(
                "AllGather", ALU.bypass,
                replica_groups=[list(range(NCORES))],
                ins=[u_loc.ap().opt()],
                outs=[u_gth.ap().opt()],
            ).then_inc(smcc.inc(), 1)

        @blk.sync
        def _(sync):
            sync.wait_ge(smcc.h, 1)
            sync.wait_ge(smc.h, smc.v)
            sync.wait_ge(smgo.h, 16)
            sync.wait_ge(smdbo.h, 64)
            sync.wait_ge(sm0[18].h, sm0[18].v)
            sync.wait_ge(sm0[19].h, 32)
            uoff = sync.value_load(s_uoff[0:1, 0:1])
            sync.dma_start(out=s_pp2[:], in_=AP(pp_d, 0, [[16384, 128], [1, 8192]])).then_inc(smucs.inc(16), 16)
            for r in range(NCORES):
                b, h = r // 2, r % 2
                sync.dma_start(
                    out=AP(s_ucs[:].tensor, b * 8192 + h * 4096,
                           [[32768, 128], [128, 32], [1, 128]]),
                    in_=AP(u_gth, uoff + r * T * D, [[D, 128], [128 * D, 32], [1, 128]]),
                ).then_inc(smucs.inc(16), 16)

        @blk.tensor
        def _(tensor):
            tensor.wait_ge(smucs.h, smucs.v)
            for pair in range(64):
                pe_bank = [p_a, p_b][pair % 2]
                if pair >= 2:
                    tensor.wait_ge(sme2.h, pair - 1)
                for two in range(2):
                    ch = 2 * pair + two
                    for ri in range(2):
                        mm = tensor.matmul(
                            pe_bank[32 * ri:32 * ri + 32, 256 * two:256 * two + 256],
                            s_pp2[:, ri * 4096 + ch * 32: ri * 4096 + ch * 32 + 32],
                            AP(s_ucs[:].tensor, ch, [[32768, 128], [128, 64], [8192, 4]]),
                            start=True, stop=True, tile_position=(0, 32 * ri))
                mm.then_inc(sme1.inc(), 1)

        @blk.scalar
        def _(scalar):
            for pair in range(64):
                pe_bank = [p_a, p_b][pair % 2]
                scalar.wait_ge(sme1.h, pair + 1)
                for two in range(2):
                    ch = 2 * pair + two
                    hp2, c2 = ch // 64, ch % 64
                    scalar.activation(
                        AP(s_a2[:].tensor, 16384 * hp2 + 4 * c2, [[32768, 64], [256, 64], [1, 4]]),
                        AP(pe_bank[:].tensor, 256 * two, [[512, 64], [1, 256]]),
                        AF.Identity)
                scalar.drain().then_inc(sme2.inc(), 1)

        smsw = sem("smsw")

        @blk.sync
        def _(sync):
            for hp in range(2):
                if hp == 0:
                    sync.wait_ge(sme2.h, sme2.v)
                else:
                    sync.wait_ge(smra.h, 1)
                sync.dma_start(out=s_aswp[0:32, :],
                               in_=s_a2[32:64, 16384 * hp:16384 * hp + 16384]).then_inc(smsw.inc(16), 16)
                sync.dma_start(out=s_aswp[32:64, :],
                               in_=s_a2[0:32, 16384 * hp:16384 * hp + 16384]).then_inc(smsw.inc(16), 16)

        @blk.vector
        def _(vector):
            vector.wait_ge(sm0[8].h, 1)
            for hp in range(2):
                vector.wait_ge(smsw.h, 32 * (hp + 1))
                vector.memset(s_sN[:], 0.0)
                vector.memset(s_sS[:], 0.0)
                vector.drain()
                w1b = s_W1[:, 64 * hp:64 * hp + 64].unsqueeze(2).broadcast_to([64, 64, 4])
                w2b = s_W2[:, 64 * hp:64 * hp + 64].unsqueeze(2).broadcast_to([64, 64, 4])
                w3b = s_W3[:, 64 * hp:64 * hp + 64].unsqueeze(2).broadcast_to([64, 64, 4])
                t1 = s_rx[:, 0:256].rearrange("p (c b) -> p c b", b=4)
                t2 = s_rx[:, 256:512].rearrange("p (c b) -> p c b", b=4)
                t3 = s_rx[:, 512:768].rearrange("p (c b) -> p c b", b=4)
                t4 = s_rx[:, 768:1024].rearrange("p (c b) -> p c b", b=4)
                for J in range(64):
                    pv, cu = (J % 2), ((J + 1) % 2)
                    nP = s_sN[:, 256 * pv:256 * pv + 256].rearrange("p (c b) -> p c b", b=4)
                    nC = s_sN[:, 256 * cu:256 * cu + 256]
                    sP = s_sS[:, 256 * pv:256 * pv + 256].rearrange("p (c b) -> p c b", b=4)
                    sC = s_sS[:, 256 * cu:256 * cu + 256]
                    aj0 = 16384 * hp + 256 * J
                    vector.tensor_tensor(t1, nP, w1b, ALU.mult)
                    vector.tensor_tensor(t2, sP, w2b, ALU.mult)
                    vector.tensor_tensor(t3, nP, w3b, ALU.mult)
                    vector.tensor_tensor(t4, sP, w1b, ALU.mult)
                    if J >= 1:
                        vector.tensor_copy(s_a2[:, aj0 - 256:aj0],
                                           s_sN[:, 256 * pv:256 * pv + 256])
                    dr1 = vector.drain()
                    if hp == 1 and J == 32:
                        dr1.then_inc(smr1.inc(), 1)
                    vector.tensor_add(s_rx[:, 0:256], s_rx[:, 0:256], s_rx[:, 256:512])
                    vector.tensor_add(s_rx[:, 768:1024], s_rx[:, 768:1024], s_rx[:, 512:768])
                    vector.drain()
                    vector.tensor_tensor(nC, s_rx[:, 0:256], s_a2[:, aj0:aj0 + 256], ALU.add)
                    vector.tensor_tensor(sC, s_rx[:, 768:1024], s_aswp[:, 256 * J:256 * J + 256], ALU.add)
                    dr3 = vector.drain()
                    if J == 63:
                        dr3.then_inc([smra, smr1][hp].inc(), 1)

        g3m.close()
        g3mA.close()

        g3b = ExitStack()
        s_tp = [sbuf("s_tp0", [128, 32 * 128], BF16, g3b)]
        s_d2w = [sbuf("s_d2w0", [64, 4096], BF16, g3b)]
        s_yst = sbuf("s_yst", [128, 4 * 32 * 128], BF16, g3b)

        smtl = sem("smtl")
        smd1 = sem("smd1")
        smd2 = sem("smd2")
        smyo2 = sem("smyo2")

        @blk.sync
        def _(sync):
            sync.wait_ge(sm0[18].h, sm0[18].v)
            sync.wait_ge(smdbo.h, smdbo.v)
            for half in range(2):
                for slab in range(4):
                    idx = half * 4 + slab
                    pp = 0
                    if idx >= 1:
                        sync.wait_ge(smd1.h, 8 * idx)
                    sync.dma_start(out=s_tp[pp][:].rearrange("p (t c) -> p t c", c=32),
                                   in_=AP(k2r, 128 * 128 + 32 * slab, [[128, 128], [-128, 128], [1, 32]]),
                                   allow_slow_non_contiguous=True).then_inc(smtl.inc(16), 16)
                    sync.dma_start(out=s_d2w[pp][:],
                                   in_=AP(d2_d, slab * 4096, [[4 * 4096, 64], [1, 4096]])).then_inc(smtl.inc(16), 16)
                for b4 in range(4):
                    sync.wait_ge(smytr.h, half * 64 + 16 * (b4 + 1))
                    sync.dma_start(
                        out=AP(y_loc, b4 * 8192 + half * 4096, [[4 * 64 * 128, 128], [1, 4096]]),
                        in_=s_yst[:, b4 * 4096:b4 * 4096 + 4096],
                    ).then_inc(smyo2.inc(16), 16)

        @blk.tensor
        def _(tensor):
            for half in range(2):
                tensor.wait_ge(smr1.h, half + 1)
                for slab in range(4):
                    idx = half * 4 + slab
                    pp = 0
                    tensor.wait_ge(smtl.h, 32 * (idx + 1))
                    nJ = 31 if half == 0 else 32
                    oJ = 1 if half == 0 else 0
                    sJ0 = 0 if half == 0 else 31
                    for grp in range(8):
                        pe_bank = [p_c, p_d][grp % 2]
                        gi = idx * 8 + grp
                        if gi >= 2:
                            tensor.wait_ge(smd2.h, gi - 1)
                        for cp in range(4):
                            cl = grp * 4 + cp
                            ch = 32 * slab + cl
                            tensor.matmul(
                                pe_bank[:, 128 * cp:128 * cp + 128].rearrange("p (j b) -> p j b", b=4),
                                AP(s_tp[pp][:].tensor, cl, [[4096, 128], [32, 128]]),
                                AP(s_ucs[:].tensor, ch + half * 32 * 128,
                                   [[32768, 128], [128, 32], [8192, 4]]),
                                start=True, stop=False)
                            hp2, c2 = ch // 64, ch % 64
                            mm = tensor.matmul(
                                AP(pe_bank[:].tensor, 128 * cp + 4 * oJ, [[512, 128], [1, 4 * nJ]]),
                                s_d2w[pp][:, 128 * cl:128 * cl + 128],
                                AP(s_a2[:].tensor, 16384 * hp2 + 4 * c2 + 256 * sJ0,
                                   [[32768, 64], [256, nJ], [1, 4]]),
                                start=False, stop=True)
                        mm.then_inc(smd1.inc(), 1)
                tensor.wait_ge(smd2.h, 32 * (half + 1))
                for tt in range(128):
                    gi2 = half * 128 + tt
                    b4, Jl = tt // 32, tt % 32
                    if gi2 >= 2:
                        tensor.wait_ge(smytr.h, gi2 // 2)
                    tensor.transpose([p_tr, p_tr2][tt % 2][:],
                                     s_yst[:, b4 * 4096 + Jl * 128:b4 * 4096 + Jl * 128 + 128],
                                     s_id[:]).then_inc(smyt1.inc(), 1)

        @blk.scalar
        def _(scalar):
            for idx in range(8):
                half, slab = idx // 4, idx % 4
                for grp in range(8):
                    pe_bank = [p_c, p_d][grp % 2]
                    gi = idx * 8 + grp
                    if gi == 32:
                        scalar.wait_ge(smyo2.h, 64)
                    scalar.wait_ge(smd1.h, gi + 1)
                    scalar.activation(
                        AP(s_yst[:].tensor, 32 * slab + 4 * grp,
                           [[16384, 128], [1, 4], [128, 32], [4096, 4]]),
                        AP(pe_bank[:].tensor, 0, [[512, 128], [128, 4], [4, 32], [1, 4]]),
                        AF.Gelu).then_inc(smd2.inc(), 1)

        @blk.vector
        def _(vector):
            for half in range(2):
                for tt in range(0, 128, 2):
                    gi2 = half * 128 + tt
                    vector.wait_ge(smyt1.h, gi2 + 2)
                    for u2 in range(2):
                        b4, Jl = (tt + u2) // 32, (tt + u2) % 32
                        vector.tensor_copy(
                            s_yst[:, b4 * 4096 + Jl * 128:b4 * 4096 + Jl * 128 + 128],
                            [p_tr, p_tr2][(tt + u2) % 2][:])
                    vector.drain().then_inc(smytr.inc(), 1)

        @blk.gpsimd
        def _(gp):
            gp.wait_ge(smyo2.h, smyo2.v)
            gp.collective_compute(
                "AllGather", ALU.bypass,
                replica_groups=[list(range(NCORES))],
                ins=[y_loc.ap().opt()],
                outs=[y_gth.ap().opt()],
            ).then_inc(smcc.inc(), 1)

        g3b.close()
        g3.close()

        g45 = ExitStack()
        gluT = [sbuf(f"gluT{k}", [128, 2 * D], BF16, g45) for k in range(8)]
        outT = [sbuf(f"outT{k}", [128, D], BF16, g45) for k in range(8)]
        g4 = ExitStack()
        s_key = [sbuf(f"s_key{e}", [128, 8 * 256], BF16, g4) for e in range(8)]
        s_val = [sbuf(f"s_val{e}", [128, 2 * D], BF16, g4) for e in range(8)]
        s_sal = sbuf("s_sal", [128, 16 * 512], BF16, g4)
        s_grep = sbuf("s_grep", [128, 8 * 512], BF16, g4)
        s_xw2 = [[sbuf(f"s_xw2{pp}_{k}", [128, 512], BF16, g4) for k in range(8)] for pp in range(2)]
        s_fst = [sbuf(f"s_fst{pp}", [128, D], F32, g4) for pp in range(2)]

        @blk.sync
        def _(sync):
            sync.wait_ge(smcastC.h, 32)
            sync.wait_ge(smcastB.h, 32)
            sync.wait_ge(smgo.h, 16)
            sync.wait_ge(smxst.h, 512)
            sync.wait_ge(smyo2.h, 128)
            for k in range(8):
                sync.dma_start(out=gluT[k][:],
                               in_=AP(glubf, 128 * k * 2 * D, [[2 * D, 128], [1, 2 * D]])).then_inc(smwt3.inc(16), 16)
                sync.dma_start(out=outT[k][:],
                               in_=AP(outbf, 128 * k * D, [[D, 128], [1, D]])).then_inc(smwt3.inc(16), 16)
            def store_slab(sync, slab):
                for cc in range(4):
                    chunk = slab * 4 + cc
                    sync.wait_ge(smv2.h, chunk + 1)
                    dst = [ffnP, ffnQ][(chunk // 32)]
                    sync.dma_start(out=AP(dst, (chunk % 32) * 128 * D, [[D, 128], [1, D]]),
                                   in_=s_fst[chunk % 2][:]).then_inc(smv3[chunk % 2].inc(16), 16)

            for half in range(2):
                if half == 1:
                    store_slab(sync, 7)
                    sync.wait_ge(smv1.h, 32)
                for e in range(8):
                    eg = 8 * half + e
                    sync.dma_start(out=s_key[e][:].rearrange("p (k h2) -> p k h2", h2=256),
                                   in_=AP(keysbf, eg * D * 256, [[256, 128], [128 * 256, 8], [1, 256]])).then_inc(smkv[half].inc(16), 16)
                    sync.dma_start(out=s_val[e][:].rearrange("p (k d) -> p k d", d=D),
                                   in_=AP(valsbf, eg * 256 * D, [[D, 128], [128 * D, 2], [1, D]])).then_inc(smkv[half].inc(16), 16)
                for s in range(8):
                    slab = half * 8 + s
                    pp = s % 2
                    if s >= 1:
                        store_slab(sync, slab - 1)
                    for k in range(8):
                        sync.dma_start(
                            out=s_xw2[pp][k][:],
                            in_=AP(xnT_d, (128 * k) * T + 512 * s, [[T, 128], [1, 512]])).then_inc(smxw2[pp].inc(16), 16)
                    for e in range(8):
                        eg = 8 * half + e
                        sync.dma_start(out=s_grep[:, 512 * e:512 * e + 512],
                                       in_=AP(gT_d, eg * T + 512 * s, [[0, 128], [1, 512]])).then_inc(smgx.inc(16), 16)
            store_slab(sync, 15)

        @blk.tensor
        def _(tensor):
            for half in range(2):
                tensor.wait_ge(smkv[half].h, smkv[half].v)
                for s in range(8):
                    slab = half * 8 + s
                    pp = s % 2
                    tensor.wait_ge(smxw2[pp].h, 16 * 8 * (slab // 2 + 1))
                    for e in range(8):
                        if slab * 8 + e >= 1:
                            tensor.wait_ge(smk2.h, 2 * (slab * 8 + e))
                        for hh in range(2):
                            ps = [p_a, p_b][hh]
                            for k in range(8):
                                mm = tensor.matmul(ps[:], s_key[e][:, 256 * k + 128 * hh:256 * k + 128 * hh + 128],
                                                   s_xw2[pp][k][:], start=(k == 0), stop=(k == 7))
                            mm.then_inc(smk1.inc(), 1)
                    for cc in range(4):
                        chunk = slab * 4 + cc
                        if chunk >= 1:
                            tensor.wait_ge(smv2.h, chunk)
                        for dd in range(2):
                            po = [p_c, p_d][dd]
                            first = True
                            for e in range(8):
                                if dd == 0 and cc == 0:
                                    tensor.wait_ge(smk3.h, slab * 8 + e + 1)
                                for hh in range(2):
                                    mm = tensor.matmul(
                                        po[:],
                                        s_sal[:, (e * 2 + hh) * 512 + 128 * cc:(e * 2 + hh) * 512 + 128 * cc + 128],
                                        s_val[e][:, hh * D + 512 * dd:hh * D + 512 * dd + 512],
                                        start=first, stop=(e == 7 and hh == 1))
                                    first = False
                        mm.then_inc(smv1.inc(), 1)

        @blk.scalar
        def _(scalar):
            for half in range(2):
                for s in range(8):
                    slab = half * 8 + s
                    for e in range(8):
                        if slab >= 1:
                            scalar.wait_ge(smv1.h, 4 * slab)
                        for hh in range(2):
                            idx = (slab * 8 + e) * 2 + hh
                            scalar.wait_ge(smk1.h, idx + 1)
                            ps = [p_a, p_b][hh]
                            scalar.activation(s_sal[:, (e * 2 + hh) * 512:(e * 2 + hh) * 512 + 512],
                                              ps[:], AF.Relu).then_inc(smk2.inc(), 1)

        @blk.vector
        def _(vector):
            for half in range(2):
                for s in range(8):
                    slab = half * 8 + s
                    vector.wait_ge(smgx.h, 16 * 8 * (slab + 1))
                    for e in range(8):
                        vector.wait_ge(smk2.h, 2 * (slab * 8 + e + 1))
                        ge = s_grep[:, 512 * e:512 * e + 512]
                        vector.tensor_tensor(s_sal[:, (e * 2) * 512:(e * 2) * 512 + 512],
                                             s_sal[:, (e * 2) * 512:(e * 2) * 512 + 512], ge, ALU.mult)
                        vector.drain()
                        vector.tensor_tensor(s_sal[:, (e * 2 + 1) * 512:(e * 2 + 1) * 512 + 512],
                                             s_sal[:, (e * 2 + 1) * 512:(e * 2 + 1) * 512 + 512], ge, ALU.mult)
                        vector.drain().then_inc(smk3.inc(), 1)
                    for cc in range(4):
                        chunk = slab * 4 + cc
                        pp3 = chunk % 2
                        vector.wait_ge(smv1.h, chunk + 1)
                        if chunk >= 2:
                            vector.wait_ge(smv3[chunk % 2].h, 16 * (chunk // 2))
                        if half == 0:
                            vector.tensor_copy(s_fst[pp3][:, 0:512], p_c[:])
                            vector.tensor_copy(s_fst[pp3][:, 512:1024], p_d[:])
                        else:
                            vector.tensor_add(s_fst[pp3][:, 0:512], p_c[:], s_outb[:, 0:512])
                            vector.tensor_add(s_fst[pp3][:, 512:1024], p_d[:], s_outb[:, 512:1024])
                        vector.drain().then_inc(smv2.inc(), 1)

        g4.close()

        g5 = ExitStack()
        y_fm = [sbuf(f"y_fm{r}", [128, T], BF16, g5) for r in range(8)]
        s_ag = [sbuf(f"s_ag{o}", [128, 512], BF16, g5) for o in range(16)]
        s_y2 = [sbuf(f"s_y2{k}", [128, 512], BF16, g5) for k in range(8)]
        s_xr = [sbuf(f"s_xr{pp}", [128, D], F32, g5) for pp in range(2)]
        s_fp = [sbuf(f"s_fp{pp}", [128, D], F32, g5) for pp in range(2)]
        s_fq = [sbuf(f"s_fq{pp}", [128, D], F32, g5) for pp in range(2)]
        s_fin = [sbuf(f"s_fin{pp}", [128, D], F32, g5) for pp in range(2)]
        s_aff = sbuf("s_aff", [128, 4], F32, g5)

        @blk.sync
        def _(sync):
            sync.wait_ge(smcc.h, 2)
            sync.wait_ge(smcastB.h, 32)
            sync.wait_ge(smv3[0].h, smv3[0].v)
            sync.wait_ge(smv3[1].h, smv3[1].v)
            yoff = sync.value_load(s_yoff[0:1, 0:1])
            for r in range(8):
                sync.dma_start(
                    out=y_fm[r][:],
                    in_=AP(y_gth, yoff + r * 128 * 32768, [[32768, 128], [1, 4096]])).then_inc(smyf.inc(16), 16)
            for chunk in range(2):
                p9 = chunk % 2
                sync.dma_start(out=s_xr[p9][:],
                               in_=AP(x_in, chunk * 128 * D, [[D, 128], [1, D]])).then_inc(smf1[p9].inc(16), 16)
                sync.dma_start(out=s_fp[p9][:],
                               in_=AP(ffnP, chunk * 128 * D, [[D, 128], [1, D]])).then_inc(smf2[p9].inc(16), 16)
                sync.dma_start(out=s_fq[p9][:],
                               in_=AP(ffnQ, chunk * 128 * D, [[D, 128], [1, D]])).then_inc(smf3[p9].inc(16), 16)
            for chunk in range(32):
                p9 = chunk % 2
                sync.wait_ge(smo2.h, chunk + 1)
                sync.dma_start(out=AP(out_t, chunk * 128 * D, [[D, 128], [1, D]]),
                               in_=s_fin[p9][:]).then_inc(smout[p9].inc(16), 16)
                if chunk + 2 < 32:
                    sync.dma_start(out=s_xr[p9][:],
                                   in_=AP(x_in, (chunk + 2) * 128 * D, [[D, 128], [1, D]])).then_inc(smf1[p9].inc(16), 16)
                    sync.dma_start(out=s_fp[p9][:],
                                   in_=AP(ffnP, (chunk + 2) * 128 * D, [[D, 128], [1, D]])).then_inc(smf2[p9].inc(16), 16)
                    sync.dma_start(out=s_fq[p9][:],
                                   in_=AP(ffnQ, (chunk + 2) * 128 * D, [[D, 128], [1, D]])).then_inc(smf3[p9].inc(16), 16)
            sync.wait_ge(smout[0].h, smout[0].v)
            sync.wait_ge(smout[1].h, smout[1].v)

        @blk.tensor
        def _(tensor):
            tensor.wait_ge(smyf.h, smyf.v)
            tensor.wait_ge(smwt3.h, smwt3.v)
            for s in range(8):
                for o in range(16):
                    pg = [p_a, p_b][o % 2]
                    if s * 16 + o >= 2:
                        tensor.wait_ge(smg2.h, s * 16 + o - 1)
                    for r in range(8):
                        mm = tensor.matmul(pg[:], gluT[r][:, 128 * o:128 * o + 128],
                                           y_fm[r][:, 512 * s:512 * s + 512],
                                           start=(r == 0), stop=(r == 7))
                    mm.then_inc(smg1.inc(), 1)
                for cc in range(4):
                    chunk = s * 4 + cc
                    if chunk >= 1:
                        tensor.wait_ge(smo2.h, chunk)
                    for k in range(8):
                        tensor.wait_ge(smg3.h, s * 8 + k + 1)
                        lhs = s_y2[k][:, 128 * cc:128 * cc + 128]
                        tensor.matmul(p_c[:], lhs, outT[k][:, 0:512], start=(k == 0), stop=(k == 7))
                        mm = tensor.matmul(p_d[:], lhs, outT[k][:, 512:1024], start=(k == 0), stop=(k == 7))
                    mm.then_inc(smo1.inc(), 1)

        @blk.scalar
        def _(scalar):
            for s in range(8):
                for o in range(16):
                    pg = [p_a, p_b][o % 2]
                    scalar.wait_ge(smg1.h, s * 16 + o + 1)
                    if s >= 1:
                        scalar.wait_ge(smg3.h, 8 * s)
                    if o < 8:
                        scalar.activation(s_ag[o][:], pg[:], AF.Identity,
                                          bias=s_glb[:, o:o + 1]).then_inc(smg2.inc(), 1)
                    else:
                        scalar.activation(s_ag[o][:], pg[:], AF.Sigmoid,
                                          bias=s_glb[:, o:o + 1]).then_inc(smg2.inc(), 1)

        @blk.vector
        def _(vector):
            for s in range(8):
                for k in range(8):
                    vector.wait_ge(smg2.h, s * 16 + 16)
                    if s >= 1:
                        vector.wait_ge(smo1.h, 4 * s)
                    vector.tensor_mul(s_y2[k][:], s_ag[k][:], s_ag[k + 8][:])
                    vector.drain().then_inc(smg3.inc(), 1)
                for cc in range(4):
                    chunk = s * 4 + cc
                    pp4 = chunk % 2
                    vector.wait_ge(smo1.h, chunk + 1)
                    vector.wait_ge(smf1[pp4].h, 16 * (chunk // 2 + 1))
                    vector.wait_ge(smf2[pp4].h, 16 * (chunk // 2 + 1))
                    vector.wait_ge(smf3[pp4].h, 16 * (chunk // 2 + 1))
                    if chunk >= 2:
                        vector.wait_ge(smout[pp4].h, 16 * (chunk // 2))
                    vector.tensor_scalar(s_aff[:, 0:1], s_rs[:, chunk:chunk + 1], 1.0, None, ALU.add)
                    vector.tensor_mul(s_aff[:, 1:2], s_mu[:, chunk:chunk + 1], s_rs[:, chunk:chunk + 1])
                    vector.drain()
                    vector.tensor_scalar(s_aff[:, 1:2], s_aff[:, 1:2], -1.0, None, ALU.mult)
                    vector.drain()
                    vector.tensor_scalar(s_fin[pp4][:], s_xr[pp4][:], s_aff[:, 0:1], s_aff[:, 1:2],
                                         ALU.mult, ALU.add)
                    vector.drain()
                    vector.tensor_add(s_fin[pp4][:, 0:512], s_fin[pp4][:, 0:512], p_c[:])
                    vector.tensor_add(s_fin[pp4][:, 512:1024], s_fin[pp4][:, 512:1024], p_d[:])
                    vector.drain()
                    vector.tensor_add(s_fin[pp4][:], s_fin[pp4][:], s_fp[pp4][:])
                    vector.drain()
                    vector.tensor_add(s_fin[pp4][:], s_fin[pp4][:], s_fq[pp4][:])
                    vector.drain().then_inc(smo2.inc(), 1)

        g5.close()
        g45.close()

    es.close()
    return nc


_NC_CACHE = None


def kernel(L0_attn_x, L0_attn_in_W, L0_attn_in_b, L0_attn_log_dt, L0_attn_C,
           L0_attn_log_A_real, L0_attn_A_imag, L0_attn_D_skip, L0_attn_glu_W,
           L0_attn_glu_b, L0_attn_out_W, L0_attn_out_b, L0_ffn_expert_sel,
           L0_ffn_keys, L0_ffn_values, L0_ffn_route_scale):
    global _NC_CACHE
    if _NC_CACHE is None:
        _NC_CACHE = build_nc()
    nc = _NC_CACHE

    x = np.ascontiguousarray(np.asarray(L0_attn_x, dtype=np.float32))
    in_WT = np.ascontiguousarray(np.asarray(L0_attn_in_W, np.float32).T)
    eselT_h = np.ascontiguousarray(np.asarray(L0_ffn_expert_sel, np.float32).T)
    glu_WT = np.ascontiguousarray(np.asarray(L0_attn_glu_W, np.float32).T)
    out_WT = np.ascontiguousarray(np.asarray(L0_attn_out_W, np.float32).T)
    ident = np.eye(128, dtype=ml_dtypes.bfloat16)
    t_idx = np.arange(128, dtype=np.float32)
    trQ = np.tile((t_idx + 1.0)[None, :], (128, 1))
    trK = np.tile((127.0 - t_idx)[None, :], (128, 1))
    trP = (127.0 - t_idx)[:, None].copy()
    trP2 = np.stack([128.0 - t_idx, 126.0 - t_idx], axis=1).astype(np.float32)

    in_maps = []
    for c in range(NCORES):
        b, h = c // 2, c % 2
        cs = slice(128 * c, 128 * (c + 1))
        in_maps.append(dict(
            x_in=np.ascontiguousarray(x[b, h * 4096:(h + 1) * 4096, :]),
            in_W=in_WT,
            in_b=np.asarray(L0_attn_in_b, np.float32),
            log_dt=np.ascontiguousarray(np.asarray(L0_attn_log_dt, np.float32)[cs]),
            C_p=np.ascontiguousarray(np.asarray(L0_attn_C, np.float32)[cs]),
            lar_t=np.ascontiguousarray(np.asarray(L0_attn_log_A_real, np.float32)[cs]),
            aim_t=np.ascontiguousarray(np.asarray(L0_attn_A_imag, np.float32)[cs]),
            dsk_t=np.ascontiguousarray(np.asarray(L0_attn_D_skip, np.float32)[cs]),
            glu_W=glu_WT,
            glu_b=np.asarray(L0_attn_glu_b, np.float32),
            out_W=out_WT,
            out_b=np.asarray(L0_attn_out_b, np.float32),
            esel=eselT_h,
            keys_t=np.asarray(L0_ffn_keys, np.float32),
            vals_t=np.asarray(L0_ffn_values, np.float32),
            rscale=np.asarray(L0_ffn_route_scale, np.float32),
            ident=ident, trQ_t=trQ, trK_t=trK, trP_t=trP, trP2_t=trP2,
            uoff_t=np.array([128 * c], np.int32),
            yoff_t=np.array([b * 8192 + h * 4096], np.int32),
        ))

    res = run_bass_kernel_spmd(nc, in_maps, core_ids=list(range(NCORES)))
    out = np.empty((4, 8192, 1024), np.float32)
    for c in range(NCORES):
        b, h = c // 2, c % 2
        out[b, h * 4096:(h + 1) * 4096, :] = np.asarray(res.results[c]["out"])
    return out


# version 15
# speedup vs baseline: 1.1017x; 1.0067x over previous
import numpy as np
import ml_dtypes
import concourse.bass as bass
import concourse.mybir as mybir
from concourse.ap import AP
from concourse.bass_utils import run_bass_kernel_spmd
from contextlib import ExitStack

F32 = mybir.dt.float32
BF16 = mybir.dt.bfloat16
I32 = mybir.dt.int32
AF = mybir.ActivationFunctionType
ALU = mybir.AluOpType
AX = mybir.AxisListType.X

NCORES = 8
T = 4096
D = 1024
NST = 32
TWO_PI = 6.283185307179586
INV_2PI = 0.15915494309189535
MAGIC = 12582912.0
HALF_PI = 1.5707963267948966


class Ctr:
    def __init__(self, h):
        self.h = h
        self.v = 0

    def inc(self, n=1):
        self.v += n
        return self.h


def build_nc():
    nc = bass.Bass("TRN2", target_bir_lowering=False)

    x_in = nc.dram_tensor("x_in", [T, D], F32, kind="ExternalInput")
    in_W = nc.dram_tensor("in_W", [D, D], F32, kind="ExternalInput")
    in_b = nc.dram_tensor("in_b", [D], F32, kind="ExternalInput")
    log_dt = nc.dram_tensor("log_dt", [128], F32, kind="ExternalInput")
    C_p = nc.dram_tensor("C_p", [128, NST, 2], F32, kind="ExternalInput")
    lar_t = nc.dram_tensor("lar_t", [128, NST], F32, kind="ExternalInput")
    aim_t = nc.dram_tensor("aim_t", [128, NST], F32, kind="ExternalInput")
    dsk_t = nc.dram_tensor("dsk_t", [128], F32, kind="ExternalInput")
    glu_W = nc.dram_tensor("glu_W", [D, 2 * D], F32, kind="ExternalInput")
    glu_b = nc.dram_tensor("glu_b", [2 * D], F32, kind="ExternalInput")
    out_W = nc.dram_tensor("out_W", [D, D], F32, kind="ExternalInput")
    out_b = nc.dram_tensor("out_b", [D], F32, kind="ExternalInput")
    esel = nc.dram_tensor("esel", [D, 16], F32, kind="ExternalInput")
    keys_t = nc.dram_tensor("keys_t", [16, D, 256], F32, kind="ExternalInput")
    vals_t = nc.dram_tensor("vals_t", [16, 256, D], F32, kind="ExternalInput")
    rscale = nc.dram_tensor("rscale", [1], F32, kind="ExternalInput")
    ident = nc.dram_tensor("ident", [128, 128], BF16, kind="ExternalInput")
    trQ_t = nc.dram_tensor("trQ_t", [128, 128], F32, kind="ExternalInput")
    trK_t = nc.dram_tensor("trK_t", [128, 128], F32, kind="ExternalInput")
    trP_t = nc.dram_tensor("trP_t", [128, 1], F32, kind="ExternalInput")
    trP2_t = nc.dram_tensor("trP2_t", [128, 2], F32, kind="ExternalInput")
    uoff_t = nc.dram_tensor("uoff_t", [1], I32, kind="ExternalInput")
    yoff_t = nc.dram_tensor("yoff_t", [1], I32, kind="ExternalInput")
    out_t = nc.dram_tensor("out", [T, D], F32, kind="ExternalOutput")

    xnT_d = nc.dram_tensor("xnT_d", [D, T], BF16)
    w1bf = nc.dram_tensor("w1bf", [D, D], BF16)
    glubf = nc.dram_tensor("glubf", [D, 2 * D], BF16)
    outbf = nc.dram_tensor("outbf", [D, D], BF16)
    eselbf = nc.dram_tensor("eselbf", [D, 16], BF16)
    keysbf = nc.dram_tensor("keysbf", [16, D, 256], BF16)
    valsbf = nc.dram_tensor("valsbf", [16, 256, D], BF16)
    prow = nc.dram_tensor("prow", [4, 4096], F32)
    k2r = nc.dram_tensor("k2r", [256, 128], BF16)
    u_loc = nc.dram_tensor("u_loc", [T, D], BF16)
    u_gth = nc.dram_tensor("u_gth", [NCORES * T, D], BF16, addr_space="Shared")
    y_loc = nc.dram_tensor("y_loc", [128, 4 * 64 * 128], BF16)
    y_gth = nc.dram_tensor("y_gth", [NCORES * 128, 4 * 64 * 128], BF16, addr_space="Shared")
    gT_d = nc.dram_tensor("gT_d", [16, T], BF16)
    d2_d = nc.dram_tensor("d2_d", [64, 4 * 4096], BF16)
    pp_d = nc.dram_tensor("pp_d", [128, 4 * 4096], BF16)
    ffnP = nc.dram_tensor("ffnP", [T, D], F32)
    ffnQ = nc.dram_tensor("ffnQ", [T, D], F32)

    es = ExitStack()

    def sbuf(name, shape, dt, st=None):
        return (st or es).enter_context(nc.sbuf_tensor(name, shape, dt))

    def sem(name):
        return Ctr(es.enter_context(nc.semaphore(name)))

    s_id = sbuf("s_id", [128, 128], BF16)
    s_trP = sbuf("s_trP", [128, 1], F32)
    s_trP2 = sbuf("s_trP2", [128, 2], F32)
    s_hpi = sbuf("s_hpi", [128, 1], F32)
    s_mu = sbuf("s_mu", [128, 32], F32)
    s_rs = sbuf("s_rs", [128, 32], F32)
    s_inb = sbuf("s_inb", [128, D], F32)
    s_outb = sbuf("s_outb", [128, D], F32)
    s_glb = sbuf("s_glb", [128, 16], F32)
    s_g = sbuf("s_g", [128, 32 * 16], F32)
    s_rsc = sbuf("s_rsc", [128, 1], F32)
    s_uoff = sbuf("s_uoff", [1, 1], I32)
    s_yoff = sbuf("s_yoff", [1, 1], I32)
    s_q2dr = sbuf("s_q2dr", [64, 128], F32)
    s_q2di = sbuf("s_q2di", [64, 128], F32)
    s_q2er = sbuf("s_q2er", [64, 128], F32)
    s_q2ei = sbuf("s_q2ei", [64, 128], F32)
    s_W1 = sbuf("s_W1", [64, 128], F32)
    s_W2 = sbuf("s_W2", [64, 128], F32)
    s_W3 = sbuf("s_W3", [64, 128], F32)

    p_a = es.enter_context(nc.psum_tensor("p_a", [128, 512], F32))
    p_b = es.enter_context(nc.psum_tensor("p_b", [128, 512], F32))
    p_c = es.enter_context(nc.psum_tensor("p_c", [128, 512], F32))
    p_d = es.enter_context(nc.psum_tensor("p_d", [128, 512], F32))
    p_sel = es.enter_context(nc.psum_tensor("p_sel", [128, 16], F32))
    p_tr = es.enter_context(nc.psum_tensor("p_tr", [128, 128], BF16))
    p_tr2 = es.enter_context(nc.psum_tensor("p_tr2", [128, 128], BF16))
    p_tr3 = es.enter_context(nc.psum_tensor("p_tr3", [128, 128], BF16))

    smc = sem("smc"); smprm = sem("smprm")
    smcastA = sem("smcastA"); smcastB = sem("smcastB"); smcastC = sem("smcastC")
    sm0 = [sem(f"sm0_{i}") for i in range(20)]
    smx = [sem("smx0"), sem("smx1")]
    smln = sem("smln"); smrs = sem("smrs"); smlv = sem("smlv")
    smtp = sem("smtp"); smcpy = sem("smcpy"); smxst = sem("smxst")
    smyt1 = sem("smyt1"); smytr = sem("smytr")
    smw2 = sem("smw2")
    smu1 = sem("smu1"); smu2 = sem("smu2")
    smu3 = [sem("smu3A"), sem("smu3B")]
    smlg = sem("smlg"); smsg = sem("smsg"); smtk = sem("smtk")
    smgt = sem("smgt"); smge = sem("smge"); smgo = sem("smgo")
    smglb = sem("smglb")
    smcc = sem("smcc"); smucs = sem("smucs")
    sme1 = sem("sme1"); sme2 = sem("sme2")
    smr1 = sem("smr1"); smra = sem("smra")
    smkv = [sem("smkv0"), sem("smkv1")]
    smxw2 = [sem("smxw2_0"), sem("smxw2_1")]
    smgx = sem("smgx")
    smk1 = sem("smk1"); smk2 = sem("smk2"); smk3 = sem("smk3")
    smv1 = sem("smv1"); smv2 = sem("smv2")
    smv3 = [sem("smv3A"), sem("smv3B")]
    smyf = sem("smyf"); smwt3 = sem("smwt3")
    smg1 = sem("smg1"); smg2 = sem("smg2"); smg3 = sem("smg3")
    smo1 = sem("smo1"); smo2 = sem("smo2")
    smf1 = [sem("smf1A"), sem("smf1B")]
    smf2 = [sem("smf2A"), sem("smf2B")]
    smf3 = [sem("smf3A"), sem("smf3B")]
    smout = [sem("smoutA"), sem("smoutB")]

    def wrap_sin_cos(vector, src_ang, dst_sin_arg, dst_cos_arg):
        vector.tensor_scalar(dst_sin_arg, src_ang, INV_2PI, MAGIC, ALU.mult, ALU.add)
        vector.drain()
        vector.tensor_scalar(dst_sin_arg, dst_sin_arg, MAGIC, None, ALU.subtract)
        vector.drain()
        vector.scalar_tensor_tensor(dst_sin_arg, dst_sin_arg, -TWO_PI, src_ang, ALU.mult, ALU.add)
        vector.drain()
        vector.tensor_scalar_min(dst_sin_arg, dst_sin_arg, 3.141592653589793)
        vector.drain()
        vector.tensor_scalar_max(dst_sin_arg, dst_sin_arg, -3.141592653589793)
        vector.drain()

    def act_sin_cos(scalar, wrapped, cos_out, pcount):
        scalar.activation(cos_out, wrapped, AF.Abs)
        scalar.drain()
        scalar.activation(cos_out, cos_out, AF.Sin, bias=s_hpi[0:pcount, 0:1], scale=-1.0)
        return scalar.activation(wrapped, wrapped, AF.Sin)

    with nc.Block() as blk:
        @blk.sync
        def _(sync):
            sync.dma_start(out=s_id[:], in_=ident.ap()).then_inc(smc.inc(16), 16)
            sync.dma_start(out=s_trP[:], in_=trP_t.ap()).then_inc(smc.inc(16), 16)
            sync.dma_start(out=s_trP2[:], in_=trP2_t.ap()).then_inc(smc.inc(16), 16)
            sync.dma_start(out=s_inb[:], in_=AP(in_b, 0, [[0, 128], [1, D]])).then_inc(smc.inc(16), 16)
            sync.dma_start(out=s_outb[:], in_=AP(out_b, 0, [[0, 128], [1, D]])).then_inc(smc.inc(16), 16)
            sync.dma_start(out=s_glb[:], in_=AP(glu_b, 0, [[1, 128], [128, 16]]),
                           allow_slow_non_contiguous=True).then_inc(smglb.inc(16), 16)
            sync.dma_start(out=s_rsc[:], in_=AP(rscale, 0, [[0, 128], [1, 1]])).then_inc(smc.inc(16), 16)
            sync.dma_start(out=s_uoff[:], in_=uoff_t.ap().unsqueeze(0)).then_inc(smc.inc(16), 16)
            sync.dma_start(out=s_yoff[:], in_=yoff_t.ap().unsqueeze(0)).then_inc(smc.inc(16), 16)

        @blk.gpsimd
        def _(gp):
            gp.dma_start(out=w1bf.ap(), in_=in_W.ap()).then_inc(smcastA.inc(16), 16)
            gp.dma_start(out=eselbf.ap(), in_=esel.ap()).then_inc(smcastA.inc(16), 16)
            gp.wait_ge(smx[1].h, 16)
            gp.dma_start(out=glubf.ap(), in_=glu_W.ap()).then_inc(smcastB.inc(16), 16)
            gp.dma_start(out=outbf.ap(), in_=out_W.ap()).then_inc(smcastB.inc(16), 16)
            gp.dma_start(out=keysbf.ap(), in_=keys_t.ap()).then_inc(smcastC.inc(16), 16)
            gp.dma_start(out=valsbf.ap(), in_=vals_t.ap()).then_inc(smcastC.inc(16), 16)

        g0 = ExitStack()
        s_dtc = sbuf("s_dtc", [128, 1], F32, g0)
        s_lar = sbuf("s_lar", [128, NST], F32, g0)
        s_aim = sbuf("s_aim", [128, NST], F32, g0)
        s_cr = sbuf("s_cr", [128, NST], F32, g0)
        s_ci = sbuf("s_ci", [128, NST], F32, g0)
        s_cri = sbuf("s_cri", [128, 2 * NST], F32, g0)
        s_dskc = sbuf("s_dskc", [128, 1], F32, g0)
        s_ema = sbuf("s_ema", [128, NST], F32, g0)
        s_dtar = sbuf("s_dtar", [128, NST], F32, g0)
        s_dtai = sbuf("s_dtai", [128, NST], F32, g0)
        s_absw = sbuf("s_absw", [128, NST], F32, g0)
        s_ang = sbuf("s_ang", [128, NST], F32, g0)
        s_ang2 = sbuf("s_ang2", [128, NST], F32, g0)
        s_sinw = sbuf("s_sinw", [128, NST], F32, g0)
        s_cosw = sbuf("s_cosw", [128, NST], F32, g0)
        s_t1 = sbuf("s_t1", [128, NST], F32, g0)
        s_t2 = sbuf("s_t2", [128, NST], F32, g0)
        s_t3 = sbuf("s_t3", [128, NST], F32, g0)
        s_t4 = sbuf("s_t4", [128, NST], F32, g0)
        s_den = sbuf("s_den", [128, NST], F32, g0)
        s_trr = sbuf("s_trr", [128, NST], F32, g0)
        s_tii = sbuf("s_tii", [128, NST], F32, g0)
        s_er = sbuf("s_er", [128, NST], F32, g0)
        s_ei = sbuf("s_ei", [128, NST], F32, g0)
        s_wa = sbuf("s_wa", [64, 128], F32, g0)
        s_wb = sbuf("s_wb", [64, 128], F32, g0)

        @blk.sync
        def _(sync):
            sync.dma_start(out=s_dtc[:], in_=log_dt.ap().unsqueeze(1)).then_inc(smprm.inc(16), 16)
            sync.dma_start(out=s_lar[:], in_=lar_t.ap()).then_inc(smprm.inc(16), 16)
            sync.dma_start(out=s_aim[:], in_=aim_t.ap()).then_inc(smprm.inc(16), 16)
            sync.dma_start(out=s_cri[:], in_=AP(C_p, 0, [[NST * 2, 128], [1, NST * 2]])).then_inc(smprm.inc(16), 16)
            sync.dma_start(out=s_dskc[:], in_=dsk_t.ap().unsqueeze(1)).then_inc(smprm.inc(16), 16)

        @blk.scalar
        def _(scalar):
            scalar.wait_ge(smprm.h, smprm.v)
            scalar.activation(s_dtc[:], s_dtc[:], AF.Exp)
            scalar.drain()
            scalar.activation(s_ema[:], s_lar[:], AF.Exp).then_inc(sm0[0].inc(), 1)

        @blk.vector
        def _(vector):
            vector.wait_ge(sm0[0].h, 1)
            vector.tensor_copy(s_cr[:], AP(s_cri[:].tensor, 0, [[2 * NST, 128], [2, NST]]))
            vector.tensor_copy(s_ci[:], AP(s_cri[:].tensor, 1, [[2 * NST, 128], [2, NST]]))
            vector.memset(s_hpi[:], HALF_PI)
            vector.drain()
            vector.tensor_scalar(s_dtar[:], s_ema[:], s_dtc[:, 0:1], -1.0, ALU.mult, ALU.mult)
            vector.tensor_scalar(s_dtai[:], s_aim[:], s_dtc[:, 0:1], None, ALU.mult)
            vector.drain()
            wrap_sin_cos(vector, s_dtai[:], s_ang[:], s_ang2[:])
            vector.tensor_copy(s_t1[:], s_dtar[:])
            vector.drain().then_inc(sm0[1].inc(), 1)

        @blk.scalar
        def _(scalar):
            scalar.wait_ge(sm0[1].h, 1)
            scalar.activation(s_absw[:], s_dtar[:], AF.Exp)
            act_sin_cos(scalar, s_ang[:], s_cosw[:], 128)
            scalar.drain()
            scalar.activation(s_sinw[:], s_ang[:], AF.Identity).then_inc(sm0[2].inc(), 1)

        @blk.vector
        def _(vector):
            vector.wait_ge(sm0[2].h, 1)
            vector.tensor_mul(s_t1[:], s_absw[:], s_cosw[:])
            vector.tensor_mul(s_t2[:], s_absw[:], s_sinw[:])
            vector.tensor_mul(s_den[:], s_ema[:], s_ema[:])
            vector.tensor_mul(s_t3[:], s_aim[:], s_aim[:])
            vector.drain()
            vector.tensor_add(s_den[:], s_den[:], s_t3[:])
            vector.drain()
            vector.reciprocal(s_den[:], s_den[:])
            vector.tensor_scalar(s_t1[:], s_t1[:], -1.0, None, ALU.add)
            vector.drain()
            vector.tensor_mul(s_t3[:], s_t1[:], s_ema[:])
            vector.tensor_mul(s_t4[:], s_t2[:], s_aim[:])
            vector.drain()
            vector.tensor_sub(s_trr[:], s_t4[:], s_t3[:])
            vector.drain()
            vector.tensor_mul(s_t3[:], s_t2[:], s_ema[:])
            vector.tensor_mul(s_t4[:], s_t1[:], s_aim[:])
            vector.drain()
            vector.tensor_add(s_tii[:], s_t3[:], s_t4[:])
            vector.drain()
            vector.tensor_scalar(s_tii[:], s_tii[:], -1.0, None, ALU.mult)
            vector.drain()
            vector.tensor_mul(s_t3[:], s_cr[:], s_trr[:])
            vector.tensor_mul(s_t4[:], s_ci[:], s_tii[:])
            vector.drain()
            vector.tensor_sub(s_er[:], s_t3[:], s_t4[:])
            vector.drain()
            vector.tensor_mul(s_t3[:], s_cr[:], s_tii[:])
            vector.tensor_mul(s_t4[:], s_ci[:], s_trr[:])
            vector.drain()
            vector.tensor_add(s_ei[:], s_t3[:], s_t4[:])
            vector.drain()
            vector.tensor_mul(s_er[:], s_er[:], s_den[:])
            vector.tensor_mul(s_ei[:], s_ei[:], s_den[:])
            vector.drain().then_inc(sm0[3].inc(), 1)

        @blk.sync
        def _(sync):
            sync.wait_ge(sm0[3].h, 1)
            for row, src in [(0, s_dtar), (1, s_dtai), (2, s_er), (3, s_ei)]:
                sync.dma_start(out=AP(prow, row * 4096, [[NST, 128], [1, NST]]),
                               in_=src[:]).then_inc(sm0[4].inc(16), 16)
            sync.wait_ge(sm0[4].h, sm0[4].v)
            for dst, row in [(s_q2dr, 0), (s_q2di, 1), (s_q2er, 2), (s_q2ei, 3)]:
                sync.dma_start(out=dst[0:32, :], in_=AP(prow, row * 4096, [[1, NST], [NST, 128]]),
                               allow_slow_non_contiguous=True).then_inc(sm0[5].inc(16), 16)
                sync.dma_start(out=dst[32:64, :], in_=AP(prow, row * 4096, [[1, NST], [NST, 128]]),
                               allow_slow_non_contiguous=True).then_inc(sm0[5].inc(16), 16)

        @blk.vector
        def _(vector):
            vector.wait_ge(sm0[5].h, sm0[5].v)
            vector.tensor_scalar(s_W1[:], s_q2dr[:], 128.0, None, ALU.mult)
            vector.tensor_scalar(s_W2[:], s_q2di[:], 128.0, None, ALU.mult)
            vector.drain()
            wrap_sin_cos(vector, s_W2[:], s_wa[:], s_wb[:])
            vector.drain().then_inc(sm0[6].inc(), 1)

        @blk.scalar
        def _(scalar):
            scalar.wait_ge(sm0[6].h, 1)
            scalar.activation(s_W1[:], s_W1[:], AF.Exp)
            act_sin_cos(scalar, s_wa[:], s_wb[:], 64).then_inc(sm0[7].inc(), 1)

        @blk.vector
        def _(vector):
            vector.wait_ge(sm0[7].h, 1)
            vector.tensor_mul(s_wa[:], s_W1[:], s_wa[:])
            vector.drain()
            vector.tensor_mul(s_W1[:], s_W1[:], s_wb[:])
            vector.tensor_scalar(s_W2[0:32, :], s_wa[0:32, :], -1.0, None, ALU.mult)
            vector.tensor_copy(s_W2[32:64, :], s_wa[32:64, :])
            vector.tensor_copy(s_W3[0:32, :], s_wa[0:32, :])
            vector.tensor_scalar(s_W3[32:64, :], s_wa[32:64, :], -1.0, None, ALU.mult)
            vector.drain().then_inc(sm0[8].inc(), 1)

        g2 = ExitStack()
        s_x = [sbuf("s_x0", [128, D], F32, g2), sbuf("s_x1", [128, D], F32, g2)]
        s_xsq = sbuf("s_xsq", [128, D], F32, g2)
        s_xn = [sbuf("s_xn0", [128, D], BF16, g2), sbuf("s_xn1", [128, D], BF16, g2)]
        s_st = sbuf("s_st", [128, 8], F32, g2)
        w1t = [sbuf(f"w1t{k}", [128, D], BF16, g2) for k in range(8)]
        selT = sbuf("selT", [128, 8 * 16], BF16, g2)
        s_xg = [[sbuf(f"s_xg{g}_{k}", [128, 1024], BF16, g2) for k in range(8)] for g in range(2)]
        s_ust = [sbuf(f"s_ust{pp}", [128, D], BF16, g2) for pp in range(2)]
        s_lgo = sbuf("s_lgo", [128, 16], F32, g2)
        s_lpr = sbuf("s_lpr", [128, 16], F32, g2)
        s_lgw = sbuf("s_lgw", [128, 16], F32, g2)
        s_lmx = sbuf("s_lmx", [128, 1], F32, g2)
        s_leq = sbuf("s_leq", [128, 16], F32, g2)
        s_lmk = sbuf("s_lmk", [128, 16], F32, g2)
        s_lsum = sbuf("s_lsum", [128, 1], F32, g2)
        s_gbf = sbuf("s_gbf", [128, 32 * 16], BF16, g2)
        s_gT = sbuf("s_gT", [16, T], BF16, g2)

        @blk.sync
        def _(sync):
            sync.dma_start(out=s_x[0][:], in_=AP(x_in, 0, [[D, 128], [1, D]])).then_inc(smx[0].inc(16), 16)
            sync.dma_start(out=s_x[1][:], in_=AP(x_in, 128 * D, [[D, 128], [1, D]])).then_inc(smx[1].inc(16), 16)
            sync.wait_ge(smcastA.h, 32)
            for k in range(8):
                sync.dma_start(out=w1t[k][:],
                               in_=AP(w1bf, 128 * k * D, [[D, 128], [1, D]])).then_inc(smw2.inc(16), 16)
                sync.dma_start(out=selT[:, 16 * k:16 * k + 16],
                               in_=AP(eselbf, 128 * k * 16, [[16, 128], [1, 16]])).then_inc(smw2.inc(16), 16)
            for i in range(32):
                if i + 2 < 32:
                    sync.wait_ge(smlv.h, i + 1)
                    sync.dma_start(out=s_x[i % 2][:],
                                   in_=AP(x_in, (i + 2) * 128 * D, [[D, 128], [1, D]])).then_inc(smx[i % 2].inc(16), 16)
                sync.wait_ge(smu2.h, i + 1)
                sync.dma_start(out=AP(u_loc, i * 128 * D, [[D, 128], [1, D]]),
                               in_=s_ust[i % 2][:]).then_inc(smu3[i % 2].inc(16), 16)
                if i % 8 == 7:
                    g = i // 8
                    sync.wait_ge(smcpy.h, 8 * (i + 1))
                    for k in range(8):
                        sync.dma_start(out=AP(xnT_d, (128 * k) * T + 1024 * g, [[T, 128], [1, 1024]]),
                                       in_=s_xg[g % 2][k][:]).then_inc(smxst.inc(16), 16)
            sync.wait_ge(smge.h, 32)
            sync.dma_start(out=gT_d.ap(), in_=s_gT[:]).then_inc(smgo.inc(16), 16)

        @blk.tensor
        def _(tensor):
            tensor.wait_ge(smc.h, smc.v)
            for i in range(32):
                tensor.wait_ge(smlv.h, i + 1)
                for k in range(8):
                    t_idx = 8 * i + k
                    if t_idx >= 2:
                        tensor.wait_ge(smcpy.h, t_idx - 1)
                    tensor.transpose([p_tr, p_tr2][k % 2][:], s_xn[i % 2][:, 128 * k:128 * k + 128],
                                     s_id[:]).then_inc(smtp.inc(), 1)
                if i == 0:
                    tensor.wait_ge(smw2.h, 256)
                tensor.wait_ge(smcpy.h, 8 * (i + 1))
                if i >= 1:
                    tensor.wait_ge(smu2.h, i)
                pu = [(p_a, p_b), (p_c, p_d)][i % 2]
                col = 128 * (i % 8)
                gb = (i // 8) % 2
                for k in range(8):
                    lhs = s_xg[gb][k][:, col:col + 128]
                    tensor.matmul(pu[0][:], lhs, w1t[k][:, 0:512], start=(k == 0), stop=(k == 7))
                    tensor.matmul(pu[1][:], lhs, w1t[k][:, 512:1024], start=(k == 0), stop=(k == 7))
                    mm = tensor.matmul(p_sel[:], lhs, selT[:, 16 * k:16 * k + 16],
                                       start=(k == 0), stop=(k == 7))
                mm.then_inc(smu1.inc(), 1)
            for ch in range(32):
                tensor.wait_ge(smtk.h, ch + 1)
                if ch >= 1:
                    tensor.wait_ge(smge.h, ch)
                tensor.transpose(p_tr3[0:16, :], s_gbf[:, 16 * ch:16 * ch + 16], s_id[:]).then_inc(smgt.inc(), 1)

        @blk.scalar
        def _(scalar):
            for i in range(32):
                scalar.wait_ge(smln.h, i + 1)
                scalar.activation(s_st[:, 5:6], s_st[:, 4:5], AF.Ln)
                scalar.drain()
                scalar.activation(s_rs[:, i:i + 1], s_st[:, 5:6], AF.Exp, scale=-0.5).then_inc(smrs.inc(), 1)
                g = i // 8
                if i % 8 == 0 and g >= 2:
                    scalar.wait_ge(smxst.h, 128 * (g - 1))
                for k in range(8):
                    scalar.wait_ge(smtp.h, 8 * i + k + 1)
                    scalar.activation(s_xg[g % 2][k][:, 128 * (i % 8):128 * (i % 8) + 128],
                                      [p_tr, p_tr2][k % 2][:], AF.Identity).then_inc(smcpy.inc(), 1)
                if i >= 1:
                    scalar.wait_ge(smlg.h, i)
                    scalar.activation(s_lpr[:], s_lgo[:], AF.Sigmoid).then_inc(smsg.inc(), 1)
            scalar.wait_ge(smlg.h, 32)
            scalar.activation(s_lpr[:], s_lgo[:], AF.Sigmoid).then_inc(smsg.inc(), 1)

        @blk.vector
        def _(vector):
            vector.wait_ge(smc.h, smc.v)

            def evac_topk(j):
                vector.wait_ge(smu1.h, j + 1)
                if j >= 2:
                    vector.wait_ge(smu3[j % 2].h, 16 * (j // 2))
                pu = [(p_a, p_b), (p_c, p_d)][j % 2]
                vector.tensor_add(s_ust[j % 2][:, 0:512], pu[0][:], s_inb[:, 0:512])
                vector.tensor_add(s_ust[j % 2][:, 512:1024], pu[1][:], s_inb[:, 512:1024])
                vector.tensor_copy(s_lgo[:], p_sel[:])
                vector.drain().then_inc(smu2.inc(), 1)
                vector.tensor_copy(s_lgw[:], s_lgo[:])
                vector.memset(s_lmk[:], 0.0)
                vector.drain().then_inc(smlg.inc(), 1)
                for r in range(4):
                    vector.reduce_max(s_lmx[:], s_lgw[:], axis=AX)
                    vector.drain()
                    vector.tensor_scalar(s_leq[:], s_lgw[:], s_lmx[:, 0:1], None, ALU.is_ge)
                    vector.drain()
                    vector.tensor_add(s_lmk[:], s_lmk[:], s_leq[:])
                    vector.scalar_tensor_tensor(s_lgw[:], s_leq[:], -1e30, s_lgw[:], ALU.mult, ALU.add)
                    vector.drain()
                vector.wait_ge(smsg.h, j + 1)
                vector.tensor_mul(s_lpr[:], s_lpr[:], s_lmk[:])
                vector.drain()
                vector.reduce_sum(s_lsum[:], s_lpr[:], axis=AX)
                vector.drain()
                vector.tensor_scalar(s_lsum[:], s_lsum[:], 1e-9, None, ALU.max)
                vector.drain()
                vector.reciprocal(s_lsum[:], s_lsum[:])
                vector.drain()
                vector.tensor_mul(s_lsum[:], s_lsum[:], s_rsc[:])
                vector.drain()
                vector.tensor_scalar(s_g[:, 16 * j:16 * j + 16], s_lpr[:], s_lsum[:, 0:1], None, ALU.mult)
                vector.drain()
                vector.tensor_copy(s_gbf[:, 16 * j:16 * j + 16], s_g[:, 16 * j:16 * j + 16])
                vector.drain().then_inc(smtk.inc(), 1)

            for i in range(33):
                if i < 32:
                    vector.wait_ge(smx[i % 2].h, 16 * (i // 2 + 1))
                    if i >= 2:
                        vector.wait_ge(smtp.h, 8 * i - 8)
                    xb = s_x[i % 2]
                    vector.tensor_mul(s_xsq[:], xb[:], xb[:])
                    vector.reduce_sum(s_st[:, 0:1], xb[:], axis=AX)
                    vector.drain()
                    vector.reduce_sum(s_st[:, 1:2], s_xsq[:], axis=AX)
                    vector.drain()
                    vector.tensor_scalar(s_mu[:, i:i + 1], s_st[:, 0:1], 1.0 / D, None, ALU.mult)
                    vector.tensor_scalar(s_st[:, 2:3], s_st[:, 1:2], 1.0 / D, None, ALU.mult)
                    vector.drain()
                    vector.tensor_mul(s_st[:, 3:4], s_mu[:, i:i + 1], s_mu[:, i:i + 1])
                    vector.drain()
                    vector.tensor_sub(s_st[:, 4:5], s_st[:, 2:3], s_st[:, 3:4])
                    vector.drain()
                    vector.tensor_scalar(s_st[:, 4:5], s_st[:, 4:5], 1e-5, None, ALU.add)
                    vector.drain().then_inc(smln.inc(), 1)
                    vector.wait_ge(smrs.h, i + 1)
                    vector.tensor_scalar(s_xn[i % 2][:], xb[:], s_mu[:, i:i + 1], s_rs[:, i:i + 1],
                                         ALU.subtract, ALU.mult)
                    vector.drain().then_inc(smlv.inc(), 1)
                if i >= 1:
                    evac_topk(i - 1)
            for ch in range(32):
                vector.wait_ge(smgt.h, ch + 1)
                vector.tensor_copy(s_gT[:, 128 * ch:128 * ch + 128], p_tr3[0:16, :])
                vector.drain().then_inc(smge.inc(), 1)


        gB = ExitStack()
        s_ba = sbuf("s_ba", [128, 4096], F32, gB)
        s_bb = sbuf("s_bb", [128, 4096], F32, gB)
        s_bc = sbuf("s_bc", [128, 4096], F32, gB)
        s_bd = sbuf("s_bd", [128, 4096], F32, gB)
        s_trK = sbuf("s_trK", [128, 128], F32, gB)
        s_ksb = sbuf("s_ksb", [128, 128], F32, gB)
        s_kbf = sbuf("s_kbf", [128, 128], BF16, gB)
        s_ktr = sbuf("s_ktr", [128, 128], BF16, gB)
        s_zro = sbuf("s_zro", [128, 128], BF16, gB)
        s_trQ0 = sbuf("s_trQ0", [128, 128], F32, gB)
        s_d2s = sbuf("s_d2s", [64, 4096], BF16, gB)
        s_ppt = sbuf("s_ppt", [128, 4096], BF16, gB)

        @blk.sync
        def _(sync):
            sync.wait_ge(sm0[3].h, 1)
            sync.wait_ge(sm0[4].h, sm0[4].v)
            sync.dma_start(out=s_trK[:], in_=trK_t.ap()).then_inc(sm0[9].inc(16), 16)
            sync.dma_start(out=s_ba[:], in_=AP(prow, 0, [[0, 128], [1, 4096]])).then_inc(sm0[9].inc(16), 16)
            sync.dma_start(out=s_bb[:], in_=AP(prow, 4096, [[0, 128], [1, 4096]])).then_inc(sm0[9].inc(16), 16)
            sync.wait_ge(sm0[12].h, 1)
            sync.dma_start(out=AP(pp_d, 0, [[16384, 128], [1, 4096]]),
                           in_=s_ppt[:]).then_inc(sm0[19].inc(16), 16)
            sync.wait_ge(sm0[12].h, 2)
            sync.dma_start(out=AP(pp_d, 4096, [[16384, 128], [1, 4096]]),
                           in_=s_ppt[:]).then_inc(sm0[19].inc(16), 16)

        @blk.vector
        def _(vector):
            vector.wait_ge(smc.h, smc.v)
            vector.wait_ge(sm0[9].h, 48)
            vector.tensor_scalar(s_ba[:], s_ba[:], s_trP[:, 0:1], None, ALU.mult)
            vector.tensor_scalar(s_bb[:], s_bb[:], s_trP[:, 0:1], None, ALU.mult)
            vector.drain()
            wrap_sin_cos(vector, s_bb[:], s_bc[:], s_bd[:])
            vector.tensor_copy(s_t2[:, 0:1], s_t1[:, 0:1])
            vector.drain().then_inc(sm0[10].inc(), 1)
            vector.wait_ge(sm0[11].h, 1)
            vector.tensor_mul(s_ppt[:], s_ba[:], s_bd[:])
            vector.drain()
            vector.tensor_copy(s_t2[:, 1:2], s_t1[:, 0:1])
            vector.drain().then_inc(sm0[12].inc(), 1)
            vector.wait_ge(sm0[19].h, 16)
            vector.tensor_mul(s_ppt[:], s_ba[:], s_bc[:])
            vector.drain()
            vector.tensor_copy(s_t2[:, 2:3], s_t1[:, 0:1])
            vector.drain().then_inc(sm0[12].inc(), 1)

        @blk.scalar
        def _(scalar):
            scalar.wait_ge(sm0[10].h, 1)
            scalar.activation(s_ba[:], s_ba[:], AF.Exp)
            act_sin_cos(scalar, s_bc[:], s_bd[:], 128).then_inc(sm0[11].inc(), 1)


        @blk.vector
        def _(vector):
            vector.wait_ge(sm0[12].h, 2)
            a_dr = s_dtar[:].unsqueeze(2).broadcast_to([128, 32, 128])
            a_di = s_dtai[:].unsqueeze(2).broadcast_to([128, 32, 128])
            a_tk = s_trK[:].unsqueeze(1).broadcast_to([128, 32, 128])
            vector.tensor_tensor(s_ba[:].rearrange("p (n m) -> p n m", m=128), a_dr, a_tk, ALU.mult)
            vector.tensor_tensor(s_bb[:].rearrange("p (n m) -> p n m", m=128), a_di, a_tk, ALU.mult)
            vector.drain()
            wrap_sin_cos(vector, s_bb[:], s_bc[:], s_bd[:])
            vector.tensor_copy(s_t2[:], s_t1[:])
            vector.drain().then_inc(sm0[13].inc(), 1)

        @blk.scalar
        def _(scalar):
            scalar.wait_ge(sm0[13].h, 1)
            scalar.activation(s_ba[:], s_ba[:], AF.Exp)
            act_sin_cos(scalar, s_bc[:], s_bd[:], 128).then_inc(sm0[14].inc(), 1)

        @blk.vector
        def _(vector):
            vector.wait_ge(sm0[14].h, 1)
            vector.tensor_mul(s_bd[:], s_ba[:], s_bd[:])
            vector.tensor_mul(s_bc[:], s_ba[:], s_bc[:])
            vector.drain()
            e_r = s_er[:].unsqueeze(2).broadcast_to([128, 32, 128])
            e_i = s_ei[:].unsqueeze(2).broadcast_to([128, 32, 128])
            v_ba = s_ba[:].rearrange("p (n m) -> p n m", m=128)
            v_bb = s_bb[:].rearrange("p (n m) -> p n m", m=128)
            v_bc = s_bc[:].rearrange("p (n m) -> p n m", m=128)
            v_bd = s_bd[:].rearrange("p (n m) -> p n m", m=128)
            vector.tensor_tensor(v_ba, e_r, v_bd, ALU.mult)
            vector.tensor_tensor(v_bb, e_i, v_bc, ALU.mult)
            vector.drain()
            vector.tensor_sub(s_ba[:], s_ba[:], s_bb[:])
            vector.drain()
            vector.reduce_sum(s_ksb[:].unsqueeze(2),
                              AP(s_ba[:].tensor, 0, [[4096, 128], [1, 128], [128, 32]]),
                              axis=AX)
            vector.drain()
            vector.tensor_scalar(s_ksb[:], s_ksb[:], 2.0, None, ALU.mult)
            vector.drain()
            vector.tensor_add(s_ksb[:, 127:128], s_ksb[:, 127:128], s_dskc[:])
            vector.drain()
            vector.tensor_copy(s_kbf[:], s_ksb[:])
            vector.drain()
            vector.memset(s_zro[:], 0.0)
            vector.drain().then_inc(sm0[15].inc(), 1)

        @blk.tensor
        def _(tensor):
            tensor.wait_ge(sm0[15].h, 1)
            tensor.wait_ge(smc.h, smc.v)
            tensor.transpose(p_tr[:], s_kbf[:], s_id[:]).then_inc(sm0[16].inc(), 1)

        @blk.vector
        def _(vector):
            vector.wait_ge(sm0[16].h, 1)
            vector.tensor_copy(s_ktr[:], p_tr[:])
            vector.drain().then_inc(sm0[17].inc(), 1)

        @blk.sync
        def _(sync):
            sync.wait_ge(sm0[17].h, 1)
            sync.dma_start(out=AP(k2r, 129 * 128, [[128, 127], [1, 128]]),
                           in_=s_zro[0:127, :]).then_inc(sm0[18].inc(16), 16)
            sync.dma_start(out=AP(k2r, 0, [[128, 1], [1, 128]]),
                           in_=s_zro[0:1, :]).then_inc(sm0[18].inc(16), 16)
            sync.dma_start(out=AP(k2r, 128, [[128, 128], [1, 128]]),
                           in_=s_ktr[:]).then_inc(sm0[18].inc(16), 16)
            sync.dma_start(out=s_trQ0[:], in_=trQ_t.ap()).then_inc(sm0[18].inc(16), 16)

        smdb = sem("smdb"); smdbs = sem("smdbs"); smdbv = sem("smdbv"); smdbo = sem("smdbo")
        s_P2 = sbuf("s_P2", [64, 128], F32, gB)
        s_Q2 = sbuf("s_Q2", [64, 128], F32, gB)

        @blk.vector
        def _(vector):
            vector.wait_ge(sm0[15].h, 1)
            vector.wait_ge(sm0[18].h, sm0[18].v)
            vector.tensor_copy(s_P2[0:32, :], s_q2er[0:32, :])
            vector.tensor_scalar(s_P2[32:64, :], s_q2ei[32:64, :], -1.0, None, ALU.mult)
            vector.tensor_scalar(s_Q2[0:32, :], s_q2ei[0:32, :], -1.0, None, ALU.mult)
            vector.tensor_scalar(s_Q2[32:64, :], s_q2er[32:64, :], -1.0, None, ALU.mult)
            vector.drain()
            for slab in range(4):
                if slab >= 1:
                    vector.wait_ge(smdbo.h, 16 * slab)
                dr_b = s_q2dr[:, 32 * slab:32 * slab + 32].unsqueeze(2).broadcast_to([64, 32, 128])
                di_b = s_q2di[:, 32 * slab:32 * slab + 32].unsqueeze(2).broadcast_to([64, 32, 128])
                tq_b = s_trQ0[0:64, :].unsqueeze(1).broadcast_to([64, 32, 128])
                vector.tensor_tensor(s_ba[0:64, :].rearrange("p (c t) -> p c t", t=128), dr_b, tq_b, ALU.mult)
                vector.tensor_tensor(s_bb[0:64, :].rearrange("p (c t) -> p c t", t=128), di_b, tq_b, ALU.mult)
                vector.drain()
                wrap_sin_cos(vector, s_bb[0:64, :], s_bc[0:64, :], s_bd[0:64, :])
                vector.tensor_copy(s_t2[:, 3:4], s_t1[:, 0:1])
                vector.drain().then_inc(smdb.inc(), 1)
                vector.wait_ge(smdbs.h, slab + 1)
                vector.tensor_mul(s_bd[0:64, :], s_ba[0:64, :], s_bd[0:64, :])
                vector.tensor_mul(s_bc[0:64, :], s_ba[0:64, :], s_bc[0:64, :])
                vector.drain()
                p_b_ = s_P2[:, 32 * slab:32 * slab + 32].unsqueeze(2).broadcast_to([64, 32, 128])
                q_b_ = s_Q2[:, 32 * slab:32 * slab + 32].unsqueeze(2).broadcast_to([64, 32, 128])
                v_ba = s_ba[0:64, :].rearrange("p (c t) -> p c t", t=128)
                v_bb = s_bb[0:64, :].rearrange("p (c t) -> p c t", t=128)
                v_bc = s_bc[0:64, :].rearrange("p (c t) -> p c t", t=128)
                v_bd = s_bd[0:64, :].rearrange("p (c t) -> p c t", t=128)
                vector.tensor_tensor(v_ba, p_b_, v_bd, ALU.mult)
                vector.tensor_tensor(v_bb, q_b_, v_bc, ALU.mult)
                vector.drain()
                vector.tensor_add(s_ba[0:64, :], s_ba[0:64, :], s_bb[0:64, :])
                vector.drain()
                vector.tensor_scalar(s_d2s[:], s_ba[0:64, :], 2.0, None, ALU.mult)
                vector.drain().then_inc(smdbv.inc(), 1)

        @blk.scalar
        def _(scalar):
            for slab in range(4):
                scalar.wait_ge(smdb.h, slab + 1)
                scalar.activation(s_ba[0:64, :], s_ba[0:64, :], AF.Exp)
                act_sin_cos(scalar, s_bc[0:64, :], s_bd[0:64, :], 64).then_inc(smdbs.inc(), 1)

        @blk.sync
        def _(sync):
            for slab in range(4):
                sync.wait_ge(smdbv.h, slab + 1)
                sync.dma_start(out=AP(d2_d, slab * 4096, [[4 * 4096, 64], [1, 4096]]),
                               in_=s_d2s[:]).then_inc(smdbo.inc(16), 16)

        gB.close()
        g2.close()
        g0.close()
        g3 = ExitStack()
        s_ucs = sbuf("s_ucs", [128, 4 * 64 * 128], BF16, g3)
        s_a2 = sbuf("s_a2", [64, 128 * 4 * 64], BF16, g3)
        g3mA = ExitStack()
        s_pp2 = sbuf("s_pp2", [128, 8192], BF16, g3mA)
        g3m = ExitStack()
        s_aswp = sbuf("s_aswp", [64, 16384], BF16, g3m)
        s_sN = sbuf("s_sN", [64, 512], F32, g3m)
        s_sS = sbuf("s_sS", [64, 512], F32, g3m)
        s_rx = sbuf("s_rx", [64, 1024], F32, g3m)

        @blk.gpsimd
        def _(gp):
            gp.wait_ge(smu3[0].h, smu3[0].v)
            gp.wait_ge(smu3[1].h, smu3[1].v)
            gp.collective_compute(
                "AllGather", ALU.bypass,
                replica_groups=[list(range(NCORES))],
                ins=[u_loc.ap().opt()],
                outs=[u_gth.ap().opt()],
            ).then_inc(smcc.inc(), 1)

        @blk.sync
        def _(sync):
            sync.wait_ge(smcc.h, 1)
            sync.wait_ge(smc.h, smc.v)
            sync.wait_ge(smgo.h, 16)
            sync.wait_ge(smdbo.h, 64)
            sync.wait_ge(sm0[18].h, sm0[18].v)
            sync.wait_ge(sm0[19].h, 32)
            uoff = sync.value_load(s_uoff[0:1, 0:1])
            sync.dma_start(out=s_pp2[:], in_=AP(pp_d, 0, [[16384, 128], [1, 8192]])).then_inc(smucs.inc(16), 16)
            for r in range(NCORES):
                b, h = r // 2, r % 2
                sync.dma_start(
                    out=AP(s_ucs[:].tensor, b * 8192 + h * 4096,
                           [[32768, 128], [128, 32], [1, 128]]),
                    in_=AP(u_gth, uoff + r * T * D, [[D, 128], [128 * D, 32], [1, 128]]),
                ).then_inc(smucs.inc(16), 16)

        @blk.tensor
        def _(tensor):
            tensor.wait_ge(smucs.h, smucs.v)
            for pair in range(64):
                pe_bank = [p_a, p_b][pair % 2]
                if pair >= 2:
                    tensor.wait_ge(sme2.h, pair - 1)
                for two in range(2):
                    ch = 2 * pair + two
                    for ri in range(2):
                        mm = tensor.matmul(
                            pe_bank[32 * ri:32 * ri + 32, 256 * two:256 * two + 256],
                            s_pp2[:, ri * 4096 + ch * 32: ri * 4096 + ch * 32 + 32],
                            AP(s_ucs[:].tensor, ch, [[32768, 128], [128, 64], [8192, 4]]),
                            start=True, stop=True, tile_position=(0, 32 * ri))
                mm.then_inc(sme1.inc(), 1)

        @blk.scalar
        def _(scalar):
            for pair in range(64):
                pe_bank = [p_a, p_b][pair % 2]
                scalar.wait_ge(sme1.h, pair + 1)
                for two in range(2):
                    ch = 2 * pair + two
                    hp2, c2 = ch // 64, ch % 64
                    scalar.activation(
                        AP(s_a2[:].tensor, 16384 * hp2 + 4 * c2, [[32768, 64], [256, 64], [1, 4]]),
                        AP(pe_bank[:].tensor, 256 * two, [[512, 64], [1, 256]]),
                        AF.Identity)
                scalar.drain().then_inc(sme2.inc(), 1)

        smsw = sem("smsw")

        @blk.sync
        def _(sync):
            for hp in range(2):
                if hp == 0:
                    sync.wait_ge(sme2.h, sme2.v)
                else:
                    sync.wait_ge(smra.h, 1)
                sync.dma_start(out=s_aswp[0:32, :],
                               in_=s_a2[32:64, 16384 * hp:16384 * hp + 16384]).then_inc(smsw.inc(16), 16)
                sync.dma_start(out=s_aswp[32:64, :],
                               in_=s_a2[0:32, 16384 * hp:16384 * hp + 16384]).then_inc(smsw.inc(16), 16)

        @blk.vector
        def _(vector):
            vector.wait_ge(sm0[8].h, 1)
            for hp in range(2):
                vector.wait_ge(smsw.h, 32 * (hp + 1))
                vector.memset(s_sN[:], 0.0)
                vector.memset(s_sS[:], 0.0)
                vector.drain()
                w1b = s_W1[:, 64 * hp:64 * hp + 64].unsqueeze(2).broadcast_to([64, 64, 4])
                w2b = s_W2[:, 64 * hp:64 * hp + 64].unsqueeze(2).broadcast_to([64, 64, 4])
                w3b = s_W3[:, 64 * hp:64 * hp + 64].unsqueeze(2).broadcast_to([64, 64, 4])
                t1 = s_rx[:, 0:256].rearrange("p (c b) -> p c b", b=4)
                t2 = s_rx[:, 256:512].rearrange("p (c b) -> p c b", b=4)
                t3 = s_rx[:, 512:768].rearrange("p (c b) -> p c b", b=4)
                t4 = s_rx[:, 768:1024].rearrange("p (c b) -> p c b", b=4)
                for J in range(64):
                    pv, cu = (J % 2), ((J + 1) % 2)
                    nP = s_sN[:, 256 * pv:256 * pv + 256].rearrange("p (c b) -> p c b", b=4)
                    nC = s_sN[:, 256 * cu:256 * cu + 256]
                    sP = s_sS[:, 256 * pv:256 * pv + 256].rearrange("p (c b) -> p c b", b=4)
                    sC = s_sS[:, 256 * cu:256 * cu + 256]
                    aj0 = 16384 * hp + 256 * J
                    vector.tensor_tensor(t1, nP, w1b, ALU.mult)
                    vector.tensor_tensor(t2, sP, w2b, ALU.mult)
                    vector.tensor_tensor(t3, nP, w3b, ALU.mult)
                    vector.tensor_tensor(t4, sP, w1b, ALU.mult)
                    if J >= 1:
                        vector.tensor_copy(s_a2[:, aj0 - 256:aj0],
                                           s_sN[:, 256 * pv:256 * pv + 256])
                    dr1 = vector.drain()
                    if hp == 1 and J == 32:
                        dr1.then_inc(smr1.inc(), 1)
                    vector.tensor_add(s_rx[:, 0:256], s_rx[:, 0:256], s_rx[:, 256:512])
                    vector.tensor_add(s_rx[:, 768:1024], s_rx[:, 768:1024], s_rx[:, 512:768])
                    vector.drain()
                    vector.tensor_tensor(nC, s_rx[:, 0:256], s_a2[:, aj0:aj0 + 256], ALU.add)
                    vector.tensor_tensor(sC, s_rx[:, 768:1024], s_aswp[:, 256 * J:256 * J + 256], ALU.add)
                    dr3 = vector.drain()
                    if J == 63:
                        dr3.then_inc([smra, smr1][hp].inc(), 1)

        g3m.close()
        g3mA.close()

        g3b = ExitStack()
        s_tp = [sbuf("s_tp0", [128, 32 * 128], BF16, g3b)]
        s_d2w = [sbuf("s_d2w0", [64, 4096], BF16, g3b)]
        s_yst = sbuf("s_yst", [128, 4 * 32 * 128], BF16, g3b)

        smtl = sem("smtl")
        smd1 = sem("smd1")
        smd2 = sem("smd2")
        smyo2 = sem("smyo2")

        @blk.sync
        def _(sync):
            sync.wait_ge(sm0[18].h, sm0[18].v)
            sync.wait_ge(smdbo.h, smdbo.v)
            for half in range(2):
                for slab in range(4):
                    idx = half * 4 + slab
                    pp = 0
                    if idx >= 1:
                        sync.wait_ge(smd1.h, 8 * idx)
                    sync.dma_start(out=s_tp[pp][:].rearrange("p (t c) -> p t c", c=32),
                                   in_=AP(k2r, 128 * 128 + 32 * slab, [[128, 128], [-128, 128], [1, 32]]),
                                   allow_slow_non_contiguous=True).then_inc(smtl.inc(16), 16)
                    sync.dma_start(out=s_d2w[pp][:],
                                   in_=AP(d2_d, slab * 4096, [[4 * 4096, 64], [1, 4096]])).then_inc(smtl.inc(16), 16)
                for b4 in range(4):
                    sync.wait_ge(smytr.h, half * 64 + 16 * (b4 + 1))
                    sync.dma_start(
                        out=AP(y_loc, b4 * 8192 + half * 4096, [[4 * 64 * 128, 128], [1, 4096]]),
                        in_=s_yst[:, b4 * 4096:b4 * 4096 + 4096],
                    ).then_inc(smyo2.inc(16), 16)

        @blk.tensor
        def _(tensor):
            for half in range(2):
                tensor.wait_ge(smr1.h, half + 1)
                for slab in range(4):
                    idx = half * 4 + slab
                    pp = 0
                    tensor.wait_ge(smtl.h, 32 * (idx + 1))
                    nJ = 31 if half == 0 else 32
                    oJ = 1 if half == 0 else 0
                    sJ0 = 0 if half == 0 else 31
                    for grp in range(8):
                        pe_bank = [p_c, p_d][grp % 2]
                        gi = idx * 8 + grp
                        if gi >= 2:
                            tensor.wait_ge(smd2.h, gi - 1)
                        for cp in range(4):
                            cl = grp * 4 + cp
                            ch = 32 * slab + cl
                            tensor.matmul(
                                pe_bank[:, 128 * cp:128 * cp + 128].rearrange("p (j b) -> p j b", b=4),
                                AP(s_tp[pp][:].tensor, cl, [[4096, 128], [32, 128]]),
                                AP(s_ucs[:].tensor, ch + half * 32 * 128,
                                   [[32768, 128], [128, 32], [8192, 4]]),
                                start=True, stop=False)
                            hp2, c2 = ch // 64, ch % 64
                            mm = tensor.matmul(
                                AP(pe_bank[:].tensor, 128 * cp + 4 * oJ, [[512, 128], [1, 4 * nJ]]),
                                s_d2w[pp][:, 128 * cl:128 * cl + 128],
                                AP(s_a2[:].tensor, 16384 * hp2 + 4 * c2 + 256 * sJ0,
                                   [[32768, 64], [256, nJ], [1, 4]]),
                                start=False, stop=True)
                        mm.then_inc(smd1.inc(), 1)
                tensor.wait_ge(smd2.h, 32 * (half + 1))
                for tt in range(128):
                    gi2 = half * 128 + tt
                    b4, Jl = tt // 32, tt % 32
                    if gi2 >= 2:
                        tensor.wait_ge(smytr.h, gi2 // 2)
                    tensor.transpose([p_tr, p_tr2][tt % 2][:],
                                     s_yst[:, b4 * 4096 + Jl * 128:b4 * 4096 + Jl * 128 + 128],
                                     s_id[:]).then_inc(smyt1.inc(), 1)

        @blk.scalar
        def _(scalar):
            for idx in range(8):
                half, slab = idx // 4, idx % 4
                for grp in range(8):
                    pe_bank = [p_c, p_d][grp % 2]
                    gi = idx * 8 + grp
                    if gi == 32:
                        scalar.wait_ge(smyo2.h, 64)
                    scalar.wait_ge(smd1.h, gi + 1)
                    scalar.activation(
                        AP(s_yst[:].tensor, 32 * slab + 4 * grp,
                           [[16384, 128], [1, 4], [128, 32], [4096, 4]]),
                        AP(pe_bank[:].tensor, 0, [[512, 128], [128, 4], [4, 32], [1, 4]]),
                        AF.Gelu).then_inc(smd2.inc(), 1)

        @blk.vector
        def _(vector):
            for half in range(2):
                for tt in range(0, 128, 2):
                    gi2 = half * 128 + tt
                    vector.wait_ge(smyt1.h, gi2 + 2)
                    for u2 in range(2):
                        b4, Jl = (tt + u2) // 32, (tt + u2) % 32
                        vector.tensor_copy(
                            s_yst[:, b4 * 4096 + Jl * 128:b4 * 4096 + Jl * 128 + 128],
                            [p_tr, p_tr2][(tt + u2) % 2][:])
                    vector.drain().then_inc(smytr.inc(), 1)

        @blk.gpsimd
        def _(gp):
            gp.wait_ge(smyo2.h, smyo2.v)
            gp.wait_ge(smkv[0].h, 256)
            gp.collective_compute(
                "AllGather", ALU.bypass,
                replica_groups=[list(range(NCORES))],
                ins=[y_loc.ap().opt()],
                outs=[y_gth.ap().opt()],
            ).then_inc(smcc.inc(), 1)

        g3b.close()
        g3.close()

        g45 = ExitStack()
        gluT = [sbuf(f"gluT{k}", [128, 2 * D], BF16, g45) for k in range(8)]
        outT = [sbuf(f"outT{k}", [128, D], BF16, g45) for k in range(8)]
        g4 = ExitStack()
        s_key = [sbuf(f"s_key{e}", [128, 8 * 256], BF16, g4) for e in range(8)]
        s_val = [sbuf(f"s_val{e}", [128, 2 * D], BF16, g4) for e in range(8)]
        s_sal = sbuf("s_sal", [128, 16 * 512], BF16, g4)
        s_grep = sbuf("s_grep", [128, 8 * 512], BF16, g4)
        s_xw2 = [[sbuf(f"s_xw2{pp}_{k}", [128, 512], BF16, g4) for k in range(8)] for pp in range(2)]
        s_fst = [sbuf(f"s_fst{pp}", [128, D], F32, g4) for pp in range(2)]

        @blk.sync
        def _(sync):
            sync.wait_ge(smcastC.h, 32)
            sync.wait_ge(smcastB.h, 32)
            sync.wait_ge(smgo.h, 16)
            sync.wait_ge(smxst.h, 512)
            sync.wait_ge(smyo2.h, 128)
            def kv_loads(sync, half):
                for e in range(8):
                    eg = 8 * half + e
                    sync.dma_start(out=s_key[e][:].rearrange("p (k h2) -> p k h2", h2=256),
                                   in_=AP(keysbf, eg * D * 256, [[256, 128], [128 * 256, 8], [1, 256]])).then_inc(smkv[half].inc(16), 16)
                    sync.dma_start(out=s_val[e][:].rearrange("p (k d) -> p k d", d=D),
                                   in_=AP(valsbf, eg * 256 * D, [[D, 128], [128 * D, 2], [1, D]])).then_inc(smkv[half].inc(16), 16)
            kv_loads(sync, 0)
            for k in range(8):
                sync.dma_start(out=gluT[k][:],
                               in_=AP(glubf, 128 * k * 2 * D, [[2 * D, 128], [1, 2 * D]])).then_inc(smwt3.inc(16), 16)
                sync.dma_start(out=outT[k][:],
                               in_=AP(outbf, 128 * k * D, [[D, 128], [1, D]])).then_inc(smwt3.inc(16), 16)
            def store_slab(sync, slab):
                for cc in range(4):
                    chunk = slab * 4 + cc
                    sync.wait_ge(smv2.h, chunk + 1)
                    dst = [ffnP, ffnQ][(chunk // 32)]
                    sync.dma_start(out=AP(dst, (chunk % 32) * 128 * D, [[D, 128], [1, D]]),
                                   in_=s_fst[chunk % 2][:]).then_inc(smv3[chunk % 2].inc(16), 16)

            for half in range(2):
                if half == 1:
                    store_slab(sync, 7)
                    sync.wait_ge(smv1.h, 32)
                    kv_loads(sync, 1)
                for s in range(8):
                    slab = half * 8 + s
                    pp = s % 2
                    if s >= 1:
                        store_slab(sync, slab - 1)
                    for k in range(8):
                        sync.dma_start(
                            out=s_xw2[pp][k][:],
                            in_=AP(xnT_d, (128 * k) * T + 512 * s, [[T, 128], [1, 512]])).then_inc(smxw2[pp].inc(16), 16)
                    for e in range(8):
                        eg = 8 * half + e
                        sync.dma_start(out=s_grep[:, 512 * e:512 * e + 512],
                                       in_=AP(gT_d, eg * T + 512 * s, [[0, 128], [1, 512]])).then_inc(smgx.inc(16), 16)
            store_slab(sync, 15)

        @blk.tensor
        def _(tensor):
            for half in range(2):
                tensor.wait_ge(smkv[half].h, smkv[half].v)
                for s in range(8):
                    slab = half * 8 + s
                    pp = s % 2
                    tensor.wait_ge(smxw2[pp].h, 16 * 8 * (slab // 2 + 1))
                    for e in range(8):
                        if slab * 8 + e >= 1:
                            tensor.wait_ge(smk2.h, 2 * (slab * 8 + e))
                        for hh in range(2):
                            ps = [p_a, p_b][hh]
                            for k in range(8):
                                mm = tensor.matmul(ps[:], s_key[e][:, 256 * k + 128 * hh:256 * k + 128 * hh + 128],
                                                   s_xw2[pp][k][:], start=(k == 0), stop=(k == 7))
                            mm.then_inc(smk1.inc(), 1)
                    for cc in range(4):
                        chunk = slab * 4 + cc
                        if chunk >= 1:
                            tensor.wait_ge(smv2.h, chunk)
                        for dd in range(2):
                            po = [p_c, p_d][dd]
                            first = True
                            for e in range(8):
                                if dd == 0 and cc == 0:
                                    tensor.wait_ge(smk3.h, slab * 8 + e + 1)
                                for hh in range(2):
                                    mm = tensor.matmul(
                                        po[:],
                                        s_sal[:, (e * 2 + hh) * 512 + 128 * cc:(e * 2 + hh) * 512 + 128 * cc + 128],
                                        s_val[e][:, hh * D + 512 * dd:hh * D + 512 * dd + 512],
                                        start=first, stop=(e == 7 and hh == 1))
                                    first = False
                        mm.then_inc(smv1.inc(), 1)

        @blk.scalar
        def _(scalar):
            for half in range(2):
                for s in range(8):
                    slab = half * 8 + s
                    for e in range(8):
                        if slab >= 1:
                            scalar.wait_ge(smv1.h, 4 * slab)
                        for hh in range(2):
                            idx = (slab * 8 + e) * 2 + hh
                            scalar.wait_ge(smk1.h, idx + 1)
                            ps = [p_a, p_b][hh]
                            scalar.activation(s_sal[:, (e * 2 + hh) * 512:(e * 2 + hh) * 512 + 512],
                                              ps[:], AF.Relu).then_inc(smk2.inc(), 1)

        @blk.vector
        def _(vector):
            for half in range(2):
                for s in range(8):
                    slab = half * 8 + s
                    vector.wait_ge(smgx.h, 16 * 8 * (slab + 1))
                    for e in range(8):
                        vector.wait_ge(smk2.h, 2 * (slab * 8 + e + 1))
                        ge = s_grep[:, 512 * e:512 * e + 512]
                        vector.tensor_tensor(s_sal[:, (e * 2) * 512:(e * 2) * 512 + 512],
                                             s_sal[:, (e * 2) * 512:(e * 2) * 512 + 512], ge, ALU.mult)
                        vector.drain()
                        vector.tensor_tensor(s_sal[:, (e * 2 + 1) * 512:(e * 2 + 1) * 512 + 512],
                                             s_sal[:, (e * 2 + 1) * 512:(e * 2 + 1) * 512 + 512], ge, ALU.mult)
                        vector.drain().then_inc(smk3.inc(), 1)
                    for cc in range(4):
                        chunk = slab * 4 + cc
                        pp3 = chunk % 2
                        vector.wait_ge(smv1.h, chunk + 1)
                        if chunk >= 2:
                            vector.wait_ge(smv3[chunk % 2].h, 16 * (chunk // 2))
                        if half == 0:
                            vector.tensor_copy(s_fst[pp3][:, 0:512], p_c[:])
                            vector.tensor_copy(s_fst[pp3][:, 512:1024], p_d[:])
                        else:
                            vector.tensor_add(s_fst[pp3][:, 0:512], p_c[:], s_outb[:, 0:512])
                            vector.tensor_add(s_fst[pp3][:, 512:1024], p_d[:], s_outb[:, 512:1024])
                        vector.drain().then_inc(smv2.inc(), 1)

        g4.close()

        g5 = ExitStack()
        y_fm = [sbuf(f"y_fm{r}", [128, T], BF16, g5) for r in range(8)]
        s_ag = [sbuf(f"s_ag{o}", [128, 512], BF16, g5) for o in range(16)]
        s_y2 = [sbuf(f"s_y2{k}", [128, 512], BF16, g5) for k in range(8)]
        s_xr = [sbuf(f"s_xr{pp}", [128, D], F32, g5) for pp in range(2)]
        s_fp = [sbuf(f"s_fp{pp}", [128, D], F32, g5) for pp in range(2)]
        s_fq = [sbuf(f"s_fq{pp}", [128, D], F32, g5) for pp in range(2)]
        s_fin = [sbuf(f"s_fin{pp}", [128, D], F32, g5) for pp in range(2)]
        s_aff = sbuf("s_aff", [128, 4], F32, g5)

        @blk.sync
        def _(sync):
            sync.wait_ge(smcc.h, 2)
            sync.wait_ge(smcastB.h, 32)
            sync.wait_ge(smv3[0].h, smv3[0].v)
            sync.wait_ge(smv3[1].h, smv3[1].v)
            yoff = sync.value_load(s_yoff[0:1, 0:1])
            for r in range(8):
                sync.dma_start(
                    out=y_fm[r][:],
                    in_=AP(y_gth, yoff + r * 128 * 32768, [[32768, 128], [1, 4096]])).then_inc(smyf.inc(16), 16)
            for chunk in range(2):
                p9 = chunk % 2
                sync.dma_start(out=s_xr[p9][:],
                               in_=AP(x_in, chunk * 128 * D, [[D, 128], [1, D]])).then_inc(smf1[p9].inc(16), 16)
                sync.dma_start(out=s_fp[p9][:],
                               in_=AP(ffnP, chunk * 128 * D, [[D, 128], [1, D]])).then_inc(smf2[p9].inc(16), 16)
                sync.dma_start(out=s_fq[p9][:],
                               in_=AP(ffnQ, chunk * 128 * D, [[D, 128], [1, D]])).then_inc(smf3[p9].inc(16), 16)
            for chunk in range(32):
                p9 = chunk % 2
                sync.wait_ge(smo2.h, chunk + 1)
                sync.dma_start(out=AP(out_t, chunk * 128 * D, [[D, 128], [1, D]]),
                               in_=s_fin[p9][:]).then_inc(smout[p9].inc(16), 16)
                if chunk + 2 < 32:
                    sync.dma_start(out=s_xr[p9][:],
                                   in_=AP(x_in, (chunk + 2) * 128 * D, [[D, 128], [1, D]])).then_inc(smf1[p9].inc(16), 16)
                    sync.dma_start(out=s_fp[p9][:],
                                   in_=AP(ffnP, (chunk + 2) * 128 * D, [[D, 128], [1, D]])).then_inc(smf2[p9].inc(16), 16)
                    sync.dma_start(out=s_fq[p9][:],
                                   in_=AP(ffnQ, (chunk + 2) * 128 * D, [[D, 128], [1, D]])).then_inc(smf3[p9].inc(16), 16)
            sync.wait_ge(smout[0].h, smout[0].v)
            sync.wait_ge(smout[1].h, smout[1].v)

        @blk.tensor
        def _(tensor):
            tensor.wait_ge(smyf.h, smyf.v)
            tensor.wait_ge(smwt3.h, smwt3.v)
            for s in range(8):
                for o in range(16):
                    pg = [p_a, p_b][o % 2]
                    if s * 16 + o >= 2:
                        tensor.wait_ge(smg2.h, s * 16 + o - 1)
                    for r in range(8):
                        mm = tensor.matmul(pg[:], gluT[r][:, 128 * o:128 * o + 128],
                                           y_fm[r][:, 512 * s:512 * s + 512],
                                           start=(r == 0), stop=(r == 7))
                    mm.then_inc(smg1.inc(), 1)
                for cc in range(4):
                    chunk = s * 4 + cc
                    if chunk >= 1:
                        tensor.wait_ge(smo2.h, chunk)
                    for k in range(8):
                        tensor.wait_ge(smg3.h, s * 8 + k + 1)
                        lhs = s_y2[k][:, 128 * cc:128 * cc + 128]
                        tensor.matmul(p_c[:], lhs, outT[k][:, 0:512], start=(k == 0), stop=(k == 7))
                        mm = tensor.matmul(p_d[:], lhs, outT[k][:, 512:1024], start=(k == 0), stop=(k == 7))
                    mm.then_inc(smo1.inc(), 1)

        @blk.scalar
        def _(scalar):
            scalar.wait_ge(smglb.h, 16)
            for s in range(8):
                for o in range(16):
                    pg = [p_a, p_b][o % 2]
                    scalar.wait_ge(smg1.h, s * 16 + o + 1)
                    if s >= 1:
                        scalar.wait_ge(smg3.h, 8 * s)
                    if o < 8:
                        scalar.activation(s_ag[o][:], pg[:], AF.Identity,
                                          bias=s_glb[:, o:o + 1]).then_inc(smg2.inc(), 1)
                    else:
                        scalar.activation(s_ag[o][:], pg[:], AF.Sigmoid,
                                          bias=s_glb[:, o:o + 1]).then_inc(smg2.inc(), 1)

        @blk.vector
        def _(vector):
            for s in range(8):
                for k in range(8):
                    vector.wait_ge(smg2.h, s * 16 + 16)
                    if s >= 1:
                        vector.wait_ge(smo1.h, 4 * s)
                    vector.tensor_mul(s_y2[k][:], s_ag[k][:], s_ag[k + 8][:])
                    vector.drain().then_inc(smg3.inc(), 1)
                for cc in range(4):
                    chunk = s * 4 + cc
                    pp4 = chunk % 2
                    vector.wait_ge(smo1.h, chunk + 1)
                    vector.wait_ge(smf1[pp4].h, 16 * (chunk // 2 + 1))
                    vector.wait_ge(smf2[pp4].h, 16 * (chunk // 2 + 1))
                    vector.wait_ge(smf3[pp4].h, 16 * (chunk // 2 + 1))
                    if chunk >= 2:
                        vector.wait_ge(smout[pp4].h, 16 * (chunk // 2))
                    vector.tensor_scalar(s_aff[:, 0:1], s_rs[:, chunk:chunk + 1], 1.0, None, ALU.add)
                    vector.tensor_mul(s_aff[:, 1:2], s_mu[:, chunk:chunk + 1], s_rs[:, chunk:chunk + 1])
                    vector.drain()
                    vector.tensor_scalar(s_aff[:, 1:2], s_aff[:, 1:2], -1.0, None, ALU.mult)
                    vector.drain()
                    vector.tensor_scalar(s_fin[pp4][:], s_xr[pp4][:], s_aff[:, 0:1], s_aff[:, 1:2],
                                         ALU.mult, ALU.add)
                    vector.drain()
                    vector.tensor_add(s_fin[pp4][:, 0:512], s_fin[pp4][:, 0:512], p_c[:])
                    vector.tensor_add(s_fin[pp4][:, 512:1024], s_fin[pp4][:, 512:1024], p_d[:])
                    vector.drain()
                    vector.tensor_add(s_fin[pp4][:], s_fin[pp4][:], s_fp[pp4][:])
                    vector.drain()
                    vector.tensor_add(s_fin[pp4][:], s_fin[pp4][:], s_fq[pp4][:])
                    vector.drain().then_inc(smo2.inc(), 1)

        g5.close()
        g45.close()

    es.close()
    return nc


_NC_CACHE = None


def kernel(L0_attn_x, L0_attn_in_W, L0_attn_in_b, L0_attn_log_dt, L0_attn_C,
           L0_attn_log_A_real, L0_attn_A_imag, L0_attn_D_skip, L0_attn_glu_W,
           L0_attn_glu_b, L0_attn_out_W, L0_attn_out_b, L0_ffn_expert_sel,
           L0_ffn_keys, L0_ffn_values, L0_ffn_route_scale):
    global _NC_CACHE
    if _NC_CACHE is None:
        _NC_CACHE = build_nc()
    nc = _NC_CACHE

    x = np.ascontiguousarray(np.asarray(L0_attn_x, dtype=np.float32))
    in_WT = np.ascontiguousarray(np.asarray(L0_attn_in_W, np.float32).T)
    eselT_h = np.ascontiguousarray(np.asarray(L0_ffn_expert_sel, np.float32).T)
    glu_WT = np.ascontiguousarray(np.asarray(L0_attn_glu_W, np.float32).T)
    out_WT = np.ascontiguousarray(np.asarray(L0_attn_out_W, np.float32).T)
    ident = np.eye(128, dtype=ml_dtypes.bfloat16)
    t_idx = np.arange(128, dtype=np.float32)
    trQ = np.tile((t_idx + 1.0)[None, :], (128, 1))
    trK = np.tile((127.0 - t_idx)[None, :], (128, 1))
    trP = (127.0 - t_idx)[:, None].copy()
    trP2 = np.stack([128.0 - t_idx, 126.0 - t_idx], axis=1).astype(np.float32)

    in_maps = []
    for c in range(NCORES):
        b, h = c // 2, c % 2
        cs = slice(128 * c, 128 * (c + 1))
        in_maps.append(dict(
            x_in=np.ascontiguousarray(x[b, h * 4096:(h + 1) * 4096, :]),
            in_W=in_WT,
            in_b=np.asarray(L0_attn_in_b, np.float32),
            log_dt=np.ascontiguousarray(np.asarray(L0_attn_log_dt, np.float32)[cs]),
            C_p=np.ascontiguousarray(np.asarray(L0_attn_C, np.float32)[cs]),
            lar_t=np.ascontiguousarray(np.asarray(L0_attn_log_A_real, np.float32)[cs]),
            aim_t=np.ascontiguousarray(np.asarray(L0_attn_A_imag, np.float32)[cs]),
            dsk_t=np.ascontiguousarray(np.asarray(L0_attn_D_skip, np.float32)[cs]),
            glu_W=glu_WT,
            glu_b=np.asarray(L0_attn_glu_b, np.float32),
            out_W=out_WT,
            out_b=np.asarray(L0_attn_out_b, np.float32),
            esel=eselT_h,
            keys_t=np.asarray(L0_ffn_keys, np.float32),
            vals_t=np.asarray(L0_ffn_values, np.float32),
            rscale=np.asarray(L0_ffn_route_scale, np.float32),
            ident=ident, trQ_t=trQ, trK_t=trK, trP_t=trP, trP2_t=trP2,
            uoff_t=np.array([128 * c], np.int32),
            yoff_t=np.array([b * 8192 + h * 4096], np.int32),
        ))

    res = run_bass_kernel_spmd(nc, in_maps, core_ids=list(range(NCORES)))
    out = np.empty((4, 8192, 1024), np.float32)
    for c in range(NCORES):
        b, h = c // 2, c % 2
        out[b, h * 4096:(h + 1) * 4096, :] = np.asarray(res.results[c]["out"])
    return out


# version 17
# speedup vs baseline: 1.1172x; 1.0141x over previous
import numpy as np
import ml_dtypes
import concourse.bass as bass
import concourse.mybir as mybir
from concourse.ap import AP
from concourse.bass_utils import run_bass_kernel_spmd
from contextlib import ExitStack

F32 = mybir.dt.float32
BF16 = mybir.dt.bfloat16
I32 = mybir.dt.int32
AF = mybir.ActivationFunctionType
ALU = mybir.AluOpType
AX = mybir.AxisListType.X

NCORES = 8
T = 4096
D = 1024
NST = 32
TWO_PI = 6.283185307179586
INV_2PI = 0.15915494309189535
MAGIC = 12582912.0
HALF_PI = 1.5707963267948966


class Ctr:
    def __init__(self, h):
        self.h = h
        self.v = 0

    def inc(self, n=1):
        self.v += n
        return self.h


def build_nc():
    nc = bass.Bass("TRN2", target_bir_lowering=False)

    x_in = nc.dram_tensor("x_in", [T, D], F32, kind="ExternalInput")
    in_W = nc.dram_tensor("in_W", [D, D], F32, kind="ExternalInput")
    in_b = nc.dram_tensor("in_b", [D], F32, kind="ExternalInput")
    log_dt = nc.dram_tensor("log_dt", [128], F32, kind="ExternalInput")
    C_p = nc.dram_tensor("C_p", [128, NST, 2], F32, kind="ExternalInput")
    lar_t = nc.dram_tensor("lar_t", [128, NST], F32, kind="ExternalInput")
    aim_t = nc.dram_tensor("aim_t", [128, NST], F32, kind="ExternalInput")
    dsk_t = nc.dram_tensor("dsk_t", [128], F32, kind="ExternalInput")
    glu_W = nc.dram_tensor("glu_W", [D, 2 * D], F32, kind="ExternalInput")
    glu_b = nc.dram_tensor("glu_b", [2 * D], F32, kind="ExternalInput")
    out_W = nc.dram_tensor("out_W", [D, D], F32, kind="ExternalInput")
    out_b = nc.dram_tensor("out_b", [D], F32, kind="ExternalInput")
    esel = nc.dram_tensor("esel", [D, 16], F32, kind="ExternalInput")
    keys_t = nc.dram_tensor("keys_t", [16, D, 256], F32, kind="ExternalInput")
    vals_t = nc.dram_tensor("vals_t", [16, 256, D], F32, kind="ExternalInput")
    rscale = nc.dram_tensor("rscale", [1], F32, kind="ExternalInput")
    ident = nc.dram_tensor("ident", [128, 128], BF16, kind="ExternalInput")
    trQ_t = nc.dram_tensor("trQ_t", [128, 128], F32, kind="ExternalInput")
    trK_t = nc.dram_tensor("trK_t", [128, 128], F32, kind="ExternalInput")
    trP_t = nc.dram_tensor("trP_t", [128, 1], F32, kind="ExternalInput")
    trP2_t = nc.dram_tensor("trP2_t", [128, 2], F32, kind="ExternalInput")
    uoff_t = nc.dram_tensor("uoff_t", [1], I32, kind="ExternalInput")
    yoff_t = nc.dram_tensor("yoff_t", [1], I32, kind="ExternalInput")
    out_t = nc.dram_tensor("out", [T, D], F32, kind="ExternalOutput")

    xnT_d = nc.dram_tensor("xnT_d", [D, T], BF16)
    w1bf = nc.dram_tensor("w1bf", [D, D], BF16)
    glubf = nc.dram_tensor("glubf", [D, 2 * D], BF16)
    outbf = nc.dram_tensor("outbf", [D, D], BF16)
    eselbf = nc.dram_tensor("eselbf", [D, 16], BF16)
    keysbf = nc.dram_tensor("keysbf", [16, D, 256], BF16)
    valsbf = nc.dram_tensor("valsbf", [16, 256, D], BF16)
    prow = nc.dram_tensor("prow", [4, 4096], F32)
    k2r = nc.dram_tensor("k2r", [256, 128], BF16)
    u_loc = nc.dram_tensor("u_loc", [T, D], BF16)
    u_gth = nc.dram_tensor("u_gth", [NCORES * T, D], BF16, addr_space="Shared")
    y_loc = nc.dram_tensor("y_loc", [128, 4 * 64 * 128], BF16)
    y_gth = nc.dram_tensor("y_gth", [NCORES * 128, 4 * 64 * 128], BF16, addr_space="Shared")
    gT_d = nc.dram_tensor("gT_d", [16, T], BF16)
    d2_d = nc.dram_tensor("d2_d", [64, 4 * 4096], BF16)
    pp_d = nc.dram_tensor("pp_d", [128, 4 * 4096], BF16)
    ffnP = nc.dram_tensor("ffnP", [T, D], F32)
    ffnQ = nc.dram_tensor("ffnQ", [T, D], F32)

    es = ExitStack()

    def sbuf(name, shape, dt, st=None):
        return (st or es).enter_context(nc.sbuf_tensor(name, shape, dt))

    def sem(name):
        return Ctr(es.enter_context(nc.semaphore(name)))

    s_id = sbuf("s_id", [128, 128], BF16)
    s_trP = sbuf("s_trP", [128, 1], F32)
    s_trP2 = sbuf("s_trP2", [128, 2], F32)
    s_hpi = sbuf("s_hpi", [128, 1], F32)
    s_mu = sbuf("s_mu", [128, 32], F32)
    s_rs = sbuf("s_rs", [128, 32], F32)
    s_inb = sbuf("s_inb", [128, D], F32)
    s_outb = sbuf("s_outb", [128, D], F32)
    s_glb = sbuf("s_glb", [128, 16], F32)
    s_g = sbuf("s_g", [128, 32 * 16], F32)
    s_rsc = sbuf("s_rsc", [128, 1], F32)
    s_uoff = sbuf("s_uoff", [1, 1], I32)
    s_yoff = sbuf("s_yoff", [1, 1], I32)
    s_q2dr = sbuf("s_q2dr", [64, 128], F32)
    s_q2di = sbuf("s_q2di", [64, 128], F32)
    s_q2er = sbuf("s_q2er", [64, 128], F32)
    s_q2ei = sbuf("s_q2ei", [64, 128], F32)
    s_W1 = sbuf("s_W1", [64, 128], F32)
    s_W2 = sbuf("s_W2", [64, 128], F32)
    s_W3 = sbuf("s_W3", [64, 128], F32)

    g2a = ExitStack()
    s_xE = [sbuf("s_x0", [128, D], F32, g2a), sbuf("s_x1", [128, D], F32, g2a)]

    p_a = es.enter_context(nc.psum_tensor("p_a", [128, 512], F32))
    p_b = es.enter_context(nc.psum_tensor("p_b", [128, 512], F32))
    p_c = es.enter_context(nc.psum_tensor("p_c", [128, 512], F32))
    p_d = es.enter_context(nc.psum_tensor("p_d", [128, 512], F32))
    p_sel = es.enter_context(nc.psum_tensor("p_sel", [128, 16], F32))
    p_tr = es.enter_context(nc.psum_tensor("p_tr", [128, 128], BF16))
    p_tr2 = es.enter_context(nc.psum_tensor("p_tr2", [128, 128], BF16))
    p_tr3 = es.enter_context(nc.psum_tensor("p_tr3", [128, 128], BF16))

    smc = sem("smc"); smprm = sem("smprm")
    smcastA = sem("smcastA"); smcastB = sem("smcastB"); smcastC = sem("smcastC")
    sm0 = [sem(f"sm0_{i}") for i in range(20)]
    smx = [sem("smx0"), sem("smx1")]
    smln = sem("smln"); smrs = sem("smrs"); smlv = sem("smlv")
    smtp = sem("smtp"); smcpy = sem("smcpy"); smxst = sem("smxst")
    smyt1 = sem("smyt1"); smytr = sem("smytr")
    smw2 = sem("smw2")
    smu1 = sem("smu1"); smu2 = sem("smu2")
    smu3 = [sem("smu3A"), sem("smu3B")]
    smlg = sem("smlg"); smsg = sem("smsg"); smtk = sem("smtk")
    smgt = sem("smgt"); smge = sem("smge"); smgo = sem("smgo")
    smglb = sem("smglb")
    smcc = sem("smcc"); smucs = sem("smucs")
    sme1 = sem("sme1"); sme2 = sem("sme2")
    smr1 = sem("smr1"); smra = sem("smra")
    smkv = [sem("smkv0"), sem("smkv1")]
    smxw2 = [sem("smxw2_0"), sem("smxw2_1")]
    smgx = sem("smgx")
    smk1 = sem("smk1"); smk2 = sem("smk2"); smk3 = sem("smk3")
    smv1 = sem("smv1"); smv2 = sem("smv2")
    smv3 = [sem("smv3A"), sem("smv3B")]
    smyf = sem("smyf"); smwt3 = sem("smwt3")
    smg1 = sem("smg1"); smg2 = sem("smg2"); smg3 = sem("smg3")
    smo1 = sem("smo1"); smo2 = sem("smo2")
    smf1 = [sem("smf1A"), sem("smf1B")]
    smf2 = [sem("smf2A"), sem("smf2B")]
    smf3 = [sem("smf3A"), sem("smf3B")]
    smout = [sem("smoutA"), sem("smoutB")]

    def wrap_sin_cos(vector, src_ang, dst_sin_arg, dst_cos_arg):
        vector.tensor_scalar(dst_sin_arg, src_ang, INV_2PI, MAGIC, ALU.mult, ALU.add)
        vector.drain()
        vector.tensor_scalar(dst_sin_arg, dst_sin_arg, MAGIC, None, ALU.subtract)
        vector.drain()
        vector.scalar_tensor_tensor(dst_sin_arg, dst_sin_arg, -TWO_PI, src_ang, ALU.mult, ALU.add)
        vector.drain()
        vector.tensor_scalar_min(dst_sin_arg, dst_sin_arg, 3.141592653589793)
        vector.drain()
        vector.tensor_scalar_max(dst_sin_arg, dst_sin_arg, -3.141592653589793)
        vector.drain()

    def act_sin_cos(scalar, wrapped, cos_out, pcount):
        scalar.activation(cos_out, wrapped, AF.Abs)
        scalar.drain()
        scalar.activation(cos_out, cos_out, AF.Sin, bias=s_hpi[0:pcount, 0:1], scale=-1.0)
        return scalar.activation(wrapped, wrapped, AF.Sin)

    with nc.Block() as blk:
        @blk.sync
        def _(sync):
            sync.dma_start(out=s_xE[0][:], in_=AP(x_in, 0, [[D, 128], [1, D]])).then_inc(smx[0].inc(16), 16)
            sync.dma_start(out=s_id[:], in_=ident.ap()).then_inc(smc.inc(16), 16)
            sync.dma_start(out=s_trP[:], in_=trP_t.ap()).then_inc(smc.inc(16), 16)
            sync.dma_start(out=s_trP2[:], in_=trP2_t.ap()).then_inc(smc.inc(16), 16)
            sync.dma_start(out=s_inb[:], in_=AP(in_b, 0, [[0, 128], [1, D]])).then_inc(smc.inc(16), 16)
            sync.dma_start(out=s_outb[:], in_=AP(out_b, 0, [[0, 128], [1, D]])).then_inc(smc.inc(16), 16)
            sync.dma_start(out=s_rsc[:], in_=AP(rscale, 0, [[0, 128], [1, 1]])).then_inc(smc.inc(16), 16)
            sync.dma_start(out=s_uoff[:], in_=uoff_t.ap().unsqueeze(0)).then_inc(smc.inc(16), 16)
            sync.dma_start(out=s_yoff[:], in_=yoff_t.ap().unsqueeze(0)).then_inc(smc.inc(16), 16)

        @blk.gpsimd
        def _(gp):
            gp.dma_start(out=s_xE[1][:], in_=AP(x_in, 128 * D, [[D, 128], [1, D]])).then_inc(smx[1].inc(16), 16)
            gp.dma_start(out=w1bf.ap(), in_=in_W.ap()).then_inc(smcastA.inc(16), 16)
            gp.dma_start(out=eselbf.ap(), in_=esel.ap()).then_inc(smcastA.inc(16), 16)
            gp.wait_ge(smx[1].h, 16)
            gp.dma_start(out=glubf.ap(), in_=glu_W.ap()).then_inc(smcastB.inc(16), 16)
            gp.dma_start(out=outbf.ap(), in_=out_W.ap()).then_inc(smcastB.inc(16), 16)
            gp.dma_start(out=keysbf.ap(), in_=keys_t.ap()).then_inc(smcastC.inc(16), 16)
            gp.dma_start(out=valsbf.ap(), in_=vals_t.ap()).then_inc(smcastC.inc(16), 16)

        g0 = ExitStack()
        s_dtc = sbuf("s_dtc", [128, 1], F32, g0)
        s_lar = sbuf("s_lar", [128, NST], F32, g0)
        s_aim = sbuf("s_aim", [128, NST], F32, g0)
        s_cr = sbuf("s_cr", [128, NST], F32, g0)
        s_ci = sbuf("s_ci", [128, NST], F32, g0)
        s_cri = sbuf("s_cri", [128, 2 * NST], F32, g0)
        s_dskc = sbuf("s_dskc", [128, 1], F32, g0)
        s_ema = sbuf("s_ema", [128, NST], F32, g0)
        s_dtar = sbuf("s_dtar", [128, NST], F32, g0)
        s_dtai = sbuf("s_dtai", [128, NST], F32, g0)
        s_absw = sbuf("s_absw", [128, NST], F32, g0)
        s_ang = sbuf("s_ang", [128, NST], F32, g0)
        s_ang2 = sbuf("s_ang2", [128, NST], F32, g0)
        s_sinw = sbuf("s_sinw", [128, NST], F32, g0)
        s_cosw = sbuf("s_cosw", [128, NST], F32, g0)
        s_t1 = sbuf("s_t1", [128, NST], F32, g0)
        s_t2 = sbuf("s_t2", [128, NST], F32, g0)
        s_t3 = sbuf("s_t3", [128, NST], F32, g0)
        s_t4 = sbuf("s_t4", [128, NST], F32, g0)
        s_den = sbuf("s_den", [128, NST], F32, g0)
        s_trr = sbuf("s_trr", [128, NST], F32, g0)
        s_tii = sbuf("s_tii", [128, NST], F32, g0)
        s_er = sbuf("s_er", [128, NST], F32, g0)
        s_ei = sbuf("s_ei", [128, NST], F32, g0)
        s_wa = sbuf("s_wa", [64, 128], F32, g0)
        s_wb = sbuf("s_wb", [64, 128], F32, g0)

        @blk.sync
        def _(sync):
            sync.dma_start(out=s_dtc[:], in_=log_dt.ap().unsqueeze(1)).then_inc(smprm.inc(16), 16)
            sync.dma_start(out=s_lar[:], in_=lar_t.ap()).then_inc(smprm.inc(16), 16)
            sync.dma_start(out=s_aim[:], in_=aim_t.ap()).then_inc(smprm.inc(16), 16)
            sync.dma_start(out=s_cri[:], in_=AP(C_p, 0, [[NST * 2, 128], [1, NST * 2]])).then_inc(smprm.inc(16), 16)
            sync.dma_start(out=s_dskc[:], in_=dsk_t.ap().unsqueeze(1)).then_inc(smprm.inc(16), 16)

        @blk.scalar
        def _(scalar):
            scalar.wait_ge(smprm.h, smprm.v)
            scalar.activation(s_dtc[:], s_dtc[:], AF.Exp)
            scalar.drain()
            scalar.activation(s_ema[:], s_lar[:], AF.Exp).then_inc(sm0[0].inc(), 1)

        @blk.vector
        def _(vector):
            vector.wait_ge(sm0[0].h, 1)
            vector.tensor_copy(s_cr[:], AP(s_cri[:].tensor, 0, [[2 * NST, 128], [2, NST]]))
            vector.tensor_copy(s_ci[:], AP(s_cri[:].tensor, 1, [[2 * NST, 128], [2, NST]]))
            vector.memset(s_hpi[:], HALF_PI)
            vector.drain()
            vector.tensor_scalar(s_dtar[:], s_ema[:], s_dtc[:, 0:1], -1.0, ALU.mult, ALU.mult)
            vector.tensor_scalar(s_dtai[:], s_aim[:], s_dtc[:, 0:1], None, ALU.mult)
            vector.drain()
            wrap_sin_cos(vector, s_dtai[:], s_ang[:], s_ang2[:])
            vector.tensor_copy(s_t1[:], s_dtar[:])
            vector.drain().then_inc(sm0[1].inc(), 1)

        @blk.scalar
        def _(scalar):
            scalar.wait_ge(sm0[1].h, 1)
            scalar.activation(s_absw[:], s_dtar[:], AF.Exp)
            act_sin_cos(scalar, s_ang[:], s_cosw[:], 128)
            scalar.drain()
            scalar.activation(s_sinw[:], s_ang[:], AF.Identity).then_inc(sm0[2].inc(), 1)

        @blk.vector
        def _(vector):
            vector.wait_ge(sm0[2].h, 1)
            vector.tensor_mul(s_t1[:], s_absw[:], s_cosw[:])
            vector.tensor_mul(s_t2[:], s_absw[:], s_sinw[:])
            vector.tensor_mul(s_den[:], s_ema[:], s_ema[:])
            vector.tensor_mul(s_t3[:], s_aim[:], s_aim[:])
            vector.drain()
            vector.tensor_add(s_den[:], s_den[:], s_t3[:])
            vector.drain()
            vector.reciprocal(s_den[:], s_den[:])
            vector.tensor_scalar(s_t1[:], s_t1[:], -1.0, None, ALU.add)
            vector.drain()
            vector.tensor_mul(s_t3[:], s_t1[:], s_ema[:])
            vector.tensor_mul(s_t4[:], s_t2[:], s_aim[:])
            vector.drain()
            vector.tensor_sub(s_trr[:], s_t4[:], s_t3[:])
            vector.drain()
            vector.tensor_mul(s_t3[:], s_t2[:], s_ema[:])
            vector.tensor_mul(s_t4[:], s_t1[:], s_aim[:])
            vector.drain()
            vector.tensor_add(s_tii[:], s_t3[:], s_t4[:])
            vector.drain()
            vector.tensor_scalar(s_tii[:], s_tii[:], -1.0, None, ALU.mult)
            vector.drain()
            vector.tensor_mul(s_t3[:], s_cr[:], s_trr[:])
            vector.tensor_mul(s_t4[:], s_ci[:], s_tii[:])
            vector.drain()
            vector.tensor_sub(s_er[:], s_t3[:], s_t4[:])
            vector.drain()
            vector.tensor_mul(s_t3[:], s_cr[:], s_tii[:])
            vector.tensor_mul(s_t4[:], s_ci[:], s_trr[:])
            vector.drain()
            vector.tensor_add(s_ei[:], s_t3[:], s_t4[:])
            vector.drain()
            vector.tensor_mul(s_er[:], s_er[:], s_den[:])
            vector.tensor_mul(s_ei[:], s_ei[:], s_den[:])
            vector.drain().then_inc(sm0[3].inc(), 1)

        @blk.sync
        def _(sync):
            sync.wait_ge(sm0[3].h, 1)
            for row, src in [(0, s_dtar), (1, s_dtai), (2, s_er), (3, s_ei)]:
                sync.dma_start(out=AP(prow, row * 4096, [[NST, 128], [1, NST]]),
                               in_=src[:]).then_inc(sm0[4].inc(16), 16)
            sync.wait_ge(sm0[4].h, sm0[4].v)
            for dst, row in [(s_q2dr, 0), (s_q2di, 1), (s_q2er, 2), (s_q2ei, 3)]:
                sync.dma_start(out=dst[0:32, :], in_=AP(prow, row * 4096, [[1, NST], [NST, 128]]),
                               allow_slow_non_contiguous=True).then_inc(sm0[5].inc(16), 16)
                sync.dma_start(out=dst[32:64, :], in_=AP(prow, row * 4096, [[1, NST], [NST, 128]]),
                               allow_slow_non_contiguous=True).then_inc(sm0[5].inc(16), 16)

        g2 = ExitStack()
        s_x = s_xE
        s_xsq = sbuf("s_xsq", [128, D], F32, g2)
        s_xn = [sbuf("s_xn0", [128, D], BF16, g2), sbuf("s_xn1", [128, D], BF16, g2)]
        s_st = sbuf("s_st", [128, 8], F32, g2)
        w1t = [sbuf(f"w1t{k}", [128, D], BF16, g2) for k in range(8)]
        selT = sbuf("selT", [128, 8 * 16], BF16, g2)
        s_xg = [[sbuf(f"s_xg{g}_{k}", [128, 1024], BF16, g2) for k in range(8)] for g in range(2)]
        s_ust = [sbuf(f"s_ust{pp}", [128, D], BF16, g2) for pp in range(2)]
        s_lgo = sbuf("s_lgo", [128, 16], F32, g2)
        s_lpr = sbuf("s_lpr", [128, 16], F32, g2)
        s_lgw = sbuf("s_lgw", [128, 16], F32, g2)
        s_lmx = sbuf("s_lmx", [128, 1], F32, g2)
        s_leq = sbuf("s_leq", [128, 16], F32, g2)
        s_lmk = sbuf("s_lmk", [128, 16], F32, g2)
        s_lsum = sbuf("s_lsum", [128, 1], F32, g2)
        s_gbf = sbuf("s_gbf", [128, 32 * 16], BF16, g2)
        s_gT = sbuf("s_gT", [16, T], BF16, g2)

        @blk.sync
        def _(sync):
            sync.wait_ge(smcastA.h, 32)
            for k in range(8):
                sync.dma_start(out=w1t[k][:],
                               in_=AP(w1bf, 128 * k * D, [[D, 128], [1, D]])).then_inc(smw2.inc(16), 16)
                sync.dma_start(out=selT[:, 16 * k:16 * k + 16],
                               in_=AP(eselbf, 128 * k * 16, [[16, 128], [1, 16]])).then_inc(smw2.inc(16), 16)
            for i in range(32):
                if i + 2 < 32:
                    sync.wait_ge(smlv.h, i + 1)
                    sync.dma_start(out=s_x[i % 2][:],
                                   in_=AP(x_in, (i + 2) * 128 * D, [[D, 128], [1, D]])).then_inc(smx[i % 2].inc(16), 16)
                sync.wait_ge(smu2.h, i + 1)
                sync.dma_start(out=AP(u_loc, i * 128 * D, [[D, 128], [1, D]]),
                               in_=s_ust[i % 2][:]).then_inc(smu3[i % 2].inc(16), 16)
                if i % 8 == 7:
                    g = i // 8
                    sync.wait_ge(smcpy.h, 8 * (i + 1))
                    for k in range(8):
                        sync.dma_start(out=AP(xnT_d, (128 * k) * T + 1024 * g, [[T, 128], [1, 1024]]),
                                       in_=s_xg[g % 2][k][:]).then_inc(smxst.inc(16), 16)
            sync.wait_ge(smge.h, 32)
            sync.dma_start(out=gT_d.ap(), in_=s_gT[:]).then_inc(smgo.inc(16), 16)

        @blk.tensor
        def _(tensor):
            tensor.wait_ge(smc.h, smc.v)
            for i in range(32):
                tensor.wait_ge(smlv.h, i + 1)
                for k in range(8):
                    t_idx = 8 * i + k
                    if t_idx >= 2:
                        tensor.wait_ge(smcpy.h, t_idx - 1)
                    tensor.transpose([p_tr, p_tr2][k % 2][:], s_xn[i % 2][:, 128 * k:128 * k + 128],
                                     s_id[:]).then_inc(smtp.inc(), 1)
                if i == 0:
                    tensor.wait_ge(smw2.h, 256)
                tensor.wait_ge(smcpy.h, 8 * (i + 1))
                if i >= 1:
                    tensor.wait_ge(smu2.h, i)
                pu = [(p_a, p_b), (p_c, p_d)][i % 2]
                col = 128 * (i % 8)
                gb = (i // 8) % 2
                for k in range(8):
                    lhs = s_xg[gb][k][:, col:col + 128]
                    tensor.matmul(pu[0][:], lhs, w1t[k][:, 0:512], start=(k == 0), stop=(k == 7))
                    tensor.matmul(pu[1][:], lhs, w1t[k][:, 512:1024], start=(k == 0), stop=(k == 7))
                    mm = tensor.matmul(p_sel[:], lhs, selT[:, 16 * k:16 * k + 16],
                                       start=(k == 0), stop=(k == 7))
                mm.then_inc(smu1.inc(), 1)
            for ch in range(32):
                tensor.wait_ge(smtk.h, ch + 1)
                if ch >= 1:
                    tensor.wait_ge(smge.h, ch)
                tensor.transpose(p_tr3[0:16, :], s_gbf[:, 16 * ch:16 * ch + 16], s_id[:]).then_inc(smgt.inc(), 1)

        @blk.scalar
        def _(scalar):
            for i in range(32):
                scalar.wait_ge(smln.h, i + 1)
                scalar.activation(s_st[:, 5:6], s_st[:, 4:5], AF.Ln)
                scalar.drain()
                scalar.activation(s_rs[:, i:i + 1], s_st[:, 5:6], AF.Exp, scale=-0.5).then_inc(smrs.inc(), 1)
                g = i // 8
                if i % 8 == 0 and g >= 2:
                    scalar.wait_ge(smxst.h, 128 * (g - 1))
                for k in range(8):
                    scalar.wait_ge(smtp.h, 8 * i + k + 1)
                    scalar.activation(s_xg[g % 2][k][:, 128 * (i % 8):128 * (i % 8) + 128],
                                      [p_tr, p_tr2][k % 2][:], AF.Identity).then_inc(smcpy.inc(), 1)
                if i >= 1:
                    scalar.wait_ge(smlg.h, i)
                    scalar.activation(s_lpr[:], s_lgo[:], AF.Sigmoid).then_inc(smsg.inc(), 1)
            scalar.wait_ge(smlg.h, 32)
            scalar.activation(s_lpr[:], s_lgo[:], AF.Sigmoid).then_inc(smsg.inc(), 1)

        @blk.vector
        def _(vector):
            vector.wait_ge(smc.h, smc.v)

            def evac_topk(j):
                vector.wait_ge(smu1.h, j + 1)
                if j >= 2:
                    vector.wait_ge(smu3[j % 2].h, 16 * (j // 2))
                pu = [(p_a, p_b), (p_c, p_d)][j % 2]
                vector.tensor_add(s_ust[j % 2][:, 0:512], pu[0][:], s_inb[:, 0:512])
                vector.tensor_add(s_ust[j % 2][:, 512:1024], pu[1][:], s_inb[:, 512:1024])
                vector.tensor_copy(s_lgo[:], p_sel[:])
                vector.drain().then_inc(smu2.inc(), 1)
                vector.tensor_copy(s_lgw[:], s_lgo[:])
                vector.memset(s_lmk[:], 0.0)
                vector.drain().then_inc(smlg.inc(), 1)
                for r in range(4):
                    vector.reduce_max(s_lmx[:], s_lgw[:], axis=AX)
                    vector.drain()
                    vector.tensor_scalar(s_leq[:], s_lgw[:], s_lmx[:, 0:1], None, ALU.is_ge)
                    vector.drain()
                    vector.tensor_add(s_lmk[:], s_lmk[:], s_leq[:])
                    vector.scalar_tensor_tensor(s_lgw[:], s_leq[:], -1e30, s_lgw[:], ALU.mult, ALU.add)
                    vector.drain()
                vector.wait_ge(smsg.h, j + 1)
                vector.tensor_mul(s_lpr[:], s_lpr[:], s_lmk[:])
                vector.drain()
                vector.reduce_sum(s_lsum[:], s_lpr[:], axis=AX)
                vector.drain()
                vector.tensor_scalar(s_lsum[:], s_lsum[:], 1e-9, None, ALU.max)
                vector.drain()
                vector.reciprocal(s_lsum[:], s_lsum[:])
                vector.drain()
                vector.tensor_mul(s_lsum[:], s_lsum[:], s_rsc[:])
                vector.drain()
                vector.tensor_scalar(s_g[:, 16 * j:16 * j + 16], s_lpr[:], s_lsum[:, 0:1], None, ALU.mult)
                vector.drain()
                vector.tensor_copy(s_gbf[:, 16 * j:16 * j + 16], s_g[:, 16 * j:16 * j + 16])
                vector.drain().then_inc(smtk.inc(), 1)

            for i in range(33):
                if i < 32:
                    vector.wait_ge(smx[i % 2].h, 16 * (i // 2 + 1))
                    if i >= 2:
                        vector.wait_ge(smtp.h, 8 * i - 8)
                    xb = s_x[i % 2]
                    vector.tensor_mul(s_xsq[:], xb[:], xb[:])
                    vector.reduce_sum(s_st[:, 0:1], xb[:], axis=AX)
                    vector.drain()
                    vector.reduce_sum(s_st[:, 1:2], s_xsq[:], axis=AX)
                    vector.drain()
                    vector.tensor_scalar(s_mu[:, i:i + 1], s_st[:, 0:1], 1.0 / D, None, ALU.mult)
                    vector.tensor_scalar(s_st[:, 2:3], s_st[:, 1:2], 1.0 / D, None, ALU.mult)
                    vector.drain()
                    vector.tensor_mul(s_st[:, 3:4], s_mu[:, i:i + 1], s_mu[:, i:i + 1])
                    vector.drain()
                    vector.tensor_sub(s_st[:, 4:5], s_st[:, 2:3], s_st[:, 3:4])
                    vector.drain()
                    vector.tensor_scalar(s_st[:, 4:5], s_st[:, 4:5], 1e-5, None, ALU.add)
                    vector.drain().then_inc(smln.inc(), 1)
                    vector.wait_ge(smrs.h, i + 1)
                    vector.tensor_scalar(s_xn[i % 2][:], xb[:], s_mu[:, i:i + 1], s_rs[:, i:i + 1],
                                         ALU.subtract, ALU.mult)
                    vector.drain().then_inc(smlv.inc(), 1)
                if i >= 1:
                    evac_topk(i - 1)
            for ch in range(32):
                vector.wait_ge(smgt.h, ch + 1)
                vector.tensor_copy(s_gT[:, 128 * ch:128 * ch + 128], p_tr3[0:16, :])
                vector.drain().then_inc(smge.inc(), 1)


        @blk.vector
        def _(vector):
            vector.wait_ge(sm0[5].h, sm0[5].v)
            vector.tensor_scalar(s_W1[:], s_q2dr[:], 128.0, None, ALU.mult)
            vector.tensor_scalar(s_W2[:], s_q2di[:], 128.0, None, ALU.mult)
            vector.drain()
            wrap_sin_cos(vector, s_W2[:], s_wa[:], s_wb[:])
            vector.drain().then_inc(sm0[6].inc(), 1)

        @blk.scalar
        def _(scalar):
            scalar.wait_ge(sm0[6].h, 1)
            scalar.activation(s_W1[:], s_W1[:], AF.Exp)
            act_sin_cos(scalar, s_wa[:], s_wb[:], 64).then_inc(sm0[7].inc(), 1)

        @blk.vector
        def _(vector):
            vector.wait_ge(sm0[7].h, 1)
            vector.tensor_mul(s_wa[:], s_W1[:], s_wa[:])
            vector.drain()
            vector.tensor_mul(s_W1[:], s_W1[:], s_wb[:])
            vector.tensor_scalar(s_W2[0:32, :], s_wa[0:32, :], -1.0, None, ALU.mult)
            vector.tensor_copy(s_W2[32:64, :], s_wa[32:64, :])
            vector.tensor_copy(s_W3[0:32, :], s_wa[0:32, :])
            vector.tensor_scalar(s_W3[32:64, :], s_wa[32:64, :], -1.0, None, ALU.mult)
            vector.drain().then_inc(sm0[8].inc(), 1)

        gB = ExitStack()
        s_ba = sbuf("s_ba", [128, 4096], F32, gB)
        s_bb = sbuf("s_bb", [128, 4096], F32, gB)
        s_bc = sbuf("s_bc", [128, 4096], F32, gB)
        s_bd = sbuf("s_bd", [128, 4096], F32, gB)
        s_trK = sbuf("s_trK", [128, 128], F32, gB)
        s_ksb = sbuf("s_ksb", [128, 128], F32, gB)
        s_kbf = sbuf("s_kbf", [128, 128], BF16, gB)
        s_ktr = sbuf("s_ktr", [128, 128], BF16, gB)
        s_zro = sbuf("s_zro", [128, 128], BF16, gB)
        s_trQ0 = sbuf("s_trQ0", [128, 128], F32, gB)
        s_d2s = sbuf("s_d2s", [64, 4096], BF16, gB)
        s_ppt = sbuf("s_ppt", [128, 4096], BF16, gB)

        @blk.sync
        def _(sync):
            sync.wait_ge(sm0[3].h, 1)
            sync.wait_ge(sm0[4].h, sm0[4].v)
            sync.dma_start(out=s_trK[:], in_=trK_t.ap()).then_inc(sm0[9].inc(16), 16)
            sync.dma_start(out=s_ba[:], in_=AP(prow, 0, [[0, 128], [1, 4096]])).then_inc(sm0[9].inc(16), 16)
            sync.dma_start(out=s_bb[:], in_=AP(prow, 4096, [[0, 128], [1, 4096]])).then_inc(sm0[9].inc(16), 16)
            sync.wait_ge(sm0[12].h, 1)
            sync.dma_start(out=AP(pp_d, 0, [[16384, 128], [1, 4096]]),
                           in_=s_ppt[:]).then_inc(sm0[19].inc(16), 16)
            sync.wait_ge(sm0[12].h, 2)
            sync.dma_start(out=AP(pp_d, 4096, [[16384, 128], [1, 4096]]),
                           in_=s_ppt[:]).then_inc(sm0[19].inc(16), 16)

        @blk.vector
        def _(vector):
            vector.wait_ge(smc.h, smc.v)
            vector.wait_ge(sm0[9].h, 48)
            vector.tensor_scalar(s_ba[:], s_ba[:], s_trP[:, 0:1], None, ALU.mult)
            vector.tensor_scalar(s_bb[:], s_bb[:], s_trP[:, 0:1], None, ALU.mult)
            vector.drain()
            wrap_sin_cos(vector, s_bb[:], s_bc[:], s_bd[:])
            vector.tensor_copy(s_t2[:, 0:1], s_t1[:, 0:1])
            vector.drain().then_inc(sm0[10].inc(), 1)
            vector.wait_ge(sm0[11].h, 1)
            vector.tensor_mul(s_ppt[:], s_ba[:], s_bd[:])
            vector.drain()
            vector.tensor_copy(s_t2[:, 1:2], s_t1[:, 0:1])
            vector.drain().then_inc(sm0[12].inc(), 1)
            vector.wait_ge(sm0[19].h, 16)
            vector.tensor_mul(s_ppt[:], s_ba[:], s_bc[:])
            vector.drain()
            vector.tensor_copy(s_t2[:, 2:3], s_t1[:, 0:1])
            vector.drain().then_inc(sm0[12].inc(), 1)

        @blk.scalar
        def _(scalar):
            scalar.wait_ge(sm0[10].h, 1)
            scalar.activation(s_ba[:], s_ba[:], AF.Exp)
            act_sin_cos(scalar, s_bc[:], s_bd[:], 128).then_inc(sm0[11].inc(), 1)


        @blk.vector
        def _(vector):
            vector.wait_ge(sm0[12].h, 2)
            a_dr = s_dtar[:].unsqueeze(2).broadcast_to([128, 32, 128])
            a_di = s_dtai[:].unsqueeze(2).broadcast_to([128, 32, 128])
            a_tk = s_trK[:].unsqueeze(1).broadcast_to([128, 32, 128])
            vector.tensor_tensor(s_ba[:].rearrange("p (n m) -> p n m", m=128), a_dr, a_tk, ALU.mult)
            vector.tensor_tensor(s_bb[:].rearrange("p (n m) -> p n m", m=128), a_di, a_tk, ALU.mult)
            vector.drain()
            wrap_sin_cos(vector, s_bb[:], s_bc[:], s_bd[:])
            vector.tensor_copy(s_t2[:], s_t1[:])
            vector.drain().then_inc(sm0[13].inc(), 1)

        @blk.scalar
        def _(scalar):
            scalar.wait_ge(sm0[13].h, 1)
            scalar.activation(s_ba[:], s_ba[:], AF.Exp)
            act_sin_cos(scalar, s_bc[:], s_bd[:], 128).then_inc(sm0[14].inc(), 1)

        @blk.vector
        def _(vector):
            vector.wait_ge(sm0[14].h, 1)
            vector.tensor_mul(s_bd[:], s_ba[:], s_bd[:])
            vector.tensor_mul(s_bc[:], s_ba[:], s_bc[:])
            vector.drain()
            e_r = s_er[:].unsqueeze(2).broadcast_to([128, 32, 128])
            e_i = s_ei[:].unsqueeze(2).broadcast_to([128, 32, 128])
            v_ba = s_ba[:].rearrange("p (n m) -> p n m", m=128)
            v_bb = s_bb[:].rearrange("p (n m) -> p n m", m=128)
            v_bc = s_bc[:].rearrange("p (n m) -> p n m", m=128)
            v_bd = s_bd[:].rearrange("p (n m) -> p n m", m=128)
            vector.tensor_tensor(v_ba, e_r, v_bd, ALU.mult)
            vector.tensor_tensor(v_bb, e_i, v_bc, ALU.mult)
            vector.drain()
            vector.tensor_sub(s_ba[:], s_ba[:], s_bb[:])
            vector.drain()
            vector.reduce_sum(s_ksb[:].unsqueeze(2),
                              AP(s_ba[:].tensor, 0, [[4096, 128], [1, 128], [128, 32]]),
                              axis=AX)
            vector.drain()
            vector.tensor_scalar(s_ksb[:], s_ksb[:], 2.0, None, ALU.mult)
            vector.drain()
            vector.tensor_add(s_ksb[:, 127:128], s_ksb[:, 127:128], s_dskc[:])
            vector.drain()
            vector.tensor_copy(s_kbf[:], s_ksb[:])
            vector.drain()
            vector.memset(s_zro[:], 0.0)
            vector.drain().then_inc(sm0[15].inc(), 1)

        @blk.tensor
        def _(tensor):
            tensor.wait_ge(sm0[15].h, 1)
            tensor.wait_ge(smc.h, smc.v)
            tensor.transpose(p_tr[:], s_kbf[:], s_id[:]).then_inc(sm0[16].inc(), 1)

        @blk.vector
        def _(vector):
            vector.wait_ge(sm0[16].h, 1)
            vector.tensor_copy(s_ktr[:], p_tr[:])
            vector.drain().then_inc(sm0[17].inc(), 1)

        @blk.sync
        def _(sync):
            sync.wait_ge(sm0[17].h, 1)
            sync.dma_start(out=AP(k2r, 129 * 128, [[128, 127], [1, 128]]),
                           in_=s_zro[0:127, :]).then_inc(sm0[18].inc(16), 16)
            sync.dma_start(out=AP(k2r, 0, [[128, 1], [1, 128]]),
                           in_=s_zro[0:1, :]).then_inc(sm0[18].inc(16), 16)
            sync.dma_start(out=AP(k2r, 128, [[128, 128], [1, 128]]),
                           in_=s_ktr[:]).then_inc(sm0[18].inc(16), 16)
            sync.dma_start(out=s_trQ0[:], in_=trQ_t.ap()).then_inc(sm0[18].inc(16), 16)

        smdb = sem("smdb"); smdbs = sem("smdbs"); smdbv = sem("smdbv"); smdbo = sem("smdbo")
        s_P2 = sbuf("s_P2", [64, 128], F32, gB)
        s_Q2 = sbuf("s_Q2", [64, 128], F32, gB)

        @blk.vector
        def _(vector):
            vector.wait_ge(sm0[15].h, 1)
            vector.wait_ge(sm0[18].h, sm0[18].v)
            vector.tensor_copy(s_P2[0:32, :], s_q2er[0:32, :])
            vector.tensor_scalar(s_P2[32:64, :], s_q2ei[32:64, :], -1.0, None, ALU.mult)
            vector.tensor_scalar(s_Q2[0:32, :], s_q2ei[0:32, :], -1.0, None, ALU.mult)
            vector.tensor_scalar(s_Q2[32:64, :], s_q2er[32:64, :], -1.0, None, ALU.mult)
            vector.drain()
            for slab in range(4):
                if slab >= 1:
                    vector.wait_ge(smdbo.h, 16 * slab)
                dr_b = s_q2dr[:, 32 * slab:32 * slab + 32].unsqueeze(2).broadcast_to([64, 32, 128])
                di_b = s_q2di[:, 32 * slab:32 * slab + 32].unsqueeze(2).broadcast_to([64, 32, 128])
                tq_b = s_trQ0[0:64, :].unsqueeze(1).broadcast_to([64, 32, 128])
                vector.tensor_tensor(s_ba[0:64, :].rearrange("p (c t) -> p c t", t=128), dr_b, tq_b, ALU.mult)
                vector.tensor_tensor(s_bb[0:64, :].rearrange("p (c t) -> p c t", t=128), di_b, tq_b, ALU.mult)
                vector.drain()
                wrap_sin_cos(vector, s_bb[0:64, :], s_bc[0:64, :], s_bd[0:64, :])
                vector.tensor_copy(s_t2[:, 3:4], s_t1[:, 0:1])
                vector.drain().then_inc(smdb.inc(), 1)
                vector.wait_ge(smdbs.h, slab + 1)
                vector.tensor_mul(s_bd[0:64, :], s_ba[0:64, :], s_bd[0:64, :])
                vector.tensor_mul(s_bc[0:64, :], s_ba[0:64, :], s_bc[0:64, :])
                vector.drain()
                p_b_ = s_P2[:, 32 * slab:32 * slab + 32].unsqueeze(2).broadcast_to([64, 32, 128])
                q_b_ = s_Q2[:, 32 * slab:32 * slab + 32].unsqueeze(2).broadcast_to([64, 32, 128])
                v_ba = s_ba[0:64, :].rearrange("p (c t) -> p c t", t=128)
                v_bb = s_bb[0:64, :].rearrange("p (c t) -> p c t", t=128)
                v_bc = s_bc[0:64, :].rearrange("p (c t) -> p c t", t=128)
                v_bd = s_bd[0:64, :].rearrange("p (c t) -> p c t", t=128)
                vector.tensor_tensor(v_ba, p_b_, v_bd, ALU.mult)
                vector.tensor_tensor(v_bb, q_b_, v_bc, ALU.mult)
                vector.drain()
                vector.tensor_add(s_ba[0:64, :], s_ba[0:64, :], s_bb[0:64, :])
                vector.drain()
                vector.tensor_scalar(s_d2s[:], s_ba[0:64, :], 2.0, None, ALU.mult)
                vector.drain().then_inc(smdbv.inc(), 1)

        @blk.scalar
        def _(scalar):
            for slab in range(4):
                scalar.wait_ge(smdb.h, slab + 1)
                scalar.activation(s_ba[0:64, :], s_ba[0:64, :], AF.Exp)
                act_sin_cos(scalar, s_bc[0:64, :], s_bd[0:64, :], 64).then_inc(smdbs.inc(), 1)

        @blk.sync
        def _(sync):
            for slab in range(4):
                sync.wait_ge(smdbv.h, slab + 1)
                sync.dma_start(out=AP(d2_d, slab * 4096, [[4 * 4096, 64], [1, 4096]]),
                               in_=s_d2s[:]).then_inc(smdbo.inc(16), 16)

        gB.close()
        g2.close()
        g0.close()
        g2a.close()
        g3 = ExitStack()
        s_ucs = sbuf("s_ucs", [128, 4 * 64 * 128], BF16, g3)
        s_a2 = sbuf("s_a2", [64, 128 * 4 * 64], BF16, g3)
        g3mA = ExitStack()
        s_pp2 = sbuf("s_pp2", [128, 8192], BF16, g3mA)
        g3m = ExitStack()
        s_aswp = sbuf("s_aswp", [64, 16384], BF16, g3m)
        s_sN = sbuf("s_sN", [64, 512], F32, g3m)
        s_sS = sbuf("s_sS", [64, 512], F32, g3m)
        s_rx = sbuf("s_rx", [64, 1024], F32, g3m)

        @blk.gpsimd
        def _(gp):
            gp.wait_ge(smu3[0].h, smu3[0].v)
            gp.wait_ge(smu3[1].h, smu3[1].v)
            gp.collective_compute(
                "AllGather", ALU.bypass,
                replica_groups=[list(range(NCORES))],
                ins=[u_loc.ap().opt()],
                outs=[u_gth.ap().opt()],
            ).then_inc(smcc.inc(), 1)

        @blk.sync
        def _(sync):
            sync.wait_ge(smcc.h, 1)
            sync.wait_ge(smc.h, smc.v)
            sync.wait_ge(smgo.h, 16)
            sync.wait_ge(smdbo.h, 64)
            sync.wait_ge(sm0[18].h, sm0[18].v)
            sync.wait_ge(sm0[19].h, 32)
            uoff = sync.value_load(s_uoff[0:1, 0:1])
            sync.dma_start(out=s_pp2[:], in_=AP(pp_d, 0, [[16384, 128], [1, 8192]])).then_inc(smucs.inc(16), 16)
            for r in range(NCORES):
                b, h = r // 2, r % 2
                sync.dma_start(
                    out=AP(s_ucs[:].tensor, b * 8192 + h * 4096,
                           [[32768, 128], [128, 32], [1, 128]]),
                    in_=AP(u_gth, uoff + r * T * D, [[D, 128], [128 * D, 32], [1, 128]]),
                ).then_inc(smucs.inc(16), 16)

        @blk.tensor
        def _(tensor):
            tensor.wait_ge(smucs.h, smucs.v)
            for pair in range(64):
                pe_bank = [p_a, p_b][pair % 2]
                if pair >= 2:
                    tensor.wait_ge(sme2.h, pair - 1)
                for two in range(2):
                    ch = 2 * pair + two
                    for ri in range(2):
                        mm = tensor.matmul(
                            pe_bank[32 * ri:32 * ri + 32, 256 * two:256 * two + 256],
                            s_pp2[:, ri * 4096 + ch * 32: ri * 4096 + ch * 32 + 32],
                            AP(s_ucs[:].tensor, ch, [[32768, 128], [128, 64], [8192, 4]]),
                            start=True, stop=True, tile_position=(0, 32 * ri))
                mm.then_inc(sme1.inc(), 1)

        @blk.scalar
        def _(scalar):
            for pair in range(64):
                pe_bank = [p_a, p_b][pair % 2]
                scalar.wait_ge(sme1.h, pair + 1)
                for two in range(2):
                    ch = 2 * pair + two
                    hp2, c2 = ch // 64, ch % 64
                    scalar.activation(
                        AP(s_a2[:].tensor, 16384 * hp2 + 4 * c2, [[32768, 64], [256, 64], [1, 4]]),
                        AP(pe_bank[:].tensor, 256 * two, [[512, 64], [1, 256]]),
                        AF.Identity)
                scalar.drain().then_inc(sme2.inc(), 1)

        smsw = sem("smsw")

        @blk.sync
        def _(sync):
            for hp in range(2):
                if hp == 0:
                    sync.wait_ge(sme2.h, 32)
                else:
                    sync.wait_ge(smra.h, 1)
                sync.dma_start(out=s_aswp[0:32, :],
                               in_=s_a2[32:64, 16384 * hp:16384 * hp + 16384]).then_inc(smsw.inc(16), 16)
                sync.dma_start(out=s_aswp[32:64, :],
                               in_=s_a2[0:32, 16384 * hp:16384 * hp + 16384]).then_inc(smsw.inc(16), 16)

        @blk.vector
        def _(vector):
            vector.wait_ge(sm0[8].h, 1)
            for hp in range(2):
                vector.wait_ge(smsw.h, 32 * (hp + 1))
                vector.memset(s_sN[:], 0.0)
                vector.memset(s_sS[:], 0.0)
                vector.drain()
                w1b = s_W1[:, 64 * hp:64 * hp + 64].unsqueeze(2).broadcast_to([64, 64, 4])
                w2b = s_W2[:, 64 * hp:64 * hp + 64].unsqueeze(2).broadcast_to([64, 64, 4])
                w3b = s_W3[:, 64 * hp:64 * hp + 64].unsqueeze(2).broadcast_to([64, 64, 4])
                t1 = s_rx[:, 0:256].rearrange("p (c b) -> p c b", b=4)
                t2 = s_rx[:, 256:512].rearrange("p (c b) -> p c b", b=4)
                t3 = s_rx[:, 512:768].rearrange("p (c b) -> p c b", b=4)
                t4 = s_rx[:, 768:1024].rearrange("p (c b) -> p c b", b=4)
                for J in range(64):
                    pv, cu = (J % 2), ((J + 1) % 2)
                    nP = s_sN[:, 256 * pv:256 * pv + 256].rearrange("p (c b) -> p c b", b=4)
                    nC = s_sN[:, 256 * cu:256 * cu + 256]
                    sP = s_sS[:, 256 * pv:256 * pv + 256].rearrange("p (c b) -> p c b", b=4)
                    sC = s_sS[:, 256 * cu:256 * cu + 256]
                    aj0 = 16384 * hp + 256 * J
                    vector.tensor_tensor(t1, nP, w1b, ALU.mult)
                    vector.tensor_tensor(t2, sP, w2b, ALU.mult)
                    vector.tensor_tensor(t3, nP, w3b, ALU.mult)
                    vector.tensor_tensor(t4, sP, w1b, ALU.mult)
                    if J >= 1:
                        vector.tensor_copy(s_a2[:, aj0 - 256:aj0],
                                           s_sN[:, 256 * pv:256 * pv + 256])
                    dr1 = vector.drain()
                    if hp == 1 and J == 32:
                        dr1.then_inc(smr1.inc(), 1)
                    vector.tensor_add(s_rx[:, 0:256], s_rx[:, 0:256], s_rx[:, 256:512])
                    vector.tensor_add(s_rx[:, 768:1024], s_rx[:, 768:1024], s_rx[:, 512:768])
                    vector.drain()
                    vector.tensor_tensor(nC, s_rx[:, 0:256], s_a2[:, aj0:aj0 + 256], ALU.add)
                    vector.tensor_tensor(sC, s_rx[:, 768:1024], s_aswp[:, 256 * J:256 * J + 256], ALU.add)
                    dr3 = vector.drain()
                    if J == 63:
                        dr3.then_inc([smra, smr1][hp].inc(), 1)

        g3m.close()
        g3mA.close()

        g3b = ExitStack()
        s_tp = [sbuf("s_tp0", [128, 32 * 128], BF16, g3b)]
        s_d2w = [sbuf("s_d2w0", [64, 4096], BF16, g3b)]
        s_yst = sbuf("s_yst", [128, 4 * 32 * 128], BF16, g3b)

        smtl = sem("smtl")
        smd1 = sem("smd1")
        smd2 = sem("smd2")
        smyo2 = sem("smyo2")

        @blk.sync
        def _(sync):
            sync.wait_ge(sm0[18].h, sm0[18].v)
            sync.wait_ge(smdbo.h, smdbo.v)
            for half in range(2):
                for slab in range(4):
                    idx = half * 4 + slab
                    pp = 0
                    if idx >= 1:
                        sync.wait_ge(smd1.h, 8 * idx)
                    sync.dma_start(out=s_tp[pp][:].rearrange("p (t c) -> p t c", c=32),
                                   in_=AP(k2r, 128 * 128 + 32 * slab, [[128, 128], [-128, 128], [1, 32]]),
                                   allow_slow_non_contiguous=True).then_inc(smtl.inc(16), 16)
                    sync.dma_start(out=s_d2w[pp][:],
                                   in_=AP(d2_d, slab * 4096, [[4 * 4096, 64], [1, 4096]])).then_inc(smtl.inc(16), 16)
                for b4 in range(4):
                    sync.wait_ge(smytr.h, half * 64 + 16 * (b4 + 1))
                    sync.dma_start(
                        out=AP(y_loc, b4 * 8192 + half * 4096, [[4 * 64 * 128, 128], [1, 4096]]),
                        in_=s_yst[:, b4 * 4096:b4 * 4096 + 4096],
                    ).then_inc(smyo2.inc(16), 16)

        @blk.tensor
        def _(tensor):
            for half in range(2):
                tensor.wait_ge(smr1.h, half + 1)
                for slab in range(4):
                    idx = half * 4 + slab
                    pp = 0
                    tensor.wait_ge(smtl.h, 32 * (idx + 1))
                    nJ = 31 if half == 0 else 32
                    oJ = 1 if half == 0 else 0
                    sJ0 = 0 if half == 0 else 31
                    for grp in range(8):
                        pe_bank = [p_c, p_d][grp % 2]
                        gi = idx * 8 + grp
                        if gi >= 2:
                            tensor.wait_ge(smd2.h, gi - 1)
                        for cp in range(4):
                            cl = grp * 4 + cp
                            ch = 32 * slab + cl
                            tensor.matmul(
                                pe_bank[:, 128 * cp:128 * cp + 128].rearrange("p (j b) -> p j b", b=4),
                                AP(s_tp[pp][:].tensor, cl, [[4096, 128], [32, 128]]),
                                AP(s_ucs[:].tensor, ch + half * 32 * 128,
                                   [[32768, 128], [128, 32], [8192, 4]]),
                                start=True, stop=False)
                            hp2, c2 = ch // 64, ch % 64
                            mm = tensor.matmul(
                                AP(pe_bank[:].tensor, 128 * cp + 4 * oJ, [[512, 128], [1, 4 * nJ]]),
                                s_d2w[pp][:, 128 * cl:128 * cl + 128],
                                AP(s_a2[:].tensor, 16384 * hp2 + 4 * c2 + 256 * sJ0,
                                   [[32768, 64], [256, nJ], [1, 4]]),
                                start=False, stop=True)
                        mm.then_inc(smd1.inc(), 1)
                tensor.wait_ge(smd2.h, 32 * (half + 1))
                for tt in range(128):
                    gi2 = half * 128 + tt
                    b4, Jl = tt // 32, tt % 32
                    if gi2 >= 2:
                        tensor.wait_ge(smytr.h, gi2 // 2)
                    tensor.transpose([p_tr, p_tr2][tt % 2][:],
                                     s_yst[:, b4 * 4096 + Jl * 128:b4 * 4096 + Jl * 128 + 128],
                                     s_id[:]).then_inc(smyt1.inc(), 1)

        @blk.scalar
        def _(scalar):
            for idx in range(8):
                half, slab = idx // 4, idx % 4
                for grp in range(8):
                    pe_bank = [p_c, p_d][grp % 2]
                    gi = idx * 8 + grp
                    if gi == 32:
                        scalar.wait_ge(smyo2.h, 64)
                    scalar.wait_ge(smd1.h, gi + 1)
                    scalar.activation(
                        AP(s_yst[:].tensor, 32 * slab + 4 * grp,
                           [[16384, 128], [1, 4], [128, 32], [4096, 4]]),
                        AP(pe_bank[:].tensor, 0, [[512, 128], [128, 4], [4, 32], [1, 4]]),
                        AF.Gelu).then_inc(smd2.inc(), 1)

        @blk.vector
        def _(vector):
            for half in range(2):
                for tt in range(0, 128, 2):
                    gi2 = half * 128 + tt
                    vector.wait_ge(smyt1.h, gi2 + 2)
                    for u2 in range(2):
                        b4, Jl = (tt + u2) // 32, (tt + u2) % 32
                        vector.tensor_copy(
                            s_yst[:, b4 * 4096 + Jl * 128:b4 * 4096 + Jl * 128 + 128],
                            [p_tr, p_tr2][(tt + u2) % 2][:])
                    vector.drain().then_inc(smytr.inc(), 1)

        @blk.gpsimd
        def _(gp):
            gp.wait_ge(smyo2.h, smyo2.v)
            gp.wait_ge(smkv[0].h, 256)
            gp.collective_compute(
                "AllGather", ALU.bypass,
                replica_groups=[list(range(NCORES))],
                ins=[y_loc.ap().opt()],
                outs=[y_gth.ap().opt()],
            ).then_inc(smcc.inc(), 1)

        g3b.close()
        g3.close()

        g45 = ExitStack()
        gluT = [sbuf(f"gluT{k}", [128, 2 * D], BF16, g45) for k in range(8)]
        outT = [sbuf(f"outT{k}", [128, D], BF16, g45) for k in range(8)]
        g4 = ExitStack()
        s_key = [sbuf(f"s_key{e}", [128, 8 * 256], BF16, g4) for e in range(8)]
        s_val = [sbuf(f"s_val{e}", [128, 2 * D], BF16, g4) for e in range(8)]
        s_sal = sbuf("s_sal", [128, 16 * 512], BF16, g4)
        s_grep = sbuf("s_grep", [128, 8 * 512], BF16, g4)
        s_xw2 = [[sbuf(f"s_xw2{pp}_{k}", [128, 512], BF16, g4) for k in range(8)] for pp in range(2)]
        s_fst = [sbuf(f"s_fst{pp}", [128, D], F32, g4) for pp in range(2)]

        @blk.sync
        def _(sync):
            sync.wait_ge(smcastC.h, 32)
            sync.wait_ge(smcastB.h, 32)
            sync.wait_ge(smgo.h, 16)
            sync.wait_ge(smxst.h, 512)
            sync.wait_ge(smyo2.h, 128)
            def kv_loads(sync, half):
                for e in range(8):
                    eg = 8 * half + e
                    sync.dma_start(out=s_key[e][:].rearrange("p (k h2) -> p k h2", h2=256),
                                   in_=AP(keysbf, eg * D * 256, [[256, 128], [128 * 256, 8], [1, 256]])).then_inc(smkv[half].inc(16), 16)
                    sync.dma_start(out=s_val[e][:].rearrange("p (k d) -> p k d", d=D),
                                   in_=AP(valsbf, eg * 256 * D, [[D, 128], [128 * D, 2], [1, D]])).then_inc(smkv[half].inc(16), 16)
            def xw_loads(sync, half, s):
                slab = half * 8 + s
                pp = s % 2
                for k in range(8):
                    sync.dma_start(
                        out=s_xw2[pp][k][:],
                        in_=AP(xnT_d, (128 * k) * T + 512 * s, [[T, 128], [1, 512]])).then_inc(smxw2[pp].inc(16), 16)
                for e in range(8):
                    eg = 8 * half + e
                    sync.dma_start(out=s_grep[:, 512 * e:512 * e + 512],
                                   in_=AP(gT_d, eg * T + 512 * s, [[0, 128], [1, 512]])).then_inc(smgx.inc(16), 16)
            xw_loads(sync, 0, 0)
            kv_loads(sync, 0)
            for k in range(8):
                sync.dma_start(out=gluT[k][:],
                               in_=AP(glubf, 128 * k * 2 * D, [[2 * D, 128], [1, 2 * D]])).then_inc(smwt3.inc(16), 16)
                sync.dma_start(out=outT[k][:],
                               in_=AP(outbf, 128 * k * D, [[D, 128], [1, D]])).then_inc(smwt3.inc(16), 16)
            def store_slab(sync, slab):
                for cc in range(4):
                    chunk = slab * 4 + cc
                    sync.wait_ge(smv2.h, chunk + 1)
                    dst = [ffnP, ffnQ][(chunk // 32)]
                    sync.dma_start(out=AP(dst, (chunk % 32) * 128 * D, [[D, 128], [1, D]]),
                                   in_=s_fst[chunk % 2][:]).then_inc(smv3[chunk % 2].inc(16), 16)

            for half in range(2):
                if half == 1:
                    store_slab(sync, 7)
                    sync.wait_ge(smv1.h, 32)
                    kv_loads(sync, 1)
                for s in range(8):
                    slab = half * 8 + s
                    if s >= 1:
                        store_slab(sync, slab - 1)
                    if slab != 0:
                        xw_loads(sync, half, s)
            store_slab(sync, 15)

        @blk.tensor
        def _(tensor):
            for half in range(2):
                for s in range(8):
                    slab = half * 8 + s
                    pp = s % 2
                    tensor.wait_ge(smxw2[pp].h, 16 * 8 * (slab // 2 + 1))
                    for e in range(8):
                        if s == 0:
                            tensor.wait_ge(smkv[half].h, 32 * (e + 1))
                        if slab * 8 + e >= 1:
                            tensor.wait_ge(smk2.h, 2 * (slab * 8 + e))
                        for hh in range(2):
                            ps = [p_a, p_b][hh]
                            for k in range(8):
                                mm = tensor.matmul(ps[:], s_key[e][:, 256 * k + 128 * hh:256 * k + 128 * hh + 128],
                                                   s_xw2[pp][k][:], start=(k == 0), stop=(k == 7))
                            mm.then_inc(smk1.inc(), 1)
                    for cc in range(4):
                        chunk = slab * 4 + cc
                        if chunk >= 1:
                            tensor.wait_ge(smv2.h, chunk)
                        for dd in range(2):
                            po = [p_c, p_d][dd]
                            first = True
                            for e in range(8):
                                if dd == 0 and cc == 0:
                                    tensor.wait_ge(smk3.h, slab * 8 + e + 1)
                                for hh in range(2):
                                    mm = tensor.matmul(
                                        po[:],
                                        s_sal[:, (e * 2 + hh) * 512 + 128 * cc:(e * 2 + hh) * 512 + 128 * cc + 128],
                                        s_val[e][:, hh * D + 512 * dd:hh * D + 512 * dd + 512],
                                        start=first, stop=(e == 7 and hh == 1))
                                    first = False
                        mm.then_inc(smv1.inc(), 1)

        @blk.scalar
        def _(scalar):
            for half in range(2):
                for s in range(8):
                    slab = half * 8 + s
                    for e in range(8):
                        if slab >= 1:
                            scalar.wait_ge(smv1.h, 4 * slab)
                        for hh in range(2):
                            idx = (slab * 8 + e) * 2 + hh
                            scalar.wait_ge(smk1.h, idx + 1)
                            ps = [p_a, p_b][hh]
                            scalar.activation(s_sal[:, (e * 2 + hh) * 512:(e * 2 + hh) * 512 + 512],
                                              ps[:], AF.Relu).then_inc(smk2.inc(), 1)

        @blk.vector
        def _(vector):
            for half in range(2):
                for s in range(8):
                    slab = half * 8 + s
                    vector.wait_ge(smgx.h, 16 * 8 * (slab + 1))
                    for e in range(8):
                        vector.wait_ge(smk2.h, 2 * (slab * 8 + e + 1))
                        ge = s_grep[:, 512 * e:512 * e + 512]
                        vector.tensor_tensor(s_sal[:, (e * 2) * 512:(e * 2) * 512 + 512],
                                             s_sal[:, (e * 2) * 512:(e * 2) * 512 + 512], ge, ALU.mult)
                        vector.drain()
                        vector.tensor_tensor(s_sal[:, (e * 2 + 1) * 512:(e * 2 + 1) * 512 + 512],
                                             s_sal[:, (e * 2 + 1) * 512:(e * 2 + 1) * 512 + 512], ge, ALU.mult)
                        vector.drain().then_inc(smk3.inc(), 1)
                    for cc in range(4):
                        chunk = slab * 4 + cc
                        pp3 = chunk % 2
                        vector.wait_ge(smv1.h, chunk + 1)
                        if chunk >= 2:
                            vector.wait_ge(smv3[chunk % 2].h, 16 * (chunk // 2))
                        if half == 0:
                            vector.tensor_copy(s_fst[pp3][:, 0:512], p_c[:])
                            vector.tensor_copy(s_fst[pp3][:, 512:1024], p_d[:])
                        else:
                            vector.tensor_add(s_fst[pp3][:, 0:512], p_c[:], s_outb[:, 0:512])
                            vector.tensor_add(s_fst[pp3][:, 512:1024], p_d[:], s_outb[:, 512:1024])
                        vector.drain().then_inc(smv2.inc(), 1)

        g4.close()

        g5 = ExitStack()
        y_fm = [sbuf(f"y_fm{r}", [128, T], BF16, g5) for r in range(8)]
        s_ag = [sbuf(f"s_ag{o}", [128, 512], BF16, g5) for o in range(16)]
        s_y2 = [sbuf(f"s_y2{k}", [128, 512], BF16, g5) for k in range(8)]
        s_xr = [sbuf(f"s_xr{pp}", [128, D], F32, g5) for pp in range(2)]
        s_fp = [sbuf(f"s_fp{pp}", [128, D], F32, g5) for pp in range(2)]
        s_fq = [sbuf(f"s_fq{pp}", [128, D], F32, g5) for pp in range(2)]
        s_fin = [sbuf(f"s_fin{pp}", [128, D], F32, g5) for pp in range(2)]
        s_aff = sbuf("s_aff", [128, 4], F32, g5)

        @blk.sync
        def _(sync):
            sync.dma_start(out=s_glb[:], in_=AP(glu_b, 0, [[1, 128], [128, 16]]),
                           allow_slow_non_contiguous=True).then_inc(smglb.inc(16), 16)
            sync.wait_ge(smcc.h, 2)
            sync.wait_ge(smcastB.h, 32)
            sync.wait_ge(smv3[0].h, smv3[0].v)
            sync.wait_ge(smv3[1].h, smv3[1].v)
            yoff = sync.value_load(s_yoff[0:1, 0:1])
            for r in range(8):
                sync.dma_start(
                    out=y_fm[r][:],
                    in_=AP(y_gth, yoff + r * 128 * 32768, [[32768, 128], [1, 4096]])).then_inc(smyf.inc(16), 16)
            for chunk in range(2):
                p9 = chunk % 2
                sync.dma_start(out=s_xr[p9][:],
                               in_=AP(x_in, chunk * 128 * D, [[D, 128], [1, D]])).then_inc(smf1[p9].inc(16), 16)
                sync.dma_start(out=s_fp[p9][:],
                               in_=AP(ffnP, chunk * 128 * D, [[D, 128], [1, D]])).then_inc(smf2[p9].inc(16), 16)
                sync.dma_start(out=s_fq[p9][:],
                               in_=AP(ffnQ, chunk * 128 * D, [[D, 128], [1, D]])).then_inc(smf3[p9].inc(16), 16)
            for chunk in range(32):
                p9 = chunk % 2
                sync.wait_ge(smo2.h, chunk + 1)
                sync.dma_start(out=AP(out_t, chunk * 128 * D, [[D, 128], [1, D]]),
                               in_=s_fin[p9][:]).then_inc(smout[p9].inc(16), 16)
                if chunk + 2 < 32:
                    sync.dma_start(out=s_xr[p9][:],
                                   in_=AP(x_in, (chunk + 2) * 128 * D, [[D, 128], [1, D]])).then_inc(smf1[p9].inc(16), 16)
                    sync.dma_start(out=s_fp[p9][:],
                                   in_=AP(ffnP, (chunk + 2) * 128 * D, [[D, 128], [1, D]])).then_inc(smf2[p9].inc(16), 16)
                    sync.dma_start(out=s_fq[p9][:],
                                   in_=AP(ffnQ, (chunk + 2) * 128 * D, [[D, 128], [1, D]])).then_inc(smf3[p9].inc(16), 16)
            sync.wait_ge(smout[0].h, smout[0].v)
            sync.wait_ge(smout[1].h, smout[1].v)

        @blk.tensor
        def _(tensor):
            tensor.wait_ge(smyf.h, smyf.v)
            tensor.wait_ge(smwt3.h, smwt3.v)
            for s in range(8):
                for o in range(16):
                    pg = [p_a, p_b][o % 2]
                    if s * 16 + o >= 2:
                        tensor.wait_ge(smg2.h, s * 16 + o - 1)
                    for r in range(8):
                        mm = tensor.matmul(pg[:], gluT[r][:, 128 * o:128 * o + 128],
                                           y_fm[r][:, 512 * s:512 * s + 512],
                                           start=(r == 0), stop=(r == 7))
                    mm.then_inc(smg1.inc(), 1)
                for cc in range(4):
                    chunk = s * 4 + cc
                    if chunk >= 1:
                        tensor.wait_ge(smo2.h, chunk)
                    for k in range(8):
                        tensor.wait_ge(smg3.h, s * 8 + k + 1)
                        lhs = s_y2[k][:, 128 * cc:128 * cc + 128]
                        tensor.matmul(p_c[:], lhs, outT[k][:, 0:512], start=(k == 0), stop=(k == 7))
                        mm = tensor.matmul(p_d[:], lhs, outT[k][:, 512:1024], start=(k == 0), stop=(k == 7))
                    mm.then_inc(smo1.inc(), 1)

        @blk.scalar
        def _(scalar):
            scalar.wait_ge(smglb.h, 16)
            for s in range(8):
                for o in range(16):
                    pg = [p_a, p_b][o % 2]
                    scalar.wait_ge(smg1.h, s * 16 + o + 1)
                    if s >= 1:
                        scalar.wait_ge(smg3.h, 8 * s)
                    if o < 8:
                        scalar.activation(s_ag[o][:], pg[:], AF.Identity,
                                          bias=s_glb[:, o:o + 1]).then_inc(smg2.inc(), 1)
                    else:
                        scalar.activation(s_ag[o][:], pg[:], AF.Sigmoid,
                                          bias=s_glb[:, o:o + 1]).then_inc(smg2.inc(), 1)

        @blk.vector
        def _(vector):
            for s in range(8):
                for k in range(8):
                    vector.wait_ge(smg2.h, s * 16 + 16)
                    if s >= 1:
                        vector.wait_ge(smo1.h, 4 * s)
                    vector.tensor_mul(s_y2[k][:], s_ag[k][:], s_ag[k + 8][:])
                    vector.drain().then_inc(smg3.inc(), 1)
                for cc in range(4):
                    chunk = s * 4 + cc
                    pp4 = chunk % 2
                    vector.wait_ge(smo1.h, chunk + 1)
                    vector.wait_ge(smf1[pp4].h, 16 * (chunk // 2 + 1))
                    vector.wait_ge(smf2[pp4].h, 16 * (chunk // 2 + 1))
                    vector.wait_ge(smf3[pp4].h, 16 * (chunk // 2 + 1))
                    if chunk >= 2:
                        vector.wait_ge(smout[pp4].h, 16 * (chunk // 2))
                    vector.tensor_scalar(s_aff[:, 0:1], s_rs[:, chunk:chunk + 1], 1.0, None, ALU.add)
                    vector.tensor_mul(s_aff[:, 1:2], s_mu[:, chunk:chunk + 1], s_rs[:, chunk:chunk + 1])
                    vector.drain()
                    vector.tensor_scalar(s_aff[:, 1:2], s_aff[:, 1:2], -1.0, None, ALU.mult)
                    vector.drain()
                    vector.tensor_scalar(s_fin[pp4][:], s_xr[pp4][:], s_aff[:, 0:1], s_aff[:, 1:2],
                                         ALU.mult, ALU.add)
                    vector.drain()
                    vector.tensor_add(s_fin[pp4][:, 0:512], s_fin[pp4][:, 0:512], p_c[:])
                    vector.tensor_add(s_fin[pp4][:, 512:1024], s_fin[pp4][:, 512:1024], p_d[:])
                    vector.drain()
                    vector.tensor_add(s_fin[pp4][:], s_fin[pp4][:], s_fp[pp4][:])
                    vector.drain()
                    vector.tensor_add(s_fin[pp4][:], s_fin[pp4][:], s_fq[pp4][:])
                    vector.drain().then_inc(smo2.inc(), 1)

        g5.close()
        g45.close()

    es.close()
    return nc


_NC_CACHE = None


def kernel(L0_attn_x, L0_attn_in_W, L0_attn_in_b, L0_attn_log_dt, L0_attn_C,
           L0_attn_log_A_real, L0_attn_A_imag, L0_attn_D_skip, L0_attn_glu_W,
           L0_attn_glu_b, L0_attn_out_W, L0_attn_out_b, L0_ffn_expert_sel,
           L0_ffn_keys, L0_ffn_values, L0_ffn_route_scale):
    global _NC_CACHE
    if _NC_CACHE is None:
        _NC_CACHE = build_nc()
    nc = _NC_CACHE

    x = np.ascontiguousarray(np.asarray(L0_attn_x, dtype=np.float32))
    in_WT = np.ascontiguousarray(np.asarray(L0_attn_in_W, np.float32).T)
    eselT_h = np.ascontiguousarray(np.asarray(L0_ffn_expert_sel, np.float32).T)
    glu_WT = np.ascontiguousarray(np.asarray(L0_attn_glu_W, np.float32).T)
    out_WT = np.ascontiguousarray(np.asarray(L0_attn_out_W, np.float32).T)
    ident = np.eye(128, dtype=ml_dtypes.bfloat16)
    t_idx = np.arange(128, dtype=np.float32)
    trQ = np.tile((t_idx + 1.0)[None, :], (128, 1))
    trK = np.tile((127.0 - t_idx)[None, :], (128, 1))
    trP = (127.0 - t_idx)[:, None].copy()
    trP2 = np.stack([128.0 - t_idx, 126.0 - t_idx], axis=1).astype(np.float32)

    in_maps = []
    for c in range(NCORES):
        b, h = c // 2, c % 2
        cs = slice(128 * c, 128 * (c + 1))
        in_maps.append(dict(
            x_in=np.ascontiguousarray(x[b, h * 4096:(h + 1) * 4096, :]),
            in_W=in_WT,
            in_b=np.asarray(L0_attn_in_b, np.float32),
            log_dt=np.ascontiguousarray(np.asarray(L0_attn_log_dt, np.float32)[cs]),
            C_p=np.ascontiguousarray(np.asarray(L0_attn_C, np.float32)[cs]),
            lar_t=np.ascontiguousarray(np.asarray(L0_attn_log_A_real, np.float32)[cs]),
            aim_t=np.ascontiguousarray(np.asarray(L0_attn_A_imag, np.float32)[cs]),
            dsk_t=np.ascontiguousarray(np.asarray(L0_attn_D_skip, np.float32)[cs]),
            glu_W=glu_WT,
            glu_b=np.asarray(L0_attn_glu_b, np.float32),
            out_W=out_WT,
            out_b=np.asarray(L0_attn_out_b, np.float32),
            esel=eselT_h,
            keys_t=np.asarray(L0_ffn_keys, np.float32),
            vals_t=np.asarray(L0_ffn_values, np.float32),
            rscale=np.asarray(L0_ffn_route_scale, np.float32),
            ident=ident, trQ_t=trQ, trK_t=trK, trP_t=trP, trP2_t=trP2,
            uoff_t=np.array([128 * c], np.int32),
            yoff_t=np.array([b * 8192 + h * 4096], np.int32),
        ))

    res = run_bass_kernel_spmd(nc, in_maps, core_ids=list(range(NCORES)))
    out = np.empty((4, 8192, 1024), np.float32)
    for c in range(NCORES):
        b, h = c // 2, c % 2
        out[b, h * 4096:(h + 1) * 4096, :] = np.asarray(res.results[c]["out"])
    return out


# version 18 (final)
# speedup vs baseline: 1.1339x; 1.0150x over previous
import numpy as np
import ml_dtypes
import concourse.bass as bass
import concourse.mybir as mybir
from concourse.ap import AP
from concourse.bass_utils import run_bass_kernel_spmd
from contextlib import ExitStack

F32 = mybir.dt.float32
BF16 = mybir.dt.bfloat16
I32 = mybir.dt.int32
AF = mybir.ActivationFunctionType
ALU = mybir.AluOpType
AX = mybir.AxisListType.X

NCORES = 8
T = 4096
D = 1024
NST = 32
TWO_PI = 6.283185307179586
INV_2PI = 0.15915494309189535
MAGIC = 12582912.0
HALF_PI = 1.5707963267948966


class Ctr:
    def __init__(self, h):
        self.h = h
        self.v = 0

    def inc(self, n=1):
        self.v += n
        return self.h


def build_nc():
    nc = bass.Bass("TRN2", target_bir_lowering=False)

    x_in = nc.dram_tensor("x_in", [T, D], F32, kind="ExternalInput")
    in_W = nc.dram_tensor("in_W", [D, D], F32, kind="ExternalInput")
    in_b = nc.dram_tensor("in_b", [D], F32, kind="ExternalInput")
    log_dt = nc.dram_tensor("log_dt", [128], F32, kind="ExternalInput")
    C_p = nc.dram_tensor("C_p", [128, NST, 2], F32, kind="ExternalInput")
    lar_t = nc.dram_tensor("lar_t", [128, NST], F32, kind="ExternalInput")
    aim_t = nc.dram_tensor("aim_t", [128, NST], F32, kind="ExternalInput")
    dsk_t = nc.dram_tensor("dsk_t", [128], F32, kind="ExternalInput")
    glu_W = nc.dram_tensor("glu_W", [D, 2 * D], F32, kind="ExternalInput")
    glu_b = nc.dram_tensor("glu_b", [2 * D], F32, kind="ExternalInput")
    out_W = nc.dram_tensor("out_W", [D, D], F32, kind="ExternalInput")
    out_b = nc.dram_tensor("out_b", [D], F32, kind="ExternalInput")
    esel = nc.dram_tensor("esel", [D, 16], F32, kind="ExternalInput")
    keys_t = nc.dram_tensor("keys_t", [16, D, 256], F32, kind="ExternalInput")
    vals_t = nc.dram_tensor("vals_t", [16, 256, D], F32, kind="ExternalInput")
    rscale = nc.dram_tensor("rscale", [1], F32, kind="ExternalInput")
    ident = nc.dram_tensor("ident", [128, 128], BF16, kind="ExternalInput")
    trQ_t = nc.dram_tensor("trQ_t", [128, 128], F32, kind="ExternalInput")
    trK_t = nc.dram_tensor("trK_t", [128, 128], F32, kind="ExternalInput")
    trP_t = nc.dram_tensor("trP_t", [128, 1], F32, kind="ExternalInput")
    trP2_t = nc.dram_tensor("trP2_t", [128, 2], F32, kind="ExternalInput")
    uoff_t = nc.dram_tensor("uoff_t", [1], I32, kind="ExternalInput")
    yoff_t = nc.dram_tensor("yoff_t", [1], I32, kind="ExternalInput")
    out_t = nc.dram_tensor("out", [T, D], F32, kind="ExternalOutput")

    xnT_d = nc.dram_tensor("xnT_d", [D, T], BF16)
    w1bf = nc.dram_tensor("w1bf", [D, D], BF16)
    glubf = nc.dram_tensor("glubf", [D, 2 * D], BF16)
    outbf = nc.dram_tensor("outbf", [D, D], BF16)
    eselbf = nc.dram_tensor("eselbf", [D, 16], BF16)
    keysbf = nc.dram_tensor("keysbf", [16, D, 256], BF16)
    valsbf = nc.dram_tensor("valsbf", [16, 256, D], BF16)
    prow = nc.dram_tensor("prow", [4, 4096], F32)
    k2r = nc.dram_tensor("k2r", [256, 128], BF16)
    u_loc = nc.dram_tensor("u_loc", [T, D], BF16)
    u_gth = nc.dram_tensor("u_gth", [NCORES * T, D], BF16, addr_space="Shared")
    y_loc = nc.dram_tensor("y_loc", [128, 4 * 64 * 128], BF16)
    y_gth = nc.dram_tensor("y_gth", [NCORES * 128, 4 * 64 * 128], BF16, addr_space="Shared")
    gT_d = nc.dram_tensor("gT_d", [16, T], BF16)
    d2_d = nc.dram_tensor("d2_d", [64, 4 * 4096], BF16)
    pp_d = nc.dram_tensor("pp_d", [128, 4 * 4096], BF16)
    ffnP = nc.dram_tensor("ffnP", [T, D], F32)
    ffnQ = nc.dram_tensor("ffnQ", [T, D], F32)

    es = ExitStack()

    def sbuf(name, shape, dt, st=None):
        return (st or es).enter_context(nc.sbuf_tensor(name, shape, dt))

    def sem(name):
        return Ctr(es.enter_context(nc.semaphore(name)))

    s_id = sbuf("s_id", [128, 128], BF16)
    s_trP = sbuf("s_trP", [128, 1], F32)
    s_trP2 = sbuf("s_trP2", [128, 2], F32)
    s_hpi = sbuf("s_hpi", [128, 1], F32)
    s_mu = sbuf("s_mu", [128, 32], F32)
    s_rs = sbuf("s_rs", [128, 32], F32)
    s_inb = sbuf("s_inb", [128, D], F32)
    s_outb = sbuf("s_outb", [128, D], F32)
    s_glb = sbuf("s_glb", [128, 16], F32)
    s_g = sbuf("s_g", [128, 32 * 16], F32)
    s_rsc = sbuf("s_rsc", [128, 1], F32)
    s_uoff = sbuf("s_uoff", [1, 1], I32)
    s_yoff = sbuf("s_yoff", [1, 1], I32)
    s_q2dr = sbuf("s_q2dr", [64, 128], F32)
    s_q2di = sbuf("s_q2di", [64, 128], F32)
    s_q2er = sbuf("s_q2er", [64, 128], F32)
    s_q2ei = sbuf("s_q2ei", [64, 128], F32)
    s_W1 = sbuf("s_W1", [64, 128], F32)
    s_W2 = sbuf("s_W2", [64, 128], F32)
    s_W3 = sbuf("s_W3", [64, 128], F32)

    g2a = ExitStack()
    s_xE = [sbuf("s_x0", [128, D], F32, g2a), sbuf("s_x1", [128, D], F32, g2a)]

    p_a = es.enter_context(nc.psum_tensor("p_a", [128, 512], F32))
    p_b = es.enter_context(nc.psum_tensor("p_b", [128, 512], F32))
    p_c = es.enter_context(nc.psum_tensor("p_c", [128, 512], F32))
    p_d = es.enter_context(nc.psum_tensor("p_d", [128, 512], F32))
    p_sel = es.enter_context(nc.psum_tensor("p_sel", [128, 16], F32))
    p_tr = es.enter_context(nc.psum_tensor("p_tr", [128, 128], BF16))
    p_tr2 = es.enter_context(nc.psum_tensor("p_tr2", [128, 128], BF16))
    p_tr3 = es.enter_context(nc.psum_tensor("p_tr3", [128, 128], BF16))

    smc = sem("smc"); smprm = sem("smprm")
    smcastA = sem("smcastA"); smcastB = sem("smcastB"); smcastC = sem("smcastC")
    sm0 = [sem(f"sm0_{i}") for i in range(20)]
    smx = [sem("smx0"), sem("smx1")]
    smln = sem("smln"); smrs = sem("smrs"); smlv = sem("smlv")
    smtp = sem("smtp"); smcpy = sem("smcpy"); smxst = sem("smxst")
    smyt1 = sem("smyt1"); smytr = sem("smytr")
    smw2 = sem("smw2")
    smu1 = sem("smu1"); smu2 = sem("smu2")
    smu3 = [sem("smu3A"), sem("smu3B")]
    smlg = sem("smlg"); smsg = sem("smsg"); smtk = sem("smtk")
    smgt = sem("smgt"); smge = sem("smge"); smgo = sem("smgo")
    smglb = sem("smglb"); sminb = sem("sminb")
    smcc = sem("smcc"); smucs = sem("smucs")
    sme1 = sem("sme1"); sme2 = sem("sme2")
    smr1 = sem("smr1"); smra = sem("smra")
    smkv = [sem("smkv0"), sem("smkv1")]
    smxw2 = [sem("smxw2_0"), sem("smxw2_1")]
    smgx = sem("smgx")
    smk1 = sem("smk1"); smk2 = sem("smk2"); smk3 = sem("smk3")
    smv1 = sem("smv1"); smv2 = sem("smv2")
    smv3 = [sem("smv3A"), sem("smv3B")]
    smyf = sem("smyf"); smwt3 = sem("smwt3")
    smg1 = sem("smg1"); smg2 = sem("smg2"); smg3 = sem("smg3")
    smo1 = sem("smo1"); smo2 = sem("smo2")
    smf1 = [sem("smf1A"), sem("smf1B")]
    smf2 = [sem("smf2A"), sem("smf2B")]
    smf3 = [sem("smf3A"), sem("smf3B")]
    smout = [sem("smoutA"), sem("smoutB")]

    def wrap_sin_cos(vector, src_ang, dst_sin_arg, dst_cos_arg):
        vector.tensor_scalar(dst_sin_arg, src_ang, INV_2PI, MAGIC, ALU.mult, ALU.add)
        vector.drain()
        vector.tensor_scalar(dst_sin_arg, dst_sin_arg, MAGIC, None, ALU.subtract)
        vector.drain()
        vector.scalar_tensor_tensor(dst_sin_arg, dst_sin_arg, -TWO_PI, src_ang, ALU.mult, ALU.add)
        vector.drain()
        vector.tensor_scalar_min(dst_sin_arg, dst_sin_arg, 3.141592653589793)
        vector.drain()
        vector.tensor_scalar_max(dst_sin_arg, dst_sin_arg, -3.141592653589793)
        vector.drain()

    def act_sin_cos(scalar, wrapped, cos_out, pcount):
        scalar.activation(cos_out, wrapped, AF.Abs)
        scalar.drain()
        scalar.activation(cos_out, cos_out, AF.Sin, bias=s_hpi[0:pcount, 0:1], scale=-1.0)
        return scalar.activation(wrapped, wrapped, AF.Sin)

    g0 = ExitStack()
    s_dtc = sbuf("s_dtc", [128, 1], F32, g0)
    s_lar = sbuf("s_lar", [128, NST], F32, g0)
    s_aim = sbuf("s_aim", [128, NST], F32, g0)
    s_cr = sbuf("s_cr", [128, NST], F32, g0)
    s_ci = sbuf("s_ci", [128, NST], F32, g0)
    s_cri = sbuf("s_cri", [128, 2 * NST], F32, g0)
    s_dskc = sbuf("s_dskc", [128, 1], F32, g0)
    s_ema = sbuf("s_ema", [128, NST], F32, g0)
    s_dtar = sbuf("s_dtar", [128, NST], F32, g0)
    s_dtai = sbuf("s_dtai", [128, NST], F32, g0)
    s_absw = sbuf("s_absw", [128, NST], F32, g0)
    s_ang = sbuf("s_ang", [128, NST], F32, g0)
    s_ang2 = sbuf("s_ang2", [128, NST], F32, g0)
    s_sinw = sbuf("s_sinw", [128, NST], F32, g0)
    s_cosw = sbuf("s_cosw", [128, NST], F32, g0)
    s_t1 = sbuf("s_t1", [128, NST], F32, g0)
    s_t2 = sbuf("s_t2", [128, NST], F32, g0)
    s_t3 = sbuf("s_t3", [128, NST], F32, g0)
    s_t4 = sbuf("s_t4", [128, NST], F32, g0)
    s_den = sbuf("s_den", [128, NST], F32, g0)
    s_trr = sbuf("s_trr", [128, NST], F32, g0)
    s_tii = sbuf("s_tii", [128, NST], F32, g0)
    s_er = sbuf("s_er", [128, NST], F32, g0)
    s_ei = sbuf("s_ei", [128, NST], F32, g0)
    s_wa = sbuf("s_wa", [64, 128], F32, g0)
    s_wb = sbuf("s_wb", [64, 128], F32, g0)

    with nc.Block() as blk:
        @blk.sync
        def _(sync):
            sync.dma_start(out=s_xE[0][:], in_=AP(x_in, 0, [[D, 128], [1, D]])).then_inc(smx[0].inc(16), 16)
            sync.dma_start(out=s_id[:], in_=ident.ap()).then_inc(smc.inc(16), 16)
            sync.dma_start(out=s_trP[:], in_=trP_t.ap()).then_inc(smc.inc(16), 16)
            sync.dma_start(out=s_trP2[:], in_=trP2_t.ap()).then_inc(smc.inc(16), 16)
            sync.dma_start(out=s_inb[:], in_=AP(in_b, 0, [[0, 128], [1, D]])).then_inc(sminb.inc(16), 16)
            sync.dma_start(out=s_outb[:], in_=AP(out_b, 0, [[0, 128], [1, D]])).then_inc(sminb.inc(16), 16)
            sync.dma_start(out=s_rsc[:], in_=AP(rscale, 0, [[0, 128], [1, 1]])).then_inc(smc.inc(16), 16)
            sync.dma_start(out=s_uoff[:], in_=uoff_t.ap().unsqueeze(0)).then_inc(smc.inc(16), 16)
            sync.dma_start(out=s_yoff[:], in_=yoff_t.ap().unsqueeze(0)).then_inc(smc.inc(16), 16)

        @blk.gpsimd
        def _(gp):
            gp.dma_start(out=s_xE[1][:], in_=AP(x_in, 128 * D, [[D, 128], [1, D]])).then_inc(smx[1].inc(16), 16)
            gp.dma_start(out=s_dtc[:], in_=log_dt.ap().unsqueeze(1)).then_inc(smprm.inc(16), 16)
            gp.dma_start(out=s_lar[:], in_=lar_t.ap()).then_inc(smprm.inc(16), 16)
            gp.dma_start(out=s_aim[:], in_=aim_t.ap()).then_inc(smprm.inc(16), 16)
            gp.dma_start(out=s_cri[:], in_=AP(C_p, 0, [[NST * 2, 128], [1, NST * 2]])).then_inc(smprm.inc(16), 16)
            gp.dma_start(out=s_dskc[:], in_=dsk_t.ap().unsqueeze(1)).then_inc(smprm.inc(16), 16)
            gp.dma_start(out=w1bf.ap(), in_=in_W.ap()).then_inc(smcastA.inc(16), 16)
            gp.dma_start(out=eselbf.ap(), in_=esel.ap()).then_inc(smcastA.inc(16), 16)
            gp.wait_ge(smx[1].h, 16)
            gp.dma_start(out=glubf.ap(), in_=glu_W.ap()).then_inc(smcastB.inc(16), 16)
            gp.dma_start(out=outbf.ap(), in_=out_W.ap()).then_inc(smcastB.inc(16), 16)
            gp.dma_start(out=keysbf.ap(), in_=keys_t.ap()).then_inc(smcastC.inc(16), 16)
            gp.dma_start(out=valsbf.ap(), in_=vals_t.ap()).then_inc(smcastC.inc(16), 16)

        @blk.scalar
        def _(scalar):
            scalar.wait_ge(smprm.h, smprm.v)
            scalar.activation(s_dtc[:], s_dtc[:], AF.Exp)
            scalar.drain()
            scalar.activation(s_ema[:], s_lar[:], AF.Exp).then_inc(sm0[0].inc(), 1)

        @blk.vector
        def _(vector):
            vector.wait_ge(sm0[0].h, 1)
            vector.tensor_copy(s_cr[:], AP(s_cri[:].tensor, 0, [[2 * NST, 128], [2, NST]]))
            vector.tensor_copy(s_ci[:], AP(s_cri[:].tensor, 1, [[2 * NST, 128], [2, NST]]))
            vector.memset(s_hpi[:], HALF_PI)
            vector.drain()
            vector.tensor_scalar(s_dtar[:], s_ema[:], s_dtc[:, 0:1], -1.0, ALU.mult, ALU.mult)
            vector.tensor_scalar(s_dtai[:], s_aim[:], s_dtc[:, 0:1], None, ALU.mult)
            vector.drain()
            wrap_sin_cos(vector, s_dtai[:], s_ang[:], s_ang2[:])
            vector.tensor_copy(s_t1[:], s_dtar[:])
            vector.drain().then_inc(sm0[1].inc(), 1)

        @blk.scalar
        def _(scalar):
            scalar.wait_ge(sm0[1].h, 1)
            scalar.activation(s_absw[:], s_dtar[:], AF.Exp)
            act_sin_cos(scalar, s_ang[:], s_cosw[:], 128)
            scalar.drain()
            scalar.activation(s_sinw[:], s_ang[:], AF.Identity).then_inc(sm0[2].inc(), 1)

        @blk.vector
        def _(vector):
            vector.wait_ge(sm0[2].h, 1)
            vector.tensor_mul(s_t1[:], s_absw[:], s_cosw[:])
            vector.tensor_mul(s_t2[:], s_absw[:], s_sinw[:])
            vector.tensor_mul(s_den[:], s_ema[:], s_ema[:])
            vector.tensor_mul(s_t3[:], s_aim[:], s_aim[:])
            vector.drain()
            vector.tensor_add(s_den[:], s_den[:], s_t3[:])
            vector.drain()
            vector.reciprocal(s_den[:], s_den[:])
            vector.tensor_scalar(s_t1[:], s_t1[:], -1.0, None, ALU.add)
            vector.drain()
            vector.tensor_mul(s_t3[:], s_t1[:], s_ema[:])
            vector.tensor_mul(s_t4[:], s_t2[:], s_aim[:])
            vector.drain()
            vector.tensor_sub(s_trr[:], s_t4[:], s_t3[:])
            vector.drain()
            vector.tensor_mul(s_t3[:], s_t2[:], s_ema[:])
            vector.tensor_mul(s_t4[:], s_t1[:], s_aim[:])
            vector.drain()
            vector.tensor_add(s_tii[:], s_t3[:], s_t4[:])
            vector.drain()
            vector.tensor_scalar(s_tii[:], s_tii[:], -1.0, None, ALU.mult)
            vector.drain()
            vector.tensor_mul(s_t3[:], s_cr[:], s_trr[:])
            vector.tensor_mul(s_t4[:], s_ci[:], s_tii[:])
            vector.drain()
            vector.tensor_sub(s_er[:], s_t3[:], s_t4[:])
            vector.drain()
            vector.tensor_mul(s_t3[:], s_cr[:], s_tii[:])
            vector.tensor_mul(s_t4[:], s_ci[:], s_trr[:])
            vector.drain()
            vector.tensor_add(s_ei[:], s_t3[:], s_t4[:])
            vector.drain()
            vector.tensor_mul(s_er[:], s_er[:], s_den[:])
            vector.tensor_mul(s_ei[:], s_ei[:], s_den[:])
            vector.drain().then_inc(sm0[3].inc(), 1)

        @blk.sync
        def _(sync):
            sync.wait_ge(sm0[3].h, 1)
            for row, src in [(0, s_dtar), (1, s_dtai), (2, s_er), (3, s_ei)]:
                sync.dma_start(out=AP(prow, row * 4096, [[NST, 128], [1, NST]]),
                               in_=src[:]).then_inc(sm0[4].inc(16), 16)
            sync.wait_ge(sm0[4].h, sm0[4].v)
            for dst, row in [(s_q2dr, 0), (s_q2di, 1), (s_q2er, 2), (s_q2ei, 3)]:
                sync.dma_start(out=dst[0:32, :], in_=AP(prow, row * 4096, [[1, NST], [NST, 128]]),
                               allow_slow_non_contiguous=True).then_inc(sm0[5].inc(16), 16)
                sync.dma_start(out=dst[32:64, :], in_=AP(prow, row * 4096, [[1, NST], [NST, 128]]),
                               allow_slow_non_contiguous=True).then_inc(sm0[5].inc(16), 16)

        g2 = ExitStack()
        s_x = s_xE
        s_xsq = sbuf("s_xsq", [128, D], F32, g2)
        s_xn = [sbuf("s_xn0", [128, D], BF16, g2), sbuf("s_xn1", [128, D], BF16, g2)]
        s_st = sbuf("s_st", [128, 8], F32, g2)
        w1t = [sbuf(f"w1t{k}", [128, D], BF16, g2) for k in range(8)]
        selT = sbuf("selT", [128, 8 * 16], BF16, g2)
        s_xg = [[sbuf(f"s_xg{g}_{k}", [128, 1024], BF16, g2) for k in range(8)] for g in range(2)]
        s_ust = [sbuf(f"s_ust{pp}", [128, D], BF16, g2) for pp in range(2)]
        s_lgo = sbuf("s_lgo", [128, 16], F32, g2)
        s_lpr = sbuf("s_lpr", [128, 16], F32, g2)
        s_lgw = sbuf("s_lgw", [128, 16], F32, g2)
        s_lmx = sbuf("s_lmx", [128, 1], F32, g2)
        s_leq = sbuf("s_leq", [128, 16], F32, g2)
        s_lmk = sbuf("s_lmk", [128, 16], F32, g2)
        s_lsum = sbuf("s_lsum", [128, 1], F32, g2)
        s_gbf = sbuf("s_gbf", [128, 32 * 16], BF16, g2)
        s_gT = sbuf("s_gT", [16, T], BF16, g2)

        @blk.sync
        def _(sync):
            sync.wait_ge(smcastA.h, 32)
            for k in range(8):
                sync.dma_start(out=w1t[k][:],
                               in_=AP(w1bf, 128 * k * D, [[D, 128], [1, D]])).then_inc(smw2.inc(16), 16)
                sync.dma_start(out=selT[:, 16 * k:16 * k + 16],
                               in_=AP(eselbf, 128 * k * 16, [[16, 128], [1, 16]])).then_inc(smw2.inc(16), 16)
            for i in range(32):
                if i + 2 < 32:
                    sync.wait_ge(smlv.h, i + 1)
                    sync.dma_start(out=s_x[i % 2][:],
                                   in_=AP(x_in, (i + 2) * 128 * D, [[D, 128], [1, D]])).then_inc(smx[i % 2].inc(16), 16)
                sync.wait_ge(smu2.h, i + 1)
                sync.dma_start(out=AP(u_loc, i * 128 * D, [[D, 128], [1, D]]),
                               in_=s_ust[i % 2][:]).then_inc(smu3[i % 2].inc(16), 16)
                if i % 8 == 7:
                    g = i // 8
                    sync.wait_ge(smcpy.h, 8 * (i + 1))
                    for k in range(8):
                        sync.dma_start(out=AP(xnT_d, (128 * k) * T + 1024 * g, [[T, 128], [1, 1024]]),
                                       in_=s_xg[g % 2][k][:]).then_inc(smxst.inc(16), 16)
            sync.wait_ge(smge.h, 32)
            sync.dma_start(out=gT_d.ap(), in_=s_gT[:]).then_inc(smgo.inc(16), 16)

        @blk.tensor
        def _(tensor):
            tensor.wait_ge(smc.h, smc.v)
            for i in range(32):
                tensor.wait_ge(smlv.h, i + 1)
                for k in range(8):
                    t_idx = 8 * i + k
                    if t_idx >= 2:
                        tensor.wait_ge(smcpy.h, t_idx - 1)
                    tensor.transpose([p_tr, p_tr2][k % 2][:], s_xn[i % 2][:, 128 * k:128 * k + 128],
                                     s_id[:]).then_inc(smtp.inc(), 1)
                if i == 0:
                    tensor.wait_ge(smw2.h, 256)
                tensor.wait_ge(smcpy.h, 8 * (i + 1))
                if i >= 1:
                    tensor.wait_ge(smu2.h, i)
                pu = [(p_a, p_b), (p_c, p_d)][i % 2]
                col = 128 * (i % 8)
                gb = (i // 8) % 2
                for k in range(8):
                    lhs = s_xg[gb][k][:, col:col + 128]
                    tensor.matmul(pu[0][:], lhs, w1t[k][:, 0:512], start=(k == 0), stop=(k == 7))
                    tensor.matmul(pu[1][:], lhs, w1t[k][:, 512:1024], start=(k == 0), stop=(k == 7))
                    mm = tensor.matmul(p_sel[:], lhs, selT[:, 16 * k:16 * k + 16],
                                       start=(k == 0), stop=(k == 7))
                mm.then_inc(smu1.inc(), 1)
            for ch in range(32):
                tensor.wait_ge(smtk.h, ch + 1)
                if ch >= 1:
                    tensor.wait_ge(smge.h, ch)
                tensor.transpose(p_tr3[0:16, :], s_gbf[:, 16 * ch:16 * ch + 16], s_id[:]).then_inc(smgt.inc(), 1)

        @blk.scalar
        def _(scalar):
            for i in range(32):
                scalar.wait_ge(smln.h, i + 1)
                scalar.activation(s_st[:, 5:6], s_st[:, 4:5], AF.Ln)
                scalar.drain()
                scalar.activation(s_rs[:, i:i + 1], s_st[:, 5:6], AF.Exp, scale=-0.5).then_inc(smrs.inc(), 1)
                g = i // 8
                if i % 8 == 0 and g >= 2:
                    scalar.wait_ge(smxst.h, 128 * (g - 1))
                for k in range(8):
                    scalar.wait_ge(smtp.h, 8 * i + k + 1)
                    scalar.activation(s_xg[g % 2][k][:, 128 * (i % 8):128 * (i % 8) + 128],
                                      [p_tr, p_tr2][k % 2][:], AF.Identity).then_inc(smcpy.inc(), 1)
                if i >= 1:
                    scalar.wait_ge(smlg.h, i)
                    scalar.activation(s_lpr[:], s_lgo[:], AF.Sigmoid).then_inc(smsg.inc(), 1)
            scalar.wait_ge(smlg.h, 32)
            scalar.activation(s_lpr[:], s_lgo[:], AF.Sigmoid).then_inc(smsg.inc(), 1)

        @blk.vector
        def _(vector):
            vector.wait_ge(smc.h, smc.v)

            def evac_topk(j):
                if j == 0:
                    vector.wait_ge(sminb.h, 32)
                vector.wait_ge(smu1.h, j + 1)
                if j >= 2:
                    vector.wait_ge(smu3[j % 2].h, 16 * (j // 2))
                pu = [(p_a, p_b), (p_c, p_d)][j % 2]
                vector.tensor_add(s_ust[j % 2][:, 0:512], pu[0][:], s_inb[:, 0:512])
                vector.tensor_add(s_ust[j % 2][:, 512:1024], pu[1][:], s_inb[:, 512:1024])
                vector.tensor_copy(s_lgo[:], p_sel[:])
                vector.drain().then_inc(smu2.inc(), 1)
                vector.tensor_copy(s_lgw[:], s_lgo[:])
                vector.memset(s_lmk[:], 0.0)
                vector.drain().then_inc(smlg.inc(), 1)
                for r in range(4):
                    vector.reduce_max(s_lmx[:], s_lgw[:], axis=AX)
                    vector.drain()
                    vector.tensor_scalar(s_leq[:], s_lgw[:], s_lmx[:, 0:1], None, ALU.is_ge)
                    vector.drain()
                    vector.tensor_add(s_lmk[:], s_lmk[:], s_leq[:])
                    vector.scalar_tensor_tensor(s_lgw[:], s_leq[:], -1e30, s_lgw[:], ALU.mult, ALU.add)
                    vector.drain()
                vector.wait_ge(smsg.h, j + 1)
                vector.tensor_mul(s_lpr[:], s_lpr[:], s_lmk[:])
                vector.drain()
                vector.reduce_sum(s_lsum[:], s_lpr[:], axis=AX)
                vector.drain()
                vector.tensor_scalar(s_lsum[:], s_lsum[:], 1e-9, None, ALU.max)
                vector.drain()
                vector.reciprocal(s_lsum[:], s_lsum[:])
                vector.drain()
                vector.tensor_mul(s_lsum[:], s_lsum[:], s_rsc[:])
                vector.drain()
                vector.tensor_scalar(s_g[:, 16 * j:16 * j + 16], s_lpr[:], s_lsum[:, 0:1], None, ALU.mult)
                vector.drain()
                vector.tensor_copy(s_gbf[:, 16 * j:16 * j + 16], s_g[:, 16 * j:16 * j + 16])
                vector.drain().then_inc(smtk.inc(), 1)

            for i in range(33):
                if i < 32:
                    vector.wait_ge(smx[i % 2].h, 16 * (i // 2 + 1))
                    if i >= 2:
                        vector.wait_ge(smtp.h, 8 * i - 8)
                    xb = s_x[i % 2]
                    vector.tensor_mul(s_xsq[:], xb[:], xb[:])
                    vector.reduce_sum(s_st[:, 0:1], xb[:], axis=AX)
                    vector.drain()
                    vector.reduce_sum(s_st[:, 1:2], s_xsq[:], axis=AX)
                    vector.drain()
                    vector.tensor_scalar(s_mu[:, i:i + 1], s_st[:, 0:1], 1.0 / D, None, ALU.mult)
                    vector.tensor_scalar(s_st[:, 2:3], s_st[:, 1:2], 1.0 / D, None, ALU.mult)
                    vector.drain()
                    vector.tensor_mul(s_st[:, 3:4], s_mu[:, i:i + 1], s_mu[:, i:i + 1])
                    vector.drain()
                    vector.tensor_sub(s_st[:, 4:5], s_st[:, 2:3], s_st[:, 3:4])
                    vector.drain()
                    vector.tensor_scalar(s_st[:, 4:5], s_st[:, 4:5], 1e-5, None, ALU.add)
                    vector.drain().then_inc(smln.inc(), 1)
                    vector.wait_ge(smrs.h, i + 1)
                    vector.tensor_scalar(s_xn[i % 2][:], xb[:], s_mu[:, i:i + 1], s_rs[:, i:i + 1],
                                         ALU.subtract, ALU.mult)
                    vector.drain().then_inc(smlv.inc(), 1)
                if i >= 1:
                    evac_topk(i - 1)
            for ch in range(32):
                vector.wait_ge(smgt.h, ch + 1)
                vector.tensor_copy(s_gT[:, 128 * ch:128 * ch + 128], p_tr3[0:16, :])
                vector.drain().then_inc(smge.inc(), 1)


        @blk.vector
        def _(vector):
            vector.wait_ge(sm0[5].h, sm0[5].v)
            vector.tensor_scalar(s_W1[:], s_q2dr[:], 128.0, None, ALU.mult)
            vector.tensor_scalar(s_W2[:], s_q2di[:], 128.0, None, ALU.mult)
            vector.drain()
            wrap_sin_cos(vector, s_W2[:], s_wa[:], s_wb[:])
            vector.drain().then_inc(sm0[6].inc(), 1)

        @blk.scalar
        def _(scalar):
            scalar.wait_ge(sm0[6].h, 1)
            scalar.activation(s_W1[:], s_W1[:], AF.Exp)
            act_sin_cos(scalar, s_wa[:], s_wb[:], 64).then_inc(sm0[7].inc(), 1)

        @blk.vector
        def _(vector):
            vector.wait_ge(sm0[7].h, 1)
            vector.tensor_mul(s_wa[:], s_W1[:], s_wa[:])
            vector.drain()
            vector.tensor_mul(s_W1[:], s_W1[:], s_wb[:])
            vector.tensor_scalar(s_W2[0:32, :], s_wa[0:32, :], -1.0, None, ALU.mult)
            vector.tensor_copy(s_W2[32:64, :], s_wa[32:64, :])
            vector.tensor_copy(s_W3[0:32, :], s_wa[0:32, :])
            vector.tensor_scalar(s_W3[32:64, :], s_wa[32:64, :], -1.0, None, ALU.mult)
            vector.drain().then_inc(sm0[8].inc(), 1)

        gB = ExitStack()
        s_ba = sbuf("s_ba", [128, 4096], F32, gB)
        s_bb = sbuf("s_bb", [128, 4096], F32, gB)
        s_bc = sbuf("s_bc", [128, 4096], F32, gB)
        s_bd = sbuf("s_bd", [128, 4096], F32, gB)
        s_trK = sbuf("s_trK", [128, 128], F32, gB)
        s_ksb = sbuf("s_ksb", [128, 128], F32, gB)
        s_kbf = sbuf("s_kbf", [128, 128], BF16, gB)
        s_ktr = sbuf("s_ktr", [128, 128], BF16, gB)
        s_zro = sbuf("s_zro", [128, 128], BF16, gB)
        s_trQ0 = sbuf("s_trQ0", [128, 128], F32, gB)
        s_d2s = sbuf("s_d2s", [64, 4096], BF16, gB)
        s_ppt = sbuf("s_ppt", [128, 4096], BF16, gB)

        @blk.sync
        def _(sync):
            sync.wait_ge(sm0[3].h, 1)
            sync.wait_ge(sm0[4].h, sm0[4].v)
            sync.dma_start(out=s_trK[:], in_=trK_t.ap()).then_inc(sm0[9].inc(16), 16)
            sync.dma_start(out=s_ba[:], in_=AP(prow, 0, [[0, 128], [1, 4096]])).then_inc(sm0[9].inc(16), 16)
            sync.dma_start(out=s_bb[:], in_=AP(prow, 4096, [[0, 128], [1, 4096]])).then_inc(sm0[9].inc(16), 16)
            sync.wait_ge(sm0[12].h, 1)
            sync.dma_start(out=AP(pp_d, 0, [[16384, 128], [1, 4096]]),
                           in_=s_ppt[:]).then_inc(sm0[19].inc(16), 16)
            sync.wait_ge(sm0[12].h, 2)
            sync.dma_start(out=AP(pp_d, 4096, [[16384, 128], [1, 4096]]),
                           in_=s_ppt[:]).then_inc(sm0[19].inc(16), 16)

        @blk.vector
        def _(vector):
            vector.wait_ge(smc.h, smc.v)
            vector.wait_ge(sm0[9].h, 48)
            vector.tensor_scalar(s_ba[:], s_ba[:], s_trP[:, 0:1], None, ALU.mult)
            vector.tensor_scalar(s_bb[:], s_bb[:], s_trP[:, 0:1], None, ALU.mult)
            vector.drain()
            wrap_sin_cos(vector, s_bb[:], s_bc[:], s_bd[:])
            vector.tensor_copy(s_t2[:, 0:1], s_t1[:, 0:1])
            vector.drain().then_inc(sm0[10].inc(), 1)
            vector.wait_ge(sm0[11].h, 1)
            vector.tensor_mul(s_ppt[:], s_ba[:], s_bd[:])
            vector.drain()
            vector.tensor_copy(s_t2[:, 1:2], s_t1[:, 0:1])
            vector.drain().then_inc(sm0[12].inc(), 1)
            vector.wait_ge(sm0[19].h, 16)
            vector.tensor_mul(s_ppt[:], s_ba[:], s_bc[:])
            vector.drain()
            vector.tensor_copy(s_t2[:, 2:3], s_t1[:, 0:1])
            vector.drain().then_inc(sm0[12].inc(), 1)

        @blk.scalar
        def _(scalar):
            scalar.wait_ge(sm0[10].h, 1)
            scalar.activation(s_ba[:], s_ba[:], AF.Exp)
            act_sin_cos(scalar, s_bc[:], s_bd[:], 128).then_inc(sm0[11].inc(), 1)


        @blk.vector
        def _(vector):
            vector.wait_ge(sm0[12].h, 2)
            a_dr = s_dtar[:].unsqueeze(2).broadcast_to([128, 32, 128])
            a_di = s_dtai[:].unsqueeze(2).broadcast_to([128, 32, 128])
            a_tk = s_trK[:].unsqueeze(1).broadcast_to([128, 32, 128])
            vector.tensor_tensor(s_ba[:].rearrange("p (n m) -> p n m", m=128), a_dr, a_tk, ALU.mult)
            vector.tensor_tensor(s_bb[:].rearrange("p (n m) -> p n m", m=128), a_di, a_tk, ALU.mult)
            vector.drain()
            wrap_sin_cos(vector, s_bb[:], s_bc[:], s_bd[:])
            vector.tensor_copy(s_t2[:], s_t1[:])
            vector.drain().then_inc(sm0[13].inc(), 1)

        @blk.scalar
        def _(scalar):
            scalar.wait_ge(sm0[13].h, 1)
            scalar.activation(s_ba[:], s_ba[:], AF.Exp)
            act_sin_cos(scalar, s_bc[:], s_bd[:], 128).then_inc(sm0[14].inc(), 1)

        @blk.vector
        def _(vector):
            vector.wait_ge(sm0[14].h, 1)
            vector.tensor_mul(s_bd[:], s_ba[:], s_bd[:])
            vector.tensor_mul(s_bc[:], s_ba[:], s_bc[:])
            vector.drain()
            e_r = s_er[:].unsqueeze(2).broadcast_to([128, 32, 128])
            e_i = s_ei[:].unsqueeze(2).broadcast_to([128, 32, 128])
            v_ba = s_ba[:].rearrange("p (n m) -> p n m", m=128)
            v_bb = s_bb[:].rearrange("p (n m) -> p n m", m=128)
            v_bc = s_bc[:].rearrange("p (n m) -> p n m", m=128)
            v_bd = s_bd[:].rearrange("p (n m) -> p n m", m=128)
            vector.tensor_tensor(v_ba, e_r, v_bd, ALU.mult)
            vector.tensor_tensor(v_bb, e_i, v_bc, ALU.mult)
            vector.drain()
            vector.tensor_sub(s_ba[:], s_ba[:], s_bb[:])
            vector.drain()
            vector.reduce_sum(s_ksb[:].unsqueeze(2),
                              AP(s_ba[:].tensor, 0, [[4096, 128], [1, 128], [128, 32]]),
                              axis=AX)
            vector.drain()
            vector.tensor_scalar(s_ksb[:], s_ksb[:], 2.0, None, ALU.mult)
            vector.drain()
            vector.tensor_add(s_ksb[:, 127:128], s_ksb[:, 127:128], s_dskc[:])
            vector.drain()
            vector.tensor_copy(s_kbf[:], s_ksb[:])
            vector.drain()
            vector.memset(s_zro[:], 0.0)
            vector.drain().then_inc(sm0[15].inc(), 1)

        @blk.tensor
        def _(tensor):
            tensor.wait_ge(sm0[15].h, 1)
            tensor.wait_ge(smc.h, smc.v)
            tensor.transpose(p_tr[:], s_kbf[:], s_id[:]).then_inc(sm0[16].inc(), 1)

        @blk.vector
        def _(vector):
            vector.wait_ge(sm0[16].h, 1)
            vector.tensor_copy(s_ktr[:], p_tr[:])
            vector.drain().then_inc(sm0[17].inc(), 1)

        @blk.sync
        def _(sync):
            sync.wait_ge(sm0[17].h, 1)
            sync.dma_start(out=AP(k2r, 129 * 128, [[128, 127], [1, 128]]),
                           in_=s_zro[0:127, :]).then_inc(sm0[18].inc(16), 16)
            sync.dma_start(out=AP(k2r, 0, [[128, 1], [1, 128]]),
                           in_=s_zro[0:1, :]).then_inc(sm0[18].inc(16), 16)
            sync.dma_start(out=AP(k2r, 128, [[128, 128], [1, 128]]),
                           in_=s_ktr[:]).then_inc(sm0[18].inc(16), 16)
            sync.dma_start(out=s_trQ0[:], in_=trQ_t.ap()).then_inc(sm0[18].inc(16), 16)

        smdb = sem("smdb"); smdbs = sem("smdbs"); smdbv = sem("smdbv"); smdbo = sem("smdbo")
        s_P2 = sbuf("s_P2", [64, 128], F32, gB)
        s_Q2 = sbuf("s_Q2", [64, 128], F32, gB)

        @blk.vector
        def _(vector):
            vector.wait_ge(sm0[15].h, 1)
            vector.wait_ge(sm0[18].h, sm0[18].v)
            vector.tensor_copy(s_P2[0:32, :], s_q2er[0:32, :])
            vector.tensor_scalar(s_P2[32:64, :], s_q2ei[32:64, :], -1.0, None, ALU.mult)
            vector.tensor_scalar(s_Q2[0:32, :], s_q2ei[0:32, :], -1.0, None, ALU.mult)
            vector.tensor_scalar(s_Q2[32:64, :], s_q2er[32:64, :], -1.0, None, ALU.mult)
            vector.drain()
            for slab in range(4):
                if slab >= 1:
                    vector.wait_ge(smdbo.h, 16 * slab)
                dr_b = s_q2dr[:, 32 * slab:32 * slab + 32].unsqueeze(2).broadcast_to([64, 32, 128])
                di_b = s_q2di[:, 32 * slab:32 * slab + 32].unsqueeze(2).broadcast_to([64, 32, 128])
                tq_b = s_trQ0[0:64, :].unsqueeze(1).broadcast_to([64, 32, 128])
                vector.tensor_tensor(s_ba[0:64, :].rearrange("p (c t) -> p c t", t=128), dr_b, tq_b, ALU.mult)
                vector.tensor_tensor(s_bb[0:64, :].rearrange("p (c t) -> p c t", t=128), di_b, tq_b, ALU.mult)
                vector.drain()
                wrap_sin_cos(vector, s_bb[0:64, :], s_bc[0:64, :], s_bd[0:64, :])
                vector.tensor_copy(s_t2[:, 3:4], s_t1[:, 0:1])
                vector.drain().then_inc(smdb.inc(), 1)
                vector.wait_ge(smdbs.h, slab + 1)
                vector.tensor_mul(s_bd[0:64, :], s_ba[0:64, :], s_bd[0:64, :])
                vector.tensor_mul(s_bc[0:64, :], s_ba[0:64, :], s_bc[0:64, :])
                vector.drain()
                p_b_ = s_P2[:, 32 * slab:32 * slab + 32].unsqueeze(2).broadcast_to([64, 32, 128])
                q_b_ = s_Q2[:, 32 * slab:32 * slab + 32].unsqueeze(2).broadcast_to([64, 32, 128])
                v_ba = s_ba[0:64, :].rearrange("p (c t) -> p c t", t=128)
                v_bb = s_bb[0:64, :].rearrange("p (c t) -> p c t", t=128)
                v_bc = s_bc[0:64, :].rearrange("p (c t) -> p c t", t=128)
                v_bd = s_bd[0:64, :].rearrange("p (c t) -> p c t", t=128)
                vector.tensor_tensor(v_ba, p_b_, v_bd, ALU.mult)
                vector.tensor_tensor(v_bb, q_b_, v_bc, ALU.mult)
                vector.drain()
                vector.tensor_add(s_ba[0:64, :], s_ba[0:64, :], s_bb[0:64, :])
                vector.drain()
                vector.tensor_scalar(s_d2s[:], s_ba[0:64, :], 2.0, None, ALU.mult)
                vector.drain().then_inc(smdbv.inc(), 1)

        @blk.scalar
        def _(scalar):
            for slab in range(4):
                scalar.wait_ge(smdb.h, slab + 1)
                scalar.activation(s_ba[0:64, :], s_ba[0:64, :], AF.Exp)
                act_sin_cos(scalar, s_bc[0:64, :], s_bd[0:64, :], 64).then_inc(smdbs.inc(), 1)

        @blk.sync
        def _(sync):
            for slab in range(4):
                sync.wait_ge(smdbv.h, slab + 1)
                sync.dma_start(out=AP(d2_d, slab * 4096, [[4 * 4096, 64], [1, 4096]]),
                               in_=s_d2s[:]).then_inc(smdbo.inc(16), 16)

        gB.close()
        g2.close()
        g0.close()
        g2a.close()
        g3 = ExitStack()
        s_ucs = sbuf("s_ucs", [128, 4 * 64 * 128], BF16, g3)
        s_a2 = sbuf("s_a2", [64, 128 * 4 * 64], BF16, g3)
        g3mA = ExitStack()
        s_pp2 = sbuf("s_pp2", [128, 8192], BF16, g3mA)
        g3m = ExitStack()
        s_aswp = sbuf("s_aswp", [64, 16384], BF16, g3m)
        s_sN = sbuf("s_sN", [64, 512], F32, g3m)
        s_sS = sbuf("s_sS", [64, 512], F32, g3m)
        s_rx = sbuf("s_rx", [64, 1024], F32, g3m)

        @blk.gpsimd
        def _(gp):
            gp.wait_ge(smu3[0].h, smu3[0].v)
            gp.wait_ge(smu3[1].h, smu3[1].v)
            gp.collective_compute(
                "AllGather", ALU.bypass,
                replica_groups=[list(range(NCORES))],
                ins=[u_loc.ap().opt()],
                outs=[u_gth.ap().opt()],
            ).then_inc(smcc.inc(), 1)

        @blk.sync
        def _(sync):
            sync.wait_ge(smcc.h, 1)
            sync.wait_ge(smc.h, smc.v)
            sync.wait_ge(smgo.h, 16)
            sync.wait_ge(smdbo.h, 64)
            sync.wait_ge(sm0[18].h, sm0[18].v)
            sync.wait_ge(sm0[19].h, 32)
            uoff = sync.value_load(s_uoff[0:1, 0:1])
            sync.dma_start(out=s_pp2[:], in_=AP(pp_d, 0, [[16384, 128], [1, 8192]])).then_inc(smucs.inc(16), 16)
            for r in range(NCORES):
                b, h = r // 2, r % 2
                sync.dma_start(
                    out=AP(s_ucs[:].tensor, b * 8192 + h * 4096,
                           [[32768, 128], [128, 32], [1, 128]]),
                    in_=AP(u_gth, uoff + r * T * D, [[D, 128], [128 * D, 32], [1, 128]]),
                ).then_inc(smucs.inc(16), 16)

        @blk.tensor
        def _(tensor):
            tensor.wait_ge(smucs.h, smucs.v)
            for pair in range(64):
                pe_bank = [p_a, p_b][pair % 2]
                if pair >= 2:
                    tensor.wait_ge(sme2.h, pair - 1)
                for two in range(2):
                    ch = 2 * pair + two
                    for ri in range(2):
                        mm = tensor.matmul(
                            pe_bank[32 * ri:32 * ri + 32, 256 * two:256 * two + 256],
                            s_pp2[:, ri * 4096 + ch * 32: ri * 4096 + ch * 32 + 32],
                            AP(s_ucs[:].tensor, ch, [[32768, 128], [128, 64], [8192, 4]]),
                            start=True, stop=True, tile_position=(0, 32 * ri))
                mm.then_inc(sme1.inc(), 1)

        @blk.scalar
        def _(scalar):
            for pair in range(64):
                pe_bank = [p_a, p_b][pair % 2]
                scalar.wait_ge(sme1.h, pair + 1)
                for two in range(2):
                    ch = 2 * pair + two
                    hp2, c2 = ch // 64, ch % 64
                    scalar.activation(
                        AP(s_a2[:].tensor, 16384 * hp2 + 4 * c2, [[32768, 64], [256, 64], [1, 4]]),
                        AP(pe_bank[:].tensor, 256 * two, [[512, 64], [1, 256]]),
                        AF.Identity)
                scalar.drain().then_inc(sme2.inc(), 1)

        smsw = sem("smsw")

        @blk.sync
        def _(sync):
            for hp in range(2):
                if hp == 0:
                    sync.wait_ge(sme2.h, 32)
                else:
                    sync.wait_ge(smra.h, 1)
                sync.dma_start(out=s_aswp[0:32, :],
                               in_=s_a2[32:64, 16384 * hp:16384 * hp + 16384]).then_inc(smsw.inc(16), 16)
                sync.dma_start(out=s_aswp[32:64, :],
                               in_=s_a2[0:32, 16384 * hp:16384 * hp + 16384]).then_inc(smsw.inc(16), 16)

        @blk.vector
        def _(vector):
            vector.wait_ge(sm0[8].h, 1)
            for hp in range(2):
                vector.wait_ge(smsw.h, 32 * (hp + 1))
                vector.memset(s_sN[:], 0.0)
                vector.memset(s_sS[:], 0.0)
                vector.drain()
                w1b = s_W1[:, 64 * hp:64 * hp + 64].unsqueeze(2).broadcast_to([64, 64, 4])
                w2b = s_W2[:, 64 * hp:64 * hp + 64].unsqueeze(2).broadcast_to([64, 64, 4])
                w3b = s_W3[:, 64 * hp:64 * hp + 64].unsqueeze(2).broadcast_to([64, 64, 4])
                t1 = s_rx[:, 0:256].rearrange("p (c b) -> p c b", b=4)
                t2 = s_rx[:, 256:512].rearrange("p (c b) -> p c b", b=4)
                t3 = s_rx[:, 512:768].rearrange("p (c b) -> p c b", b=4)
                t4 = s_rx[:, 768:1024].rearrange("p (c b) -> p c b", b=4)
                for J in range(64):
                    pv, cu = (J % 2), ((J + 1) % 2)
                    nP = s_sN[:, 256 * pv:256 * pv + 256].rearrange("p (c b) -> p c b", b=4)
                    nC = s_sN[:, 256 * cu:256 * cu + 256]
                    sP = s_sS[:, 256 * pv:256 * pv + 256].rearrange("p (c b) -> p c b", b=4)
                    sC = s_sS[:, 256 * cu:256 * cu + 256]
                    aj0 = 16384 * hp + 256 * J
                    vector.tensor_tensor(t1, nP, w1b, ALU.mult)
                    vector.tensor_tensor(t2, sP, w2b, ALU.mult)
                    vector.tensor_tensor(t3, nP, w3b, ALU.mult)
                    vector.tensor_tensor(t4, sP, w1b, ALU.mult)
                    if J >= 1:
                        vector.tensor_copy(s_a2[:, aj0 - 256:aj0],
                                           s_sN[:, 256 * pv:256 * pv + 256])
                    dr1 = vector.drain()
                    if hp == 1 and J == 32:
                        dr1.then_inc(smr1.inc(), 1)
                    vector.tensor_add(s_rx[:, 0:256], s_rx[:, 0:256], s_rx[:, 256:512])
                    vector.tensor_add(s_rx[:, 768:1024], s_rx[:, 768:1024], s_rx[:, 512:768])
                    vector.drain()
                    vector.tensor_tensor(nC, s_rx[:, 0:256], s_a2[:, aj0:aj0 + 256], ALU.add)
                    vector.tensor_tensor(sC, s_rx[:, 768:1024], s_aswp[:, 256 * J:256 * J + 256], ALU.add)
                    dr3 = vector.drain()
                    if J == 63:
                        dr3.then_inc([smra, smr1][hp].inc(), 1)

        g3m.close()
        g3mA.close()

        g3b = ExitStack()
        s_tp = [sbuf("s_tp0", [128, 32 * 128], BF16, g3b)]
        s_d2w = [sbuf("s_d2w0", [64, 4096], BF16, g3b)]
        s_yst = sbuf("s_yst", [128, 4 * 32 * 128], BF16, g3b)

        smtl = sem("smtl")
        smd1 = sem("smd1")
        smd2 = sem("smd2")
        smyo2 = sem("smyo2")

        @blk.sync
        def _(sync):
            sync.wait_ge(sm0[18].h, sm0[18].v)
            sync.wait_ge(smdbo.h, smdbo.v)
            for half in range(2):
                for slab in range(4):
                    idx = half * 4 + slab
                    pp = 0
                    if idx >= 1:
                        sync.wait_ge(smd1.h, 8 * idx)
                    sync.dma_start(out=s_tp[pp][:].rearrange("p (t c) -> p t c", c=32),
                                   in_=AP(k2r, 128 * 128 + 32 * slab, [[128, 128], [-128, 128], [1, 32]]),
                                   allow_slow_non_contiguous=True).then_inc(smtl.inc(16), 16)
                    sync.dma_start(out=s_d2w[pp][:],
                                   in_=AP(d2_d, slab * 4096, [[4 * 4096, 64], [1, 4096]])).then_inc(smtl.inc(16), 16)
                for b4 in range(4):
                    sync.wait_ge(smytr.h, half * 64 + 16 * (b4 + 1))
                    sync.dma_start(
                        out=AP(y_loc, b4 * 8192 + half * 4096, [[4 * 64 * 128, 128], [1, 4096]]),
                        in_=s_yst[:, b4 * 4096:b4 * 4096 + 4096],
                    ).then_inc(smyo2.inc(16), 16)

        @blk.tensor
        def _(tensor):
            for half in range(2):
                tensor.wait_ge(smr1.h, half + 1)
                for slab in range(4):
                    idx = half * 4 + slab
                    pp = 0
                    tensor.wait_ge(smtl.h, 32 * (idx + 1))
                    nJ = 31 if half == 0 else 32
                    oJ = 1 if half == 0 else 0
                    sJ0 = 0 if half == 0 else 31
                    for grp in range(8):
                        pe_bank = [p_c, p_d][grp % 2]
                        gi = idx * 8 + grp
                        if gi >= 2:
                            tensor.wait_ge(smd2.h, gi - 1)
                        for cp in range(4):
                            cl = grp * 4 + cp
                            ch = 32 * slab + cl
                            tensor.matmul(
                                pe_bank[:, 128 * cp:128 * cp + 128].rearrange("p (j b) -> p j b", b=4),
                                AP(s_tp[pp][:].tensor, cl, [[4096, 128], [32, 128]]),
                                AP(s_ucs[:].tensor, ch + half * 32 * 128,
                                   [[32768, 128], [128, 32], [8192, 4]]),
                                start=True, stop=False)
                            hp2, c2 = ch // 64, ch % 64
                            mm = tensor.matmul(
                                AP(pe_bank[:].tensor, 128 * cp + 4 * oJ, [[512, 128], [1, 4 * nJ]]),
                                s_d2w[pp][:, 128 * cl:128 * cl + 128],
                                AP(s_a2[:].tensor, 16384 * hp2 + 4 * c2 + 256 * sJ0,
                                   [[32768, 64], [256, nJ], [1, 4]]),
                                start=False, stop=True)
                        mm.then_inc(smd1.inc(), 1)
                tensor.wait_ge(smd2.h, 32 * (half + 1))
                for tt in range(128):
                    gi2 = half * 128 + tt
                    b4, Jl = tt // 32, tt % 32
                    if gi2 >= 2:
                        tensor.wait_ge(smytr.h, gi2 // 2)
                    tensor.transpose([p_tr, p_tr2][tt % 2][:],
                                     s_yst[:, b4 * 4096 + Jl * 128:b4 * 4096 + Jl * 128 + 128],
                                     s_id[:]).then_inc(smyt1.inc(), 1)

        @blk.scalar
        def _(scalar):
            for idx in range(8):
                half, slab = idx // 4, idx % 4
                for grp in range(8):
                    pe_bank = [p_c, p_d][grp % 2]
                    gi = idx * 8 + grp
                    if gi == 32:
                        scalar.wait_ge(smyo2.h, 64)
                    scalar.wait_ge(smd1.h, gi + 1)
                    scalar.activation(
                        AP(s_yst[:].tensor, 32 * slab + 4 * grp,
                           [[16384, 128], [1, 4], [128, 32], [4096, 4]]),
                        AP(pe_bank[:].tensor, 0, [[512, 128], [128, 4], [4, 32], [1, 4]]),
                        AF.Gelu).then_inc(smd2.inc(), 1)

        @blk.vector
        def _(vector):
            for half in range(2):
                for tt in range(0, 128, 2):
                    gi2 = half * 128 + tt
                    vector.wait_ge(smyt1.h, gi2 + 2)
                    for u2 in range(2):
                        b4, Jl = (tt + u2) // 32, (tt + u2) % 32
                        vector.tensor_copy(
                            s_yst[:, b4 * 4096 + Jl * 128:b4 * 4096 + Jl * 128 + 128],
                            [p_tr, p_tr2][(tt + u2) % 2][:])
                    vector.drain().then_inc(smytr.inc(), 1)

        @blk.gpsimd
        def _(gp):
            gp.wait_ge(smyo2.h, smyo2.v)
            gp.wait_ge(smkv[0].h, 256)
            gp.collective_compute(
                "AllGather", ALU.bypass,
                replica_groups=[list(range(NCORES))],
                ins=[y_loc.ap().opt()],
                outs=[y_gth.ap().opt()],
            ).then_inc(smcc.inc(), 1)

        g3b.close()
        g3.close()

        g45 = ExitStack()
        gluT = [sbuf(f"gluT{k}", [128, 2 * D], BF16, g45) for k in range(8)]
        outT = [sbuf(f"outT{k}", [128, D], BF16, g45) for k in range(8)]
        g4 = ExitStack()
        s_key = [sbuf(f"s_key{e}", [128, 8 * 256], BF16, g4) for e in range(8)]
        s_val = [sbuf(f"s_val{e}", [128, 2 * D], BF16, g4) for e in range(8)]
        s_sal = sbuf("s_sal", [128, 16 * 512], BF16, g4)
        s_grep = sbuf("s_grep", [128, 8 * 512], BF16, g4)
        s_xw2 = [[sbuf(f"s_xw2{pp}_{k}", [128, 512], BF16, g4) for k in range(8)] for pp in range(2)]
        s_fst = [sbuf(f"s_fst{pp}", [128, D], F32, g4) for pp in range(2)]

        @blk.sync
        def _(sync):
            sync.wait_ge(smcastC.h, 32)
            sync.wait_ge(smcastB.h, 32)
            sync.wait_ge(smgo.h, 16)
            sync.wait_ge(smxst.h, 512)
            sync.wait_ge(smyo2.h, 128)
            def kv_loads(sync, half):
                for e in range(8):
                    eg = 8 * half + e
                    sync.dma_start(out=s_key[e][:].rearrange("p (k h2) -> p k h2", h2=256),
                                   in_=AP(keysbf, eg * D * 256, [[256, 128], [128 * 256, 8], [1, 256]])).then_inc(smkv[half].inc(16), 16)
                    sync.dma_start(out=s_val[e][:].rearrange("p (k d) -> p k d", d=D),
                                   in_=AP(valsbf, eg * 256 * D, [[D, 128], [128 * D, 2], [1, D]])).then_inc(smkv[half].inc(16), 16)
            def xw_loads(sync, half, s):
                slab = half * 8 + s
                pp = s % 2
                for k in range(8):
                    sync.dma_start(
                        out=s_xw2[pp][k][:],
                        in_=AP(xnT_d, (128 * k) * T + 512 * s, [[T, 128], [1, 512]])).then_inc(smxw2[pp].inc(16), 16)
                for e in range(8):
                    eg = 8 * half + e
                    sync.dma_start(out=s_grep[:, 512 * e:512 * e + 512],
                                   in_=AP(gT_d, eg * T + 512 * s, [[0, 128], [1, 512]])).then_inc(smgx.inc(16), 16)
            xw_loads(sync, 0, 0)
            kv_loads(sync, 0)
            for k in range(8):
                sync.dma_start(out=gluT[k][:],
                               in_=AP(glubf, 128 * k * 2 * D, [[2 * D, 128], [1, 2 * D]])).then_inc(smwt3.inc(16), 16)
                sync.dma_start(out=outT[k][:],
                               in_=AP(outbf, 128 * k * D, [[D, 128], [1, D]])).then_inc(smwt3.inc(16), 16)
            def store_slab(sync, slab):
                for cc in range(4):
                    chunk = slab * 4 + cc
                    sync.wait_ge(smv2.h, chunk + 1)
                    dst = [ffnP, ffnQ][(chunk // 32)]
                    sync.dma_start(out=AP(dst, (chunk % 32) * 128 * D, [[D, 128], [1, D]]),
                                   in_=s_fst[chunk % 2][:]).then_inc(smv3[chunk % 2].inc(16), 16)

            for half in range(2):
                if half == 1:
                    store_slab(sync, 7)
                    sync.wait_ge(smv1.h, 32)
                    kv_loads(sync, 1)
                for s in range(8):
                    slab = half * 8 + s
                    if s >= 1:
                        store_slab(sync, slab - 1)
                    if slab != 0:
                        xw_loads(sync, half, s)
            store_slab(sync, 15)

        @blk.tensor
        def _(tensor):
            for half in range(2):
                for s in range(8):
                    slab = half * 8 + s
                    pp = s % 2
                    tensor.wait_ge(smxw2[pp].h, 16 * 8 * (slab // 2 + 1))
                    for e in range(8):
                        if s == 0:
                            tensor.wait_ge(smkv[half].h, 32 * (e + 1))
                        if slab * 8 + e >= 1:
                            tensor.wait_ge(smk2.h, 2 * (slab * 8 + e))
                        for hh in range(2):
                            ps = [p_a, p_b][hh]
                            for k in range(8):
                                mm = tensor.matmul(ps[:], s_key[e][:, 256 * k + 128 * hh:256 * k + 128 * hh + 128],
                                                   s_xw2[pp][k][:], start=(k == 0), stop=(k == 7))
                            mm.then_inc(smk1.inc(), 1)
                    for cc in range(4):
                        chunk = slab * 4 + cc
                        if chunk >= 1:
                            tensor.wait_ge(smv2.h, chunk)
                        for dd in range(2):
                            po = [p_c, p_d][dd]
                            first = True
                            for e in range(8):
                                if dd == 0 and cc == 0:
                                    tensor.wait_ge(smk3.h, slab * 8 + e + 1)
                                for hh in range(2):
                                    mm = tensor.matmul(
                                        po[:],
                                        s_sal[:, (e * 2 + hh) * 512 + 128 * cc:(e * 2 + hh) * 512 + 128 * cc + 128],
                                        s_val[e][:, hh * D + 512 * dd:hh * D + 512 * dd + 512],
                                        start=first, stop=(e == 7 and hh == 1))
                                    first = False
                        mm.then_inc(smv1.inc(), 1)

        @blk.scalar
        def _(scalar):
            for half in range(2):
                for s in range(8):
                    slab = half * 8 + s
                    for e in range(8):
                        if slab >= 1:
                            scalar.wait_ge(smv1.h, 4 * slab)
                        for hh in range(2):
                            idx = (slab * 8 + e) * 2 + hh
                            scalar.wait_ge(smk1.h, idx + 1)
                            ps = [p_a, p_b][hh]
                            scalar.activation(s_sal[:, (e * 2 + hh) * 512:(e * 2 + hh) * 512 + 512],
                                              ps[:], AF.Relu).then_inc(smk2.inc(), 1)

        @blk.vector
        def _(vector):
            for half in range(2):
                for s in range(8):
                    slab = half * 8 + s
                    vector.wait_ge(smgx.h, 16 * 8 * (slab + 1))
                    for e in range(8):
                        vector.wait_ge(smk2.h, 2 * (slab * 8 + e + 1))
                        ge = s_grep[:, 512 * e:512 * e + 512]
                        vector.tensor_tensor(s_sal[:, (e * 2) * 512:(e * 2) * 512 + 512],
                                             s_sal[:, (e * 2) * 512:(e * 2) * 512 + 512], ge, ALU.mult)
                        vector.drain()
                        vector.tensor_tensor(s_sal[:, (e * 2 + 1) * 512:(e * 2 + 1) * 512 + 512],
                                             s_sal[:, (e * 2 + 1) * 512:(e * 2 + 1) * 512 + 512], ge, ALU.mult)
                        vector.drain().then_inc(smk3.inc(), 1)
                    for cc in range(4):
                        chunk = slab * 4 + cc
                        pp3 = chunk % 2
                        vector.wait_ge(smv1.h, chunk + 1)
                        if chunk >= 2:
                            vector.wait_ge(smv3[chunk % 2].h, 16 * (chunk // 2))
                        if half == 0:
                            vector.tensor_copy(s_fst[pp3][:, 0:512], p_c[:])
                            vector.tensor_copy(s_fst[pp3][:, 512:1024], p_d[:])
                        else:
                            vector.tensor_add(s_fst[pp3][:, 0:512], p_c[:], s_outb[:, 0:512])
                            vector.tensor_add(s_fst[pp3][:, 512:1024], p_d[:], s_outb[:, 512:1024])
                        vector.drain().then_inc(smv2.inc(), 1)

        g4.close()

        g5 = ExitStack()
        y_fm = [sbuf(f"y_fm{r}", [128, T], BF16, g5) for r in range(8)]
        s_ag = [sbuf(f"s_ag{o}", [128, 512], BF16, g5) for o in range(16)]
        s_y2 = [sbuf(f"s_y2{k}", [128, 512], BF16, g5) for k in range(8)]
        s_xr = [sbuf(f"s_xr{pp}", [128, D], F32, g5) for pp in range(2)]
        s_fp = [sbuf(f"s_fp{pp}", [128, D], F32, g5) for pp in range(2)]
        s_fq = [sbuf(f"s_fq{pp}", [128, D], F32, g5) for pp in range(2)]
        s_fin = [sbuf(f"s_fin{pp}", [128, D], F32, g5) for pp in range(2)]
        s_aff = sbuf("s_aff", [128, 4], F32, g5)

        @blk.sync
        def _(sync):
            sync.dma_start(out=s_glb[:], in_=AP(glu_b, 0, [[1, 128], [128, 16]]),
                           allow_slow_non_contiguous=True).then_inc(smglb.inc(16), 16)
            sync.wait_ge(smcc.h, 2)
            sync.wait_ge(smcastB.h, 32)
            sync.wait_ge(smv3[0].h, smv3[0].v)
            sync.wait_ge(smv3[1].h, smv3[1].v)
            yoff = sync.value_load(s_yoff[0:1, 0:1])
            for r in range(8):
                sync.dma_start(
                    out=y_fm[r][:],
                    in_=AP(y_gth, yoff + r * 128 * 32768, [[32768, 128], [1, 4096]])).then_inc(smyf.inc(16), 16)
            for chunk in range(2):
                p9 = chunk % 2
                sync.dma_start(out=s_xr[p9][:],
                               in_=AP(x_in, chunk * 128 * D, [[D, 128], [1, D]])).then_inc(smf1[p9].inc(16), 16)
                sync.dma_start(out=s_fp[p9][:],
                               in_=AP(ffnP, chunk * 128 * D, [[D, 128], [1, D]])).then_inc(smf2[p9].inc(16), 16)
                sync.dma_start(out=s_fq[p9][:],
                               in_=AP(ffnQ, chunk * 128 * D, [[D, 128], [1, D]])).then_inc(smf3[p9].inc(16), 16)
            for chunk in range(32):
                p9 = chunk % 2
                sync.wait_ge(smo2.h, chunk + 1)
                sync.dma_start(out=AP(out_t, chunk * 128 * D, [[D, 128], [1, D]]),
                               in_=s_fin[p9][:]).then_inc(smout[p9].inc(16), 16)
                if chunk + 2 < 32:
                    sync.dma_start(out=s_xr[p9][:],
                                   in_=AP(x_in, (chunk + 2) * 128 * D, [[D, 128], [1, D]])).then_inc(smf1[p9].inc(16), 16)
                    sync.dma_start(out=s_fp[p9][:],
                                   in_=AP(ffnP, (chunk + 2) * 128 * D, [[D, 128], [1, D]])).then_inc(smf2[p9].inc(16), 16)
                    sync.dma_start(out=s_fq[p9][:],
                                   in_=AP(ffnQ, (chunk + 2) * 128 * D, [[D, 128], [1, D]])).then_inc(smf3[p9].inc(16), 16)
            sync.wait_ge(smout[0].h, smout[0].v)
            sync.wait_ge(smout[1].h, smout[1].v)

        @blk.tensor
        def _(tensor):
            tensor.wait_ge(smyf.h, smyf.v)
            tensor.wait_ge(smwt3.h, smwt3.v)
            for s in range(8):
                for o in range(16):
                    pg = [p_a, p_b][o % 2]
                    if s * 16 + o >= 2:
                        tensor.wait_ge(smg2.h, s * 16 + o - 1)
                    for r in range(8):
                        mm = tensor.matmul(pg[:], gluT[r][:, 128 * o:128 * o + 128],
                                           y_fm[r][:, 512 * s:512 * s + 512],
                                           start=(r == 0), stop=(r == 7))
                    mm.then_inc(smg1.inc(), 1)
                for cc in range(4):
                    chunk = s * 4 + cc
                    if chunk >= 1:
                        tensor.wait_ge(smo2.h, chunk)
                    for k in range(8):
                        tensor.wait_ge(smg3.h, s * 8 + k + 1)
                        lhs = s_y2[k][:, 128 * cc:128 * cc + 128]
                        tensor.matmul(p_c[:], lhs, outT[k][:, 0:512], start=(k == 0), stop=(k == 7))
                        mm = tensor.matmul(p_d[:], lhs, outT[k][:, 512:1024], start=(k == 0), stop=(k == 7))
                    mm.then_inc(smo1.inc(), 1)

        @blk.scalar
        def _(scalar):
            scalar.wait_ge(smglb.h, 16)
            for s in range(8):
                for o in range(16):
                    pg = [p_a, p_b][o % 2]
                    scalar.wait_ge(smg1.h, s * 16 + o + 1)
                    if s >= 1:
                        scalar.wait_ge(smg3.h, 8 * s)
                    if o < 8:
                        scalar.activation(s_ag[o][:], pg[:], AF.Identity,
                                          bias=s_glb[:, o:o + 1]).then_inc(smg2.inc(), 1)
                    else:
                        scalar.activation(s_ag[o][:], pg[:], AF.Sigmoid,
                                          bias=s_glb[:, o:o + 1]).then_inc(smg2.inc(), 1)

        @blk.vector
        def _(vector):
            for s in range(8):
                for k in range(8):
                    vector.wait_ge(smg2.h, s * 16 + 16)
                    if s >= 1:
                        vector.wait_ge(smo1.h, 4 * s)
                    vector.tensor_mul(s_y2[k][:], s_ag[k][:], s_ag[k + 8][:])
                    vector.drain().then_inc(smg3.inc(), 1)
                for cc in range(4):
                    chunk = s * 4 + cc
                    pp4 = chunk % 2
                    vector.wait_ge(smo1.h, chunk + 1)
                    vector.wait_ge(smf1[pp4].h, 16 * (chunk // 2 + 1))
                    vector.wait_ge(smf2[pp4].h, 16 * (chunk // 2 + 1))
                    vector.wait_ge(smf3[pp4].h, 16 * (chunk // 2 + 1))
                    if chunk >= 2:
                        vector.wait_ge(smout[pp4].h, 16 * (chunk // 2))
                    vector.tensor_scalar(s_aff[:, 0:1], s_rs[:, chunk:chunk + 1], 1.0, None, ALU.add)
                    vector.tensor_mul(s_aff[:, 1:2], s_mu[:, chunk:chunk + 1], s_rs[:, chunk:chunk + 1])
                    vector.drain()
                    vector.tensor_scalar(s_aff[:, 1:2], s_aff[:, 1:2], -1.0, None, ALU.mult)
                    vector.drain()
                    vector.tensor_scalar(s_fin[pp4][:], s_xr[pp4][:], s_aff[:, 0:1], s_aff[:, 1:2],
                                         ALU.mult, ALU.add)
                    vector.drain()
                    vector.tensor_add(s_fin[pp4][:, 0:512], s_fin[pp4][:, 0:512], p_c[:])
                    vector.tensor_add(s_fin[pp4][:, 512:1024], s_fin[pp4][:, 512:1024], p_d[:])
                    vector.drain()
                    vector.tensor_add(s_fin[pp4][:], s_fin[pp4][:], s_fp[pp4][:])
                    vector.drain()
                    vector.tensor_add(s_fin[pp4][:], s_fin[pp4][:], s_fq[pp4][:])
                    vector.drain().then_inc(smo2.inc(), 1)

        g5.close()
        g45.close()

    es.close()
    return nc


_NC_CACHE = None


def kernel(L0_attn_x, L0_attn_in_W, L0_attn_in_b, L0_attn_log_dt, L0_attn_C,
           L0_attn_log_A_real, L0_attn_A_imag, L0_attn_D_skip, L0_attn_glu_W,
           L0_attn_glu_b, L0_attn_out_W, L0_attn_out_b, L0_ffn_expert_sel,
           L0_ffn_keys, L0_ffn_values, L0_ffn_route_scale):
    global _NC_CACHE
    if _NC_CACHE is None:
        _NC_CACHE = build_nc()
    nc = _NC_CACHE

    x = np.ascontiguousarray(np.asarray(L0_attn_x, dtype=np.float32))
    in_WT = np.ascontiguousarray(np.asarray(L0_attn_in_W, np.float32).T)
    eselT_h = np.ascontiguousarray(np.asarray(L0_ffn_expert_sel, np.float32).T)
    glu_WT = np.ascontiguousarray(np.asarray(L0_attn_glu_W, np.float32).T)
    out_WT = np.ascontiguousarray(np.asarray(L0_attn_out_W, np.float32).T)
    ident = np.eye(128, dtype=ml_dtypes.bfloat16)
    t_idx = np.arange(128, dtype=np.float32)
    trQ = np.tile((t_idx + 1.0)[None, :], (128, 1))
    trK = np.tile((127.0 - t_idx)[None, :], (128, 1))
    trP = (127.0 - t_idx)[:, None].copy()
    trP2 = np.stack([128.0 - t_idx, 126.0 - t_idx], axis=1).astype(np.float32)

    in_maps = []
    for c in range(NCORES):
        b, h = c // 2, c % 2
        cs = slice(128 * c, 128 * (c + 1))
        in_maps.append(dict(
            x_in=np.ascontiguousarray(x[b, h * 4096:(h + 1) * 4096, :]),
            in_W=in_WT,
            in_b=np.asarray(L0_attn_in_b, np.float32),
            log_dt=np.ascontiguousarray(np.asarray(L0_attn_log_dt, np.float32)[cs]),
            C_p=np.ascontiguousarray(np.asarray(L0_attn_C, np.float32)[cs]),
            lar_t=np.ascontiguousarray(np.asarray(L0_attn_log_A_real, np.float32)[cs]),
            aim_t=np.ascontiguousarray(np.asarray(L0_attn_A_imag, np.float32)[cs]),
            dsk_t=np.ascontiguousarray(np.asarray(L0_attn_D_skip, np.float32)[cs]),
            glu_W=glu_WT,
            glu_b=np.asarray(L0_attn_glu_b, np.float32),
            out_W=out_WT,
            out_b=np.asarray(L0_attn_out_b, np.float32),
            esel=eselT_h,
            keys_t=np.asarray(L0_ffn_keys, np.float32),
            vals_t=np.asarray(L0_ffn_values, np.float32),
            rscale=np.asarray(L0_ffn_route_scale, np.float32),
            ident=ident, trQ_t=trQ, trK_t=trK, trP_t=trP, trP2_t=trP2,
            uoff_t=np.array([128 * c], np.int32),
            yoff_t=np.array([b * 8192 + h * 4096], np.int32),
        ))

    res = run_bass_kernel_spmd(nc, in_maps, core_ids=list(range(NCORES)))
    out = np.empty((4, 8192, 1024), np.float32)
    for c in range(NCORES):
        b, h = c // 2, c % 2
        out[b, h * 4096:(h + 1) * 4096, :] = np.asarray(res.results[c]["out"])
    return out
